# version 12
# baseline (speedup 1.0000x reference)
import sys, time
import numpy as np
import ml_dtypes
import concourse.bass as bass
import concourse.mybir as mybir
from concourse.bass import IndirectOffsetOnAxis
from concourse.bass_utils import run_bass_kernel_spmd

F32 = mybir.dt.float32
BF16 = mybir.dt.bfloat16
I32 = mybir.dt.int32
AF = mybir.ActivationFunctionType
ALU = mybir.AluOpType
AX = mybir.AxisListType
BF = ml_dtypes.bfloat16

NCORE = 8
NLOC = 4096
D = 1024
DS = 256
H = 4096
E = 8
CAP = 10240
T = 512
NCH = NLOC // T
SUB = 128
PMROWS = 82048
C1x2 = 1.5957691216057308
SQ2C2 = 0.2671232181501769
EPS = 1e-5
INV_D = 1.0 / D
TWO_PI = 6.283185307179586
INV_2PI = 1.0 / TWO_PI
PI = 3.141592653589793

ENGS = ("pe", "dve", "act", "pool", "sync")


class Sem:
    __slots__ = ("h", "n")
    def __init__(self, h):
        self.h = h
        self.n = 0


class Builder:
    def __init__(self, nc):
        self.nc = nc
        self.streams = {e: [] for e in ENGS}
        self._guards = []
        self.sems = {}

    def sem(self, name):
        if name not in self.sems:
            g = self.nc.semaphore(name)
            self.sems[name] = Sem(g.__enter__())
            self._guards.append(g)
        return self.sems[name]

    def op(self, eng, fn, inc=None, incv=1):
        if inc is not None:
            inc.n += incv
            self.streams[eng].append(lambda E, fn=fn, s=inc, d=incv: fn(E).then_inc(s.h, d))
        else:
            self.streams[eng].append(lambda E, fn=fn: fn(E))

    def wait(self, eng, s, v=None):
        v = s.n if v is None else v
        self.streams[eng].append(lambda E, s=s, v=v: E.wait_ge(s.h, v))

    def barrier(self, name, extra_waits=()):
        s = self.sem("bar_" + name)
        for eng, sem, val in extra_waits:
            self.wait(eng, sem, val)
        tgt = s.n + len(ENGS)
        for e in ENGS:
            self.op(e, lambda E: E.sem_inc(s.h, 1), None)
        s.n = tgt
        for e in ENGS:
            self.wait(e, s, tgt)


class Scope:
    def __init__(self, nc):
        self.nc = nc
        self.gs = []

    def sb(self, name, shape, dt):
        g = self.nc.sbuf_tensor(name, shape, dt)
        h = g.__enter__()
        self.gs.append(g)
        return h

    def ps(self, name, shape, dt):
        g = self.nc.psum_tensor(name, shape, dt)
        h = g.__enter__()
        self.gs.append(g)
        return h

    def close(self):
        for g in reversed(self.gs):
            g.__exit__(None, None, None)
        self.gs = []


def build_program(debug_level=0):
    nc = bass.Bass("TRN2", target_bir_lowering=False)
    b = Builder(nc)
    ap = {}

    def din(name, shape, dt=F32):
        ap[name] = nc.dram_tensor(name, list(shape), dt, kind="ExternalInput").ap()
        return ap[name]

    xT = din("xT", [D, NLOC])
    BreT = din("BreT", [D, DS]); BimT = din("BimT", [D, DS])
    CreT = din("CreT", [DS, D]); CimT = din("CimT", [DS, D])
    DT_in = din("DT", [D, D])
    W1_in = din("W1", [D, H]); W2_in = din("W2", [H, D])
    wg_in = din("wg", [D, E])
    fc1T_in = din("fc1T", [D, H]); fc2_in = din("fc2", [H, D])
    fc1b_in = din("fc1b", [128, 32])
    fc2b_in = din("fc2b", [1, D])
    lng_in = din("lng", [1, D]); lnb_in = din("lnb", [1, D])
    nu_in = din("nu", [128, 2]); th_in = din("th", [128, 2])
    rl_in = din("rl", [1, 1])
    cmask_in = din("cmask", [16, 2])
    tok0_in = din("tok0", [1, 1])

    out = nc.dram_tensor("out", [NLOC, D], F32, kind="ExternalOutput").ap()

    d_h = nc.dram_tensor("d_h", [H, NLOC], BF16, kind="Internal")
    d_w2bf = nc.dram_tensor("d_w2bf", [4096, 1024], BF16, kind="Internal")
    d_x1t = nc.dram_tensor("d_x1t", [NLOC, D], F32, kind="Internal")
    d_xn2bf = nc.dram_tensor("d_xn2bf", [NLOC, D], BF16, kind="Internal")
    d_xn2all = nc.dram_tensor("d_xn2all", [NCORE * NLOC, D], BF16, kind="Internal")
    d_cnt = nc.dram_tensor("d_cnt", [2, 8], F32, kind="Internal")
    d_cntall = nc.dram_tensor("d_cntall", [16, 8], F32, kind="Internal")
    d_pm = nc.dram_tensor("d_pm", [PMROWS, 1], F32, kind="Internal")
    d_pmy = nc.dram_tensor("d_pmy", [CAP, 1], F32, kind="Internal")
    d_disp = nc.dram_tensor("d_disp", [CAP, D], BF16, kind="Internal")
    d_y = nc.dram_tensor("d_y", [CAP, D], BF16, kind="Internal")
    d_yall = nc.dram_tensor("d_yall", [E * CAP, D], BF16, kind="Internal")
    d_fb = nc.dram_tensor("d_fb", [8, 64], F32, kind="Internal")

    dbg = {}
    if debug_level >= 1:
        dbg["o_xn"] = nc.dram_tensor("o_xn", [D, NLOC], BF16, kind="ExternalOutput").ap()
        dbg["o_h"] = nc.dram_tensor("o_h", [H, NLOC], BF16, kind="ExternalOutput").ap()
        dbg["o_bu"] = nc.dram_tensor("o_bu", [128, 2048], BF16, kind="ExternalOutput").ap()
        dbg["o_st"] = nc.dram_tensor("o_st", [128, 2048], BF16, kind="ExternalOutput").ap()
        dbg["o_y"] = nc.dram_tensor("o_y", [128, 4096], BF16, kind="ExternalOutput").ap()
        dbg["o_tab"] = nc.dram_tensor("o_tab", [128, 1032], F32, kind="ExternalOutput").ap()
    if debug_level >= 2:
        dbg["o_x1t"] = nc.dram_tensor("o_x1t", [NLOC, D], F32, kind="ExternalOutput").ap()
        dbg["o_xn2bf"] = nc.dram_tensor("o_xn2bf", [NLOC, D], BF16, kind="ExternalOutput").ap()
        dbg["o_rt"] = nc.dram_tensor("o_rt", [128, 320], F32, kind="ExternalOutput").ap()
        dbg["o_ao"] = nc.dram_tensor("o_ao", [128, 4096], F32, kind="ExternalOutput").ap()
    if debug_level >= 3:
        dbg["o_y3"] = nc.dram_tensor("o_y3", [CAP, D], BF16, kind="ExternalOutput").ap()
        dbg["o_disp"] = nc.dram_tensor("o_disp", [CAP, D], BF16, kind="ExternalOutput").ap()
        dbg["o_pmy"] = nc.dram_tensor("o_pmy", [CAP, 1], F32, kind="ExternalOutput").ap()
        dbg["o_posd"] = nc.dram_tensor("o_posd", [128, 64], F32, kind="ExternalOutput").ap()
        dbg["o_cnt"] = nc.dram_tensor("o_cnt", [16, 8], F32, kind="ExternalOutput").ap()
        dbg["o_cb2"] = nc.dram_tensor("o_cb2", [128, 2], F32, kind="ExternalOutput").ap()
        dbg["o_gkd"] = nc.dram_tensor("o_gkd", [128, 64], F32, kind="ExternalOutput").ap()
        dbg["o_xn2"] = nc.dram_tensor("o_xn2", [128, 4096], F32, kind="ExternalOutput").ap()
        dbg["o_z0"] = nc.dram_tensor("o_z0", [128, 512], F32, kind="ExternalOutput").ap()
        dbg["o_hrb"] = nc.dram_tensor("o_hrb", [128, 16384], BF16, kind="ExternalOutput").ap()
        dbg["o_w2"] = nc.dram_tensor("o_w2", [128, 2048], BF16, kind="ExternalOutput").ap()

    ident_f = din("ident_f", [128, 128], F32)
    ident_b = din("ident_b", [128, 128], BF16)
    tri_b = din("tri_b", [128, 128], BF16)

    A = nc.alloc_sbuf_tensor
    sb_identf = A("sb_identf", [128, 128], F32)
    sb_identb = A("sb_identb", [128, 128], BF16)
    sb_trib = A("sb_trib", [128, 128], BF16)
    sb_onescb = A("sb_onescb", [128, 1], BF16)
    sb_onesrf = A("sb_onesrf", [1, 128], F32)
    sb_ones128 = A("sb_ones128", [128, 128], F32)
    sb_eps = A("sb_eps", [1, 1], F32)
    sb_iota8 = A("sb_iota8", [128, 8], F32)
    sb_i32 = A("sb_i32", [128, 256], I32)
    sb_wg = A("sb_wg", [128, 64], BF16)
    sb_fc1b = A("sb_fc1b", [128, 32], F32)
    sb_sig = A("sb_sig", [128, 1], F32)
    sb_rl = A("sb_rl", [1, 1], F32)
    sb_tok0 = A("sb_tok0", [128, 1], F32)
    sb_cmask = A("sb_cmask", [16, 2], F32)
    sb_rows = A("sb_rows", [1, 1536], F32)
    sb_oh1 = A("sb_oh1", [128, 256], F32)
    sb_oh2 = A("sb_oh2", [128, 256], F32)
    sb_i1 = A("sb_i1", [128, 32], F32)
    sb_i2 = A("sb_i2", [128, 32], F32)
    sb_g1 = A("sb_g1", [128, 32], F32)
    sb_g2 = A("sb_g2", [128, 32], F32)
    sb_pre1 = A("sb_pre1", [128, 32], F32)
    sb_pre2 = A("sb_pre2", [128, 32], F32)
    sb_ct = A("sb_ct", [128, 64], F32)
    sb_tb = A("sb_tb", [128, 64], F32)
    sb_pos = A("sb_pos", [128, 64], F32)
    sb_gk = A("sb_gk", [128, 64], F32)
    sb_poi = A("sb_poi", [128, 64], I32)
    sb_tokid = A("sb_tokid", [128, 32], F32)
    sb_brow = A("sb_brow", [1, 512], F32)
    sb_iotaE = A("sb_iotaE", [128, 256], F32)
    sb_m1 = A("sb_m1", [128, 32], F32)
    sb_m2 = A("sb_m2", [128, 32], F32)
    rtA2 = A("rtA2", [128, 256], F32)
    sb_zero = A("sb_zero", [128, 129], F32)
    scW = Scope(nc)
    WBIG = scW.sb("WBIG", [128, 32768], BF16)
    WBIG2 = scW.sb("WBIG2", [128, 32768], BF16)
    XB, XSQ, STB, YB, HST = 16384, 17408, 18432, 20480, 24576

    sc1 = Scope(nc)
    PS = [sc1.ps(f"ps{i}", [128, 512], F32) for i in range(8)]
    tabPr = sc1.sb("tabPr", [128, 256], F32); tabPi = sc1.sb("tabPi", [128, 256], F32)
    tabQr = sc1.sb("tabQr", [128, 256], F32); tabQi = sc1.sb("tabQi", [128, 256], F32)
    sb_lam = sc1.sb("sb_lam", [128, 4], F32)
    sb_nt = sc1.sb("sb_nt", [128, 8], F32)
    sb_carry = sc1.sb("sb_carry", [128, 4], F32)
    sb_cs = sc1.sb("sb_cs", [128, 4], F32)
    sb_cst = sc1.sb("sb_cst", [128, 4], F32)
    scA_r = sc1.sb("scA_r", [128, 256], F32); scA_i = sc1.sb("scA_i", [128, 256], F32)
    scS_r = sc1.sb("scS_r", [128, 256], F32); scS_i = sc1.sb("scS_i", [128, 256], F32)
    scT_r = sc1.sb("scT_r", [128, 256], F32); scT_i = sc1.sb("scT_i", [128, 256], F32)
    sctmp = sc1.sb("sctmp", [128, 256], F32)
    SC0, SC1, SC2, SC3, SC4, SC5 = scA_r, scA_i, scS_r, scS_i, scT_r, scT_i
    sb_x = sc1.sb("sb_x", [128, 8192], BF16)
    sb_xnb1 = sc1.sb("sb_xnb1", [128, 4096], BF16)
    sb_st1 = sc1.sb("sb_st1", [128, 2048], BF16)
    sb_nmb = sc1.sb("sb_nmb", [128, 512], BF16)
    sb_rb = sc1.sb("sb_rb", [128, 512], BF16)
    sb_xnb = sc1.sb("sb_xnb", [128, 4096], BF16)
    sb_bu = sc1.sb("sb_bu", [128, 2048], BF16)
    sb_q = sc1.sb("sb_q", [128, 1024], BF16)
    sb_t2 = sc1.sb("sb_t2", [128, 1024], BF16)
    sb_sg2 = sb_q


    sLD = b.sem("sLD"); sG = b.sem("sG")
    sH0 = b.sem("sH0"); sW2A = b.sem("sW2A"); sW2B = b.sem("sW2B"); sW2C = b.sem("sW2C")
    sH0S = [b.sem("sH0S0"), b.sem("sH0S1")]
    sPE = b.sem("sPE"); sDV = b.sem("sDV"); sAC = b.sem("sAC"); sPO = b.sem("sPO")
    sCC = b.sem("sCC"); sST = b.sem("sST"); sXL = b.sem("sXL")
    sST1 = b.sem("sST1"); sST2 = b.sem("sST2"); sHS = [b.sem("sHS0"), b.sem("sHS1")]
    sOS = [b.sem("sOS0"), b.sem("sOS1")]
    sYS = [b.sem("sYS0"), b.sem("sYS1")]

    def dma(eng, outap, inap, sem=None):
        sem = sem if sem is not None else sLD
        b.op(eng, lambda E, o=outap, i=inap: E.dma_start(out=o, in_=i), sem, 16)

    dma("sync", sb_identf[:, :], ident_f[:, :])
    dma("sync", sb_identb[:, :], ident_b[:, :])
    dma("sync", sb_trib[:, :], tri_b[:, :])
    dma("sync", sb_nt[:, 0:2], nu_in[:, :])
    dma("sync", sb_nt[:, 2:4], th_in[:, :])
    dma("sync", sb_fc1b[:, :], fc1b_in[:, :])
    dma("sync", sb_rl[:, :], rl_in[:, :])
    dma("sync", sb_cmask[:, :], cmask_in[:, :])
    dma("sync", sb_tok0[0:1, :], tok0_in[:, :])
    v_consts = sLD.n

    for kc in range(8):
        dma("pool", WBIG2[:, kc * 256:(kc + 1) * 256], BreT[128 * kc:128 * (kc + 1), :], sG)
        dma("pool", WBIG2[:, 2048 + kc * 256:2048 + (kc + 1) * 256], BimT[128 * kc:128 * (kc + 1), :], sG)
    for cc in range(2):
        dma("pool", WBIG2[:, 4096 + cc * 1024:4096 + (cc + 1) * 1024], CreT[128 * cc:128 * (cc + 1), :], sG)
        dma("pool", WBIG2[:, 6144 + cc * 1024:6144 + (cc + 1) * 1024], CimT[128 * cc:128 * (cc + 1), :], sG)
    for kc in range(8):
        dma("pool", WBIG2[:, 8192 + kc * 1024:8192 + (kc + 1) * 1024], DT_in[128 * kc:128 * (kc + 1), :], sG)
    for kc in range(8):
        dma("pool", sb_wg[:, kc * 8:(kc + 1) * 8], wg_in[128 * kc:128 * (kc + 1), :], sG)
    v_bcd = sG.n
    for kc in range(8):
        dma("pool", WBIG[:, kc * 4096:(kc + 1) * 4096], W1_in[128 * kc:128 * (kc + 1), :], sG)
    v_w1 = sG.n

    b.op("dve", lambda E: E.memset(sb_onescb[:, :], 1.0), sDV)
    b.op("dve", lambda E: E.memset(sb_onesrf[:, :], 1.0), sDV)
    b.op("dve", lambda E: E.memset(sb_ones128[:, :], 1.0), sDV)
    b.op("dve", lambda E: E.memset(sb_eps[:, :], EPS), sDV)
    b.op("dve", lambda E: E.memset(sb_zero[:, :], 0.0), sDV)
    b.op("dve", lambda E: E.memset(sb_carry[:, :], 0.0), sDV)
    v_memset = sDV.n

    b.wait("sync", sDV, v_memset)
    _pmv = d_pm.ap().rearrange("(r p) one -> r (p one)", p=128)
    for _i in range(5):
        dma("sync", _pmv[_i * 128:(_i + 1) * 128, :], sb_zero[:, 0:128])
    dma("sync", _pmv[640:641, :], sb_zero[0:1, 0:128])

    b.wait("pool", sG, v_w1)
    b.op("pool", lambda E: E.iota(sb_i32[:, 0:8], pattern=[[1, 8]], base=0, channel_multiplier=0), sPO)
    b.op("pool", lambda E: E.iota(sb_i32[:, 8:40], pattern=[[8192, 4], [128, 8]], base=1, channel_multiplier=1), sPO)
    b.op("pool", lambda E: E.iota(sb_i32[:, 128:256], pattern=[[1, 128]], base=0, channel_multiplier=0), sPO)
    v_iota = sPO.n
    b.wait("dve", sPO, v_iota)
    b.op("dve", lambda E: E.tensor_copy(sb_iota8[:, :], sb_i32[:, 0:8]), sDV)
    b.op("dve", lambda E: E.tensor_copy(sb_tokid[:, :], sb_i32[:, 8:40]), sDV)
    v_iota_f = sDV.n
    b.op("pool", lambda E: E.iota(sb_i32[:, 40:72], pattern=[[0, 4], [1, 8]], base=0, channel_multiplier=0), sPO)
    b.wait("dve", sPO, sPO.n)
    for _q in range(8):
        b.op("dve", lambda E, _q=_q: E.tensor_copy(sb_iotaE[:, _q * 32:(_q + 1) * 32], sb_i32[:, 40:72]), sDV)
    v_iotaE = sDV.n

    b.wait("dve", sG, v_w1)
    b.op("dve", lambda E: E.tensor_scalar_mul(WBIG2[:, 6144:8192], WBIG2[:, 6144:8192], -1.0), sDV)
    v_cim = sDV.n

    b.wait("act", sLD, v_consts)
    b.op("act", lambda E: E.activation(sb_nt[:, 4:6], sb_nt[:, 0:2], AF.Exp), sAC)
    b.op("act", lambda E: E.activation(sb_nt[:, 6:8], sb_nt[:, 2:4], AF.Exp), sAC)
    v1 = sAC.n
    b.op("act", lambda E: E.activation(sb_lam[:, 0:2], sb_nt[:, 4:6], AF.Exp, scale=-2.0), sAC)
    v2 = sAC.n
    b.wait("dve", sAC, v2)
    b.op("dve", lambda E: E.tensor_scalar(sb_lam[:, 0:2], sb_lam[:, 0:2], -1.0, 1.0, ALU.mult, ALU.add), sDV)
    v3 = sDV.n
    b.wait("act", sDV, v3)
    b.op("act", lambda E: E.activation(sb_lam[:, 2:4], sb_lam[:, 0:2], AF.Ln), sAC)
    v4 = sAC.n
    b.wait("act", sAC, v4)
    b.op("act", lambda E: E.activation(sb_nt[:, 0:2], sb_lam[:, 2:4], AF.Exp, scale=0.5), sAC)
    b.op("act", lambda E: E.activation(sb_lam[:, 0:2], sb_nt[:, 4:6], AF.Exp, scale=-1.0), sAC)
    v_gam = sAC.n

    b.wait("dve", sPO, v_iota)
    b.op("dve", lambda E: E.tensor_copy(SC0[:, 0:128], sb_i32[:, 128:256]), sDV)
    b.op("dve", lambda E: E.tensor_copy(SC0[:, 128:256], sb_i32[:, 128:256]), sDV)
    v_tau = sDV.n
    b.wait("dve", sAC, v_gam)
    for cc in range(2):
        b.op("dve", lambda E, cc=cc: E.tensor_scalar_mul(SC1[:, cc * 128:(cc + 1) * 128], SC0[:, cc * 128:(cc + 1) * 128], sb_nt[:, 6 + cc:7 + cc]), sDV)
    b.wait("dve", sDV, sDV.n)
    for cc in range(2):
        b.op("dve", lambda E, cc=cc: E.tensor_scalar_mul(SC0[:, cc * 128:(cc + 1) * 128], SC0[:, cc * 128:(cc + 1) * 128], sb_nt[:, 4 + cc:5 + cc]), sDV)
    v_T = sDV.n
    b.wait("dve", sDV, v_T)
    b.op("dve", lambda E: E.tensor_scalar_mul(SC4[:, :], SC1[:, :], INV_2PI), sDV)
    b.wait("dve", sDV, sDV.n)
    b.op("dve", lambda E: E.tensor_copy(sb_i32[:, 0:256], SC4[:, :]), sDV)
    b.wait("dve", sDV, sDV.n)
    b.op("dve", lambda E: E.tensor_copy(SC4[:, :], sb_i32[:, 0:256]), sDV)
    b.wait("dve", sDV, sDV.n)
    b.op("dve", lambda E: E.scalar_tensor_tensor(SC1[:, :], SC4[:, :], -TWO_PI, SC1[:, :], ALU.mult, ALU.add), sDV)
    b.wait("dve", sDV, sDV.n)

    def wrap_once(dst, src):
        b.op("dve", lambda E: E.tensor_scalar(SC4[:, :], src, PI, None, ALU.is_gt), sDV)
        b.wait("dve", sDV, sDV.n)
        b.op("dve", lambda E: E.scalar_tensor_tensor(dst, SC4[:, :], -TWO_PI, src, ALU.mult, ALU.add), sDV)
        b.wait("dve", sDV, sDV.n)
        b.op("dve", lambda E: E.tensor_scalar(SC4[:, :], dst, -PI, None, ALU.is_lt), sDV)
        b.wait("dve", sDV, sDV.n)
        b.op("dve", lambda E: E.scalar_tensor_tensor(dst, SC4[:, :], TWO_PI, dst, ALU.mult, ALU.add), sDV)
        b.wait("dve", sDV, sDV.n)

    wrap_once(SC1[:, :], SC1[:, :])
    b.wait("act", sDV, sDV.n)
    b.op("act", lambda E: E.activation(SC2[:, :], SC1[:, :], AF.Sin), sAC)
    v_sin = sAC.n
    b.wait("dve", sAC, v_sin)
    b.op("dve", lambda E: E.tensor_scalar(SC1[:, :], SC1[:, :], PI / 2.0, None, ALU.add), sDV)
    b.wait("dve", sDV, sDV.n)
    wrap_once(SC1[:, :], SC1[:, :])
    b.wait("act", sDV, sDV.n)
    b.op("act", lambda E: E.activation(SC3[:, :], SC1[:, :], AF.Sin), sAC)
    b.op("act", lambda E: E.activation(SC4[:, :], SC0[:, :], AF.Exp), sAC)
    b.op("act", lambda E: E.activation(SC5[:, :], SC0[:, :], AF.Exp, scale=-1.0), sAC)
    v_trig = sAC.n
    b.wait("dve", sAC, v_trig)
    for cc in range(2):
        b.op("dve", lambda E, cc=cc: E.tensor_scalar_mul(SC4[:, cc * 128:(cc + 1) * 128], SC4[:, cc * 128:(cc + 1) * 128], sb_nt[:, cc:cc + 1]), sDV)
    b.wait("dve", sDV, sDV.n)
    b.op("dve", lambda E: E.tensor_mul(tabPr[:, :], SC4[:, :], SC3[:, :]), sDV)
    b.op("dve", lambda E: E.scalar_tensor_tensor(tabPi[:, :], SC2[:, :], -1.0, SC4[:, :], ALU.mult, ALU.mult), sDV)
    b.op("dve", lambda E: E.tensor_mul(tabQr[:, :], SC5[:, :], SC3[:, :]), sDV)
    b.op("dve", lambda E: E.tensor_mul(tabQi[:, :], SC5[:, :], SC2[:, :]), sDV)
    b.wait("dve", sDV, sDV.n)
    for cc in range(2):
        b.op("dve", lambda E, cc=cc: E.tensor_mul(sb_lam[:, 2 + cc:3 + cc], sb_lam[:, cc:cc + 1],
                                                  SC2[:, cc * 128 + 1:cc * 128 + 2]), sDV)
    b.wait("dve", sDV, sDV.n)
    for cc in range(2):
        b.op("dve", lambda E, cc=cc: E.tensor_mul(sb_lam[:, cc:cc + 1], sb_lam[:, cc:cc + 1],
                                                  SC3[:, cc * 128 + 1:cc * 128 + 2]), sDV)
    v_tables = sDV.n
    if debug_level >= 1:
        b.wait("sync", sDV, v_tables)
        dma("sync", dbg["o_tab"][:, 0:256], tabPr[:, :])
        dma("sync", dbg["o_tab"][:, 256:512], tabPi[:, :])
        dma("sync", dbg["o_tab"][:, 512:768], tabQr[:, :])
        dma("sync", dbg["o_tab"][:, 768:1024], tabQi[:, :])
        dma("sync", dbg["o_tab"][:, 1024:1028], sb_lam[:, :])
        dma("sync", dbg["o_tab"][:, 1028:1032], sb_nt[:, 0:4])

    b.wait("pe", sLD, v_consts)
    b.wait("pe", sDV, v_memset)
    b.op("pe", lambda E: E.matmul(PS[0][:, 0:1], sb_onesrf[:, :], sb_rl[:, :], start=True, stop=True), sPE)
    b.op("pe", lambda E: E.matmul(PS[0][:, 4:5], sb_onesrf[:, :], sb_tok0[0:1, :], start=True, stop=True), sPE)
    v_bc0 = sPE.n
    b.wait("act", sPE, v_bc0)
    b.op("act", lambda E: E.activation(sb_sig[:, :], PS[0][:, 0:1], AF.Sigmoid), sAC)
    v_sig = sAC.n
    b.wait("dve", sPE, v_bc0)
    b.op("dve", lambda E: E.tensor_copy(sb_tok0[:, :], PS[0][:, 4:5]), sDV)
    b.wait("dve", sDV, sDV.n)
    b.wait("dve", sDV, v_iota_f)
    b.op("dve", lambda E: E.tensor_scalar_add(sb_tokid[:, :], sb_tokid[:, :], sb_tok0[:, :]), sDV)
    v_p0 = sDV.n


    psU, psB0, psB1, psB2, psB3, psYa, psYb, psUa = PS
    psUb = psB0

    b.wait("pe", sG, v_bcd)
    b.wait("pe", sDV, v_p0)
    b.wait("pe", sAC, v_sig)
    b.wait("dve", sDV, v_tables)
    b.wait("dve", sDV, v_cim)

    M = {}

    def p1_chunk(k):
        t0 = k * T
        xpar = k % 2
        xsl = sb_x[:, xpar * 4096:(xpar + 1) * 4096]
        xnbuf = sb_xnb if xpar == 0 else sb_xnb1

        b.wait("act", sXL, M[("xld", k)])
        b.wait("pe", sXL, M[("xld", k)])
        if k > 0:
            b.wait("pe", sAC, M[("rbev", k - 1)])
        for kc in range(8):
            par = kc % 2
            xk = xsl[:, kc * T:(kc + 1) * T]
            xsv = WBIG2[:, XSQ + par * 512:XSQ + (par + 1) * 512]
            if kc >= 2:
                b.wait("act", sPE, M[("stmm", k, kc - 2)])
            elif k > 0:
                b.wait("act", sPE, M[("stats", k - 1)])
            b.op("act", lambda E, xk=xk, xsv=xsv: E.activation(xsv, xk, AF.Square), sAC)
            vs = sAC.n
            b.wait("pe", sAC, vs)
            b.op("pe", lambda E, kc=kc, xk=xk: E.matmul(psU[0:1, :], sb_onescb[:, :], xk, start=(kc == 0), stop=(kc == 7)), sPE)
            b.op("pe", lambda E, kc=kc, xsv=xsv: E.matmul(psU[32:33, :], sb_onescb[:, :], xsv, start=(kc == 0), stop=(kc == 7)), sPE)
            M[("stmm", k, kc)] = sPE.n
        M[("stats", k)] = sPE.n

        b.wait("act", sPE, M[("stats", k)])
        b.op("act", lambda E: E.activation(sb_rows[:, 0:512], psU[0:1, :], AF.Copy, scale=INV_D), sAC)
        vm = sAC.n
        b.wait("act", sAC, vm)
        b.op("act", lambda E: E.activation(sb_rows[:, 512:1024], sb_rows[:, 0:512], AF.Square), sAC)
        vq = sAC.n
        b.wait("dve", sAC, vq)
        b.op("dve", lambda E: E.scalar_tensor_tensor(sb_rows[:, 512:1024], psU[32:33, :], INV_D, sb_rows[:, 512:1024], ALU.mult, ALU.subtract), sDV)
        vv = sDV.n
        b.wait("act", sDV, vv)
        b.op("act", lambda E: E.activation(sb_rows[:, 1024:1536], sb_rows[:, 512:1024], AF.Ln, bias=sb_eps[:, :]), sAC)
        b.wait("act", sAC, sAC.n)
        b.op("act", lambda E: E.activation(sb_rows[:, 1024:1536], sb_rows[:, 1024:1536], AF.Exp, scale=-0.5), sAC)
        vr = sAC.n
        b.wait("dve", sAC, vr)
        b.op("dve", lambda E: E.tensor_scalar_mul(sb_rows[:, 512:1024], sb_rows[:, 0:512], -1.0), sDV)
        vnm = sDV.n

        b.wait("pe", sDV, vnm)
        b.wait("pe", sDV, vv)
        b.op("pe", lambda E: E.matmul(psU[:, :], sb_onesrf[:, :], sb_rows[:, 512:1024], start=True, stop=True), sPE)
        vb1 = sPE.n
        b.wait("act", sPE, vb1)
        if k > 0:
            b.wait("act", sDV, M[("pass1", k - 1)])
        b.op("act", lambda E: E.copy(sb_nmb[:, :], psU[:, :]), sAC)
        vnmb = sAC.n
        b.wait("dve", sAC, vnmb)
        for kc in range(8):
            b.op("dve", lambda E, kc=kc, xsl=xsl: E.tensor_add(xsl[:, kc * T:(kc + 1) * T], xsl[:, kc * T:(kc + 1) * T], sb_nmb[:, :]), sDV)
        M[("pass1", k)] = sDV.n
        b.wait("pe", sAC, vnmb)
        b.wait("pe", sAC, vr)
        b.op("pe", lambda E: E.matmul(psU[:, :], sb_onesrf[:, :], sb_rows[:, 1024:1536], start=True, stop=True), sPE)
        vb2 = sPE.n
        b.wait("act", sPE, vb2)
        if k > 0:
            b.wait("act", sDV, M[("xn", k - 1)])
        b.op("act", lambda E: E.copy(sb_rb[:, :], psU[:, :]), sAC)
        M[("rbev", k)] = sAC.n
        b.wait("dve", sAC, M[("rbev", k)])
        if k >= 2:
            b.wait("dve", sPE, M[("ymm", k - 2, 3)])
        for kc in range(8):
            b.op("dve", lambda E, kc=kc, xsl=xsl, xnbuf=xnbuf: E.tensor_mul(xnbuf[:, kc * T:(kc + 1) * T], xsl[:, kc * T:(kc + 1) * T], sb_rb[:, :]), sDV)
        M[("xn", k)] = sDV.n
        M[("xnb", k)] = M[("xn", k)]
        if debug_level >= 1:
            b.wait("sync", sDV, M[("xn", k)])
            dma("sync",
                dbg["o_xn"].rearrange("(kc p) t -> p kc t", p=128)[:, :, t0:t0 + T],
                xnbuf.rearrange("p (kc t) -> p kc t", t=T))
            M[("xnst", k)] = sLD.n

        b.wait("pe", sDV, M[("xnb", k)])
        if k >= 2:
            b.wait("pe", sAC, M[("uev", k - 2, 15)])
        elif k == 1:
            b.wait("pe", sAC, M[("buev", 0)])
        for kc in range(8):
            rsl = xnbuf[:, kc * T:(kc + 1) * T]
            for (dst, base, nh) in ((psB0, 0, 0), (psB1, 0, 1), (psB2, 2048, 0), (psB3, 2048, 1)):
                b.op("pe", lambda E, dst=dst, base=base, nh=nh, kc=kc, rsl=rsl:
                     E.matmul(dst[:, :], WBIG2[:, base + kc * 256 + nh * 128:base + kc * 256 + (nh + 1) * 128],
                              rsl, start=(kc == 0), stop=(kc == 7)), sPE)
        M[("bu", k)] = sPE.n
        b.wait("act", sPE, M[("bu", k)])
        if k > 0:
            b.wait("act", sDV, M[("scan", k - 1)])
        if debug_level >= 1 and k == 1:
            b.wait("act", sLD, M[("dmp_bu", 0)])
        for i, src in enumerate((psB0, psB1, psB2, psB3)):
            b.op("act", lambda E, i=i, src=src: E.copy(sb_bu[:, i * 512:(i + 1) * 512], src[:, :]), sAC)
        M[("buev", k)] = sAC.n

        if debug_level >= 1 and k == 0:
            b.wait("sync", sAC, M[("buev", k)])
            dma("sync", dbg["o_bu"][:, :], sb_bu[:, :])
            M[("dmp_bu", 0)] = sLD.n
        emit_scan(k)
        if debug_level >= 1 and k == 0:
            b.wait("sync", sDV, M[("scan", k)])
            dma("sync", dbg["o_st"][:, :], WBIG2[:, STB:STB + 2048])
            M[("dmp_st", 0)] = sLD.n

        if k > 0:
            emit_y(k - 1)
            emit_w1(k - 1, M[("buev", k)])

        if k + 1 < NCH:
            if k >= 1:
                b.wait("pool", sDV, M[("xn", k - 1)])
            nsl = sb_x[:, ((k + 1) % 2) * 4096:(((k + 1) % 2) + 1) * 4096]
            b.op("pool", lambda E, t0n=(k + 1) * T, nsl=nsl: E.dma_start(
                out=nsl.rearrange("p (kc t) -> p kc t", t=T),
                in_=xT.rearrange("(kc p) t -> p kc t", p=128)[:, :, t0n:t0n + T]), sXL, 16)
            M[("xld", k + 1)] = sXL.n

        if k == NCH - 1:
            emit_y(k)
            emit_w1(k, M[("buev", k)])
        if debug_level >= 1 and k == 1:
            b.wait("sync", sAC, M[("yev", 0, 3)])
            dma("sync", dbg["o_y"][:, :], WBIG2[:, YB:YB + 4096])
            M[("dmp_y", 0)] = sLD.n

    def emit_scan(k):
        stv = WBIG2[:, STB:STB + 2048] if k % 2 == 0 else sb_st1
        b.wait("dve", sAC, M[("buev", k)])
        if k >= 2:
            b.wait("dve", sPE, M[("ymm", k - 2, 3)])
        if debug_level >= 1 and k == 1:
            b.wait("dve", sLD, M[("dmp_st", 0)])
        if k % 4 == 0:
            b.op("dve", lambda E: E.memset(sb_carry[:, :], 0.0), sDV)
            b.wait("dve", sDV, sDV.n)
        for sub in range(4):
            b.op("dve", lambda E: E.tensor_mul(sb_cs[:, 0:2], sb_lam[:, 0:2], sb_carry[:, 0:2]), sDV)
            b.op("dve", lambda E: E.tensor_mul(sb_cst[:, 0:2], sb_lam[:, 2:4], sb_carry[:, 2:4]), sDV)
            b.op("dve", lambda E: E.tensor_mul(sb_cs[:, 2:4], sb_lam[:, 2:4], sb_carry[:, 0:2]), sDV)
            b.op("dve", lambda E: E.tensor_mul(sb_cst[:, 2:4], sb_lam[:, 0:2], sb_carry[:, 2:4]), sDV)
            b.wait("dve", sDV, sDV.n)
            b.op("dve", lambda E: E.tensor_sub(sb_cs[:, 0:2], sb_cs[:, 0:2], sb_cst[:, 0:2]), sDV)
            b.op("dve", lambda E: E.tensor_add(sb_cs[:, 2:4], sb_cs[:, 2:4], sb_cst[:, 2:4]), sDV)
            b.wait("dve", sDV, sDV.n)
            bur = sb_bu[:, 0:1024].rearrange("p (cc t) -> p cc t", cc=2)[:, :, sub * SUB:(sub + 1) * SUB]
            bui = sb_bu[:, 1024:2048].rearrange("p (cc t) -> p cc t", cc=2)[:, :, sub * SUB:(sub + 1) * SUB]
            a_r = scA_r.rearrange("p (cc t) -> p cc t", cc=2)
            a_i = scA_i.rearrange("p (cc t) -> p cc t", cc=2)
            tmpv = sctmp.rearrange("p (cc t) -> p cc t", cc=2)
            prv = tabPr.rearrange("p (cc t) -> p cc t", cc=2)
            piv = tabPi.rearrange("p (cc t) -> p cc t", cc=2)
            b.op("dve", lambda E, a_r=a_r, prv=prv, bur=bur: E.tensor_mul(a_r, prv, bur), sDV)
            b.op("dve", lambda E, tmpv=tmpv, piv=piv, bui=bui: E.tensor_mul(tmpv, piv, bui), sDV)
            b.op("dve", lambda E: E.tensor_sub(scA_r[:, :], scA_r[:, :], sctmp[:, :]), sDV)
            b.op("dve", lambda E, a_i=a_i, prv=prv, bui=bui: E.tensor_mul(a_i, prv, bui), sDV)
            b.op("dve", lambda E, tmpv=tmpv, piv=piv, bur=bur: E.tensor_mul(tmpv, piv, bur), sDV)
            b.op("dve", lambda E: E.tensor_add(scA_i[:, :], scA_i[:, :], sctmp[:, :]), sDV)
            for cc in range(2):
                b.op("dve", lambda E, cc=cc: E.tensor_tensor_scan(scS_r[:, cc * 128:(cc + 1) * 128], sb_ones128[:, :], scA_r[:, cc * 128:(cc + 1) * 128], sb_cs[:, cc:cc + 1], ALU.mult, ALU.add), sDV)
                b.op("dve", lambda E, cc=cc: E.tensor_tensor_scan(scS_i[:, cc * 128:(cc + 1) * 128], sb_ones128[:, :], scA_i[:, cc * 128:(cc + 1) * 128], sb_cs[:, 2 + cc:3 + cc], ALU.mult, ALU.add), sDV)
            b.op("dve", lambda E: E.tensor_mul(scT_r[:, :], tabQr[:, :], scS_r[:, :]), sDV)
            b.op("dve", lambda E: E.tensor_mul(sctmp[:, :], tabQi[:, :], scS_i[:, :]), sDV)
            b.op("dve", lambda E: E.tensor_sub(scT_r[:, :], scT_r[:, :], sctmp[:, :]), sDV)
            b.op("dve", lambda E: E.tensor_mul(scT_i[:, :], tabQr[:, :], scS_i[:, :]), sDV)
            b.op("dve", lambda E: E.tensor_mul(sctmp[:, :], tabQi[:, :], scS_r[:, :]), sDV)
            b.op("dve", lambda E: E.tensor_add(scT_i[:, :], scT_i[:, :], sctmp[:, :]), sDV)
            b.wait("dve", sDV, sDV.n)
            b.op("dve", lambda E: E.tensor_copy(sb_carry[:, 0:2], scT_r.rearrange("p (cc t) -> p cc t", cc=2)[:, :, 127:128]), sDV)
            b.op("dve", lambda E: E.tensor_copy(sb_carry[:, 2:4], scT_i.rearrange("p (cc t) -> p cc t", cc=2)[:, :, 127:128]), sDV)
            str_v = stv[:, 0:1024].rearrange("p (cc t) -> p cc t", cc=2)[:, :, sub * SUB:(sub + 1) * SUB]
            sti_v = stv[:, 1024:2048].rearrange("p (cc t) -> p cc t", cc=2)[:, :, sub * SUB:(sub + 1) * SUB]
            b.op("dve", lambda E, str_v=str_v: E.tensor_copy(str_v, scT_r.rearrange("p (cc t) -> p cc t", cc=2)), sDV)
            b.op("dve", lambda E, sti_v=sti_v: E.tensor_copy(sti_v, scT_i.rearrange("p (cc t) -> p cc t", cc=2)), sDV)
            b.wait("dve", sDV, sDV.n)
            M[("tts", k, sub)] = sDV.n
        M[("scan", k)] = sDV.n

    def emit_y(k):
        stv = WBIG2[:, STB:STB + 2048] if k % 2 == 0 else sb_st1
        ynb = sb_xnb if k % 2 == 0 else sb_xnb1
        b.wait("pe", sDV, M[("scan", k)])
        for mg in range(4):
            if mg > 0:
                b.wait("pe", sAC, M[("yev", k, mg - 1)])
            elif k > 0:
                b.wait("pe", sAC, M[("yev", k - 1, 3)])
            for mi in range(2):
                m = mg * 2 + mi
                dst = psYa if mi == 0 else psYb
                for cc in range(2):
                    b.op("pe", lambda E, dst=dst, cc=cc, m=m: E.matmul(dst[:, :], WBIG2[:, 4096 + cc * 1024 + m * 128:4096 + cc * 1024 + (m + 1) * 128], stv[:, cc * 512:(cc + 1) * 512], start=(cc == 0), stop=False), sPE)
                for cc in range(2):
                    b.op("pe", lambda E, dst=dst, cc=cc, m=m: E.matmul(dst[:, :], WBIG2[:, 6144 + cc * 1024 + m * 128:6144 + cc * 1024 + (m + 1) * 128], stv[:, 1024 + cc * 512:1024 + (cc + 1) * 512], start=False, stop=False), sPE)
                for kc in range(8):
                    b.op("pe", lambda E, dst=dst, kc=kc, m=m: E.matmul(dst[:, :], WBIG2[:, 8192 + kc * 1024 + m * 128:8192 + kc * 1024 + (m + 1) * 128], ynb[:, kc * T:(kc + 1) * T], start=False, stop=(kc == 7)), sPE)
            M[("ymm", k, mg)] = sPE.n
            b.wait("act", sPE, M[("ymm", k, mg)])
            if debug_level >= 1 and k == 1 and mg == 0:
                b.wait("act", sLD, M[("dmp_y", 0)])
            for mi in range(2):
                m = mg * 2 + mi
                src = psYa if mi == 0 else psYb
                b.op("act", lambda E, src=src, m=m: E.copy(WBIG2[:, YB + m * 512:YB + (m + 1) * 512], src[:, :]), sAC)
            M[("yev", k, mg)] = sAC.n

    def emit_w1(k, buev_wait):
        t0 = k * T
        if k == 0:
            b.wait("pe", sG, v_w1)
        b.wait("pe", sAC, buev_wait)
        b.wait("pe", sAC, M[("yev", k, 3)])
        UP = [(psUa, psB1), (psB2, psB3)]
        for hp in range(16):
            pp = hp % 2
            bk0, bk1 = UP[pp]
            if hp >= 2:
                b.wait("pe", sAC, M[("uev", k, hp - 2)])
            elif k > 0:
                b.wait("pe", sAC, M[("uev", k - 1, 15)])
            for kc in range(8):
                b.op("pe", lambda E, bk0=bk0, kc=kc, hp=hp: E.matmul(bk0[:, :], WBIG[:, kc * 4096 + (2 * hp) * 128:kc * 4096 + (2 * hp + 1) * 128], WBIG2[:, YB + kc * 512:YB + (kc + 1) * 512], start=(kc == 0), stop=(kc == 7)), sPE)
            for kc in range(8):
                b.op("pe", lambda E, bk1=bk1, kc=kc, hp=hp: E.matmul(bk1[:, :], WBIG[:, kc * 4096 + (2 * hp + 1) * 128:kc * 4096 + (2 * hp + 2) * 128], WBIG2[:, YB + kc * 512:YB + (kc + 1) * 512], start=(kc == 0), stop=(kc == 7)), sPE)
            M[("u", k, hp)] = sPE.n
            USB = 26624 + pp * 1024
            b.wait("act", sPE, M[("u", k, hp)])
            if hp >= 2:
                b.wait("act", sDV, M[("gel", k, hp - 2)])
            elif k > 0:
                b.wait("act", sDV, M[("gel", k - 1, 14 + pp)])
            b.op("act", lambda E, bk0=bk0, USB=USB: E.copy(WBIG2[:, USB:USB + 512], bk0[:, :]), sAC)
            b.op("act", lambda E, bk1=bk1, USB=USB: E.copy(WBIG2[:, USB + 512:USB + 1024], bk1[:, :]), sAC)
            M[("uev", k, hp)] = sAC.n
            if hp > 0:
                b.wait("act", sDV, M[("t2", k, hp - 1)])
            elif k > 0:
                b.wait("act", sDV, M[("t2", k - 1, 15)])
            b.op("act", lambda E, USB=USB: E.activation(sb_q[:, :], WBIG2[:, USB:USB + 1024], AF.Square, scale=SQ2C2), sAC)
            vq2 = sAC.n
            b.wait("dve", sAC, vq2)
            b.op("dve", lambda E, USB=USB: E.scalar_tensor_tensor(sb_t2[:, :], sb_q[:, :], C1x2, WBIG2[:, USB:USB + 1024], ALU.add, ALU.mult), sDV)
            vt2 = sDV.n
            M[("t2", k, hp)] = vt2
            b.wait("act", sDV, vt2)
            b.op("act", lambda E: E.activation(sb_sg2[:, :], sb_t2[:, :], AF.Sigmoid), sAC)
            vsg = sAC.n
            b.wait("dve", sAC, vsg)
            b.wait("dve", sHS[pp], M[("hst", k, hp - 2)] if hp >= 2 else (M[("hst", k - 1, 14 + pp)] if k > 0 else 0))
            b.op("dve", lambda E, pp=pp, USB=USB: E.tensor_mul(WBIG2[:, HST + pp * 1024:HST + (pp + 1) * 1024], sb_sg2[:, :], WBIG2[:, USB:USB + 1024]), sDV)
            M[("gel", k, hp)] = sDV.n
            b.wait("sync", sDV, M[("gel", k, hp)])
            b.op("sync", lambda E, hp=hp, t0=t0, pp=pp: E.dma_start(
                out=d_h.ap().rearrange("(H p) t -> p H t", p=128)[:, 2 * hp:2 * hp + 2, t0:t0 + T],
                in_=WBIG2[:, HST + pp * 1024:HST + (pp + 1) * 1024].rearrange("p (H t) -> p H t", t=T)), sHS[pp], 16)
            if debug_level >= 1:
                b.op("sync", lambda E, hp=hp, t0=t0, pp=pp: E.dma_start(
                    out=dbg["o_h"].rearrange("(H p) t -> p H t", p=128)[:, 2 * hp:2 * hp + 2, t0:t0 + T],
                    in_=WBIG2[:, HST + pp * 1024:HST + (pp + 1) * 1024].rearrange("p (H t) -> p H t", t=T)), sHS[pp], 16)
            M[("hst", k, hp)] = sHS[pp].n

    b.op("pool", lambda E: E.dma_start(
        out=sb_x[:, 0:4096].rearrange("p (kc t) -> p kc t", t=T),
        in_=xT.rearrange("(kc p) t -> p kc t", p=128)[:, :, 0:T]), sXL, 16)
    M[("xld", 0)] = sXL.n
    for k in range(NCH):
        p1_chunk(k)
        b.op("pool", lambda E, j=k: E.dma_start(out=d_w2bf.ap()[512 * j:512 * (j + 1), :], in_=W2_in[512 * j:512 * (j + 1), :]), sW2C, 16)
    v_w2cast = sW2C.n
    sG_pre = sG.n
    for qeng, qsem, j0, j1 in (("pool", sG, 0, 10), ("act", sW2A, 10, 21), ("sync", sW2B, 21, 32)):
        b.wait(qeng, sAC, M[("uev", NCH - 1, 15)])
        b.wait(qeng, sW2C, v_w2cast)
        for j in range(j0, j1):
            b.op(qeng, lambda E, j=j: E.dma_start(out=WBIG[:, j * 1024:(j + 1) * 1024], in_=d_w2bf.ap()[128 * j:128 * (j + 1), :]), qsem, 16)
    M[("vw2",)] = (sG.n, sW2A.n, sW2B.n)

    b.barrier("p1", extra_waits=[("sync", sHS[0], sHS[0].n), ("sync", sHS[1], sHS[1].n), ("sync", sLD, sLD.n), ("pool", sG, sG_pre)])
    print("sbuf remaining before sc1.close:", nc.sbuf_bytes_remaining)
    sc1.close()
    print("sbuf remaining after sc1.close:", nc.sbuf_bytes_remaining)

    sc2 = Scope(nc)
    sb_x2 = sc2.sb("sb_x2", [128, 4096], F32)
    sb_xn2 = sc2.sb("sb_xn2", [128, 4096], F32)
    sb_ao = sb_xn2
    sb_xnb2 = sc2.sb("sb_xnb2", [128, 4096], BF16)
    sb_cb = sc2.sb("sb_cb", [128, 2048], BF16)
    stg1 = sc2.sb("stg1", [128, 1024], F32)
    stg2 = sc2.sb("stg2", [128, 1024], BF16)
    dbgz0_t = sc2.sb("dbgz0_t", [128, 512], F32)
    rtA = sc2.sb("rtA", [128, 256], F32)
    rtB = sc2.sb("rtB", [128, 256], F32)
    psU2 = sc2.ps("psU2", [128, 512], F32)
    psZ = [sc2.ps(f"psZ{i}", [128, 512], F32) for i in range(4)]
    psLG = sc2.ps("psLG", [128, 512], F32)
    psTRf = sc2.ps("psTRf", [128, 512], F32)
    psTRb = sc2.ps("psTRb", [128, 512], BF16)

    v_w2, v_w2a, v_w2b = M[("vw2",)]
    b.wait("pe", sG, v_w2)
    b.wait("pe", sW2A, v_w2a)
    b.wait("pe", sW2B, v_w2b)
    if debug_level >= 2:
        b.wait("sync", sG, v_w2)
        b.wait("sync", sW2A, v_w2a)
        b.wait("sync", sW2B, v_w2b)
        dma("sync", dbg["o_w2"][:, 0:1024], WBIG[:, 0:1024])
        dma("sync", dbg["o_w2"][:, 1024:2048], WBIG[:, 1024:2048])

    M2 = {}

    def emit_ln2(k, src_t, tag, out_mode, pe_wait=None, src_dv=None):
        if pe_wait is not None:
            b.wait("pe", sDV, pe_wait)
        if src_dv is not None:
            b.wait("act", sDV, src_dv)
        for kc in range(8):
            par = kc % 2
            sk = src_t[:, kc * T:(kc + 1) * T]
            cbv = sb_cb[:, par * 512:(par + 1) * 512]
            sqv = sb_cb[:, 1024 + par * 512:1024 + (par + 1) * 512]
            if kc >= 2:
                b.wait("dve", sPE, M2[("stmm", tag, k, kc - 2)])
                b.wait("act", sPE, M2[("stmm", tag, k, kc - 2)])
            b.op("dve", lambda E, sk=sk, cbv=cbv: E.tensor_copy(cbv, sk), sDV)
            vc = sDV.n
            b.op("act", lambda E, sk=sk, sqv=sqv: E.activation(sqv, sk, AF.Square), sAC)
            vs = sAC.n
            b.wait("pe", sDV, vc)
            b.wait("pe", sAC, vs)
            b.op("pe", lambda E, kc=kc, cbv=cbv: E.matmul(psU2[0:1, :], sb_onescb[:, :], cbv, start=(kc == 0), stop=(kc == 7)), sPE)
            b.op("pe", lambda E, kc=kc, sqv=sqv: E.matmul(psU2[32:33, :], sb_onescb[:, :], sqv, start=(kc == 0), stop=(kc == 7)), sPE)
            M2[("stmm", tag, k, kc)] = sPE.n
        b.wait("act", sPE, sPE.n)
        b.op("act", lambda E: E.activation(sb_rows[:, 0:512], psU2[0:1, :], AF.Copy, scale=INV_D), sAC)
        vm = sAC.n
        b.wait("act", sAC, vm)
        b.op("act", lambda E: E.activation(sb_rows[:, 512:1024], sb_rows[:, 0:512], AF.Square), sAC)
        vq = sAC.n
        b.wait("dve", sAC, vq)
        b.op("dve", lambda E: E.scalar_tensor_tensor(sb_rows[:, 512:1024], psU2[32:33, :], INV_D, sb_rows[:, 512:1024], ALU.mult, ALU.subtract), sDV)
        vv = sDV.n
        b.wait("act", sDV, vv)
        b.op("act", lambda E: E.activation(sb_rows[:, 1024:1536], sb_rows[:, 512:1024], AF.Ln, bias=sb_eps[:, :]), sAC)
        b.wait("act", sAC, sAC.n)
        b.op("act", lambda E: E.activation(sb_rows[:, 1024:1536], sb_rows[:, 1024:1536], AF.Exp, scale=-0.5), sAC)
        vr = sAC.n
        b.wait("dve", sAC, vr)
        b.op("dve", lambda E: E.tensor_scalar_mul(sb_rows[:, 512:1024], sb_rows[:, 0:512], -1.0), sDV)
        vnm = sDV.n
        b.wait("pe", sDV, vnm)
        b.wait("pe", sDV, vv)
        b.op("pe", lambda E: E.matmul(psU2[:, :], sb_onesrf[:, :], sb_rows[:, 512:1024], start=True, stop=True), sPE)
        vb1 = sPE.n
        b.wait("dve", sPE, vb1)
        if out_mode == "bf":
            for kc in range(8):
                b.op("dve", lambda E, kc=kc, src_t=src_t: E.tensor_add(sb_xn2[:, kc * T:(kc + 1) * T], src_t[:, kc * T:(kc + 1) * T], psU2[:, :]), sDV)
        else:
            for kc in range(8):
                b.op("dve", lambda E, kc=kc, src_t=src_t: E.tensor_add((sb_xn2 if out_mode == "xn" else src_t)[:, kc * T:(kc + 1) * T], src_t[:, kc * T:(kc + 1) * T], psU2[:, :]), sDV)
        vsub = sDV.n
        b.wait("pe", sDV, vsub)
        b.wait("pe", sAC, vr)
        b.op("pe", lambda E: E.matmul(psU2[:, :], sb_onesrf[:, :], sb_rows[:, 1024:1536], start=True, stop=True), sPE)
        vb2 = sPE.n
        M2[("bc2", tag, k)] = sPE.n
        b.wait("dve", sPE, vb2)
        for kc in range(8):
            if out_mode == "xn":
                b.op("dve", lambda E, kc=kc: E.tensor_mul(sb_xn2[:, kc * T:(kc + 1) * T], sb_xn2[:, kc * T:(kc + 1) * T], psU2[:, :]), sDV)
            elif out_mode == "inplace":
                b.op("dve", lambda E, kc=kc, src_t=src_t: E.tensor_mul(src_t[:, kc * T:(kc + 1) * T], src_t[:, kc * T:(kc + 1) * T], psU2[:, :]), sDV)
            else:
                b.op("dve", lambda E, kc=kc: E.tensor_mul(sb_xnb2[:, kc * T:(kc + 1) * T], sb_xn2[:, kc * T:(kc + 1) * T], psU2[:, :]), sDV)
        M2[("ln", tag, k)] = sDV.n

    for k in range(NCH):
        par = k % 2
        t0 = k * T
        HOF = par * 16384
        if k >= 2:
            b.wait("act", sPE, M2[("zmm", k - 2)])
        b.op("act", lambda E, t0=t0, HOF=HOF: E.dma_start(
            out=WBIG2[:, HOF:HOF + 16384].rearrange("p (kc t) -> p kc t", t=T),
            in_=d_h.ap().rearrange("(kc p) t -> p kc t", p=128)[:, :, t0:t0 + T]), sXL, 16)
        M2[("hld", k)] = sXL.n
        if k > 0:
            b.wait("act", sPE, M2[("trx1", k - 1)])
        b.op("act", lambda E, t0=t0: E.dma_start(
            out=sb_x2.rearrange("p (kc t) -> p kc t", t=T),
            in_=xT.rearrange("(kc p) t -> p kc t", p=128)[:, :, t0:t0 + T]), sXL, 16)
        M2[("xld", k)] = sXL.n
        b.wait("dve", sXL, M2[("xld", k)])
        b.wait("act", sXL, M2[("xld", k)])
        if k > 0:
            b.wait("dve", sPE, M2[("trx1", k - 1)])
        emit_ln2(k, sb_x2, "x", "xn",
                 pe_wait=(M2[("trcpf", k - 1, 3, 1)] if k > 0 else None))
        if debug_level >= 2 and k == 0:
            b.wait("sync", sDV, M2[("ln", "x", k)])
            dma("sync", dbg["o_xn2"][:, :], sb_xn2[:, :])
            M2[("xndmp",)] = sLD.n
            b.wait("sync", sLD, M2[("hld", k)])
            dma("sync", dbg["o_hrb"][:, :], WBIG2[:, 0:16384])
        b.wait("pe", sXL, M2[("hld", k)])
        for m in range(8):
            bank = psZ[m % 4]
            if m >= 4:
                b.wait("pe", sDV, M2[("ao", k, m - 4)])
            elif k > 0:
                b.wait("pe", sDV, M2[("ao", k - 1, m + 4 - 8 + 8)])
            for kc in range(32):
                b.op("pe", lambda E, bank=bank, kc=kc, m=m, HOF=HOF: E.matmul(bank[:, :], WBIG[:, kc * 1024 + m * 128:kc * 1024 + (m + 1) * 128], WBIG2[:, HOF + kc * 512:HOF + (kc + 1) * 512], start=(kc == 0), stop=(kc == 31)), sPE)
            M2[("zmm_m", k, m)] = sPE.n
            b.wait("dve", sPE, M2[("zmm_m", k, m)])
            b.wait("dve", sDV, M2[("ln", "x", k)])
            if debug_level >= 2 and k == 0 and m == 0:
                b.wait("dve", sLD, M2[("xndmp",)])
                b.op("dve", lambda E: E.tensor_copy(dbgz0_t[:, :], psZ[0][:, :]), sDV)
                b.wait("dve", sDV, sDV.n)
                b.wait("sync", sDV, sDV.n)
                dma("sync", dbg["o_z0"][:, :], dbgz0_t[:, :])
            b.op("dve", lambda E, bank=bank, m=m: E.scalar_tensor_tensor(sb_ao[:, m * 512:(m + 1) * 512], bank[:, :], sb_sig[:, :], sb_xn2[:, m * 512:(m + 1) * 512], ALU.mult, ALU.add), sDV)
            M2[("ao", k, m)] = sDV.n
        M2[("zmm", k)] = sPE.n
        if debug_level >= 2 and k == 0:
            b.wait("sync", sDV, M2[("ao", k, 7)])
            dma("sync", dbg["o_ao"][:, :], sb_ao[:, :])
            M2[("aodmp",)] = sLD.n
        if debug_level >= 2 and k == 0:
            b.wait("dve", sLD, M2[("aodmp",)])
        emit_ln2(k, sb_ao, "ao", "inplace", pe_wait=M2[("ln", "x", k)], src_dv=M2[("ao", k, 7)])
        for kc in range(8):
            b.op("dve", lambda E, kc=kc: E.tensor_add(sb_x2[:, kc * T:(kc + 1) * T], sb_ao[:, kc * T:(kc + 1) * T], sb_x2[:, kc * T:(kc + 1) * T]), sDV)
        M2[("x1", k)] = sDV.n
        emit_ln2(k, sb_x2, "x1", "bf", pe_wait=M2[("ln", "ao", k)], src_dv=M2[("x1", k)])
        b.wait("pe", sDV, M2[("ln", "x1", k)])
        for tg in range(4):
            col = (k * 4 + tg) * 8
            for kc in range(8):
                b.op("pe", lambda E, col=col, kc=kc, tg=tg: E.matmul(psLG[:, col:col + 8], sb_xnb2[:, kc * 512 + tg * 128:kc * 512 + (tg + 1) * 128], sb_wg[:, kc * 8:(kc + 1) * 8], start=(kc == 0), stop=(kc == 7)), sPE)
        M2[("lgmm", k)] = sPE.n
        for tg in range(4):
            if tg > 0:
                b.wait("dve", sST1, M2[("x1st", k, tg - 1)])
                b.wait("dve", sST2, M2[("xn2st", k, tg - 1)])
            elif k > 0:
                b.wait("dve", sST1, M2[("x1st", k - 1, 3)])
                b.wait("dve", sST2, M2[("xn2st", k - 1, 3)])
            for half in range(2):
                trb = psTRf if half == 0 else psU2
                key = ("trcpf", k, tg - 1, half) if tg > 0 else ("trcpf", k - 1, 3, half)
                if key in M2:
                    b.wait("pe", sDV, M2[key])
                for i in range(4):
                    kc = half * 4 + i
                    b.op("pe", lambda E, i=i, kc=kc, tg=tg, trb=trb: E.transpose(trb[:, i * 128:(i + 1) * 128], sb_x2[:, kc * 512 + tg * 128:kc * 512 + (tg + 1) * 128], sb_identf[:, :]), sPE)
                vtr = sPE.n
                b.wait("dve", sPE, vtr)
                b.op("dve", lambda E, half=half, trb=trb: E.tensor_copy(stg1[:, half * 512:(half + 1) * 512], trb[:, :]), sDV)
                M2[("trcpf", k, tg, half)] = sDV.n
            b.wait("sync", sDV, M2[("trcpf", k, tg, 1)])
            if (k, tg) != (0, 0):
                pass
            b.op("sync", lambda E, t0=t0, tg=tg: E.dma_start(out=d_x1t.ap()[t0 + tg * 128:t0 + (tg + 1) * 128, :], in_=stg1[:, :]), sST1, 16)
            M2[("x1st", k, tg)] = sST1.n
            for half in range(2):
                if (k, tg, half) != (0, 0, 0):
                    b.wait("pe", sDV, M2.get(("trcpb", k, tg, half - 1), M2.get(("trcpb", k, tg - 1, 1), M2.get(("trcpb", k - 1, 3, 1), 0))))
                for i in range(4):
                    kc = half * 4 + i
                    b.op("pe", lambda E, i=i, kc=kc, tg=tg: E.transpose(psTRb[:, i * 128:(i + 1) * 128], sb_xnb2[:, kc * 512 + tg * 128:kc * 512 + (tg + 1) * 128], sb_identb[:, :]), sPE)
                vtr = sPE.n
                b.wait("dve", sPE, vtr)
                b.op("dve", lambda E, half=half: E.tensor_copy(stg2[:, half * 512:(half + 1) * 512], psTRb[:, :]), sDV)
                M2[("trcpb", k, tg, half)] = sDV.n
            b.wait("sync", sDV, M2[("trcpb", k, tg, 1)])
            b.op("sync", lambda E, t0=t0, tg=tg: E.dma_start(out=d_xn2bf.ap()[t0 + tg * 128:t0 + (tg + 1) * 128, :], in_=stg2[:, :]), sST2, 16)
            M2[("xn2st", k, tg)] = sST2.n
        M2[("trx1", k)] = sPE.n
        b.wait("dve", sST2, M2[("xn2st", k, 3)])
        if k % 2 == 1:
            cch = k // 2
            b.wait("pool", sST2, M2[("xn2st", k, 3)])
            b.wait("pool", sST1, M2[("x1st", k, 3)])
            b.op("pool", lambda E, cch=cch: E.collective_compute("AllGather", ALU.bypass,
                replica_groups=[list(range(NCORE))],
                ins=[d_xn2bf.ap()[cch * 1024:(cch + 1) * 1024, :]],
                outs=[d_xn2all.ap()[cch * 8192:(cch + 1) * 8192, :]]), sCC, 1)
            M2[("xnag", cch)] = sCC.n

    b.wait("dve", sPE, M2[("lgmm", NCH - 1)])
    sb_lg = psLG[:, 0:256]
    E8 = lambda t: t.rearrange("p (tg e) -> p tg e", e=8)
    B8 = lambda t: t.rearrange("p (tg one) -> p tg one", one=1).broadcast_to([128, 32, 8])

    def dv(fn):
        b.op("dve", fn, sDV)
        b.wait("dve", sDV, sDV.n)

    dv(lambda E: E.tensor_reduce(sb_m1[:, :], E8(sb_lg), axis=AX.X, op=ALU.max))
    dv(lambda E: E.tensor_tensor(E8(rtA), E8(sb_lg), B8(sb_m1), ALU.is_lt))
    dv(lambda E: E.scalar_tensor_tensor(rtA[:, :], rtA[:, :], 1000.0, sb_iotaE[:, :], ALU.mult, ALU.add))
    dv(lambda E: E.tensor_reduce(sb_i1[:, :], E8(rtA), axis=AX.X, op=ALU.min))
    dv(lambda E: E.tensor_tensor(E8(sb_oh1), E8(sb_iotaE), B8(sb_i1), ALU.is_equal))
    dv(lambda E: E.scalar_tensor_tensor(rtB[:, :], sb_oh1[:, :], -1e9, sb_lg[:, :], ALU.mult, ALU.add))
    dv(lambda E: E.tensor_reduce(sb_m2[:, :], E8(rtB), axis=AX.X, op=ALU.max))
    dv(lambda E: E.tensor_tensor(E8(rtA), E8(rtB), B8(sb_m2), ALU.is_lt))
    dv(lambda E: E.scalar_tensor_tensor(rtA[:, :], rtA[:, :], 1000.0, sb_iotaE[:, :], ALU.mult, ALU.add))
    dv(lambda E: E.tensor_reduce(sb_i2[:, :], E8(rtA), axis=AX.X, op=ALU.min))
    dv(lambda E: E.tensor_tensor(E8(sb_oh2), E8(sb_iotaE), B8(sb_i2), ALU.is_equal))
    dv(lambda E: E.tensor_sub(sb_m2[:, :], sb_m2[:, :], sb_m1[:, :]))
    b.wait("act", sDV, sDV.n)
    b.op("act", lambda E: E.activation(sb_g2[:, :], sb_m2[:, :], AF.Sigmoid), sAC)
    b.op("act", lambda E: E.activation(sb_g1[:, :], sb_m2[:, :], AF.Sigmoid, scale=-1.0), sAC)
    v_gates = sAC.n
    dv(lambda E: E.tensor_copy(sb_cb[:, 0:256], sb_oh1[:, :]))
    dv(lambda E: E.tensor_copy(sb_cb[:, 256:512], sb_oh2[:, :]))
    v_ohb = sDV.n
    b.wait("pe", sDV, v_ohb)
    for j in range(2):
        for tg in range(32):
            b.op("pe", lambda E, j=j, tg=tg: E.matmul(psZ[0][:, j * 256 + tg * 8:j * 256 + (tg + 1) * 8], sb_trib[:, :], sb_cb[:, j * 256 + tg * 8:j * 256 + (tg + 1) * 8], start=True, stop=True), sPE)
    for j in range(2):
        for tg in range(32):
            b.op("pe", lambda E, j=j, tg=tg: E.matmul(psZ[1][0:8, j * 32 + tg:j * 32 + tg + 1], sb_cb[:, j * 256 + tg * 8:j * 256 + (tg + 1) * 8], sb_onescb[:, :], start=True, stop=True), sPE)
    v_rkmm = sPE.n
    b.wait("dve", sPE, v_rkmm)
    dv(lambda E: E.tensor_mul(rtA[:, :], psZ[0][:, 0:256], sb_oh1[:, :]))
    dv(lambda E: E.tensor_reduce(sb_pre1[:, :], E8(rtA), axis=AX.X, op=ALU.add))
    dv(lambda E: E.tensor_mul(rtA[:, :], psZ[0][:, 256:512], sb_oh2[:, :]))
    dv(lambda E: E.tensor_reduce(sb_pre2[:, :], E8(rtA), axis=AX.X, op=ALU.add))
    dv(lambda E: E.tensor_copy(sb_ct[0:8, :], psZ[1][0:8, 0:64]))
    for j in range(2):
        dv(lambda E, j=j: E.tensor_tensor_scan(sb_tb[0:8, j * 32:(j + 1) * 32], sb_ones128[0:8, 0:32], sb_ct[0:8, j * 32:(j + 1) * 32], 0.0, ALU.mult, ALU.add))
    b.wait("sync", sDV, sDV.n)
    dma("sync", d_cnt.ap()[0:1, :], sb_tb[0:8, 31:32])
    dma("sync", d_cnt.ap()[1:2, :], sb_tb[0:8, 63:64])
    M2[("cntst",)] = sLD.n
    b.wait("dve", sLD, M2[("cntst",)])
    dv(lambda E: E.tensor_sub(sb_tb[0:8, :], sb_tb[0:8, :], sb_ct[0:8, :]))
    v_p2 = sDV.n

    if debug_level >= 2:
        b.wait("sync", sLD, M2[("xn2st", NCH - 1, 3)])
        dma("sync", dbg["o_x1t"][:, :], d_x1t.ap()[:, :])
        dma("sync", dbg["o_xn2bf"][:, :], d_xn2bf.ap()[:, :])
        b.wait("sync", sDV, v_p2)
        dma("sync", dbg["o_rt"][:, 0:32], sb_i1[:, :])
        dma("sync", dbg["o_rt"][:, 32:64], sb_i2[:, :])
        dma("sync", dbg["o_rt"][:, 64:96], sb_g1[:, :])
        dma("sync", dbg["o_rt"][:, 96:128], sb_g2[:, :])
        dma("sync", dbg["o_rt"][:, 128:160], sb_pre1[:, :])
        dma("sync", dbg["o_rt"][:, 160:192], sb_pre2[:, :])
        dma("sync", dbg["o_rt"][:, 192:256], sb_tb[:, :])
        dma("sync", dbg["o_rt"][:, 256:320], sb_ct[:, :])

    b.barrier("p2", extra_waits=[("sync", sLD, sLD.n), ("sync", sST1, sST1.n), ("sync", sST2, sST2.n), ("pool", sG, sG.n)])
    sc2.close()

    sc3 = Scope(nc)
    sb_h3 = sc3.sb("sb_h3", [128, 16384], BF16)
    sb_xfm = sc3.sb("sb_xfm", [128, 8192], BF16)
    sb_ytm = sc3.sb("sb_ytm", [128, 2048], BF16)
    sb_pmy = sc3.sb("sb_pmy", [128, 128], F32)
    sb_pidi = sc3.sb("sb_pidi", [128, 80], I32)
    sb_pidf = sc3.sb("sb_pidf", [128, 80], F32)
    sb_cnt16 = sc3.sb("sb_cnt16", [16, 8], F32)
    sb_cb2 = sc3.sb("sb_cb2", [128, 2], F32)
    sb_gst = sc3.sb("sb_gst", [128, 2048], BF16)
    sb_fc2bt = sc3.sb("sb_fc2bt", [128, 1024], F32)
    sb_fc2brow = sb_rows[0:1, 0:1024]
    psM = [sc3.ps("psM0", [128, 512], F32), sc3.ps("psM1", [128, 512], F32)]
    psY3 = [sc3.ps("psY0", [128, 512], F32), sc3.ps("psY1", [128, 512], F32)]
    psX3 = sc3.ps("psX3", [128, 512], F32)

    dma("sync", sb_fc2brow, fc2b_in[:, :])
    for kc in range(8):
        dma("pool", WBIG[:, kc * 4096:(kc + 1) * 4096], fc1T_in[128 * kc:128 * (kc + 1), :], sG)
    for hc in range(32):
        dma("pool", WBIG2[:, hc * 1024:(hc + 1) * 1024], fc2_in[128 * hc:128 * (hc + 1), :], sG)
    v_fcw = sG.n
    rg = [list(range(NCORE))]
    b.op("pool", lambda E: E.collective_compute("AllGather", ALU.bypass, replica_groups=rg,
        ins=[d_cnt.ap()[:, :]], outs=[d_cntall.ap()[:, :]]), sCC, 1)
    v_cntag = sCC.n
    v_xnag = M2[("xnag", 3)]

    b.wait("sync", sCC, v_cntag)
    dma("sync", sb_cnt16[:, :], d_cntall.ap()[:, :])
    v_c16 = sLD.n
    b.wait("pe", sLD, v_c16)
    b.op("pe", lambda E: E.matmul(psX3[0:2, 0:8], sb_cmask[:, :], sb_cnt16[:, :], start=True, stop=True), sPE)
    v_cbmm = sPE.n
    b.wait("dve", sPE, v_cbmm)
    b.op("dve", lambda E: E.tensor_copy(sb_cnt16[0:2, :], psX3[0:2, 0:8]), sDV)
    b.wait("dve", sDV, sDV.n)
    b.wait("pe", sDV, sDV.n)
    b.op("pe", lambda E: E.transpose(psX3[0:8, 8:10], sb_cnt16[0:2, 0:8], sb_identf[0:2, 0:2]), sPE)
    v_cbt = sPE.n
    b.wait("dve", sPE, v_cbt)
    b.op("dve", lambda E: E.tensor_copy(sb_cb2[0:8, :], psX3[0:8, 8:10]), sDV)
    b.wait("dve", sDV, sDV.n)
    for j in range(2):
        b.op("dve", lambda E, j=j: E.tensor_scalar_add(sb_tb[0:8, j * 32:(j + 1) * 32], sb_tb[0:8, j * 32:(j + 1) * 32], sb_cb2[0:8, j:j + 1]), sDV)
    b.wait("dve", sDV, sDV.n)
    b.wait("sync", sDV, sDV.n)
    dma("sync", d_fb.ap()[:, :], sb_tb[0:8, :])
    b.wait("sync", sLD, sLD.n)
    b.op("sync", lambda E: E.dma_start(
        out=sb_brow[0:1, 0:512].rearrange("one (j tg e) -> one j tg e", j=2, e=8),
        in_=d_fb.ap().rearrange("e (j tg) -> j tg e", j=2),
        allow_slow_non_contiguous=True), sLD, 16)
    v_brow = sLD.n
    b.wait("pe", sLD, v_brow)
    b.op("pe", lambda E: E.matmul(psX3[:, :], sb_onesrf[:, :], sb_brow[0:1, 0:512], start=True, stop=True), sPE)
    v_bb = sPE.n
    b.wait("dve", sPE, v_bb)

    def dv3(fn):
        b.op("dve", fn, sDV)
        b.wait("dve", sDV, sDV.n)

    E8 = lambda t: t.rearrange("p (tg e) -> p tg e", e=8)
    for j, ohx, prex, gx, ix in ((0, sb_oh1, sb_pre1, sb_g1, sb_i1), (1, sb_oh2, sb_pre2, sb_g2, sb_i2)):
        dv3(lambda E, j=j, ohx=ohx: E.tensor_mul(rtA2[:, :], psX3[:, j * 256:(j + 1) * 256], ohx[:, :]))
        dv3(lambda E: E.tensor_reduce(sb_m1[:, :], E8(rtA2), axis=AX.X, op=ALU.add))
        dv3(lambda E, prex=prex: E.tensor_add(sb_m1[:, :], sb_m1[:, :], prex[:, :]))
        dv3(lambda E: E.tensor_scalar(sb_m2[:, :], sb_m1[:, :], 0.5, 1.0 / 128.0, ALU.add, ALU.mult))
        dv3(lambda E: E.tensor_copy(sb_i32[:, 0:32], sb_m2[:, :]))
        dv3(lambda E: E.tensor_copy(sb_m2[:, :], sb_i32[:, 0:32]))
        dv3(lambda E: E.scalar_tensor_tensor(rtA2[:, 0:32], sb_m2[:, :], -128.0, sb_m1[:, :], ALU.mult, ALU.add))
        dv3(lambda E: E.tensor_scalar(rtA2[:, 32:64], rtA2[:, 0:32], 0.0, None, ALU.is_lt))
        dv3(lambda E: E.tensor_sub(sb_m2[:, :], sb_m2[:, :], rtA2[:, 32:64]))
        dv3(lambda E: E.scalar_tensor_tensor(rtA2[:, 0:32], sb_m2[:, :], -128.0, sb_m1[:, :], ALU.mult, ALU.add))
        dv3(lambda E: E.scalar_tensor_tensor(rtA2[:, 64:96], rtA2[:, 0:32], 80.0, sb_m2[:, :], ALU.mult, ALU.add))
        dv3(lambda E: E.tensor_scalar(rtA2[:, 96:128], sb_m1[:, :], float(CAP), None, ALU.is_lt))
        dv3(lambda E, gx=gx, j=j: E.tensor_mul(sb_gk[:, j * 32:(j + 1) * 32], gx[:, :], rtA2[:, 96:128]))
        dv3(lambda E, ix=ix: E.scalar_tensor_tensor(rtA2[:, 128:160], ix[:, :], float(CAP), rtA2[:, 64:96], ALU.mult, ALU.add))
        dv3(lambda E: E.tensor_scalar(rtA2[:, 96:128], rtA2[:, 96:128], -1.0, 1.0, ALU.mult, ALU.add))
        dv3(lambda E: E.scalar_tensor_tensor(rtA2[:, 128:160], rtA2[:, 96:128], 90000.0, rtA2[:, 128:160], ALU.mult, ALU.add))
        dv3(lambda E, j=j: E.tensor_scalar(sb_pos[:, j * 32:(j + 1) * 32], rtA2[:, 128:160], 81984.0, None, ALU.min))
        dv3(lambda E, j=j: E.tensor_copy(sb_poi[:, j * 32:(j + 1) * 32], sb_pos[:, j * 32:(j + 1) * 32]))
    v_pos = sDV.n

    b.wait("pool", sDV, v_pos)
    for j in range(2):
        for tg in range(32):
            b.op("pool", lambda E, j=j, tg=tg: E.indirect_dma_start(
                out=d_pm.ap()[:, :], out_offset=IndirectOffsetOnAxis(sb_poi[:, j * 32 + tg:j * 32 + tg + 1], 0),
                in_=sb_tokid[:, tg:tg + 1], in_offset=None), sG, 16)
    v_scat = sG.n
    b.wait("dve", sG, v_scat)
    dv3(lambda E: E.tensor_copy(rtA2[:, 0:64], sb_poi[:, :]))
    dv3(lambda E: E.tensor_scalar(rtA2[:, 64:128], rtA2[:, 0:64], 0.5, 1.0 / CAP, ALU.add, ALU.mult))
    dv3(lambda E: E.tensor_copy(sb_i32[:, 0:64], rtA2[:, 64:128]))
    dv3(lambda E: E.tensor_copy(rtA2[:, 64:128], sb_i32[:, 0:64]))
    dv3(lambda E: E.scalar_tensor_tensor(rtA2[:, 128:192], rtA2[:, 64:128], -float(CAP), rtA2[:, 0:64], ALU.mult, ALU.add))
    dv3(lambda E: E.tensor_scalar(rtA2[:, 192:256], rtA2[:, 128:192], 0.0, None, ALU.is_lt))
    dv3(lambda E: E.tensor_sub(rtA2[:, 64:128], rtA2[:, 64:128], rtA2[:, 192:256]))
    dv3(lambda E: E.scalar_tensor_tensor(rtA2[:, 128:192], rtA2[:, 64:128], -float(CAP), rtA2[:, 0:64], ALU.mult, ALU.add))
    dv3(lambda E: E.tensor_scalar(rtA2[:, 0:64], rtA2[:, 128:192], 0.5, 1.0 / 2560.0, ALU.add, ALU.mult))
    dv3(lambda E: E.tensor_copy(sb_i32[:, 0:64], rtA2[:, 0:64]))
    dv3(lambda E: E.tensor_copy(rtA2[:, 0:64], sb_i32[:, 0:64]))
    dv3(lambda E: E.scalar_tensor_tensor(rtA2[:, 192:256], rtA2[:, 0:64], -2560.0, rtA2[:, 128:192], ALU.mult, ALU.add))
    dv3(lambda E: E.tensor_scalar(sb_m1[:, 0:32], rtA2[:, 192:224], 0.0, None, ALU.is_lt))
    dv3(lambda E: E.tensor_scalar(sb_m2[:, 0:32], rtA2[:, 224:256], 0.0, None, ALU.is_lt))
    dv3(lambda E: E.tensor_sub(rtA2[:, 0:32], rtA2[:, 0:32], sb_m1[:, 0:32]))
    dv3(lambda E: E.tensor_sub(rtA2[:, 32:64], rtA2[:, 32:64], sb_m2[:, 0:32]))
    dv3(lambda E: E.scalar_tensor_tensor(rtA2[:, 192:256], rtA2[:, 0:64], -2560.0, rtA2[:, 128:192], ALU.mult, ALU.add))
    dv3(lambda E: E.tensor_scalar_mul(rtA2[:, 0:64], rtA2[:, 0:64], 20480.0))
    dv3(lambda E: E.scalar_tensor_tensor(rtA2[:, 0:64], rtA2[:, 64:128], 2560.0, rtA2[:, 0:64], ALU.mult, ALU.add))
    dv3(lambda E: E.tensor_add(rtA2[:, 0:64], rtA2[:, 0:64], rtA2[:, 192:256]))
    dv3(lambda E: E.tensor_scalar(rtA2[:, 0:64], rtA2[:, 0:64], 81919.0, None, ALU.min))
    dv3(lambda E: E.tensor_copy(sb_poi[:, :], rtA2[:, 0:64]))
    b.wait("pool", sG, v_scat)
    b.op("pool", lambda E: E.collective_compute("ReduceScatter", ALU.max, replica_groups=rg,
        ins=[d_pm.ap()[0:81920, :]], outs=[d_pmy.ap()[:, :]]), sCC, 1)
    v_rs = sCC.n

    b.wait("sync", sCC, v_rs)
    dma("sync", sb_pmy[0:80, :], d_pmy.ap().rearrange("(g p) one -> g (p one)", p=128))
    v_pmyld = sLD.n
    b.wait("pe", sLD, v_pmyld)
    b.wait("pe", sDV, v_pos)
    b.op("pe", lambda E: E.transpose(psX3[:, 0:80], sb_pmy[0:80, :], sb_identf[0:80, 0:80]), sPE)
    v_pmyt = sPE.n
    b.wait("dve", sPE, v_pmyt)
    dv3(lambda E: E.tensor_scalar(sb_pidf[:, :], psX3[:, 0:80], -1.0, 0.0, ALU.add, ALU.max))
    dv3(lambda E: E.tensor_copy(sb_pidi[:, :], sb_pidf[:, :]))
    v_pidi = sDV.n

    b.wait("pe", sDV, v_pidi)
    b.wait("pe", sLD, sLD.n)
    b.op("pe", lambda E: E.matmul(psX3[:, 0:512], sb_onesrf[:, :], sb_rows[0:1, 0:512], start=True, stop=True), sPE)
    b.op("pe", lambda E: E.matmul(psM[0][:, :], sb_onesrf[:, :], sb_rows[0:1, 512:1024], start=True, stop=True), sPE)
    v_fbb = sPE.n
    b.wait("dve", sPE, v_fbb)
    dv3(lambda E: E.tensor_copy(sb_fc2bt[:, 0:512], psX3[:, 0:512]))
    dv3(lambda E: E.tensor_copy(sb_fc2bt[:, 512:1024], psM[0][:, :]))
    v_fc2bt = sDV.n

    b.wait("pool", sDV, v_pidi)
    b.wait("pool", sCC, v_xnag)

    sGS = [b.sem(f"sGS{i}") for i in range(2)]

    def emit_gathers(g0, g1):
        for g in range(g0, g1):
            gpar = g % 2
            gsl = sb_gst[:, gpar * 1024:(gpar + 1) * 1024]
            if g >= 2:
                b.wait("pool", sGS[gpar], sGS[gpar].n)
            b.op("pool", lambda E, g=g, gsl=gsl: E.indirect_dma_start(
                out=gsl, out_offset=None,
                in_=d_xn2all.ap()[:, :], in_offset=IndirectOffsetOnAxis(sb_pidi[:, g:g + 1], 0)), sG, 16)
            M2[("gatp", g)] = sG.n
            b.wait("sync", sG, M2[("gatp", g)])
            b.op("sync", lambda E, g=g, gsl=gsl: E.dma_start(out=d_disp.ap()[g * 128:(g + 1) * 128, :], in_=gsl), sGS[gpar], 16)
            M2[("gat", g)] = (sGS[0].n, sGS[1].n)

    b.wait("pe", sG, v_fcw)
    b.wait("pe", sDV, v_fc2bt)
    NSC = CAP // 512
    emit_gathers(0, 80)
    for sc in range(NSC):
        par = sc % 2
        s0 = sc * 512
        _g4 = M2[("gat", (s0 + 511) // 128)]
        for _i in range(2):
            b.wait("sync", sGS[_i], _g4[_i])
        if sc >= 2:
            b.wait("sync", sPE, M2[("fc1", sc - 2)])
        for kc in range(8):
            b.op("sync", lambda E, kc=kc, s0=s0, par=par: E.dma_start_transpose(
                sb_xfm[:, par * 4096 + kc * 512:par * 4096 + (kc + 1) * 512],
                d_disp.ap()[s0:s0 + 512, kc * 128:(kc + 1) * 128]), sLD, 16)
        M2[("xt", sc)] = sLD.n
        b.wait("pe", sLD, M2[("xt", sc)])
        for hc in range(32):
            mb = psM[hc % 2]
            if hc >= 2:
                b.wait("pe", sAC, M2[("sil", sc, hc - 2)])
            elif sc > 0:
                b.wait("pe", sAC, M2[("sil", sc - 1, 31)])
            for kc in range(8):
                b.op("pe", lambda E, mb=mb, kc=kc, hc=hc, par=par: E.matmul(mb[:, :], WBIG[:, kc * 4096 + hc * 128:kc * 4096 + (hc + 1) * 128], sb_xfm[:, par * 4096 + kc * 512:par * 4096 + (kc + 1) * 512], start=(kc == 0), stop=(kc == 7)), sPE)
            vmm = sPE.n
            b.wait("act", sPE, vmm)
            if sc > 0 and hc == 0:
                b.wait("act", sPE, M2[("fc2", sc - 1)])
            b.op("act", lambda E, mb=mb, hc=hc: E.activation(sb_h3[:, hc * 512:(hc + 1) * 512], mb[:, :], AF.Silu, bias=sb_fc1b[:, hc:hc + 1]), sAC)
            M2[("sil", sc, hc)] = sAC.n
        M2[("fc1", sc)] = sPE.n
        b.wait("pe", sAC, M2[("sil", sc, 31)])
        b.wait("pe", sDV, v_fc2bt)
        for sg in range(4):
            par2 = sg % 2
            for dh in range(2):
                yb = psY3[dh]
                if sc + sg > 0:
                    b.wait("pe", sDV, M2.get(("yev", sc, sg - 1, dh), M2.get(("yev", sc - 1, 3, dh), 0)))
                for hc in range(32):
                    b.op("pe", lambda E, yb=yb, hc=hc, sg=sg, dh=dh: E.matmul(yb[:, :], sb_h3[:, hc * 512 + sg * 128:hc * 512 + (sg + 1) * 128], WBIG2[:, hc * 1024 + dh * 512:hc * 1024 + (dh + 1) * 512], start=(hc == 0), stop=(hc == 31)), sPE)
                vy = sPE.n
                b.wait("dve", sPE, vy)
                if sc + sg > 0:
                    _pv = M2.get(("yst", sc, sg - 2), M2.get(("yst", sc - 1, 2 + sg), 0))
                    b.wait("dve", sYS[sg % 2], _pv)
                b.op("dve", lambda E, yb=yb, dh=dh, par2=par2: E.scalar_tensor_tensor(sb_ytm[:, par2 * 1024 + dh * 512:par2 * 1024 + (dh + 1) * 512], yb[:, :], 1.0, sb_fc2bt[:, dh * 512:(dh + 1) * 512], ALU.mult, ALU.add), sDV)
                M2[("yev", sc, sg, dh)] = sDV.n
            b.wait("sync", sDV, M2[("yev", sc, sg, 1)])
            b.op("sync", lambda E, s0=s0, sg=sg, par2=par2: E.dma_start(out=d_y.ap()[s0 + sg * 128:s0 + (sg + 1) * 128, :], in_=sb_ytm[:, par2 * 1024:(par2 + 1) * 1024]), sYS[par2], 16)
            M2[("yst", sc, sg)] = sYS[par2].n
        M2[("fc2", sc)] = sPE.n
        if sc % 5 == 4:
            ych = sc // 5
            b.wait("pool", sYS[0], sYS[0].n)
            b.wait("pool", sYS[1], sYS[1].n)
            b.wait("pool", sG, sG.n)
            b.wait("pool", sGS[0], sGS[0].n)
            b.wait("pool", sGS[1], sGS[1].n)
            b.op("pool", lambda E, ych=ych: E.collective_compute("AllGather", ALU.bypass,
                replica_groups=[list(range(NCORE))],
                ins=[d_y.ap()[ych * 2560:(ych + 1) * 2560, :]],
                outs=[d_yall.ap()[ych * 20480:(ych + 1) * 20480, :]]), sCC, 1)
            M2[("yag", ych)] = sCC.n


    if debug_level >= 3:
        b.wait("sync", sYS[0], sYS[0].n)
        b.wait("sync", sYS[1], sYS[1].n)
        dma("sync", dbg["o_y3"][:, :], d_y.ap()[:, :])
        dma("sync", dbg["o_disp"][:, :], d_disp.ap()[:, :])
        b.wait("sync", sCC, v_rs)
        dma("sync", dbg["o_pmy"][:, :], d_pmy.ap()[:, :])
        dma("sync", dbg["o_posd"][:, :], sb_pos[:, :])
        dma("sync", dbg["o_gkd"][:, :], sb_gk[:, :])
        dma("sync", dbg["o_cnt"][:, :], d_cntall.ap()[:, :])
        dma("sync", dbg["o_cb2"][:, :], sb_cb2[:, :])

    b.barrier("p3", extra_waits=[("sync", sLD, sLD.n), ("pool", sG, sG.n), ("pool", sCC, sCC.n)])
    sc3.close()
    scW.close()

    sc4 = Scope(nc)
    x1b = sc4.sb("x1b", [128, 8192], F32)
    yb = sc4.sb("yb", [128, 8192], BF16)
    w_scr = sc4.sb("w_scr", [128, 4096], F32)
    wA = sc4.sb("wA", [128, 4096], F32)
    wB = sc4.sb("wB", [128, 4096], F32)
    outb = sc4.sb("outb", [128, 8192], F32)
    gam4 = sc4.sb("gam4", [128, 4096], F32)
    bet4 = sc4.sb("bet4", [128, 4096], F32)
    sb_grow = sc4.sb("sb_grow", [1, 1024], F32)
    sb_brow4 = sc4.sb("sb_brow4", [1, 1024], F32)
    c4 = sc4.sb("c4", [128, 32], F32)
    sb_eps4 = sc4.sb("sb_eps4", [128, 1], F32)
    psB4 = sc4.ps("psB4", [128, 512], F32)

    dma("sync", sb_grow[:, :], lng_in[:, :])
    dma("sync", sb_brow4[:, :], lnb_in[:, :])
    v_gb = sLD.n
    b.op("dve", lambda E: E.memset(sb_eps4[:, :], EPS), sDV)
    v_ms4 = sDV.n
    b.wait("pe", sLD, v_gb)
    b.wait("pe", sDV, v_ms4)
    for rep in range(4):
        for half in range(2):
            b.op("pe", lambda E, half=half: E.matmul(psB4[:, :], sb_onesrf[:, :], sb_grow[0:1, half * 512:(half + 1) * 512], start=True, stop=True), sPE)
            v = sPE.n
            b.wait("dve", sPE, v)
            b.op("dve", lambda E, rep=rep, half=half: E.tensor_copy(gam4[:, rep * 1024 + half * 512:rep * 1024 + (half + 1) * 512], psB4[:, :]), sDV)
            b.wait("pe", sDV, sDV.n)
            b.op("pe", lambda E, half=half: E.matmul(psB4[:, :], sb_onesrf[:, :], sb_brow4[0:1, half * 512:(half + 1) * 512], start=True, stop=True), sPE)
            v = sPE.n
            b.wait("dve", sPE, v)
            b.op("dve", lambda E, rep=rep, half=half: E.tensor_copy(bet4[:, rep * 1024 + half * 512:rep * 1024 + (half + 1) * 512], psB4[:, :]), sDV)
            b.wait("pe", sDV, sDV.n)
    v_gb4 = sDV.n
    b.wait("pool", sDV, v_gb4)

    NB = 8
    GV = lambda t: t.rearrange("p (g d) -> p g d", g=4)

    def ln_stats(srcv, vready_dv, vready_ac):
        b.op("dve", lambda E, srcv=srcv: E.tensor_reduce(c4[:, 0:4], GV(srcv), axis=AX.X, op=ALU.add), sDV)
        vr1 = sDV.n
        for g in range(4):
            b.op("act", lambda E, srcv=srcv, g=g: E.activation(w_scr[:, 0:1024], srcv[:, g * 1024:(g + 1) * 1024], AF.Square, accum_out=c4[:, 4 + g:5 + g]), sAC)
        vsq = sAC.n
        b.wait("dve", sDV, vr1)
        b.op("dve", lambda E: E.tensor_scalar_mul(c4[:, 8:12], c4[:, 0:4], INV_D), sDV)
        b.wait("dve", sDV, sDV.n)
        b.op("dve", lambda E: E.tensor_mul(c4[:, 12:16], c4[:, 8:12], c4[:, 8:12]), sDV)
        b.wait("dve", sAC, vsq)
        b.wait("dve", sDV, sDV.n)
        b.op("dve", lambda E: E.scalar_tensor_tensor(c4[:, 4:8], c4[:, 4:8], INV_D, c4[:, 12:16], ALU.mult, ALU.subtract), sDV)
        vvar = sDV.n
        b.wait("act", sDV, vvar)
        b.op("act", lambda E: E.activation(c4[:, 16:20], c4[:, 4:8], AF.Ln, bias=sb_eps4[:, :]), sAC)
        b.wait("act", sAC, sAC.n)
        b.op("act", lambda E: E.activation(c4[:, 16:20], c4[:, 16:20], AF.Exp, scale=-0.5), sAC)
        vrstd = sAC.n
        b.wait("dve", sAC, vrstd)
        b.op("dve", lambda E: E.scalar_tensor_tensor(c4[:, 12:16], c4[:, 8:12], -1.0, c4[:, 16:20], ALU.mult, ALU.mult), sDV)
        b.wait("dve", sDV, sDV.n)
        return sDV.n

    def ln_apply(srcv, dstv):
        for g in range(4):
            b.op("dve", lambda E, srcv=srcv, dstv=dstv, g=g: E.tensor_scalar(dstv[:, g * 1024:(g + 1) * 1024], srcv[:, g * 1024:(g + 1) * 1024], c4[:, 16 + g:17 + g], c4[:, 12 + g:13 + g], ALU.mult, ALU.add), sDV)
        return sDV.n

    for bi in range(NB):
        par = bi % 2
        x1v = x1b[:, par * 4096:(par + 1) * 4096]
        outv = outb[:, par * 4096:(par + 1) * 4096]
        if bi >= 2:
            b.wait("sync", sPO, M2[("fin", bi - 2)])
            b.wait("sync", sOS[par], M2[("ost", bi - 2)])
        b.op("sync", lambda E, bi=bi, x1v=x1v: E.dma_start(
            out=GV(x1v),
            in_=d_x1t.ap().rearrange("(G p) d -> p G d", p=128)[:, 4 * bi:4 * bi + 4, :]), sLD, 16)
        M2[("x1ld", bi)] = sLD.n
        if bi > 0:
            b.wait("pool", sDV, M2[("gmul", bi - 1)])
        for g in range(4):
            tg = 4 * bi + g
            b.op("pool", lambda E, tg=tg, g=g: E.indirect_dma_start(
                out=yb[:, g * 1024:(g + 1) * 1024], out_offset=None, in_=d_yall.ap()[:, :],
                in_offset=IndirectOffsetOnAxis(sb_poi[:, tg:tg + 1], 0)), sG, 16)
            b.op("pool", lambda E, tg=tg, g=g: E.indirect_dma_start(
                out=yb[:, 4096 + g * 1024:4096 + (g + 1) * 1024], out_offset=None, in_=d_yall.ap()[:, :],
                in_offset=IndirectOffsetOnAxis(sb_poi[:, 32 + tg:33 + tg], 0)), sG, 16)
        M2[("ygat", bi)] = sG.n
        b.wait("dve", sLD, M2[("x1ld", bi)])
        b.wait("act", sLD, M2[("x1ld", bi)])
        if bi >= 2:
            b.wait("dve", sPO, M2[("fin", bi - 2)])
        if bi >= 1:
            b.wait("dve", sPO, M2[("res2", bi - 1)])
            b.wait("act", sPO, M2[("fin", bi - 1)])
        ln_stats(x1v, None, None)
        ln_apply(x1v, wA)
        v_xn2 = sDV.n
        b.wait("dve", sG, M2[("ygat", bi)])
        for g in range(4):
            tg = 4 * bi + g
            b.op("dve", lambda E, g=g, tg=tg: E.tensor_scalar_mul(wB[:, g * 1024:(g + 1) * 1024], yb[:, g * 1024:(g + 1) * 1024], sb_gk[:, tg:tg + 1]), sDV)
        v_g1 = sDV.n
        b.wait("pool", sDV, v_g1)
        b.op("pool", lambda E: E.tensor_add(wA[:, :], wA[:, :], wB[:, :]), sPO)
        M2[("res1", bi)] = sPO.n
        b.wait("dve", sPO, M2[("res1", bi)])
        for g in range(4):
            tg = 4 * bi + g
            b.op("dve", lambda E, g=g, tg=tg: E.tensor_scalar_mul(wB[:, g * 1024:(g + 1) * 1024], yb[:, 4096 + g * 1024:4096 + (g + 1) * 1024], sb_gk[:, 32 + tg:33 + tg]), sDV)
        M2[("gmul", bi)] = sDV.n
        b.wait("pool", sDV, M2[("gmul", bi)])
        b.op("pool", lambda E: E.tensor_add(wA[:, :], wA[:, :], wB[:, :]), sPO)
        M2[("res2", bi)] = sPO.n
        b.wait("dve", sPO, M2[("res2", bi)])
        b.wait("act", sPO, M2[("res2", bi)])
        ln_stats(wA, None, None)
        ln_apply(wA, wB)
        vap2 = sDV.n
        b.op("dve", lambda E: E.tensor_mul(wB[:, :], wB[:, :], gam4[:, :]), sDV)
        v_gm = sDV.n
        b.wait("pool", sDV, v_gm)
        b.op("pool", lambda E: E.tensor_add(wB[:, :], wB[:, :], bet4[:, :]), sPO)
        M2[("ob", bi)] = sPO.n
        b.wait("dve", sPO, M2[("ob", bi)])
        b.wait("act", sPO, M2[("ob", bi)])
        ln_stats(wB, None, None)
        ln_apply(wB, w_scr)
        v_a3 = sDV.n
        b.wait("pool", sDV, v_a3)
        if bi >= 2:
            b.wait("pool", sOS[par], M2[("ost", bi - 2)])
        b.op("pool", lambda E, outv=outv, x1v=x1v: E.tensor_add(outv, w_scr[:, :], x1v), sPO)
        M2[("fin", bi)] = sPO.n
        b.wait("sync", sPO, M2[("fin", bi)])
        b.op("sync", lambda E, bi=bi, outv=outv: E.dma_start(
            out=out.rearrange("(G p) d -> p G d", p=128)[:, 4 * bi:4 * bi + 4, :],
            in_=GV(outv)), sOS[par], 16)
        M2[("ost", bi)] = sOS[par].n

    b.wait("sync", sOS[0], sOS[0].n)
    b.wait("sync", sOS[1], sOS[1].n)
    b.wait("pool", sOS[0], sOS[0].n)
    b.wait("pool", sOS[1], sOS[1].n)

    return nc, b, ap, dbg, out


def finalize(nc, b):
    with nc.Block() as blk:
        @blk.tensor
        def _(E):
            for f in b.streams["pe"]:
                f(E)
        @blk.vector
        def _(E):
            for f in b.streams["dve"]:
                f(E)
        @blk.scalar
        def _(E):
            for f in b.streams["act"]:
                f(E)
        @blk.gpsimd
        def _(E):
            for f in b.streams["pool"]:
                f(E)
        @blk.sync
        def _(E):
            for f in b.streams["sync"]:
                f(E)
    return nc


def host_inputs(core, full):
    x = full["L0_attn_x3d"].reshape(NCORE * NLOC, D)
    m = {}
    m["xT"] = np.ascontiguousarray(x[core * NLOC:(core + 1) * NLOC].T)
    m["BreT"] = np.ascontiguousarray(full["L0_attn_B_re"].T)
    m["BimT"] = np.ascontiguousarray(full["L0_attn_B_im"].T)
    m["CreT"] = np.ascontiguousarray(full["L0_attn_C_re"].T)
    m["CimT"] = np.ascontiguousarray(full["L0_attn_C_im"].T)
    m["DT"] = np.ascontiguousarray(full["L0_attn_D_mat"].T)
    m["W1"] = np.ascontiguousarray(full["L0_attn_W1"])
    m["W2"] = np.ascontiguousarray(full["L0_attn_W2"])
    m["wg"] = np.ascontiguousarray(full["L0_ffn_wg"])
    m["fc1T"] = np.ascontiguousarray(full["L0_ffn_fc1_w"][core].T)
    m["fc2"] = np.ascontiguousarray(full["L0_ffn_fc2_w"][core])
    m["fc1b"] = np.ascontiguousarray(full["L0_ffn_fc1_b"][core].reshape(32, 128).T)
    m["fc2b"] = np.ascontiguousarray(full["L0_ffn_fc2_b"][core].reshape(1, D))
    m["lng"] = np.ascontiguousarray(full["L0_ffn_ln_gamma"].reshape(1, D))
    m["lnb"] = np.ascontiguousarray(full["L0_ffn_ln_beta"].reshape(1, D))
    m["nu"] = np.ascontiguousarray(full["L0_attn_nu_log"].reshape(2, 128).T)
    m["th"] = np.ascontiguousarray(full["L0_attn_theta_log"].reshape(2, 128).T)
    m["rl"] = np.asarray(full["L0_attn_res_logit"], np.float32).reshape(1, 1)
    cm = np.zeros((16, 2), np.float32)
    for cp in range(8):
        for jp in range(2):
            rr = 2 * cp + jp
            if jp == 0 and cp < core:
                cm[rr, 0] = 1.0
            if (jp == 1 and cp < core) or jp == 0:
                cm[rr, 1] = 1.0
    m["cmask"] = np.ascontiguousarray(cm)
    m["tok0"] = np.asarray([[core * 1024]], np.float32)
    m["ident_f"] = np.eye(128, dtype=np.float32)
    m["ident_b"] = np.eye(128, dtype=BF)
    m["tri_b"] = np.triu(np.ones((128, 128), dtype=np.float32), 1).astype(BF)
    return m




_BUILD_CACHE = {}


def kernel(**inputs):
    full = {k: np.asarray(v) for k, v in inputs.items()}
    if "prog" not in _BUILD_CACHE:
        nc, b, ap, dbg, out_ap = build_program(debug_level=0)
        nc = finalize(nc, b)
        _BUILD_CACHE["prog"] = nc
    nc = _BUILD_CACHE["prog"]
    in_maps = [host_inputs(c, full) for c in range(NCORE)]
    res = run_bass_kernel_spmd(nc, in_maps, core_ids=list(range(NCORE)))
    out = np.concatenate([res.results[c]["out"] for c in range(NCORE)], 0)
    return out.reshape(16, 2048, D)


# version 13
# speedup vs baseline: 1.0096x; 1.0096x over previous
import sys, time
import numpy as np
import ml_dtypes
import concourse.bass as bass
import concourse.mybir as mybir
from concourse.bass import IndirectOffsetOnAxis
from concourse.bass_utils import run_bass_kernel_spmd

F32 = mybir.dt.float32
BF16 = mybir.dt.bfloat16
I32 = mybir.dt.int32
AF = mybir.ActivationFunctionType
ALU = mybir.AluOpType
AX = mybir.AxisListType
BF = ml_dtypes.bfloat16

NCORE = 8
NLOC = 4096
D = 1024
DS = 256
H = 4096
E = 8
CAP = 10240
T = 512
NCH = NLOC // T
SUB = 128
PMROWS = 82048
C1x2 = 1.5957691216057308
SQ2C2 = 0.2671232181501769
EPS = 1e-5
INV_D = 1.0 / D
TWO_PI = 6.283185307179586
INV_2PI = 1.0 / TWO_PI
PI = 3.141592653589793

ENGS = ("pe", "dve", "act", "pool", "sync")


class Sem:
    __slots__ = ("h", "n")
    def __init__(self, h):
        self.h = h
        self.n = 0


class Builder:
    def __init__(self, nc):
        self.nc = nc
        self.streams = {e: [] for e in ENGS}
        self._guards = []
        self.sems = {}

    def sem(self, name):
        if name not in self.sems:
            g = self.nc.semaphore(name)
            self.sems[name] = Sem(g.__enter__())
            self._guards.append(g)
        return self.sems[name]

    def op(self, eng, fn, inc=None, incv=1):
        if inc is not None:
            inc.n += incv
            self.streams[eng].append(lambda E, fn=fn, s=inc, d=incv: fn(E).then_inc(s.h, d))
        else:
            self.streams[eng].append(lambda E, fn=fn: fn(E))

    def wait(self, eng, s, v=None):
        v = s.n if v is None else v
        self.streams[eng].append(lambda E, s=s, v=v: E.wait_ge(s.h, v))

    def barrier(self, name, extra_waits=()):
        s = self.sem("bar_" + name)
        for eng, sem, val in extra_waits:
            self.wait(eng, sem, val)
        tgt = s.n + len(ENGS)
        for e in ENGS:
            self.op(e, lambda E: E.sem_inc(s.h, 1), None)
        s.n = tgt
        for e in ENGS:
            self.wait(e, s, tgt)


class Scope:
    def __init__(self, nc):
        self.nc = nc
        self.gs = []

    def sb(self, name, shape, dt):
        g = self.nc.sbuf_tensor(name, shape, dt)
        h = g.__enter__()
        self.gs.append(g)
        return h

    def ps(self, name, shape, dt):
        g = self.nc.psum_tensor(name, shape, dt)
        h = g.__enter__()
        self.gs.append(g)
        return h

    def close(self):
        for g in reversed(self.gs):
            g.__exit__(None, None, None)
        self.gs = []


def build_program(debug_level=0):
    nc = bass.Bass("TRN2", target_bir_lowering=False)
    b = Builder(nc)
    ap = {}

    def din(name, shape, dt=F32):
        ap[name] = nc.dram_tensor(name, list(shape), dt, kind="ExternalInput").ap()
        return ap[name]

    xT = din("xT", [D, NLOC])
    BreT = din("BreT", [D, DS]); BimT = din("BimT", [D, DS])
    CreT = din("CreT", [DS, D]); CimT = din("CimT", [DS, D])
    DT_in = din("DT", [D, D])
    W1_in = din("W1", [D, H]); W2_in = din("W2", [H, D])
    wg_in = din("wg", [D, E])
    fc1T_in = din("fc1T", [D, H]); fc2_in = din("fc2", [H, D])
    fc1b_in = din("fc1b", [128, 32])
    fc2b_in = din("fc2b", [1, D])
    lng_in = din("lng", [1, D]); lnb_in = din("lnb", [1, D])
    nu_in = din("nu", [128, 2]); th_in = din("th", [128, 2])
    rl_in = din("rl", [1, 1])
    cmask_in = din("cmask", [16, 2])
    tok0_in = din("tok0", [1, 1])

    out = nc.dram_tensor("out", [NLOC, D], F32, kind="ExternalOutput").ap()

    d_h = nc.dram_tensor("d_h", [H, NLOC], BF16, kind="Internal")
    d_w2bf = nc.dram_tensor("d_w2bf", [4096, 1024], BF16, kind="Internal")
    d_x1t = nc.dram_tensor("d_x1t", [NLOC, D], F32, kind="Internal")
    d_xn2bf = nc.dram_tensor("d_xn2bf", [NLOC, D], BF16, kind="Internal")
    d_xn2all = nc.dram_tensor("d_xn2all", [NCORE * NLOC, D], BF16, kind="Internal")
    d_cnt = nc.dram_tensor("d_cnt", [2, 8], F32, kind="Internal")
    d_cntall = nc.dram_tensor("d_cntall", [16, 8], F32, kind="Internal")
    d_pm = nc.dram_tensor("d_pm", [PMROWS, 1], F32, kind="Internal")
    d_pmy = nc.dram_tensor("d_pmy", [CAP, 1], F32, kind="Internal")
    d_disp = nc.dram_tensor("d_disp", [CAP, D], BF16, kind="Internal")
    d_y = nc.dram_tensor("d_y", [CAP, D], BF16, kind="Internal")
    d_yall = nc.dram_tensor("d_yall", [E * CAP, D], BF16, kind="Internal")
    d_fb = nc.dram_tensor("d_fb", [8, 64], F32, kind="Internal")

    dbg = {}
    if debug_level >= 1:
        dbg["o_xn"] = nc.dram_tensor("o_xn", [D, NLOC], BF16, kind="ExternalOutput").ap()
        dbg["o_h"] = nc.dram_tensor("o_h", [H, NLOC], BF16, kind="ExternalOutput").ap()
        dbg["o_bu"] = nc.dram_tensor("o_bu", [128, 2048], BF16, kind="ExternalOutput").ap()
        dbg["o_st"] = nc.dram_tensor("o_st", [128, 2048], BF16, kind="ExternalOutput").ap()
        dbg["o_y"] = nc.dram_tensor("o_y", [128, 4096], BF16, kind="ExternalOutput").ap()
        dbg["o_tab"] = nc.dram_tensor("o_tab", [128, 1032], F32, kind="ExternalOutput").ap()
    if debug_level >= 2:
        dbg["o_x1t"] = nc.dram_tensor("o_x1t", [NLOC, D], F32, kind="ExternalOutput").ap()
        dbg["o_xn2bf"] = nc.dram_tensor("o_xn2bf", [NLOC, D], BF16, kind="ExternalOutput").ap()
        dbg["o_rt"] = nc.dram_tensor("o_rt", [128, 320], F32, kind="ExternalOutput").ap()
        dbg["o_ao"] = nc.dram_tensor("o_ao", [128, 4096], F32, kind="ExternalOutput").ap()
    if debug_level >= 3:
        dbg["o_y3"] = nc.dram_tensor("o_y3", [CAP, D], BF16, kind="ExternalOutput").ap()
        dbg["o_disp"] = nc.dram_tensor("o_disp", [CAP, D], BF16, kind="ExternalOutput").ap()
        dbg["o_pmy"] = nc.dram_tensor("o_pmy", [CAP, 1], F32, kind="ExternalOutput").ap()
        dbg["o_posd"] = nc.dram_tensor("o_posd", [128, 64], F32, kind="ExternalOutput").ap()
        dbg["o_cnt"] = nc.dram_tensor("o_cnt", [16, 8], F32, kind="ExternalOutput").ap()
        dbg["o_cb2"] = nc.dram_tensor("o_cb2", [128, 2], F32, kind="ExternalOutput").ap()
        dbg["o_gkd"] = nc.dram_tensor("o_gkd", [128, 64], F32, kind="ExternalOutput").ap()
        dbg["o_xn2"] = nc.dram_tensor("o_xn2", [128, 4096], F32, kind="ExternalOutput").ap()
        dbg["o_z0"] = nc.dram_tensor("o_z0", [128, 512], F32, kind="ExternalOutput").ap()
        dbg["o_hrb"] = nc.dram_tensor("o_hrb", [128, 16384], BF16, kind="ExternalOutput").ap()
        dbg["o_w2"] = nc.dram_tensor("o_w2", [128, 2048], BF16, kind="ExternalOutput").ap()

    ident_f = din("ident_f", [128, 128], F32)
    ident_b = din("ident_b", [128, 128], BF16)
    tri_b = din("tri_b", [128, 128], BF16)

    A = nc.alloc_sbuf_tensor
    sb_identf = A("sb_identf", [128, 128], F32)
    sb_identb = A("sb_identb", [128, 128], BF16)
    sb_trib = A("sb_trib", [128, 128], BF16)
    sb_onescb = A("sb_onescb", [128, 1], BF16)
    sb_onesrf = A("sb_onesrf", [1, 128], F32)
    sb_ones128 = A("sb_ones128", [128, 128], F32)
    sb_eps = A("sb_eps", [1, 1], F32)
    sb_iota8 = A("sb_iota8", [128, 8], F32)
    sb_i32 = A("sb_i32", [128, 256], I32)
    sb_wg = A("sb_wg", [128, 64], BF16)
    sb_fc1b = A("sb_fc1b", [128, 32], F32)
    sb_sig = A("sb_sig", [128, 1], F32)
    sb_rl = A("sb_rl", [1, 1], F32)
    sb_tok0 = A("sb_tok0", [128, 1], F32)
    sb_cmask = A("sb_cmask", [16, 2], F32)
    sb_rows = A("sb_rows", [1, 1536], F32)
    sb_oh1 = A("sb_oh1", [128, 256], F32)
    sb_oh2 = A("sb_oh2", [128, 256], F32)
    sb_i1 = A("sb_i1", [128, 32], F32)
    sb_i2 = A("sb_i2", [128, 32], F32)
    sb_g1 = A("sb_g1", [128, 32], F32)
    sb_g2 = A("sb_g2", [128, 32], F32)
    sb_pre1 = A("sb_pre1", [128, 32], F32)
    sb_pre2 = A("sb_pre2", [128, 32], F32)
    sb_ct = A("sb_ct", [128, 64], F32)
    sb_tb = A("sb_tb", [128, 64], F32)
    sb_pos = A("sb_pos", [128, 64], F32)
    sb_gk = A("sb_gk", [128, 64], F32)
    sb_poi = A("sb_poi", [128, 64], I32)
    sb_tokid = A("sb_tokid", [128, 32], F32)
    sb_brow = A("sb_brow", [1, 512], F32)
    sb_iotaE = A("sb_iotaE", [128, 256], F32)
    sb_m1 = A("sb_m1", [128, 32], F32)
    sb_m2 = A("sb_m2", [128, 32], F32)
    rtA2 = A("rtA2", [128, 256], F32)
    sb_zero = A("sb_zero", [128, 129], F32)
    scW = Scope(nc)
    WBIG = scW.sb("WBIG", [128, 32768], BF16)
    WBIG2 = scW.sb("WBIG2", [128, 32768], BF16)
    XB, XSQ, STB, YB, HST = 16384, 17408, 18432, 20480, 24576

    sc1 = Scope(nc)
    PS = [sc1.ps(f"ps{i}", [128, 512], F32) for i in range(8)]
    tabPr = sc1.sb("tabPr", [128, 256], F32); tabPi = sc1.sb("tabPi", [128, 256], F32)
    tabQr = sc1.sb("tabQr", [128, 256], F32); tabQi = sc1.sb("tabQi", [128, 256], F32)
    sb_lam = sc1.sb("sb_lam", [128, 4], F32)
    sb_nt = sc1.sb("sb_nt", [128, 8], F32)
    sb_carry = sc1.sb("sb_carry", [128, 4], F32)
    sb_cs = sc1.sb("sb_cs", [128, 4], F32)
    sb_cst = sc1.sb("sb_cst", [128, 4], F32)
    scA_r = sc1.sb("scA_r", [128, 256], F32); scA_i = sc1.sb("scA_i", [128, 256], F32)
    scS_r = sc1.sb("scS_r", [128, 256], F32); scS_i = sc1.sb("scS_i", [128, 256], F32)
    scT_r = sc1.sb("scT_r", [128, 256], F32); scT_i = sc1.sb("scT_i", [128, 256], F32)
    sctmp = sc1.sb("sctmp", [128, 256], F32)
    SC0, SC1, SC2, SC3, SC4, SC5 = scA_r, scA_i, scS_r, scS_i, scT_r, scT_i
    sb_x = sc1.sb("sb_x", [128, 8192], BF16)
    sb_xnb1 = sc1.sb("sb_xnb1", [128, 4096], BF16)
    sb_st1 = sc1.sb("sb_st1", [128, 2048], BF16)
    sb_nmb = sc1.sb("sb_nmb", [128, 512], BF16)
    sb_rb = sc1.sb("sb_rb", [128, 512], BF16)
    sb_xnb = sc1.sb("sb_xnb", [128, 4096], BF16)
    sb_bu = sc1.sb("sb_bu", [128, 2048], BF16)
    sb_q = sc1.sb("sb_q", [128, 1024], BF16)
    sb_t2 = sc1.sb("sb_t2", [128, 1024], BF16)
    sb_sg2 = sb_q


    sLD = b.sem("sLD"); sG = b.sem("sG")
    sH0 = b.sem("sH0"); sW2A = b.sem("sW2A"); sW2B = b.sem("sW2B"); sW2C = b.sem("sW2C")
    sH0S = [b.sem("sH0S0"), b.sem("sH0S1")]
    sPE = b.sem("sPE"); sDV = b.sem("sDV"); sAC = b.sem("sAC"); sPO = b.sem("sPO")
    sCC = b.sem("sCC"); sST = b.sem("sST"); sXL = b.sem("sXL")
    sST1 = b.sem("sST1"); sST2 = b.sem("sST2"); sHS = [b.sem("sHS0"), b.sem("sHS1")]
    sOS = [b.sem("sOS0"), b.sem("sOS1")]
    sYS = [b.sem("sYS0"), b.sem("sYS1")]

    def dma(eng, outap, inap, sem=None):
        sem = sem if sem is not None else sLD
        b.op(eng, lambda E, o=outap, i=inap: E.dma_start(out=o, in_=i), sem, 16)

    dma("sync", sb_identf[:, :], ident_f[:, :])
    dma("sync", sb_identb[:, :], ident_b[:, :])
    dma("sync", sb_trib[:, :], tri_b[:, :])
    dma("sync", sb_nt[:, 0:2], nu_in[:, :])
    dma("sync", sb_nt[:, 2:4], th_in[:, :])
    dma("sync", sb_fc1b[:, :], fc1b_in[:, :])
    dma("sync", sb_rl[:, :], rl_in[:, :])
    dma("sync", sb_cmask[:, :], cmask_in[:, :])
    dma("sync", sb_tok0[0:1, :], tok0_in[:, :])
    v_consts = sLD.n

    for kc in range(8):
        dma("pool", WBIG2[:, kc * 256:(kc + 1) * 256], BreT[128 * kc:128 * (kc + 1), :], sG)
        dma("pool", WBIG2[:, 2048 + kc * 256:2048 + (kc + 1) * 256], BimT[128 * kc:128 * (kc + 1), :], sG)
    for cc in range(2):
        dma("pool", WBIG2[:, 4096 + cc * 1024:4096 + (cc + 1) * 1024], CreT[128 * cc:128 * (cc + 1), :], sG)
        dma("pool", WBIG2[:, 6144 + cc * 1024:6144 + (cc + 1) * 1024], CimT[128 * cc:128 * (cc + 1), :], sG)
    for kc in range(8):
        dma("pool", WBIG2[:, 8192 + kc * 1024:8192 + (kc + 1) * 1024], DT_in[128 * kc:128 * (kc + 1), :], sG)
    for kc in range(8):
        dma("pool", sb_wg[:, kc * 8:(kc + 1) * 8], wg_in[128 * kc:128 * (kc + 1), :], sG)
    v_bcd = sG.n
    for kc in range(8):
        dma("pool", WBIG[:, kc * 4096:(kc + 1) * 4096], W1_in[128 * kc:128 * (kc + 1), :], sG)
    v_w1 = sG.n

    b.op("dve", lambda E: E.memset(sb_onescb[:, :], 1.0), sDV)
    b.op("dve", lambda E: E.memset(sb_onesrf[:, :], 1.0), sDV)
    b.op("dve", lambda E: E.memset(sb_ones128[:, :], 1.0), sDV)
    b.op("dve", lambda E: E.memset(sb_eps[:, :], EPS), sDV)
    b.op("dve", lambda E: E.memset(sb_zero[:, :], 0.0), sDV)
    b.op("dve", lambda E: E.memset(sb_carry[:, :], 0.0), sDV)
    v_memset = sDV.n

    b.wait("sync", sDV, v_memset)
    _pmv = d_pm.ap().rearrange("(r p) one -> r (p one)", p=128)
    for _i in range(5):
        dma("sync", _pmv[_i * 128:(_i + 1) * 128, :], sb_zero[:, 0:128])
    dma("sync", _pmv[640:641, :], sb_zero[0:1, 0:128])

    b.wait("pool", sG, v_w1)
    b.op("pool", lambda E: E.iota(sb_i32[:, 0:8], pattern=[[1, 8]], base=0, channel_multiplier=0), sPO)
    b.op("pool", lambda E: E.iota(sb_i32[:, 8:40], pattern=[[8192, 4], [128, 8]], base=1, channel_multiplier=1), sPO)
    b.op("pool", lambda E: E.iota(sb_i32[:, 128:256], pattern=[[1, 128]], base=0, channel_multiplier=0), sPO)
    v_iota = sPO.n
    b.wait("dve", sPO, v_iota)
    b.op("dve", lambda E: E.tensor_copy(sb_iota8[:, :], sb_i32[:, 0:8]), sDV)
    b.op("dve", lambda E: E.tensor_copy(sb_tokid[:, :], sb_i32[:, 8:40]), sDV)
    v_iota_f = sDV.n
    b.op("pool", lambda E: E.iota(sb_i32[:, 40:72], pattern=[[0, 4], [1, 8]], base=0, channel_multiplier=0), sPO)
    b.wait("dve", sPO, sPO.n)
    for _q in range(8):
        b.op("dve", lambda E, _q=_q: E.tensor_copy(sb_iotaE[:, _q * 32:(_q + 1) * 32], sb_i32[:, 40:72]), sDV)
    v_iotaE = sDV.n

    b.wait("dve", sG, v_w1)
    b.op("dve", lambda E: E.tensor_scalar_mul(WBIG2[:, 6144:8192], WBIG2[:, 6144:8192], -1.0), sDV)
    v_cim = sDV.n

    b.wait("act", sLD, v_consts)
    b.op("act", lambda E: E.activation(sb_nt[:, 4:6], sb_nt[:, 0:2], AF.Exp), sAC)
    b.op("act", lambda E: E.activation(sb_nt[:, 6:8], sb_nt[:, 2:4], AF.Exp), sAC)
    v1 = sAC.n
    b.op("act", lambda E: E.activation(sb_lam[:, 0:2], sb_nt[:, 4:6], AF.Exp, scale=-2.0), sAC)
    v2 = sAC.n
    b.wait("dve", sAC, v2)
    b.op("dve", lambda E: E.tensor_scalar(sb_lam[:, 0:2], sb_lam[:, 0:2], -1.0, 1.0, ALU.mult, ALU.add), sDV)
    v3 = sDV.n
    b.wait("act", sDV, v3)
    b.op("act", lambda E: E.activation(sb_lam[:, 2:4], sb_lam[:, 0:2], AF.Ln), sAC)
    v4 = sAC.n
    b.wait("act", sAC, v4)
    b.op("act", lambda E: E.activation(sb_nt[:, 0:2], sb_lam[:, 2:4], AF.Exp, scale=0.5), sAC)
    b.op("act", lambda E: E.activation(sb_lam[:, 0:2], sb_nt[:, 4:6], AF.Exp, scale=-1.0), sAC)
    v_gam = sAC.n

    b.wait("dve", sPO, v_iota)
    b.op("dve", lambda E: E.tensor_copy(SC0[:, 0:128], sb_i32[:, 128:256]), sDV)
    b.op("dve", lambda E: E.tensor_copy(SC0[:, 128:256], sb_i32[:, 128:256]), sDV)
    v_tau = sDV.n
    b.wait("dve", sAC, v_gam)
    for cc in range(2):
        b.op("dve", lambda E, cc=cc: E.tensor_scalar_mul(SC1[:, cc * 128:(cc + 1) * 128], SC0[:, cc * 128:(cc + 1) * 128], sb_nt[:, 6 + cc:7 + cc]), sDV)
    b.wait("dve", sDV, sDV.n)
    for cc in range(2):
        b.op("dve", lambda E, cc=cc: E.tensor_scalar_mul(SC0[:, cc * 128:(cc + 1) * 128], SC0[:, cc * 128:(cc + 1) * 128], sb_nt[:, 4 + cc:5 + cc]), sDV)
    v_T = sDV.n
    b.wait("dve", sDV, v_T)
    b.op("dve", lambda E: E.tensor_scalar_mul(SC4[:, :], SC1[:, :], INV_2PI), sDV)
    b.wait("dve", sDV, sDV.n)
    b.op("dve", lambda E: E.tensor_copy(sb_i32[:, 0:256], SC4[:, :]), sDV)
    b.wait("dve", sDV, sDV.n)
    b.op("dve", lambda E: E.tensor_copy(SC4[:, :], sb_i32[:, 0:256]), sDV)
    b.wait("dve", sDV, sDV.n)
    b.op("dve", lambda E: E.scalar_tensor_tensor(SC1[:, :], SC4[:, :], -TWO_PI, SC1[:, :], ALU.mult, ALU.add), sDV)
    b.wait("dve", sDV, sDV.n)

    def wrap_once(dst, src):
        b.op("dve", lambda E: E.tensor_scalar(SC4[:, :], src, PI, None, ALU.is_gt), sDV)
        b.wait("dve", sDV, sDV.n)
        b.op("dve", lambda E: E.scalar_tensor_tensor(dst, SC4[:, :], -TWO_PI, src, ALU.mult, ALU.add), sDV)
        b.wait("dve", sDV, sDV.n)
        b.op("dve", lambda E: E.tensor_scalar(SC4[:, :], dst, -PI, None, ALU.is_lt), sDV)
        b.wait("dve", sDV, sDV.n)
        b.op("dve", lambda E: E.scalar_tensor_tensor(dst, SC4[:, :], TWO_PI, dst, ALU.mult, ALU.add), sDV)
        b.wait("dve", sDV, sDV.n)

    wrap_once(SC1[:, :], SC1[:, :])
    b.wait("act", sDV, sDV.n)
    b.op("act", lambda E: E.activation(SC2[:, :], SC1[:, :], AF.Sin), sAC)
    v_sin = sAC.n
    b.wait("dve", sAC, v_sin)
    b.op("dve", lambda E: E.tensor_scalar(SC1[:, :], SC1[:, :], PI / 2.0, None, ALU.add), sDV)
    b.wait("dve", sDV, sDV.n)
    wrap_once(SC1[:, :], SC1[:, :])
    b.wait("act", sDV, sDV.n)
    b.op("act", lambda E: E.activation(SC3[:, :], SC1[:, :], AF.Sin), sAC)
    b.op("act", lambda E: E.activation(SC4[:, :], SC0[:, :], AF.Exp), sAC)
    b.op("act", lambda E: E.activation(SC5[:, :], SC0[:, :], AF.Exp, scale=-1.0), sAC)
    v_trig = sAC.n
    b.wait("dve", sAC, v_trig)
    for cc in range(2):
        b.op("dve", lambda E, cc=cc: E.tensor_scalar_mul(SC4[:, cc * 128:(cc + 1) * 128], SC4[:, cc * 128:(cc + 1) * 128], sb_nt[:, cc:cc + 1]), sDV)
    b.wait("dve", sDV, sDV.n)
    b.op("dve", lambda E: E.tensor_mul(tabPr[:, :], SC4[:, :], SC3[:, :]), sDV)
    b.op("dve", lambda E: E.scalar_tensor_tensor(tabPi[:, :], SC2[:, :], -1.0, SC4[:, :], ALU.mult, ALU.mult), sDV)
    b.op("dve", lambda E: E.tensor_mul(tabQr[:, :], SC5[:, :], SC3[:, :]), sDV)
    b.op("dve", lambda E: E.tensor_mul(tabQi[:, :], SC5[:, :], SC2[:, :]), sDV)
    b.wait("dve", sDV, sDV.n)
    for cc in range(2):
        b.op("dve", lambda E, cc=cc: E.tensor_mul(sb_lam[:, 2 + cc:3 + cc], sb_lam[:, cc:cc + 1],
                                                  SC2[:, cc * 128 + 1:cc * 128 + 2]), sDV)
    b.wait("dve", sDV, sDV.n)
    for cc in range(2):
        b.op("dve", lambda E, cc=cc: E.tensor_mul(sb_lam[:, cc:cc + 1], sb_lam[:, cc:cc + 1],
                                                  SC3[:, cc * 128 + 1:cc * 128 + 2]), sDV)
    v_tables = sDV.n
    if debug_level >= 1:
        b.wait("sync", sDV, v_tables)
        dma("sync", dbg["o_tab"][:, 0:256], tabPr[:, :])
        dma("sync", dbg["o_tab"][:, 256:512], tabPi[:, :])
        dma("sync", dbg["o_tab"][:, 512:768], tabQr[:, :])
        dma("sync", dbg["o_tab"][:, 768:1024], tabQi[:, :])
        dma("sync", dbg["o_tab"][:, 1024:1028], sb_lam[:, :])
        dma("sync", dbg["o_tab"][:, 1028:1032], sb_nt[:, 0:4])

    b.wait("pe", sLD, v_consts)
    b.wait("pe", sDV, v_memset)
    b.op("pe", lambda E: E.matmul(PS[0][:, 0:1], sb_onesrf[:, :], sb_rl[:, :], start=True, stop=True), sPE)
    b.op("pe", lambda E: E.matmul(PS[0][:, 4:5], sb_onesrf[:, :], sb_tok0[0:1, :], start=True, stop=True), sPE)
    v_bc0 = sPE.n
    b.wait("act", sPE, v_bc0)
    b.op("act", lambda E: E.activation(sb_sig[:, :], PS[0][:, 0:1], AF.Sigmoid), sAC)
    v_sig = sAC.n
    b.wait("dve", sPE, v_bc0)
    b.op("dve", lambda E: E.tensor_copy(sb_tok0[:, :], PS[0][:, 4:5]), sDV)
    b.wait("dve", sDV, sDV.n)
    b.wait("dve", sDV, v_iota_f)
    b.op("dve", lambda E: E.tensor_scalar_add(sb_tokid[:, :], sb_tokid[:, :], sb_tok0[:, :]), sDV)
    v_p0 = sDV.n


    psU, psB0, psB1, psB2, psB3, psYa, psYb, psUa = PS
    psUb = psB0

    b.wait("pe", sG, v_bcd)
    b.wait("pe", sDV, v_p0)
    b.wait("pe", sAC, v_sig)
    b.wait("dve", sDV, v_tables)
    b.wait("dve", sDV, v_cim)

    M = {}

    def p1_chunk(k):
        t0 = k * T
        xpar = k % 2
        xsl = sb_x[:, xpar * 4096:(xpar + 1) * 4096]
        xnbuf = sb_xnb if xpar == 0 else sb_xnb1

        b.wait("act", sXL, M[("xld", k)])
        b.wait("pe", sXL, M[("xld", k)])
        if k > 0:
            b.wait("pe", sAC, M[("rbev", k - 1)])
        for kc in range(8):
            par = kc % 2
            xk = xsl[:, kc * T:(kc + 1) * T]
            xsv = WBIG2[:, XSQ + par * 512:XSQ + (par + 1) * 512]
            if kc >= 2:
                b.wait("act", sPE, M[("stmm", k, kc - 2)])
            elif k > 0:
                b.wait("act", sPE, M[("stats", k - 1)])
            b.op("act", lambda E, xk=xk, xsv=xsv: E.activation(xsv, xk, AF.Square), sAC)
            vs = sAC.n
            b.wait("pe", sAC, vs)
            b.op("pe", lambda E, kc=kc, xk=xk: E.matmul(psU[0:1, :], sb_onescb[:, :], xk, start=(kc == 0), stop=(kc == 7)), sPE)
            b.op("pe", lambda E, kc=kc, xsv=xsv: E.matmul(psU[32:33, :], sb_onescb[:, :], xsv, start=(kc == 0), stop=(kc == 7)), sPE)
            M[("stmm", k, kc)] = sPE.n
        M[("stats", k)] = sPE.n

        b.wait("act", sPE, M[("stats", k)])
        b.op("act", lambda E: E.activation(sb_rows[:, 0:512], psU[0:1, :], AF.Copy, scale=INV_D), sAC)
        vm = sAC.n
        b.wait("act", sAC, vm)
        b.op("act", lambda E: E.activation(sb_rows[:, 512:1024], sb_rows[:, 0:512], AF.Square), sAC)
        vq = sAC.n
        b.wait("dve", sAC, vq)
        b.op("dve", lambda E: E.scalar_tensor_tensor(sb_rows[:, 512:1024], psU[32:33, :], INV_D, sb_rows[:, 512:1024], ALU.mult, ALU.subtract), sDV)
        vv = sDV.n
        b.wait("act", sDV, vv)
        b.op("act", lambda E: E.activation(sb_rows[:, 1024:1536], sb_rows[:, 512:1024], AF.Ln, bias=sb_eps[:, :]), sAC)
        b.wait("act", sAC, sAC.n)
        b.op("act", lambda E: E.activation(sb_rows[:, 1024:1536], sb_rows[:, 1024:1536], AF.Exp, scale=-0.5), sAC)
        vr = sAC.n
        b.wait("dve", sAC, vr)
        b.op("dve", lambda E: E.tensor_scalar_mul(sb_rows[:, 512:1024], sb_rows[:, 0:512], -1.0), sDV)
        vnm = sDV.n

        b.wait("pe", sDV, vnm)
        b.wait("pe", sDV, vv)
        b.op("pe", lambda E: E.matmul(psU[:, :], sb_onesrf[:, :], sb_rows[:, 512:1024], start=True, stop=True), sPE)
        vb1 = sPE.n
        b.wait("act", sPE, vb1)
        if k > 0:
            b.wait("act", sDV, M[("pass1", k - 1)])
        b.op("act", lambda E: E.copy(sb_nmb[:, :], psU[:, :]), sAC)
        vnmb = sAC.n
        b.wait("dve", sAC, vnmb)
        for kc in range(8):
            b.op("dve", lambda E, kc=kc, xsl=xsl: E.tensor_add(xsl[:, kc * T:(kc + 1) * T], xsl[:, kc * T:(kc + 1) * T], sb_nmb[:, :]), sDV)
        M[("pass1", k)] = sDV.n
        b.wait("pe", sAC, vnmb)
        b.wait("pe", sAC, vr)
        b.op("pe", lambda E: E.matmul(psU[:, :], sb_onesrf[:, :], sb_rows[:, 1024:1536], start=True, stop=True), sPE)
        vb2 = sPE.n
        b.wait("act", sPE, vb2)
        if k > 0:
            b.wait("act", sDV, M[("xn", k - 1)])
        b.op("act", lambda E: E.copy(sb_rb[:, :], psU[:, :]), sAC)
        M[("rbev", k)] = sAC.n
        b.wait("dve", sAC, M[("rbev", k)])
        if k >= 2:
            b.wait("dve", sPE, M[("ymm", k - 2, 3)])
        for kc in range(8):
            b.op("dve", lambda E, kc=kc, xsl=xsl, xnbuf=xnbuf: E.tensor_mul(xnbuf[:, kc * T:(kc + 1) * T], xsl[:, kc * T:(kc + 1) * T], sb_rb[:, :]), sDV)
        M[("xn", k)] = sDV.n
        M[("xnb", k)] = M[("xn", k)]
        if debug_level >= 1:
            b.wait("sync", sDV, M[("xn", k)])
            dma("sync",
                dbg["o_xn"].rearrange("(kc p) t -> p kc t", p=128)[:, :, t0:t0 + T],
                xnbuf.rearrange("p (kc t) -> p kc t", t=T))
            M[("xnst", k)] = sLD.n

        b.wait("pe", sDV, M[("xnb", k)])
        if k >= 2:
            b.wait("pe", sAC, M[("uev", k - 2, 15)])
        elif k == 1:
            b.wait("pe", sAC, M[("buev", 0)])
        for kc in range(8):
            rsl = xnbuf[:, kc * T:(kc + 1) * T]
            for (dst, base, nh) in ((psB0, 0, 0), (psB1, 0, 1), (psB2, 2048, 0), (psB3, 2048, 1)):
                b.op("pe", lambda E, dst=dst, base=base, nh=nh, kc=kc, rsl=rsl:
                     E.matmul(dst[:, :], WBIG2[:, base + kc * 256 + nh * 128:base + kc * 256 + (nh + 1) * 128],
                              rsl, start=(kc == 0), stop=(kc == 7)), sPE)
        M[("bu", k)] = sPE.n
        b.wait("act", sPE, M[("bu", k)])
        if k > 0:
            b.wait("act", sDV, M[("scan", k - 1)])
        if debug_level >= 1 and k == 1:
            b.wait("act", sLD, M[("dmp_bu", 0)])
        for i, src in enumerate((psB0, psB1, psB2, psB3)):
            b.op("act", lambda E, i=i, src=src: E.copy(sb_bu[:, i * 512:(i + 1) * 512], src[:, :]), sAC)
        M[("buev", k)] = sAC.n

        if debug_level >= 1 and k == 0:
            b.wait("sync", sAC, M[("buev", k)])
            dma("sync", dbg["o_bu"][:, :], sb_bu[:, :])
            M[("dmp_bu", 0)] = sLD.n
        emit_scan(k)
        if debug_level >= 1 and k == 0:
            b.wait("sync", sDV, M[("scan", k)])
            dma("sync", dbg["o_st"][:, :], WBIG2[:, STB:STB + 2048])
            M[("dmp_st", 0)] = sLD.n

        if k > 0:
            emit_y(k - 1)
            emit_w1(k - 1, M[("buev", k)])

        if k + 1 < NCH:
            if k >= 1:
                b.wait("pool", sDV, M[("xn", k - 1)])
            nsl = sb_x[:, ((k + 1) % 2) * 4096:(((k + 1) % 2) + 1) * 4096]
            b.op("pool", lambda E, t0n=(k + 1) * T, nsl=nsl: E.dma_start(
                out=nsl.rearrange("p (kc t) -> p kc t", t=T),
                in_=xT.rearrange("(kc p) t -> p kc t", p=128)[:, :, t0n:t0n + T]), sXL, 16)
            M[("xld", k + 1)] = sXL.n

        if k == NCH - 1:
            emit_y(k)
            emit_w1(k, M[("buev", k)])
        if debug_level >= 1 and k == 1:
            b.wait("sync", sAC, M[("yev", 0, 3)])
            dma("sync", dbg["o_y"][:, :], WBIG2[:, YB:YB + 4096])
            M[("dmp_y", 0)] = sLD.n

    def emit_scan(k):
        stv = WBIG2[:, STB:STB + 2048] if k % 2 == 0 else sb_st1
        b.wait("dve", sAC, M[("buev", k)])
        if k >= 2:
            b.wait("dve", sPE, M[("ymm", k - 2, 3)])
        if debug_level >= 1 and k == 1:
            b.wait("dve", sLD, M[("dmp_st", 0)])
        if k % 4 == 0:
            b.op("dve", lambda E: E.memset(sb_carry[:, :], 0.0), sDV)
            b.wait("dve", sDV, sDV.n)
        for sub in range(4):
            b.op("dve", lambda E: E.tensor_mul(sb_cs[:, 0:2], sb_lam[:, 0:2], sb_carry[:, 0:2]), sDV)
            b.op("dve", lambda E: E.tensor_mul(sb_cst[:, 0:2], sb_lam[:, 2:4], sb_carry[:, 2:4]), sDV)
            b.op("dve", lambda E: E.tensor_mul(sb_cs[:, 2:4], sb_lam[:, 2:4], sb_carry[:, 0:2]), sDV)
            b.op("dve", lambda E: E.tensor_mul(sb_cst[:, 2:4], sb_lam[:, 0:2], sb_carry[:, 2:4]), sDV)
            b.wait("dve", sDV, sDV.n)
            b.op("dve", lambda E: E.tensor_sub(sb_cs[:, 0:2], sb_cs[:, 0:2], sb_cst[:, 0:2]), sDV)
            b.op("dve", lambda E: E.tensor_add(sb_cs[:, 2:4], sb_cs[:, 2:4], sb_cst[:, 2:4]), sDV)
            b.wait("dve", sDV, sDV.n)
            bur = sb_bu[:, 0:1024].rearrange("p (cc t) -> p cc t", cc=2)[:, :, sub * SUB:(sub + 1) * SUB]
            bui = sb_bu[:, 1024:2048].rearrange("p (cc t) -> p cc t", cc=2)[:, :, sub * SUB:(sub + 1) * SUB]
            a_r = scA_r.rearrange("p (cc t) -> p cc t", cc=2)
            a_i = scA_i.rearrange("p (cc t) -> p cc t", cc=2)
            tmpv = sctmp.rearrange("p (cc t) -> p cc t", cc=2)
            prv = tabPr.rearrange("p (cc t) -> p cc t", cc=2)
            piv = tabPi.rearrange("p (cc t) -> p cc t", cc=2)
            b.op("dve", lambda E, a_r=a_r, prv=prv, bur=bur: E.tensor_mul(a_r, prv, bur), sDV)
            b.op("dve", lambda E, tmpv=tmpv, piv=piv, bui=bui: E.tensor_mul(tmpv, piv, bui), sDV)
            b.op("dve", lambda E: E.tensor_sub(scA_r[:, :], scA_r[:, :], sctmp[:, :]), sDV)
            b.op("dve", lambda E, a_i=a_i, prv=prv, bui=bui: E.tensor_mul(a_i, prv, bui), sDV)
            b.op("dve", lambda E, tmpv=tmpv, piv=piv, bur=bur: E.tensor_mul(tmpv, piv, bur), sDV)
            b.op("dve", lambda E: E.tensor_add(scA_i[:, :], scA_i[:, :], sctmp[:, :]), sDV)
            for cc in range(2):
                b.op("dve", lambda E, cc=cc: E.tensor_tensor_scan(scS_r[:, cc * 128:(cc + 1) * 128], sb_ones128[:, :], scA_r[:, cc * 128:(cc + 1) * 128], sb_cs[:, cc:cc + 1], ALU.mult, ALU.add), sDV)
                b.op("dve", lambda E, cc=cc: E.tensor_tensor_scan(scS_i[:, cc * 128:(cc + 1) * 128], sb_ones128[:, :], scA_i[:, cc * 128:(cc + 1) * 128], sb_cs[:, 2 + cc:3 + cc], ALU.mult, ALU.add), sDV)
            b.op("dve", lambda E: E.tensor_mul(scT_r[:, :], tabQr[:, :], scS_r[:, :]), sDV)
            b.op("dve", lambda E: E.tensor_mul(sctmp[:, :], tabQi[:, :], scS_i[:, :]), sDV)
            b.op("dve", lambda E: E.tensor_sub(scT_r[:, :], scT_r[:, :], sctmp[:, :]), sDV)
            b.op("dve", lambda E: E.tensor_mul(scT_i[:, :], tabQr[:, :], scS_i[:, :]), sDV)
            b.op("dve", lambda E: E.tensor_mul(sctmp[:, :], tabQi[:, :], scS_r[:, :]), sDV)
            b.op("dve", lambda E: E.tensor_add(scT_i[:, :], scT_i[:, :], sctmp[:, :]), sDV)
            b.wait("dve", sDV, sDV.n)
            b.op("dve", lambda E: E.tensor_copy(sb_carry[:, 0:2], scT_r.rearrange("p (cc t) -> p cc t", cc=2)[:, :, 127:128]), sDV)
            b.op("dve", lambda E: E.tensor_copy(sb_carry[:, 2:4], scT_i.rearrange("p (cc t) -> p cc t", cc=2)[:, :, 127:128]), sDV)
            str_v = stv[:, 0:1024].rearrange("p (cc t) -> p cc t", cc=2)[:, :, sub * SUB:(sub + 1) * SUB]
            sti_v = stv[:, 1024:2048].rearrange("p (cc t) -> p cc t", cc=2)[:, :, sub * SUB:(sub + 1) * SUB]
            b.op("dve", lambda E, str_v=str_v: E.tensor_copy(str_v, scT_r.rearrange("p (cc t) -> p cc t", cc=2)), sDV)
            b.op("dve", lambda E, sti_v=sti_v: E.tensor_copy(sti_v, scT_i.rearrange("p (cc t) -> p cc t", cc=2)), sDV)
            b.wait("dve", sDV, sDV.n)
            M[("tts", k, sub)] = sDV.n
        M[("scan", k)] = sDV.n

    def emit_y(k):
        stv = WBIG2[:, STB:STB + 2048] if k % 2 == 0 else sb_st1
        ynb = sb_xnb if k % 2 == 0 else sb_xnb1
        b.wait("pe", sDV, M[("scan", k)])
        for mg in range(4):
            if mg > 0:
                b.wait("pe", sAC, M[("yev", k, mg - 1)])
            elif k > 0:
                b.wait("pe", sAC, M[("yev", k - 1, 3)])
            for mi in range(2):
                m = mg * 2 + mi
                dst = psYa if mi == 0 else psYb
                for cc in range(2):
                    b.op("pe", lambda E, dst=dst, cc=cc, m=m: E.matmul(dst[:, :], WBIG2[:, 4096 + cc * 1024 + m * 128:4096 + cc * 1024 + (m + 1) * 128], stv[:, cc * 512:(cc + 1) * 512], start=(cc == 0), stop=False), sPE)
                for cc in range(2):
                    b.op("pe", lambda E, dst=dst, cc=cc, m=m: E.matmul(dst[:, :], WBIG2[:, 6144 + cc * 1024 + m * 128:6144 + cc * 1024 + (m + 1) * 128], stv[:, 1024 + cc * 512:1024 + (cc + 1) * 512], start=False, stop=False), sPE)
                for kc in range(8):
                    b.op("pe", lambda E, dst=dst, kc=kc, m=m: E.matmul(dst[:, :], WBIG2[:, 8192 + kc * 1024 + m * 128:8192 + kc * 1024 + (m + 1) * 128], ynb[:, kc * T:(kc + 1) * T], start=False, stop=(kc == 7)), sPE)
            M[("ymm", k, mg)] = sPE.n
            b.wait("act", sPE, M[("ymm", k, mg)])
            if debug_level >= 1 and k == 1 and mg == 0:
                b.wait("act", sLD, M[("dmp_y", 0)])
            for mi in range(2):
                m = mg * 2 + mi
                src = psYa if mi == 0 else psYb
                b.op("act", lambda E, src=src, m=m: E.copy(WBIG2[:, YB + m * 512:YB + (m + 1) * 512], src[:, :]), sAC)
            M[("yev", k, mg)] = sAC.n

    def emit_w1(k, buev_wait):
        t0 = k * T
        if k == 0:
            b.wait("pe", sG, v_w1)
        b.wait("pe", sAC, buev_wait)
        b.wait("pe", sAC, M[("yev", k, 3)])
        UP = [(psUa, psB1), (psB2, psB3)]
        for hp in range(16):
            pp = hp % 2
            bk0, bk1 = UP[pp]
            if hp >= 2:
                b.wait("pe", sAC, M[("uev", k, hp - 2)])
            elif k > 0:
                b.wait("pe", sAC, M[("uev", k - 1, 15)])
            for kc in range(8):
                b.op("pe", lambda E, bk0=bk0, kc=kc, hp=hp: E.matmul(bk0[:, :], WBIG[:, kc * 4096 + (2 * hp) * 128:kc * 4096 + (2 * hp + 1) * 128], WBIG2[:, YB + kc * 512:YB + (kc + 1) * 512], start=(kc == 0), stop=(kc == 7)), sPE)
            for kc in range(8):
                b.op("pe", lambda E, bk1=bk1, kc=kc, hp=hp: E.matmul(bk1[:, :], WBIG[:, kc * 4096 + (2 * hp + 1) * 128:kc * 4096 + (2 * hp + 2) * 128], WBIG2[:, YB + kc * 512:YB + (kc + 1) * 512], start=(kc == 0), stop=(kc == 7)), sPE)
            M[("u", k, hp)] = sPE.n
            USB = 26624 + pp * 1024
            b.wait("act", sPE, M[("u", k, hp)])
            if hp >= 2:
                b.wait("act", sDV, M[("gel", k, hp - 2)])
            elif k > 0:
                b.wait("act", sDV, M[("gel", k - 1, 14 + pp)])
            b.op("act", lambda E, bk0=bk0, USB=USB: E.copy(WBIG2[:, USB:USB + 512], bk0[:, :]), sAC)
            b.op("act", lambda E, bk1=bk1, USB=USB: E.copy(WBIG2[:, USB + 512:USB + 1024], bk1[:, :]), sAC)
            M[("uev", k, hp)] = sAC.n
            if hp > 0:
                b.wait("act", sDV, M[("t2", k, hp - 1)])
            elif k > 0:
                b.wait("act", sDV, M[("t2", k - 1, 15)])
            b.op("act", lambda E, USB=USB: E.activation(sb_q[:, :], WBIG2[:, USB:USB + 1024], AF.Square, scale=SQ2C2), sAC)
            vq2 = sAC.n
            b.wait("dve", sAC, vq2)
            b.op("dve", lambda E, USB=USB: E.scalar_tensor_tensor(sb_t2[:, :], sb_q[:, :], C1x2, WBIG2[:, USB:USB + 1024], ALU.add, ALU.mult), sDV)
            vt2 = sDV.n
            M[("t2", k, hp)] = vt2
            b.wait("act", sDV, vt2)
            b.op("act", lambda E: E.activation(sb_sg2[:, :], sb_t2[:, :], AF.Sigmoid), sAC)
            vsg = sAC.n
            b.wait("dve", sAC, vsg)
            b.wait("dve", sHS[pp], M[("hst", k, hp - 2)] if hp >= 2 else (M[("hst", k - 1, 14 + pp)] if k > 0 else 0))
            b.op("dve", lambda E, pp=pp, USB=USB: E.tensor_mul(WBIG2[:, HST + pp * 1024:HST + (pp + 1) * 1024], sb_sg2[:, :], WBIG2[:, USB:USB + 1024]), sDV)
            M[("gel", k, hp)] = sDV.n
            b.wait("sync", sDV, M[("gel", k, hp)])
            b.op("sync", lambda E, hp=hp, t0=t0, pp=pp: E.dma_start(
                out=d_h.ap().rearrange("(H p) t -> p H t", p=128)[:, 2 * hp:2 * hp + 2, t0:t0 + T],
                in_=WBIG2[:, HST + pp * 1024:HST + (pp + 1) * 1024].rearrange("p (H t) -> p H t", t=T)), sHS[pp], 16)
            if debug_level >= 1:
                b.op("sync", lambda E, hp=hp, t0=t0, pp=pp: E.dma_start(
                    out=dbg["o_h"].rearrange("(H p) t -> p H t", p=128)[:, 2 * hp:2 * hp + 2, t0:t0 + T],
                    in_=WBIG2[:, HST + pp * 1024:HST + (pp + 1) * 1024].rearrange("p (H t) -> p H t", t=T)), sHS[pp], 16)
            M[("hst", k, hp)] = sHS[pp].n

    b.op("pool", lambda E: E.dma_start(
        out=sb_x[:, 0:4096].rearrange("p (kc t) -> p kc t", t=T),
        in_=xT.rearrange("(kc p) t -> p kc t", p=128)[:, :, 0:T]), sXL, 16)
    M[("xld", 0)] = sXL.n
    for k in range(NCH):
        p1_chunk(k)
        b.op("pool", lambda E, j=k: E.dma_start(out=d_w2bf.ap()[512 * j:512 * (j + 1), :], in_=W2_in[512 * j:512 * (j + 1), :]), sW2C, 16)
    v_w2cast = sW2C.n
    sG_pre = sG.n
    for qeng, qsem, j0, j1 in (("pool", sG, 0, 10), ("act", sW2A, 10, 21), ("sync", sW2B, 21, 32)):
        b.wait(qeng, sAC, M[("uev", NCH - 1, 15)])
        b.wait(qeng, sW2C, v_w2cast)
        for j in range(j0, j1):
            b.op(qeng, lambda E, j=j: E.dma_start(out=WBIG[:, j * 1024:(j + 1) * 1024], in_=d_w2bf.ap()[128 * j:128 * (j + 1), :]), qsem, 16)
    M[("vw2",)] = (sG.n, sW2A.n, sW2B.n)

    b.barrier("p1", extra_waits=[("sync", sHS[0], sHS[0].n), ("sync", sHS[1], sHS[1].n), ("sync", sLD, sLD.n), ("pool", sG, sG_pre)])
    print("sbuf remaining before sc1.close:", nc.sbuf_bytes_remaining)
    sc1.close()
    print("sbuf remaining after sc1.close:", nc.sbuf_bytes_remaining)

    sc2 = Scope(nc)
    sb_x2 = sc2.sb("sb_x2", [128, 4096], F32)
    sb_xn2 = sc2.sb("sb_xn2", [128, 4096], F32)
    sb_ao = sb_xn2
    sb_xnb2 = sc2.sb("sb_xnb2", [128, 4096], BF16)
    sb_cb = sc2.sb("sb_cb", [128, 2048], BF16)
    stg1 = sc2.sb("stg1", [128, 1024], F32)
    stg2 = sc2.sb("stg2", [128, 1024], BF16)
    dbgz0_t = sc2.sb("dbgz0_t", [128, 512], F32)
    rtA = sc2.sb("rtA", [128, 256], F32)
    rtB = sc2.sb("rtB", [128, 256], F32)
    psU2 = sc2.ps("psU2", [128, 512], F32)
    psZ = [sc2.ps(f"psZ{i}", [128, 512], F32) for i in range(4)]
    psLG = sc2.ps("psLG", [128, 512], F32)
    psTRf = sc2.ps("psTRf", [128, 512], F32)
    psTRb = sc2.ps("psTRb", [128, 512], BF16)

    v_w2, v_w2a, v_w2b = M[("vw2",)]
    b.wait("pe", sG, v_w2)
    b.wait("pe", sW2A, v_w2a)
    b.wait("pe", sW2B, v_w2b)
    if debug_level >= 2:
        b.wait("sync", sG, v_w2)
        b.wait("sync", sW2A, v_w2a)
        b.wait("sync", sW2B, v_w2b)
        dma("sync", dbg["o_w2"][:, 0:1024], WBIG[:, 0:1024])
        dma("sync", dbg["o_w2"][:, 1024:2048], WBIG[:, 1024:2048])

    M2 = {}

    def emit_ln2(k, src_t, tag, out_mode, pe_wait=None, src_dv=None):
        if pe_wait is not None:
            b.wait("pe", sDV, pe_wait)
        if src_dv is not None:
            b.wait("act", sDV, src_dv)
        for kc in range(8):
            par = kc % 2
            sk = src_t[:, kc * T:(kc + 1) * T]
            cbv = sb_cb[:, par * 512:(par + 1) * 512]
            sqv = sb_cb[:, 1024 + par * 512:1024 + (par + 1) * 512]
            if kc >= 2:
                b.wait("dve", sPE, M2[("stmm", tag, k, kc - 2)])
                b.wait("act", sPE, M2[("stmm", tag, k, kc - 2)])
            b.op("dve", lambda E, sk=sk, cbv=cbv: E.tensor_copy(cbv, sk), sDV)
            vc = sDV.n
            b.op("act", lambda E, sk=sk, sqv=sqv: E.activation(sqv, sk, AF.Square), sAC)
            vs = sAC.n
            b.wait("pe", sDV, vc)
            b.wait("pe", sAC, vs)
            b.op("pe", lambda E, kc=kc, cbv=cbv: E.matmul(psU2[0:1, :], sb_onescb[:, :], cbv, start=(kc == 0), stop=(kc == 7)), sPE)
            b.op("pe", lambda E, kc=kc, sqv=sqv: E.matmul(psU2[32:33, :], sb_onescb[:, :], sqv, start=(kc == 0), stop=(kc == 7)), sPE)
            M2[("stmm", tag, k, kc)] = sPE.n
        b.wait("act", sPE, sPE.n)
        b.op("act", lambda E: E.activation(sb_rows[:, 0:512], psU2[0:1, :], AF.Copy, scale=INV_D), sAC)
        vm = sAC.n
        b.wait("act", sAC, vm)
        b.op("act", lambda E: E.activation(sb_rows[:, 512:1024], sb_rows[:, 0:512], AF.Square), sAC)
        vq = sAC.n
        b.wait("dve", sAC, vq)
        b.op("dve", lambda E: E.scalar_tensor_tensor(sb_rows[:, 512:1024], psU2[32:33, :], INV_D, sb_rows[:, 512:1024], ALU.mult, ALU.subtract), sDV)
        vv = sDV.n
        b.wait("act", sDV, vv)
        b.op("act", lambda E: E.activation(sb_rows[:, 1024:1536], sb_rows[:, 512:1024], AF.Ln, bias=sb_eps[:, :]), sAC)
        b.wait("act", sAC, sAC.n)
        b.op("act", lambda E: E.activation(sb_rows[:, 1024:1536], sb_rows[:, 1024:1536], AF.Exp, scale=-0.5), sAC)
        vr = sAC.n
        b.wait("dve", sAC, vr)
        b.op("dve", lambda E: E.tensor_scalar_mul(sb_rows[:, 512:1024], sb_rows[:, 0:512], -1.0), sDV)
        vnm = sDV.n
        b.wait("pe", sDV, vnm)
        b.wait("pe", sDV, vv)
        b.op("pe", lambda E: E.matmul(psU2[:, :], sb_onesrf[:, :], sb_rows[:, 512:1024], start=True, stop=True), sPE)
        vb1 = sPE.n
        b.wait("dve", sPE, vb1)
        if out_mode == "bf":
            for kc in range(8):
                b.op("dve", lambda E, kc=kc, src_t=src_t: E.tensor_add(sb_xn2[:, kc * T:(kc + 1) * T], src_t[:, kc * T:(kc + 1) * T], psU2[:, :]), sDV)
        else:
            for kc in range(8):
                b.op("dve", lambda E, kc=kc, src_t=src_t: E.tensor_add((sb_xn2 if out_mode == "xn" else src_t)[:, kc * T:(kc + 1) * T], src_t[:, kc * T:(kc + 1) * T], psU2[:, :]), sDV)
        vsub = sDV.n
        b.wait("pe", sDV, vsub)
        b.wait("pe", sAC, vr)
        b.op("pe", lambda E: E.matmul(psU2[:, :], sb_onesrf[:, :], sb_rows[:, 1024:1536], start=True, stop=True), sPE)
        vb2 = sPE.n
        M2[("bc2", tag, k)] = sPE.n
        b.wait("dve", sPE, vb2)
        for kc in range(8):
            if out_mode == "xn":
                b.op("dve", lambda E, kc=kc: E.tensor_mul(sb_xn2[:, kc * T:(kc + 1) * T], sb_xn2[:, kc * T:(kc + 1) * T], psU2[:, :]), sDV)
            elif out_mode == "inplace":
                b.op("dve", lambda E, kc=kc, src_t=src_t: E.tensor_mul(src_t[:, kc * T:(kc + 1) * T], src_t[:, kc * T:(kc + 1) * T], psU2[:, :]), sDV)
            else:
                b.op("dve", lambda E, kc=kc: E.tensor_mul(sb_xnb2[:, kc * T:(kc + 1) * T], sb_xn2[:, kc * T:(kc + 1) * T], psU2[:, :]), sDV)
        M2[("ln", tag, k)] = sDV.n

    for k in range(NCH):
        par = k % 2
        t0 = k * T
        HOF = par * 16384
        if k >= 2:
            b.wait("act", sPE, M2[("zmm", k - 2)])
        b.op("act", lambda E, t0=t0, HOF=HOF: E.dma_start(
            out=WBIG2[:, HOF:HOF + 16384].rearrange("p (kc t) -> p kc t", t=T),
            in_=d_h.ap().rearrange("(kc p) t -> p kc t", p=128)[:, :, t0:t0 + T]), sXL, 16)
        M2[("hld", k)] = sXL.n
        if k > 0:
            b.wait("act", sPE, M2[("trx1", k - 1)])
        b.op("act", lambda E, t0=t0: E.dma_start(
            out=sb_x2.rearrange("p (kc t) -> p kc t", t=T),
            in_=xT.rearrange("(kc p) t -> p kc t", p=128)[:, :, t0:t0 + T]), sXL, 16)
        M2[("xld", k)] = sXL.n
        b.wait("dve", sXL, M2[("xld", k)])
        b.wait("act", sXL, M2[("xld", k)])
        if k > 0:
            b.wait("dve", sPE, M2[("trx1", k - 1)])
        emit_ln2(k, sb_x2, "x", "xn",
                 pe_wait=(M2[("trcpf", k - 1, 3, 1)] if k > 0 else None))
        if debug_level >= 2 and k == 0:
            b.wait("sync", sDV, M2[("ln", "x", k)])
            dma("sync", dbg["o_xn2"][:, :], sb_xn2[:, :])
            M2[("xndmp",)] = sLD.n
            b.wait("sync", sLD, M2[("hld", k)])
            dma("sync", dbg["o_hrb"][:, :], WBIG2[:, 0:16384])
        b.wait("pe", sXL, M2[("hld", k)])
        for m in range(8):
            bank = psZ[m % 4]
            if m >= 4:
                b.wait("pe", sDV, M2[("ao", k, m - 4)])
            elif k > 0:
                b.wait("pe", sDV, M2[("ao", k - 1, m + 4 - 8 + 8)])
            for kc in range(32):
                b.op("pe", lambda E, bank=bank, kc=kc, m=m, HOF=HOF: E.matmul(bank[:, :], WBIG[:, kc * 1024 + m * 128:kc * 1024 + (m + 1) * 128], WBIG2[:, HOF + kc * 512:HOF + (kc + 1) * 512], start=(kc == 0), stop=(kc == 31)), sPE)
            M2[("zmm_m", k, m)] = sPE.n
            b.wait("dve", sPE, M2[("zmm_m", k, m)])
            b.wait("dve", sDV, M2[("ln", "x", k)])
            if debug_level >= 2 and k == 0 and m == 0:
                b.wait("dve", sLD, M2[("xndmp",)])
                b.op("dve", lambda E: E.tensor_copy(dbgz0_t[:, :], psZ[0][:, :]), sDV)
                b.wait("dve", sDV, sDV.n)
                b.wait("sync", sDV, sDV.n)
                dma("sync", dbg["o_z0"][:, :], dbgz0_t[:, :])
            b.op("dve", lambda E, bank=bank, m=m: E.scalar_tensor_tensor(sb_ao[:, m * 512:(m + 1) * 512], bank[:, :], sb_sig[:, :], sb_xn2[:, m * 512:(m + 1) * 512], ALU.mult, ALU.add), sDV)
            M2[("ao", k, m)] = sDV.n
        M2[("zmm", k)] = sPE.n
        if debug_level >= 2 and k == 0:
            b.wait("sync", sDV, M2[("ao", k, 7)])
            dma("sync", dbg["o_ao"][:, :], sb_ao[:, :])
            M2[("aodmp",)] = sLD.n
        if debug_level >= 2 and k == 0:
            b.wait("dve", sLD, M2[("aodmp",)])
        emit_ln2(k, sb_ao, "ao", "inplace", pe_wait=M2[("ln", "x", k)], src_dv=M2[("ao", k, 7)])
        for kc in range(8):
            b.op("dve", lambda E, kc=kc: E.tensor_add(sb_x2[:, kc * T:(kc + 1) * T], sb_ao[:, kc * T:(kc + 1) * T], sb_x2[:, kc * T:(kc + 1) * T]), sDV)
        M2[("x1", k)] = sDV.n
        emit_ln2(k, sb_x2, "x1", "bf", pe_wait=M2[("ln", "ao", k)], src_dv=M2[("x1", k)])
        b.wait("pe", sDV, M2[("ln", "x1", k)])
        for tg in range(4):
            col = (k * 4 + tg) * 8
            for kc in range(8):
                b.op("pe", lambda E, col=col, kc=kc, tg=tg: E.matmul(psLG[:, col:col + 8], sb_xnb2[:, kc * 512 + tg * 128:kc * 512 + (tg + 1) * 128], sb_wg[:, kc * 8:(kc + 1) * 8], start=(kc == 0), stop=(kc == 7)), sPE)
        M2[("lgmm", k)] = sPE.n
        for tg in range(4):
            if tg > 0:
                b.wait("dve", sST1, M2[("x1st", k, tg - 1)])
                b.wait("dve", sST2, M2[("xn2st", k, tg - 1)])
            elif k > 0:
                b.wait("dve", sST1, M2[("x1st", k - 1, 3)])
                b.wait("dve", sST2, M2[("xn2st", k - 1, 3)])
            for half in range(2):
                trb = psTRf if half == 0 else psU2
                key = ("trcpf", k, tg - 1, half) if tg > 0 else ("trcpf", k - 1, 3, half)
                if key in M2:
                    b.wait("pe", sDV, M2[key])
                for i in range(4):
                    kc = half * 4 + i
                    b.op("pe", lambda E, i=i, kc=kc, tg=tg, trb=trb: E.transpose(trb[:, i * 128:(i + 1) * 128], sb_x2[:, kc * 512 + tg * 128:kc * 512 + (tg + 1) * 128], sb_identf[:, :]), sPE)
                vtr = sPE.n
                b.wait("dve", sPE, vtr)
                b.op("dve", lambda E, half=half, trb=trb: E.tensor_copy(stg1[:, half * 512:(half + 1) * 512], trb[:, :]), sDV)
                M2[("trcpf", k, tg, half)] = sDV.n
            b.wait("sync", sDV, M2[("trcpf", k, tg, 1)])
            if (k, tg) != (0, 0):
                pass
            b.op("sync", lambda E, t0=t0, tg=tg: E.dma_start(out=d_x1t.ap()[t0 + tg * 128:t0 + (tg + 1) * 128, :], in_=stg1[:, :]), sST1, 16)
            M2[("x1st", k, tg)] = sST1.n
            for half in range(2):
                if (k, tg, half) != (0, 0, 0):
                    b.wait("pe", sDV, M2.get(("trcpb", k, tg, half - 1), M2.get(("trcpb", k, tg - 1, 1), M2.get(("trcpb", k - 1, 3, 1), 0))))
                for i in range(4):
                    kc = half * 4 + i
                    b.op("pe", lambda E, i=i, kc=kc, tg=tg: E.transpose(psTRb[:, i * 128:(i + 1) * 128], sb_xnb2[:, kc * 512 + tg * 128:kc * 512 + (tg + 1) * 128], sb_identb[:, :]), sPE)
                vtr = sPE.n
                b.wait("dve", sPE, vtr)
                b.op("dve", lambda E, half=half: E.tensor_copy(stg2[:, half * 512:(half + 1) * 512], psTRb[:, :]), sDV)
                M2[("trcpb", k, tg, half)] = sDV.n
            b.wait("sync", sDV, M2[("trcpb", k, tg, 1)])
            b.op("sync", lambda E, t0=t0, tg=tg: E.dma_start(out=d_xn2bf.ap()[t0 + tg * 128:t0 + (tg + 1) * 128, :], in_=stg2[:, :]), sST2, 16)
            M2[("xn2st", k, tg)] = sST2.n
        M2[("trx1", k)] = sPE.n
        b.wait("dve", sST2, M2[("xn2st", k, 3)])
        if k % 2 == 1:
            cch = k // 2
            b.wait("pool", sST2, M2[("xn2st", k, 3)])
            b.wait("pool", sST1, M2[("x1st", k, 3)])
            b.op("pool", lambda E, cch=cch: E.collective_compute("AllGather", ALU.bypass,
                replica_groups=[list(range(NCORE))],
                ins=[d_xn2bf.ap()[cch * 1024:(cch + 1) * 1024, :]],
                outs=[d_xn2all.ap()[cch * 8192:(cch + 1) * 8192, :]]), sCC, 1)
            M2[("xnag", cch)] = sCC.n

    b.wait("dve", sPE, M2[("lgmm", NCH - 1)])
    sb_lg = psLG[:, 0:256]
    E8 = lambda t: t.rearrange("p (tg e) -> p tg e", e=8)
    B8 = lambda t: t.rearrange("p (tg one) -> p tg one", one=1).broadcast_to([128, 32, 8])

    def dv(fn):
        b.op("dve", fn, sDV)
        b.wait("dve", sDV, sDV.n)

    dv(lambda E: E.tensor_reduce(sb_m1[:, :], E8(sb_lg), axis=AX.X, op=ALU.max))
    dv(lambda E: E.tensor_tensor(E8(rtA), E8(sb_lg), B8(sb_m1), ALU.is_lt))
    dv(lambda E: E.scalar_tensor_tensor(rtA[:, :], rtA[:, :], 1000.0, sb_iotaE[:, :], ALU.mult, ALU.add))
    dv(lambda E: E.tensor_reduce(sb_i1[:, :], E8(rtA), axis=AX.X, op=ALU.min))
    dv(lambda E: E.tensor_tensor(E8(sb_oh1), E8(sb_iotaE), B8(sb_i1), ALU.is_equal))
    dv(lambda E: E.scalar_tensor_tensor(rtB[:, :], sb_oh1[:, :], -1e9, sb_lg[:, :], ALU.mult, ALU.add))
    dv(lambda E: E.tensor_reduce(sb_m2[:, :], E8(rtB), axis=AX.X, op=ALU.max))
    dv(lambda E: E.tensor_tensor(E8(rtA), E8(rtB), B8(sb_m2), ALU.is_lt))
    dv(lambda E: E.scalar_tensor_tensor(rtA[:, :], rtA[:, :], 1000.0, sb_iotaE[:, :], ALU.mult, ALU.add))
    dv(lambda E: E.tensor_reduce(sb_i2[:, :], E8(rtA), axis=AX.X, op=ALU.min))
    dv(lambda E: E.tensor_tensor(E8(sb_oh2), E8(sb_iotaE), B8(sb_i2), ALU.is_equal))
    dv(lambda E: E.tensor_sub(sb_m2[:, :], sb_m2[:, :], sb_m1[:, :]))
    b.wait("act", sDV, sDV.n)
    b.op("act", lambda E: E.activation(sb_g2[:, :], sb_m2[:, :], AF.Sigmoid), sAC)
    b.op("act", lambda E: E.activation(sb_g1[:, :], sb_m2[:, :], AF.Sigmoid, scale=-1.0), sAC)
    v_gates = sAC.n
    dv(lambda E: E.tensor_copy(sb_cb[:, 0:256], sb_oh1[:, :]))
    dv(lambda E: E.tensor_copy(sb_cb[:, 256:512], sb_oh2[:, :]))
    v_ohb = sDV.n
    b.wait("pe", sDV, v_ohb)
    for j in range(2):
        for tg in range(32):
            b.op("pe", lambda E, j=j, tg=tg: E.matmul(psZ[0][:, j * 256 + tg * 8:j * 256 + (tg + 1) * 8], sb_trib[:, :], sb_cb[:, j * 256 + tg * 8:j * 256 + (tg + 1) * 8], start=True, stop=True), sPE)
    for j in range(2):
        for tg in range(32):
            b.op("pe", lambda E, j=j, tg=tg: E.matmul(psZ[1][0:8, j * 32 + tg:j * 32 + tg + 1], sb_cb[:, j * 256 + tg * 8:j * 256 + (tg + 1) * 8], sb_onescb[:, :], start=True, stop=True), sPE)
    v_rkmm = sPE.n
    b.wait("dve", sPE, v_rkmm)
    dv(lambda E: E.tensor_mul(rtA[:, :], psZ[0][:, 0:256], sb_oh1[:, :]))
    dv(lambda E: E.tensor_reduce(sb_pre1[:, :], E8(rtA), axis=AX.X, op=ALU.add))
    dv(lambda E: E.tensor_mul(rtA[:, :], psZ[0][:, 256:512], sb_oh2[:, :]))
    dv(lambda E: E.tensor_reduce(sb_pre2[:, :], E8(rtA), axis=AX.X, op=ALU.add))
    dv(lambda E: E.tensor_copy(sb_ct[0:8, :], psZ[1][0:8, 0:64]))
    for j in range(2):
        dv(lambda E, j=j: E.tensor_tensor_scan(sb_tb[0:8, j * 32:(j + 1) * 32], sb_ones128[0:8, 0:32], sb_ct[0:8, j * 32:(j + 1) * 32], 0.0, ALU.mult, ALU.add))
    b.wait("sync", sDV, sDV.n)
    dma("sync", d_cnt.ap()[0:1, :], sb_tb[0:8, 31:32])
    dma("sync", d_cnt.ap()[1:2, :], sb_tb[0:8, 63:64])
    M2[("cntst",)] = sLD.n
    b.wait("dve", sLD, M2[("cntst",)])
    dv(lambda E: E.tensor_sub(sb_tb[0:8, :], sb_tb[0:8, :], sb_ct[0:8, :]))
    v_p2 = sDV.n

    if debug_level >= 2:
        b.wait("sync", sLD, M2[("xn2st", NCH - 1, 3)])
        dma("sync", dbg["o_x1t"][:, :], d_x1t.ap()[:, :])
        dma("sync", dbg["o_xn2bf"][:, :], d_xn2bf.ap()[:, :])
        b.wait("sync", sDV, v_p2)
        dma("sync", dbg["o_rt"][:, 0:32], sb_i1[:, :])
        dma("sync", dbg["o_rt"][:, 32:64], sb_i2[:, :])
        dma("sync", dbg["o_rt"][:, 64:96], sb_g1[:, :])
        dma("sync", dbg["o_rt"][:, 96:128], sb_g2[:, :])
        dma("sync", dbg["o_rt"][:, 128:160], sb_pre1[:, :])
        dma("sync", dbg["o_rt"][:, 160:192], sb_pre2[:, :])
        dma("sync", dbg["o_rt"][:, 192:256], sb_tb[:, :])
        dma("sync", dbg["o_rt"][:, 256:320], sb_ct[:, :])

    b.barrier("p2", extra_waits=[("sync", sLD, sLD.n), ("sync", sST1, sST1.n), ("sync", sST2, sST2.n), ("pool", sG, sG.n)])
    sc2.close()

    sc3 = Scope(nc)
    sb_h3 = sc3.sb("sb_h3", [128, 16384], BF16)
    sb_xfm = sc3.sb("sb_xfm", [128, 8192], BF16)
    sb_ytm = sc3.sb("sb_ytm", [128, 2048], BF16)
    sb_pmy = sc3.sb("sb_pmy", [128, 128], F32)
    sb_pidi = sc3.sb("sb_pidi", [128, 80], I32)
    sb_pidf = sc3.sb("sb_pidf", [128, 80], F32)
    sb_cnt16 = sc3.sb("sb_cnt16", [16, 8], F32)
    sb_cb2 = sc3.sb("sb_cb2", [128, 2], F32)
    sb_gst = sc3.sb("sb_gst", [128, 2048], BF16)
    sb_fc2bt = sc3.sb("sb_fc2bt", [128, 1024], F32)
    sb_fc2brow = sb_rows[0:1, 0:1024]
    psM = [sc3.ps("psM0", [128, 512], F32), sc3.ps("psM1", [128, 512], F32)]
    psY3 = [sc3.ps("psY0", [128, 512], F32), sc3.ps("psY1", [128, 512], F32)]
    psX3 = sc3.ps("psX3", [128, 512], F32)

    dma("sync", sb_fc2brow, fc2b_in[:, :])
    for kc in range(8):
        dma("pool", WBIG[:, kc * 4096:(kc + 1) * 4096], fc1T_in[128 * kc:128 * (kc + 1), :], sG)
    for hc in range(32):
        dma("pool", WBIG2[:, hc * 1024:(hc + 1) * 1024], fc2_in[128 * hc:128 * (hc + 1), :], sG)
    v_fcw = sG.n
    rg = [list(range(NCORE))]
    b.op("pool", lambda E: E.collective_compute("AllGather", ALU.bypass, replica_groups=rg,
        ins=[d_cnt.ap()[:, :]], outs=[d_cntall.ap()[:, :]]), sCC, 1)
    v_cntag = sCC.n
    v_xnag = M2[("xnag", 3)]

    b.wait("sync", sCC, v_cntag)
    dma("sync", sb_cnt16[:, :], d_cntall.ap()[:, :])
    v_c16 = sLD.n
    b.wait("pe", sLD, v_c16)
    b.op("pe", lambda E: E.matmul(psX3[0:2, 0:8], sb_cmask[:, :], sb_cnt16[:, :], start=True, stop=True), sPE)
    v_cbmm = sPE.n
    b.wait("dve", sPE, v_cbmm)
    b.op("dve", lambda E: E.tensor_copy(sb_cnt16[0:2, :], psX3[0:2, 0:8]), sDV)
    b.wait("dve", sDV, sDV.n)
    b.wait("pe", sDV, sDV.n)
    b.op("pe", lambda E: E.transpose(psX3[0:8, 8:10], sb_cnt16[0:2, 0:8], sb_identf[0:2, 0:2]), sPE)
    v_cbt = sPE.n
    b.wait("dve", sPE, v_cbt)
    b.op("dve", lambda E: E.tensor_copy(sb_cb2[0:8, :], psX3[0:8, 8:10]), sDV)
    b.wait("dve", sDV, sDV.n)
    for j in range(2):
        b.op("dve", lambda E, j=j: E.tensor_scalar_add(sb_tb[0:8, j * 32:(j + 1) * 32], sb_tb[0:8, j * 32:(j + 1) * 32], sb_cb2[0:8, j:j + 1]), sDV)
    b.wait("dve", sDV, sDV.n)
    b.wait("sync", sDV, sDV.n)
    dma("sync", d_fb.ap()[:, :], sb_tb[0:8, :])
    b.wait("sync", sLD, sLD.n)
    b.op("sync", lambda E: E.dma_start(
        out=sb_brow[0:1, 0:512].rearrange("one (j tg e) -> one j tg e", j=2, e=8),
        in_=d_fb.ap().rearrange("e (j tg) -> j tg e", j=2),
        allow_slow_non_contiguous=True), sLD, 16)
    v_brow = sLD.n
    b.wait("pe", sLD, v_brow)
    b.op("pe", lambda E: E.matmul(psX3[:, :], sb_onesrf[:, :], sb_brow[0:1, 0:512], start=True, stop=True), sPE)
    v_bb = sPE.n
    b.wait("dve", sPE, v_bb)

    def dv3(fn):
        b.op("dve", fn, sDV)
        b.wait("dve", sDV, sDV.n)

    E8 = lambda t: t.rearrange("p (tg e) -> p tg e", e=8)
    for j, ohx, prex, gx, ix in ((0, sb_oh1, sb_pre1, sb_g1, sb_i1), (1, sb_oh2, sb_pre2, sb_g2, sb_i2)):
        dv3(lambda E, j=j, ohx=ohx: E.tensor_mul(rtA2[:, :], psX3[:, j * 256:(j + 1) * 256], ohx[:, :]))
        dv3(lambda E: E.tensor_reduce(sb_m1[:, :], E8(rtA2), axis=AX.X, op=ALU.add))
        dv3(lambda E, prex=prex: E.tensor_add(sb_m1[:, :], sb_m1[:, :], prex[:, :]))
        dv3(lambda E: E.tensor_scalar(sb_m2[:, :], sb_m1[:, :], 0.5, 1.0 / 128.0, ALU.add, ALU.mult))
        dv3(lambda E: E.tensor_copy(sb_i32[:, 0:32], sb_m2[:, :]))
        dv3(lambda E: E.tensor_copy(sb_m2[:, :], sb_i32[:, 0:32]))
        dv3(lambda E: E.scalar_tensor_tensor(rtA2[:, 0:32], sb_m2[:, :], -128.0, sb_m1[:, :], ALU.mult, ALU.add))
        dv3(lambda E: E.tensor_scalar(rtA2[:, 32:64], rtA2[:, 0:32], 0.0, None, ALU.is_lt))
        dv3(lambda E: E.tensor_sub(sb_m2[:, :], sb_m2[:, :], rtA2[:, 32:64]))
        dv3(lambda E: E.scalar_tensor_tensor(rtA2[:, 0:32], sb_m2[:, :], -128.0, sb_m1[:, :], ALU.mult, ALU.add))
        dv3(lambda E: E.scalar_tensor_tensor(rtA2[:, 64:96], rtA2[:, 0:32], 80.0, sb_m2[:, :], ALU.mult, ALU.add))
        dv3(lambda E: E.tensor_scalar(rtA2[:, 96:128], sb_m1[:, :], float(CAP), None, ALU.is_lt))
        dv3(lambda E, gx=gx, j=j: E.tensor_mul(sb_gk[:, j * 32:(j + 1) * 32], gx[:, :], rtA2[:, 96:128]))
        dv3(lambda E, ix=ix: E.scalar_tensor_tensor(rtA2[:, 128:160], ix[:, :], float(CAP), rtA2[:, 64:96], ALU.mult, ALU.add))
        dv3(lambda E: E.tensor_scalar(rtA2[:, 96:128], rtA2[:, 96:128], -1.0, 1.0, ALU.mult, ALU.add))
        dv3(lambda E: E.scalar_tensor_tensor(rtA2[:, 128:160], rtA2[:, 96:128], 90000.0, rtA2[:, 128:160], ALU.mult, ALU.add))
        dv3(lambda E, j=j: E.tensor_scalar(sb_pos[:, j * 32:(j + 1) * 32], rtA2[:, 128:160], 81984.0, None, ALU.min))
        dv3(lambda E, j=j: E.tensor_copy(sb_poi[:, j * 32:(j + 1) * 32], sb_pos[:, j * 32:(j + 1) * 32]))
    v_pos = sDV.n

    b.wait("pool", sDV, v_pos)
    for j in range(2):
        for tg in range(32):
            b.op("pool", lambda E, j=j, tg=tg: E.indirect_dma_start(
                out=d_pm.ap()[:, :], out_offset=IndirectOffsetOnAxis(sb_poi[:, j * 32 + tg:j * 32 + tg + 1], 0),
                in_=sb_tokid[:, tg:tg + 1], in_offset=None), sG, 16)
    v_scat = sG.n
    b.wait("dve", sG, v_scat)
    dv3(lambda E: E.tensor_copy(rtA2[:, 0:64], sb_poi[:, :]))
    dv3(lambda E: E.tensor_scalar(rtA2[:, 64:128], rtA2[:, 0:64], 0.5, 1.0 / CAP, ALU.add, ALU.mult))
    dv3(lambda E: E.tensor_copy(sb_i32[:, 0:64], rtA2[:, 64:128]))
    dv3(lambda E: E.tensor_copy(rtA2[:, 64:128], sb_i32[:, 0:64]))
    dv3(lambda E: E.scalar_tensor_tensor(rtA2[:, 128:192], rtA2[:, 64:128], -float(CAP), rtA2[:, 0:64], ALU.mult, ALU.add))
    dv3(lambda E: E.tensor_scalar(rtA2[:, 192:256], rtA2[:, 128:192], 0.0, None, ALU.is_lt))
    dv3(lambda E: E.tensor_sub(rtA2[:, 64:128], rtA2[:, 64:128], rtA2[:, 192:256]))
    dv3(lambda E: E.scalar_tensor_tensor(rtA2[:, 128:192], rtA2[:, 64:128], -float(CAP), rtA2[:, 0:64], ALU.mult, ALU.add))
    dv3(lambda E: E.tensor_scalar(rtA2[:, 0:64], rtA2[:, 128:192], 0.5, 1.0 / 2560.0, ALU.add, ALU.mult))
    dv3(lambda E: E.tensor_copy(sb_i32[:, 0:64], rtA2[:, 0:64]))
    dv3(lambda E: E.tensor_copy(rtA2[:, 0:64], sb_i32[:, 0:64]))
    dv3(lambda E: E.scalar_tensor_tensor(rtA2[:, 192:256], rtA2[:, 0:64], -2560.0, rtA2[:, 128:192], ALU.mult, ALU.add))
    dv3(lambda E: E.tensor_scalar(sb_m1[:, 0:32], rtA2[:, 192:224], 0.0, None, ALU.is_lt))
    dv3(lambda E: E.tensor_scalar(sb_m2[:, 0:32], rtA2[:, 224:256], 0.0, None, ALU.is_lt))
    dv3(lambda E: E.tensor_sub(rtA2[:, 0:32], rtA2[:, 0:32], sb_m1[:, 0:32]))
    dv3(lambda E: E.tensor_sub(rtA2[:, 32:64], rtA2[:, 32:64], sb_m2[:, 0:32]))
    dv3(lambda E: E.scalar_tensor_tensor(rtA2[:, 192:256], rtA2[:, 0:64], -2560.0, rtA2[:, 128:192], ALU.mult, ALU.add))
    dv3(lambda E: E.tensor_scalar_mul(rtA2[:, 0:64], rtA2[:, 0:64], 20480.0))
    dv3(lambda E: E.scalar_tensor_tensor(rtA2[:, 0:64], rtA2[:, 64:128], 2560.0, rtA2[:, 0:64], ALU.mult, ALU.add))
    dv3(lambda E: E.tensor_add(rtA2[:, 0:64], rtA2[:, 0:64], rtA2[:, 192:256]))
    dv3(lambda E: E.tensor_scalar(rtA2[:, 0:64], rtA2[:, 0:64], 81919.0, None, ALU.min))
    dv3(lambda E: E.tensor_copy(sb_poi[:, :], rtA2[:, 0:64]))
    b.wait("pool", sG, v_scat)
    b.op("pool", lambda E: E.collective_compute("ReduceScatter", ALU.max, replica_groups=rg,
        ins=[d_pm.ap()[0:81920, :]], outs=[d_pmy.ap()[:, :]]), sCC, 1)
    v_rs = sCC.n

    b.wait("sync", sCC, v_rs)
    dma("sync", sb_pmy[0:80, :], d_pmy.ap().rearrange("(g p) one -> g (p one)", p=128))
    v_pmyld = sLD.n
    b.wait("pe", sLD, v_pmyld)
    b.wait("pe", sDV, v_pos)
    b.op("pe", lambda E: E.transpose(psX3[:, 0:80], sb_pmy[0:80, :], sb_identf[0:80, 0:80]), sPE)
    v_pmyt = sPE.n
    b.wait("dve", sPE, v_pmyt)
    dv3(lambda E: E.tensor_scalar(sb_pidf[:, :], psX3[:, 0:80], -1.0, 0.0, ALU.add, ALU.max))
    dv3(lambda E: E.tensor_copy(sb_pidi[:, :], sb_pidf[:, :]))
    v_pidi = sDV.n

    b.wait("pe", sDV, v_pidi)
    b.wait("pe", sLD, sLD.n)
    b.op("pe", lambda E: E.matmul(psX3[:, 0:512], sb_onesrf[:, :], sb_rows[0:1, 0:512], start=True, stop=True), sPE)
    b.op("pe", lambda E: E.matmul(psM[0][:, :], sb_onesrf[:, :], sb_rows[0:1, 512:1024], start=True, stop=True), sPE)
    v_fbb = sPE.n
    b.wait("dve", sPE, v_fbb)
    dv3(lambda E: E.tensor_copy(sb_fc2bt[:, 0:512], psX3[:, 0:512]))
    dv3(lambda E: E.tensor_copy(sb_fc2bt[:, 512:1024], psM[0][:, :]))
    v_fc2bt = sDV.n

    b.wait("pool", sDV, v_pidi)
    b.wait("pool", sCC, v_xnag)

    sGS = [b.sem(f"sGS{i}") for i in range(2)]

    def emit_gathers(g0, g1):
        for g in range(g0, g1):
            gpar = g % 2
            gsl = sb_gst[:, gpar * 1024:(gpar + 1) * 1024]
            if g >= 2:
                b.wait("pool", sGS[gpar], sGS[gpar].n)
            b.op("pool", lambda E, g=g, gsl=gsl: E.indirect_dma_start(
                out=gsl, out_offset=None,
                in_=d_xn2all.ap()[:, :], in_offset=IndirectOffsetOnAxis(sb_pidi[:, g:g + 1], 0)), sG, 16)
            M2[("gatp", g)] = sG.n
            b.wait("sync", sG, M2[("gatp", g)])
            b.op("sync", lambda E, g=g, gsl=gsl: E.dma_start(out=d_disp.ap()[g * 128:(g + 1) * 128, :], in_=gsl), sGS[gpar], 16)
            M2[("gat", g)] = (sGS[0].n, sGS[1].n)

    b.wait("pe", sG, v_fcw)
    b.wait("pe", sDV, v_fc2bt)
    NSC = CAP // 512
    emit_gathers(0, 80)
    for sc in range(NSC):
        par = sc % 2
        s0 = sc * 512
        _g4 = M2[("gat", (s0 + 511) // 128)]
        for _i in range(2):
            b.wait("sync", sGS[_i], _g4[_i])
        if sc >= 2:
            b.wait("sync", sPE, M2[("fc1", sc - 2)])
        for kc in range(8):
            b.op("sync", lambda E, kc=kc, s0=s0, par=par: E.dma_start_transpose(
                sb_xfm[:, par * 4096 + kc * 512:par * 4096 + (kc + 1) * 512],
                d_disp.ap()[s0:s0 + 512, kc * 128:(kc + 1) * 128]), sLD, 16)
        M2[("xt", sc)] = sLD.n
        b.wait("pe", sLD, M2[("xt", sc)])
        for hc in range(32):
            mb = psM[hc % 2]
            if hc >= 2:
                b.wait("pe", sAC, M2[("sil", sc, hc - 2)])
            elif sc > 0:
                b.wait("pe", sAC, M2[("sil", sc - 1, 31)])
            for kc in range(8):
                b.op("pe", lambda E, mb=mb, kc=kc, hc=hc, par=par: E.matmul(mb[:, :], WBIG[:, kc * 4096 + hc * 128:kc * 4096 + (hc + 1) * 128], sb_xfm[:, par * 4096 + kc * 512:par * 4096 + (kc + 1) * 512], start=(kc == 0), stop=(kc == 7)), sPE)
            vmm = sPE.n
            b.wait("act", sPE, vmm)
            if sc > 0 and hc == 0:
                b.wait("act", sPE, M2[("fc2", sc - 1)])
            b.op("act", lambda E, mb=mb, hc=hc: E.activation(sb_h3[:, hc * 512:(hc + 1) * 512], mb[:, :], AF.Silu, bias=sb_fc1b[:, hc:hc + 1]), sAC)
            M2[("sil", sc, hc)] = sAC.n
        M2[("fc1", sc)] = sPE.n
        b.wait("pe", sAC, M2[("sil", sc, 31)])
        b.wait("pe", sDV, v_fc2bt)
        for sg in range(4):
            par2 = sg % 2
            for dh in range(2):
                yb = psY3[dh]
                if sc + sg > 0:
                    b.wait("pe", sDV, M2.get(("yev", sc, sg - 1, dh), M2.get(("yev", sc - 1, 3, dh), 0)))
                for hc in range(32):
                    b.op("pe", lambda E, yb=yb, hc=hc, sg=sg, dh=dh: E.matmul(yb[:, :], sb_h3[:, hc * 512 + sg * 128:hc * 512 + (sg + 1) * 128], WBIG2[:, hc * 1024 + dh * 512:hc * 1024 + (dh + 1) * 512], start=(hc == 0), stop=(hc == 31)), sPE)
                vy = sPE.n
                b.wait("dve", sPE, vy)
                if sc + sg > 0:
                    _pv = M2.get(("yst", sc, sg - 2), M2.get(("yst", sc - 1, 2 + sg), 0))
                    b.wait("dve", sYS[sg % 2], _pv)
                b.op("dve", lambda E, yb=yb, dh=dh, par2=par2: E.scalar_tensor_tensor(sb_ytm[:, par2 * 1024 + dh * 512:par2 * 1024 + (dh + 1) * 512], yb[:, :], 1.0, sb_fc2bt[:, dh * 512:(dh + 1) * 512], ALU.mult, ALU.add), sDV)
                M2[("yev", sc, sg, dh)] = sDV.n
            b.wait("sync", sDV, M2[("yev", sc, sg, 1)])
            b.op("sync", lambda E, s0=s0, sg=sg, par2=par2: E.dma_start(out=d_y.ap()[s0 + sg * 128:s0 + (sg + 1) * 128, :], in_=sb_ytm[:, par2 * 1024:(par2 + 1) * 1024]), sYS[par2], 16)
            M2[("yst", sc, sg)] = sYS[par2].n
        M2[("fc2", sc)] = sPE.n
        if sc % 5 == 4:
            ych = sc // 5
            b.wait("pool", sYS[0], sYS[0].n)
            b.wait("pool", sYS[1], sYS[1].n)
            b.wait("pool", sG, sG.n)
            b.wait("pool", sGS[0], sGS[0].n)
            b.wait("pool", sGS[1], sGS[1].n)
            b.op("pool", lambda E, ych=ych: E.collective_compute("AllGather", ALU.bypass,
                replica_groups=[list(range(NCORE))],
                ins=[d_y.ap()[ych * 2560:(ych + 1) * 2560, :]],
                outs=[d_yall.ap()[ych * 20480:(ych + 1) * 20480, :]]), sCC, 1)
            M2[("yag", ych)] = sCC.n


    if debug_level >= 3:
        b.wait("sync", sYS[0], sYS[0].n)
        b.wait("sync", sYS[1], sYS[1].n)
        dma("sync", dbg["o_y3"][:, :], d_y.ap()[:, :])
        dma("sync", dbg["o_disp"][:, :], d_disp.ap()[:, :])
        b.wait("sync", sCC, v_rs)
        dma("sync", dbg["o_pmy"][:, :], d_pmy.ap()[:, :])
        dma("sync", dbg["o_posd"][:, :], sb_pos[:, :])
        dma("sync", dbg["o_gkd"][:, :], sb_gk[:, :])
        dma("sync", dbg["o_cnt"][:, :], d_cntall.ap()[:, :])
        dma("sync", dbg["o_cb2"][:, :], sb_cb2[:, :])

    b.barrier("p3", extra_waits=[("sync", sLD, sLD.n), ("pool", sG, sG.n), ("pool", sCC, sCC.n)])
    sc3.close()
    scW.close()

    sc4 = Scope(nc)
    x1b = sc4.sb("x1b", [128, 8192], F32)
    yb = sc4.sb("yb", [128, 8192], BF16)
    w_scr = sc4.sb("w_scr", [128, 4096], F32)
    wA = sc4.sb("wA", [128, 4096], F32)
    wB = sc4.sb("wB", [128, 4096], F32)
    outb = sc4.sb("outb", [128, 8192], F32)
    gam4 = sc4.sb("gam4", [128, 4096], F32)
    bet4 = sc4.sb("bet4", [128, 4096], F32)
    sb_grow = sc4.sb("sb_grow", [1, 1024], F32)
    sb_brow4 = sc4.sb("sb_brow4", [1, 1024], F32)
    c4 = sc4.sb("c4", [128, 32], F32)
    xnb4 = sc4.sb("xnb4", [128, 4096], BF16)
    sb_eps4 = sc4.sb("sb_eps4", [128, 1], F32)
    psB4 = sc4.ps("psB4", [128, 512], F32)

    dma("sync", sb_grow[:, :], lng_in[:, :])
    dma("sync", sb_brow4[:, :], lnb_in[:, :])
    v_gb = sLD.n
    b.op("dve", lambda E: E.memset(sb_eps4[:, :], EPS), sDV)
    v_ms4 = sDV.n
    b.wait("pe", sLD, v_gb)
    b.wait("pe", sDV, v_ms4)
    for rep in range(4):
        for half in range(2):
            b.op("pe", lambda E, half=half: E.matmul(psB4[:, :], sb_onesrf[:, :], sb_grow[0:1, half * 512:(half + 1) * 512], start=True, stop=True), sPE)
            v = sPE.n
            b.wait("dve", sPE, v)
            b.op("dve", lambda E, rep=rep, half=half: E.tensor_copy(gam4[:, rep * 1024 + half * 512:rep * 1024 + (half + 1) * 512], psB4[:, :]), sDV)
            b.wait("pe", sDV, sDV.n)
            b.op("pe", lambda E, half=half: E.matmul(psB4[:, :], sb_onesrf[:, :], sb_brow4[0:1, half * 512:(half + 1) * 512], start=True, stop=True), sPE)
            v = sPE.n
            b.wait("dve", sPE, v)
            b.op("dve", lambda E, rep=rep, half=half: E.tensor_copy(bet4[:, rep * 1024 + half * 512:rep * 1024 + (half + 1) * 512], psB4[:, :]), sDV)
            b.wait("pe", sDV, sDV.n)
    v_gb4 = sDV.n
    b.wait("pool", sDV, v_gb4)

    NB = 8
    GV = lambda t: t.rearrange("p (g d) -> p g d", g=4)

    def ln_stats(srcv, sqkey=None):
        b.op("dve", lambda E, srcv=srcv: E.tensor_reduce(c4[:, 0:4], GV(srcv), axis=AX.X, op=ALU.add), sDV)
        vr1 = sDV.n
        for g in range(4):
            b.op("act", lambda E, srcv=srcv, g=g: E.activation(w_scr[:, 0:1024], srcv[:, g * 1024:(g + 1) * 1024], AF.Square, accum_out=c4[:, 4 + g:5 + g]), sAC)
        vsq = sAC.n
        if sqkey is not None:
            M2[sqkey] = vsq
        b.wait("dve", sDV, vr1)
        b.op("dve", lambda E: E.tensor_scalar_mul(c4[:, 8:12], c4[:, 0:4], INV_D), sDV)
        b.wait("dve", sDV, sDV.n)
        b.op("dve", lambda E: E.tensor_mul(c4[:, 12:16], c4[:, 8:12], c4[:, 8:12]), sDV)
        b.wait("dve", sAC, vsq)
        b.wait("dve", sDV, sDV.n)
        b.op("dve", lambda E: E.scalar_tensor_tensor(c4[:, 4:8], c4[:, 4:8], INV_D, c4[:, 12:16], ALU.mult, ALU.subtract), sDV)
        vvar = sDV.n
        b.wait("act", sDV, vvar)
        b.op("act", lambda E: E.activation(c4[:, 16:20], c4[:, 4:8], AF.Ln, bias=sb_eps4[:, :]), sAC)
        b.wait("act", sAC, sAC.n)
        b.op("act", lambda E: E.activation(c4[:, 16:20], c4[:, 16:20], AF.Exp, scale=-0.5), sAC)
        vrstd = sAC.n
        b.wait("dve", sAC, vrstd)
        b.op("dve", lambda E: E.scalar_tensor_tensor(c4[:, 12:16], c4[:, 8:12], -1.0, c4[:, 16:20], ALU.mult, ALU.mult), sDV)
        b.wait("dve", sDV, sDV.n)
        return sDV.n

    def ln_apply(srcv, dstv):
        for g in range(4):
            b.op("dve", lambda E, srcv=srcv, dstv=dstv, g=g: E.tensor_scalar(dstv[:, g * 1024:(g + 1) * 1024], srcv[:, g * 1024:(g + 1) * 1024], c4[:, 16 + g:17 + g], c4[:, 12 + g:13 + g], ALU.mult, ALU.add), sDV)
        return sDV.n

    for bi in range(NB):
        par = bi % 2
        x1v = x1b[:, par * 4096:(par + 1) * 4096]
        outv = outb[:, par * 4096:(par + 1) * 4096]
        if bi >= 2:
            b.wait("sync", sPO, M2[("fin", bi - 2)])
            b.wait("sync", sOS[par], M2[("ost", bi - 2)])
        b.op("sync", lambda E, bi=bi, x1v=x1v: E.dma_start(
            out=GV(x1v),
            in_=d_x1t.ap().rearrange("(G p) d -> p G d", p=128)[:, 4 * bi:4 * bi + 4, :]), sLD, 16)
        M2[("x1ld", bi)] = sLD.n
        if bi > 0:
            b.wait("sync", sDV, M2[("g1", bi - 1)])
        b.op("sync", lambda E, bi=bi: E.dma_start(
            out=GV(xnb4),
            in_=d_xn2bf.ap().rearrange("(G p) d -> p G d", p=128)[:, 4 * bi:4 * bi + 4, :]), sLD, 16)
        M2[("xnld", bi)] = sLD.n
        if bi > 0:
            b.wait("pool", sDV, M2[("gmul", bi - 1)])
        for g in range(4):
            tg = 4 * bi + g
            b.op("pool", lambda E, tg=tg, g=g: E.indirect_dma_start(
                out=yb[:, g * 1024:(g + 1) * 1024], out_offset=None, in_=d_yall.ap()[:, :],
                in_offset=IndirectOffsetOnAxis(sb_poi[:, tg:tg + 1], 0)), sG, 16)
            b.op("pool", lambda E, tg=tg, g=g: E.indirect_dma_start(
                out=yb[:, 4096 + g * 1024:4096 + (g + 1) * 1024], out_offset=None, in_=d_yall.ap()[:, :],
                in_offset=IndirectOffsetOnAxis(sb_poi[:, 32 + tg:33 + tg], 0)), sG, 16)
        M2[("ygat", bi)] = sG.n
        b.wait("dve", sG, M2[("ygat", bi)])
        b.wait("dve", sLD, M2[("xnld", bi)])
        if bi >= 1:
            b.wait("dve", sPO, M2[("res1", bi - 1)])
            b.wait("dve", sAC, M2[("sqA", bi - 1)])
        for g in range(4):
            tg = 4 * bi + g
            b.op("dve", lambda E, g=g, tg=tg: E.scalar_tensor_tensor(wA[:, g * 1024:(g + 1) * 1024], yb[:, g * 1024:(g + 1) * 1024], sb_gk[:, tg:tg + 1], xnb4[:, g * 1024:(g + 1) * 1024], ALU.mult, ALU.add), sDV)
        M2[("g1", bi)] = sDV.n
        if bi >= 1:
            b.wait("dve", sAC, M2[("sqB", bi - 1)])
            b.wait("dve", sPO, M2[("ob", bi - 1)])
        for g in range(4):
            tg = 4 * bi + g
            b.op("dve", lambda E, g=g, tg=tg: E.tensor_scalar_mul(wB[:, g * 1024:(g + 1) * 1024], yb[:, 4096 + g * 1024:4096 + (g + 1) * 1024], sb_gk[:, 32 + tg:33 + tg]), sDV)
        M2[("gmul", bi)] = sDV.n
        b.wait("pool", sDV, M2[("gmul", bi)])
        b.op("pool", lambda E: E.tensor_add(wA[:, :], wA[:, :], wB[:, :]), sPO)
        M2[("res1", bi)] = sPO.n
        b.wait("dve", sPO, M2[("res1", bi)])
        b.wait("act", sPO, M2[("res1", bi)])
        if bi >= 1:
            b.wait("act", sPO, M2[("fin", bi - 1)])
        ln_stats(wA, ("sqA", bi))
        ln_apply(wA, wB)
        b.op("dve", lambda E: E.tensor_mul(wB[:, :], wB[:, :], gam4[:, :]), sDV)
        v_gm = sDV.n
        b.wait("pool", sDV, v_gm)
        b.op("pool", lambda E: E.tensor_add(wB[:, :], wB[:, :], bet4[:, :]), sPO)
        M2[("ob", bi)] = sPO.n
        b.wait("dve", sPO, M2[("ob", bi)])
        b.wait("act", sPO, M2[("ob", bi)])
        ln_stats(wB, ("sqB", bi))
        b.wait("dve", sLD, M2[("x1ld", bi)])
        ln_apply(wB, w_scr)
        v_a3 = sDV.n
        b.wait("pool", sDV, v_a3)
        if bi >= 2:
            b.wait("pool", sOS[par], M2[("ost", bi - 2)])
        b.op("pool", lambda E, outv=outv, x1v=x1v: E.tensor_add(outv, w_scr[:, :], x1v), sPO)
        M2[("fin", bi)] = sPO.n
        b.wait("sync", sPO, M2[("fin", bi)])
        b.op("sync", lambda E, bi=bi, outv=outv: E.dma_start(
            out=out.rearrange("(G p) d -> p G d", p=128)[:, 4 * bi:4 * bi + 4, :],
            in_=GV(outv)), sOS[par], 16)
        M2[("ost", bi)] = sOS[par].n

    b.wait("sync", sOS[0], sOS[0].n)
    b.wait("sync", sOS[1], sOS[1].n)
    b.wait("pool", sOS[0], sOS[0].n)
    b.wait("pool", sOS[1], sOS[1].n)

    return nc, b, ap, dbg, out


def finalize(nc, b):
    with nc.Block() as blk:
        @blk.tensor
        def _(E):
            for f in b.streams["pe"]:
                f(E)
        @blk.vector
        def _(E):
            for f in b.streams["dve"]:
                f(E)
        @blk.scalar
        def _(E):
            for f in b.streams["act"]:
                f(E)
        @blk.gpsimd
        def _(E):
            for f in b.streams["pool"]:
                f(E)
        @blk.sync
        def _(E):
            for f in b.streams["sync"]:
                f(E)
    return nc


def host_inputs(core, full):
    x = full["L0_attn_x3d"].reshape(NCORE * NLOC, D)
    m = {}
    m["xT"] = np.ascontiguousarray(x[core * NLOC:(core + 1) * NLOC].T)
    m["BreT"] = np.ascontiguousarray(full["L0_attn_B_re"].T)
    m["BimT"] = np.ascontiguousarray(full["L0_attn_B_im"].T)
    m["CreT"] = np.ascontiguousarray(full["L0_attn_C_re"].T)
    m["CimT"] = np.ascontiguousarray(full["L0_attn_C_im"].T)
    m["DT"] = np.ascontiguousarray(full["L0_attn_D_mat"].T)
    m["W1"] = np.ascontiguousarray(full["L0_attn_W1"])
    m["W2"] = np.ascontiguousarray(full["L0_attn_W2"])
    m["wg"] = np.ascontiguousarray(full["L0_ffn_wg"])
    m["fc1T"] = np.ascontiguousarray(full["L0_ffn_fc1_w"][core].T)
    m["fc2"] = np.ascontiguousarray(full["L0_ffn_fc2_w"][core])
    m["fc1b"] = np.ascontiguousarray(full["L0_ffn_fc1_b"][core].reshape(32, 128).T)
    m["fc2b"] = np.ascontiguousarray(full["L0_ffn_fc2_b"][core].reshape(1, D))
    m["lng"] = np.ascontiguousarray(full["L0_ffn_ln_gamma"].reshape(1, D))
    m["lnb"] = np.ascontiguousarray(full["L0_ffn_ln_beta"].reshape(1, D))
    m["nu"] = np.ascontiguousarray(full["L0_attn_nu_log"].reshape(2, 128).T)
    m["th"] = np.ascontiguousarray(full["L0_attn_theta_log"].reshape(2, 128).T)
    m["rl"] = np.asarray(full["L0_attn_res_logit"], np.float32).reshape(1, 1)
    cm = np.zeros((16, 2), np.float32)
    for cp in range(8):
        for jp in range(2):
            rr = 2 * cp + jp
            if jp == 0 and cp < core:
                cm[rr, 0] = 1.0
            if (jp == 1 and cp < core) or jp == 0:
                cm[rr, 1] = 1.0
    m["cmask"] = np.ascontiguousarray(cm)
    m["tok0"] = np.asarray([[core * 1024]], np.float32)
    m["ident_f"] = np.eye(128, dtype=np.float32)
    m["ident_b"] = np.eye(128, dtype=BF)
    m["tri_b"] = np.triu(np.ones((128, 128), dtype=np.float32), 1).astype(BF)
    return m




_BUILD_CACHE = {}


def kernel(**inputs):
    full = {k: np.asarray(v) for k, v in inputs.items()}
    if "prog" not in _BUILD_CACHE:
        nc, b, ap, dbg, out_ap = build_program(debug_level=0)
        nc = finalize(nc, b)
        _BUILD_CACHE["prog"] = nc
    nc = _BUILD_CACHE["prog"]
    in_maps = [host_inputs(c, full) for c in range(NCORE)]
    res = run_bass_kernel_spmd(nc, in_maps, core_ids=list(range(NCORE)))
    out = np.concatenate([res.results[c]["out"] for c in range(NCORE)], 0)
    return out.reshape(16, 2048, D)


# version 14
# speedup vs baseline: 1.0157x; 1.0060x over previous
import sys, time
import numpy as np
import ml_dtypes
import concourse.bass as bass
import concourse.mybir as mybir
from concourse.bass import IndirectOffsetOnAxis
from concourse.bass_utils import run_bass_kernel_spmd

F32 = mybir.dt.float32
BF16 = mybir.dt.bfloat16
I32 = mybir.dt.int32
AF = mybir.ActivationFunctionType
ALU = mybir.AluOpType
AX = mybir.AxisListType
BF = ml_dtypes.bfloat16

NCORE = 8
NLOC = 4096
D = 1024
DS = 256
H = 4096
E = 8
CAP = 10240
T = 512
NCH = NLOC // T
SUB = 128
PMROWS = 82048
C1x2 = 1.5957691216057308
SQ2C2 = 0.2671232181501769
EPS = 1e-5
INV_D = 1.0 / D
TWO_PI = 6.283185307179586
INV_2PI = 1.0 / TWO_PI
PI = 3.141592653589793

ENGS = ("pe", "dve", "act", "pool", "sync")


class Sem:
    __slots__ = ("h", "n")
    def __init__(self, h):
        self.h = h
        self.n = 0


class Builder:
    def __init__(self, nc):
        self.nc = nc
        self.streams = {e: [] for e in ENGS}
        self._guards = []
        self.sems = {}

    def sem(self, name):
        if name not in self.sems:
            g = self.nc.semaphore(name)
            self.sems[name] = Sem(g.__enter__())
            self._guards.append(g)
        return self.sems[name]

    def op(self, eng, fn, inc=None, incv=1):
        if inc is not None:
            inc.n += incv
            self.streams[eng].append(lambda E, fn=fn, s=inc, d=incv: fn(E).then_inc(s.h, d))
        else:
            self.streams[eng].append(lambda E, fn=fn: fn(E))

    def wait(self, eng, s, v=None):
        v = s.n if v is None else v
        self.streams[eng].append(lambda E, s=s, v=v: E.wait_ge(s.h, v))

    def barrier(self, name, extra_waits=()):
        s = self.sem("bar_" + name)
        for eng, sem, val in extra_waits:
            self.wait(eng, sem, val)
        tgt = s.n + len(ENGS)
        for e in ENGS:
            self.op(e, lambda E: E.sem_inc(s.h, 1), None)
        s.n = tgt
        for e in ENGS:
            self.wait(e, s, tgt)


class Scope:
    def __init__(self, nc):
        self.nc = nc
        self.gs = []

    def sb(self, name, shape, dt):
        g = self.nc.sbuf_tensor(name, shape, dt)
        h = g.__enter__()
        self.gs.append(g)
        return h

    def ps(self, name, shape, dt):
        g = self.nc.psum_tensor(name, shape, dt)
        h = g.__enter__()
        self.gs.append(g)
        return h

    def close(self):
        for g in reversed(self.gs):
            g.__exit__(None, None, None)
        self.gs = []


def build_program(debug_level=0):
    nc = bass.Bass("TRN2", target_bir_lowering=False)
    b = Builder(nc)
    ap = {}

    def din(name, shape, dt=F32):
        ap[name] = nc.dram_tensor(name, list(shape), dt, kind="ExternalInput").ap()
        return ap[name]

    xT = din("xT", [D, NLOC])
    BreT = din("BreT", [D, DS]); BimT = din("BimT", [D, DS])
    CreT = din("CreT", [DS, D]); CimT = din("CimT", [DS, D])
    DT_in = din("DT", [D, D])
    W1_in = din("W1", [D, H]); W2_in = din("W2", [H, D])
    wg_in = din("wg", [D, E])
    fc1T_in = din("fc1T", [D, H]); fc2_in = din("fc2", [H, D])
    fc1b_in = din("fc1b", [128, 32])
    fc2b_in = din("fc2b", [1, D])
    lng_in = din("lng", [1, D]); lnb_in = din("lnb", [1, D])
    nu_in = din("nu", [128, 2]); th_in = din("th", [128, 2])
    rl_in = din("rl", [1, 1])
    cmask_in = din("cmask", [16, 2])
    tok0_in = din("tok0", [1, 1])

    out = nc.dram_tensor("out", [NLOC, D], F32, kind="ExternalOutput").ap()

    d_h = nc.dram_tensor("d_h", [H, NLOC], BF16, kind="Internal")
    d_w2bf = nc.dram_tensor("d_w2bf", [4096, 1024], BF16, kind="Internal")
    d_x1t = nc.dram_tensor("d_x1t", [NLOC, D], F32, kind="Internal")
    d_xn2bf = nc.dram_tensor("d_xn2bf", [NLOC, D], BF16, kind="Internal")
    d_xn2all = nc.dram_tensor("d_xn2all", [NCORE * NLOC, D], BF16, kind="Internal")
    d_cnt = nc.dram_tensor("d_cnt", [2, 8], F32, kind="Internal")
    d_cntall = nc.dram_tensor("d_cntall", [16, 8], F32, kind="Internal")
    d_pm = nc.dram_tensor("d_pm", [PMROWS, 1], F32, kind="Internal")
    d_pmy = nc.dram_tensor("d_pmy", [CAP, 1], F32, kind="Internal")
    d_disp = nc.dram_tensor("d_disp", [CAP, D], BF16, kind="Internal")
    d_y = nc.dram_tensor("d_y", [CAP, D], BF16, kind="Internal")
    d_yall = nc.dram_tensor("d_yall", [E * CAP, D], BF16, kind="Internal")
    d_fb = nc.dram_tensor("d_fb", [8, 64], F32, kind="Internal")

    dbg = {}
    if debug_level >= 1:
        dbg["o_xn"] = nc.dram_tensor("o_xn", [D, NLOC], BF16, kind="ExternalOutput").ap()
        dbg["o_h"] = nc.dram_tensor("o_h", [H, NLOC], BF16, kind="ExternalOutput").ap()
        dbg["o_bu"] = nc.dram_tensor("o_bu", [128, 2048], BF16, kind="ExternalOutput").ap()
        dbg["o_st"] = nc.dram_tensor("o_st", [128, 2048], BF16, kind="ExternalOutput").ap()
        dbg["o_y"] = nc.dram_tensor("o_y", [128, 4096], BF16, kind="ExternalOutput").ap()
        dbg["o_tab"] = nc.dram_tensor("o_tab", [128, 1032], F32, kind="ExternalOutput").ap()
    if debug_level >= 2:
        dbg["o_x1t"] = nc.dram_tensor("o_x1t", [NLOC, D], F32, kind="ExternalOutput").ap()
        dbg["o_xn2bf"] = nc.dram_tensor("o_xn2bf", [NLOC, D], BF16, kind="ExternalOutput").ap()
        dbg["o_rt"] = nc.dram_tensor("o_rt", [128, 320], F32, kind="ExternalOutput").ap()
        dbg["o_ao"] = nc.dram_tensor("o_ao", [128, 4096], F32, kind="ExternalOutput").ap()
    if debug_level >= 3:
        dbg["o_y3"] = nc.dram_tensor("o_y3", [CAP, D], BF16, kind="ExternalOutput").ap()
        dbg["o_disp"] = nc.dram_tensor("o_disp", [CAP, D], BF16, kind="ExternalOutput").ap()
        dbg["o_pmy"] = nc.dram_tensor("o_pmy", [CAP, 1], F32, kind="ExternalOutput").ap()
        dbg["o_posd"] = nc.dram_tensor("o_posd", [128, 64], F32, kind="ExternalOutput").ap()
        dbg["o_cnt"] = nc.dram_tensor("o_cnt", [16, 8], F32, kind="ExternalOutput").ap()
        dbg["o_cb2"] = nc.dram_tensor("o_cb2", [128, 2], F32, kind="ExternalOutput").ap()
        dbg["o_gkd"] = nc.dram_tensor("o_gkd", [128, 64], F32, kind="ExternalOutput").ap()
        dbg["o_xn2"] = nc.dram_tensor("o_xn2", [128, 4096], F32, kind="ExternalOutput").ap()
        dbg["o_z0"] = nc.dram_tensor("o_z0", [128, 512], F32, kind="ExternalOutput").ap()
        dbg["o_hrb"] = nc.dram_tensor("o_hrb", [128, 16384], BF16, kind="ExternalOutput").ap()
        dbg["o_w2"] = nc.dram_tensor("o_w2", [128, 2048], BF16, kind="ExternalOutput").ap()

    ident_f = din("ident_f", [128, 128], F32)
    ident_b = din("ident_b", [128, 128], BF16)
    tri_b = din("tri_b", [128, 128], BF16)

    A = nc.alloc_sbuf_tensor
    sb_identf = A("sb_identf", [128, 128], F32)
    sb_identb = A("sb_identb", [128, 128], BF16)
    sb_trib = A("sb_trib", [128, 128], BF16)
    sb_onescb = A("sb_onescb", [128, 1], BF16)
    sb_onesrf = A("sb_onesrf", [1, 128], F32)
    sb_ones128 = A("sb_ones128", [128, 128], F32)
    sb_eps = A("sb_eps", [1, 1], F32)
    sb_iota8 = A("sb_iota8", [128, 8], F32)
    sb_i32 = A("sb_i32", [128, 256], I32)
    sb_wg = A("sb_wg", [128, 64], BF16)
    sb_fc1b = A("sb_fc1b", [128, 32], F32)
    sb_sig = A("sb_sig", [128, 1], F32)
    sb_rl = A("sb_rl", [1, 1], F32)
    sb_tok0 = A("sb_tok0", [128, 1], F32)
    sb_cmask = A("sb_cmask", [16, 2], F32)
    sb_rows = A("sb_rows", [1, 1536], F32)
    sb_oh1 = A("sb_oh1", [128, 256], F32)
    sb_oh2 = A("sb_oh2", [128, 256], F32)
    sb_i1 = A("sb_i1", [128, 32], F32)
    sb_i2 = A("sb_i2", [128, 32], F32)
    sb_g1 = A("sb_g1", [128, 32], F32)
    sb_g2 = A("sb_g2", [128, 32], F32)
    sb_pre1 = A("sb_pre1", [128, 32], F32)
    sb_pre2 = A("sb_pre2", [128, 32], F32)
    sb_ct = A("sb_ct", [128, 64], F32)
    sb_tb = A("sb_tb", [128, 64], F32)
    sb_pos = A("sb_pos", [128, 64], F32)
    sb_gk = A("sb_gk", [128, 64], F32)
    sb_poi = A("sb_poi", [128, 64], I32)
    sb_tokid = A("sb_tokid", [128, 32], F32)
    sb_brow = A("sb_brow", [1, 512], F32)
    sb_iotaE = A("sb_iotaE", [128, 256], F32)
    sb_m1 = A("sb_m1", [128, 32], F32)
    sb_m2 = A("sb_m2", [128, 32], F32)
    rtA2 = A("rtA2", [128, 256], F32)
    sb_zero = A("sb_zero", [128, 129], F32)
    scW = Scope(nc)
    WBIG = scW.sb("WBIG", [128, 32768], BF16)
    WBIG2 = scW.sb("WBIG2", [128, 32768], BF16)
    XB, XSQ, STB, YB, HST = 16384, 17408, 18432, 20480, 24576

    sc1 = Scope(nc)
    PS = [sc1.ps(f"ps{i}", [128, 512], F32) for i in range(8)]
    tabPr = sc1.sb("tabPr", [128, 256], F32); tabPi = sc1.sb("tabPi", [128, 256], F32)
    tabQr = sc1.sb("tabQr", [128, 256], F32); tabQi = sc1.sb("tabQi", [128, 256], F32)
    sb_lam = sc1.sb("sb_lam", [128, 4], F32)
    sb_nt = sc1.sb("sb_nt", [128, 8], F32)
    sb_carry = sc1.sb("sb_carry", [128, 4], F32)
    sb_cs = sc1.sb("sb_cs", [128, 4], F32)
    sb_cst = sc1.sb("sb_cst", [128, 4], F32)
    scA_r = sc1.sb("scA_r", [128, 256], F32); scA_i = sc1.sb("scA_i", [128, 256], F32)
    scS_r = sc1.sb("scS_r", [128, 256], F32); scS_i = sc1.sb("scS_i", [128, 256], F32)
    scT_r = sc1.sb("scT_r", [128, 256], F32); scT_i = sc1.sb("scT_i", [128, 256], F32)
    sctmp = sc1.sb("sctmp", [128, 256], F32)
    SC0, SC1, SC2, SC3, SC4, SC5 = scA_r, scA_i, scS_r, scS_i, scT_r, scT_i
    sb_x = sc1.sb("sb_x", [128, 8192], BF16)
    sb_xnb1 = sc1.sb("sb_xnb1", [128, 4096], BF16)
    sb_st1 = sc1.sb("sb_st1", [128, 2048], BF16)
    sb_nmb = sc1.sb("sb_nmb", [128, 512], BF16)
    sb_rb = sc1.sb("sb_rb", [128, 512], BF16)
    sb_xnb = sc1.sb("sb_xnb", [128, 4096], BF16)
    sb_bu = sc1.sb("sb_bu", [128, 2048], BF16)
    sb_q = sc1.sb("sb_q", [128, 1024], BF16)
    sb_t2 = sc1.sb("sb_t2", [128, 1024], BF16)
    sb_sg2 = sb_q


    sLD = b.sem("sLD"); sG = b.sem("sG")
    sH0 = b.sem("sH0"); sW2A = b.sem("sW2A"); sW2B = b.sem("sW2B"); sW2C = b.sem("sW2C")
    sH0S = [b.sem("sH0S0"), b.sem("sH0S1")]
    sPE = b.sem("sPE"); sDV = b.sem("sDV"); sAC = b.sem("sAC"); sPO = b.sem("sPO")
    sCC = b.sem("sCC"); sST = b.sem("sST"); sXL = b.sem("sXL")
    sST1 = b.sem("sST1"); sST2 = b.sem("sST2"); sHS = [b.sem("sHS0"), b.sem("sHS1")]
    sOS = [b.sem("sOS0"), b.sem("sOS1")]
    sYS = [b.sem("sYS0"), b.sem("sYS1")]

    def dma(eng, outap, inap, sem=None):
        sem = sem if sem is not None else sLD
        b.op(eng, lambda E, o=outap, i=inap: E.dma_start(out=o, in_=i), sem, 16)

    dma("sync", sb_identf[:, :], ident_f[:, :])
    dma("sync", sb_identb[:, :], ident_b[:, :])
    dma("sync", sb_trib[:, :], tri_b[:, :])
    dma("sync", sb_nt[:, 0:2], nu_in[:, :])
    dma("sync", sb_nt[:, 2:4], th_in[:, :])
    dma("sync", sb_fc1b[:, :], fc1b_in[:, :])
    dma("sync", sb_rl[:, :], rl_in[:, :])
    dma("sync", sb_cmask[:, :], cmask_in[:, :])
    dma("sync", sb_tok0[0:1, :], tok0_in[:, :])
    v_consts = sLD.n

    for kc in range(8):
        dma("pool", WBIG2[:, kc * 256:(kc + 1) * 256], BreT[128 * kc:128 * (kc + 1), :], sG)
        dma("pool", WBIG2[:, 2048 + kc * 256:2048 + (kc + 1) * 256], BimT[128 * kc:128 * (kc + 1), :], sG)
    for cc in range(2):
        dma("pool", WBIG2[:, 4096 + cc * 1024:4096 + (cc + 1) * 1024], CreT[128 * cc:128 * (cc + 1), :], sG)
        dma("pool", WBIG2[:, 6144 + cc * 1024:6144 + (cc + 1) * 1024], CimT[128 * cc:128 * (cc + 1), :], sG)
    for kc in range(8):
        dma("pool", WBIG2[:, 8192 + kc * 1024:8192 + (kc + 1) * 1024], DT_in[128 * kc:128 * (kc + 1), :], sG)
    for kc in range(8):
        dma("pool", sb_wg[:, kc * 8:(kc + 1) * 8], wg_in[128 * kc:128 * (kc + 1), :], sG)
    v_bcd = sG.n
    for kc in range(8):
        dma("pool", WBIG[:, kc * 4096:(kc + 1) * 4096], W1_in[128 * kc:128 * (kc + 1), :], sG)
    v_w1 = sG.n

    b.op("dve", lambda E: E.memset(sb_onescb[:, :], 1.0), sDV)
    b.op("dve", lambda E: E.memset(sb_onesrf[:, :], 1.0), sDV)
    b.op("dve", lambda E: E.memset(sb_ones128[:, :], 1.0), sDV)
    b.op("dve", lambda E: E.memset(sb_eps[:, :], EPS), sDV)
    b.op("dve", lambda E: E.memset(sb_zero[:, :], 0.0), sDV)
    b.op("dve", lambda E: E.memset(sb_carry[:, :], 0.0), sDV)
    v_memset = sDV.n

    b.wait("sync", sDV, v_memset)
    _pmv = d_pm.ap().rearrange("(r p) one -> r (p one)", p=128)
    for _i in range(5):
        dma("sync", _pmv[_i * 128:(_i + 1) * 128, :], sb_zero[:, 0:128])
    dma("sync", _pmv[640:641, :], sb_zero[0:1, 0:128])

    b.wait("pool", sG, v_w1)
    b.op("pool", lambda E: E.iota(sb_i32[:, 0:8], pattern=[[1, 8]], base=0, channel_multiplier=0), sPO)
    b.op("pool", lambda E: E.iota(sb_i32[:, 8:40], pattern=[[8192, 4], [128, 8]], base=1, channel_multiplier=1), sPO)
    b.op("pool", lambda E: E.iota(sb_i32[:, 128:256], pattern=[[1, 128]], base=0, channel_multiplier=0), sPO)
    v_iota = sPO.n
    b.wait("dve", sPO, v_iota)
    b.op("dve", lambda E: E.tensor_copy(sb_iota8[:, :], sb_i32[:, 0:8]), sDV)
    b.op("dve", lambda E: E.tensor_copy(sb_tokid[:, :], sb_i32[:, 8:40]), sDV)
    v_iota_f = sDV.n
    b.op("pool", lambda E: E.iota(sb_i32[:, 40:72], pattern=[[0, 4], [1, 8]], base=0, channel_multiplier=0), sPO)
    b.wait("dve", sPO, sPO.n)
    for _q in range(8):
        b.op("dve", lambda E, _q=_q: E.tensor_copy(sb_iotaE[:, _q * 32:(_q + 1) * 32], sb_i32[:, 40:72]), sDV)
    v_iotaE = sDV.n

    b.wait("dve", sG, v_w1)
    b.op("dve", lambda E: E.tensor_scalar_mul(WBIG2[:, 6144:8192], WBIG2[:, 6144:8192], -1.0), sDV)
    v_cim = sDV.n

    b.wait("act", sLD, v_consts)
    b.op("act", lambda E: E.activation(sb_nt[:, 4:6], sb_nt[:, 0:2], AF.Exp), sAC)
    b.op("act", lambda E: E.activation(sb_nt[:, 6:8], sb_nt[:, 2:4], AF.Exp), sAC)
    v1 = sAC.n
    b.op("act", lambda E: E.activation(sb_lam[:, 0:2], sb_nt[:, 4:6], AF.Exp, scale=-2.0), sAC)
    v2 = sAC.n
    b.wait("dve", sAC, v2)
    b.op("dve", lambda E: E.tensor_scalar(sb_lam[:, 0:2], sb_lam[:, 0:2], -1.0, 1.0, ALU.mult, ALU.add), sDV)
    v3 = sDV.n
    b.wait("act", sDV, v3)
    b.op("act", lambda E: E.activation(sb_lam[:, 2:4], sb_lam[:, 0:2], AF.Ln), sAC)
    v4 = sAC.n
    b.wait("act", sAC, v4)
    b.op("act", lambda E: E.activation(sb_nt[:, 0:2], sb_lam[:, 2:4], AF.Exp, scale=0.5), sAC)
    b.op("act", lambda E: E.activation(sb_lam[:, 0:2], sb_nt[:, 4:6], AF.Exp, scale=-1.0), sAC)
    v_gam = sAC.n

    b.wait("dve", sPO, v_iota)
    b.op("dve", lambda E: E.tensor_copy(SC0[:, 0:128], sb_i32[:, 128:256]), sDV)
    b.op("dve", lambda E: E.tensor_copy(SC0[:, 128:256], sb_i32[:, 128:256]), sDV)
    v_tau = sDV.n
    b.wait("dve", sAC, v_gam)
    for cc in range(2):
        b.op("dve", lambda E, cc=cc: E.tensor_scalar_mul(SC1[:, cc * 128:(cc + 1) * 128], SC0[:, cc * 128:(cc + 1) * 128], sb_nt[:, 6 + cc:7 + cc]), sDV)
    b.wait("dve", sDV, sDV.n)
    for cc in range(2):
        b.op("dve", lambda E, cc=cc: E.tensor_scalar_mul(SC0[:, cc * 128:(cc + 1) * 128], SC0[:, cc * 128:(cc + 1) * 128], sb_nt[:, 4 + cc:5 + cc]), sDV)
    v_T = sDV.n
    b.wait("dve", sDV, v_T)
    b.op("dve", lambda E: E.tensor_scalar_mul(SC4[:, :], SC1[:, :], INV_2PI), sDV)
    b.wait("dve", sDV, sDV.n)
    b.op("dve", lambda E: E.tensor_copy(sb_i32[:, 0:256], SC4[:, :]), sDV)
    b.wait("dve", sDV, sDV.n)
    b.op("dve", lambda E: E.tensor_copy(SC4[:, :], sb_i32[:, 0:256]), sDV)
    b.wait("dve", sDV, sDV.n)
    b.op("dve", lambda E: E.scalar_tensor_tensor(SC1[:, :], SC4[:, :], -TWO_PI, SC1[:, :], ALU.mult, ALU.add), sDV)
    b.wait("dve", sDV, sDV.n)

    def wrap_once(dst, src):
        b.op("dve", lambda E: E.tensor_scalar(SC4[:, :], src, PI, None, ALU.is_gt), sDV)
        b.wait("dve", sDV, sDV.n)
        b.op("dve", lambda E: E.scalar_tensor_tensor(dst, SC4[:, :], -TWO_PI, src, ALU.mult, ALU.add), sDV)
        b.wait("dve", sDV, sDV.n)
        b.op("dve", lambda E: E.tensor_scalar(SC4[:, :], dst, -PI, None, ALU.is_lt), sDV)
        b.wait("dve", sDV, sDV.n)
        b.op("dve", lambda E: E.scalar_tensor_tensor(dst, SC4[:, :], TWO_PI, dst, ALU.mult, ALU.add), sDV)
        b.wait("dve", sDV, sDV.n)

    wrap_once(SC1[:, :], SC1[:, :])
    b.wait("act", sDV, sDV.n)
    b.op("act", lambda E: E.activation(SC2[:, :], SC1[:, :], AF.Sin), sAC)
    v_sin = sAC.n
    b.wait("dve", sAC, v_sin)
    b.op("dve", lambda E: E.tensor_scalar(SC1[:, :], SC1[:, :], PI / 2.0, None, ALU.add), sDV)
    b.wait("dve", sDV, sDV.n)
    wrap_once(SC1[:, :], SC1[:, :])
    b.wait("act", sDV, sDV.n)
    b.op("act", lambda E: E.activation(SC3[:, :], SC1[:, :], AF.Sin), sAC)
    b.op("act", lambda E: E.activation(SC4[:, :], SC0[:, :], AF.Exp), sAC)
    b.op("act", lambda E: E.activation(SC5[:, :], SC0[:, :], AF.Exp, scale=-1.0), sAC)
    v_trig = sAC.n
    b.wait("dve", sAC, v_trig)
    for cc in range(2):
        b.op("dve", lambda E, cc=cc: E.tensor_scalar_mul(SC4[:, cc * 128:(cc + 1) * 128], SC4[:, cc * 128:(cc + 1) * 128], sb_nt[:, cc:cc + 1]), sDV)
    b.wait("dve", sDV, sDV.n)
    b.op("dve", lambda E: E.tensor_mul(tabPr[:, :], SC4[:, :], SC3[:, :]), sDV)
    b.op("dve", lambda E: E.scalar_tensor_tensor(tabPi[:, :], SC2[:, :], -1.0, SC4[:, :], ALU.mult, ALU.mult), sDV)
    b.op("dve", lambda E: E.tensor_mul(tabQr[:, :], SC5[:, :], SC3[:, :]), sDV)
    b.op("dve", lambda E: E.tensor_mul(tabQi[:, :], SC5[:, :], SC2[:, :]), sDV)
    b.wait("dve", sDV, sDV.n)
    for cc in range(2):
        b.op("dve", lambda E, cc=cc: E.tensor_mul(sb_lam[:, 2 + cc:3 + cc], sb_lam[:, cc:cc + 1],
                                                  SC2[:, cc * 128 + 1:cc * 128 + 2]), sDV)
    b.wait("dve", sDV, sDV.n)
    for cc in range(2):
        b.op("dve", lambda E, cc=cc: E.tensor_mul(sb_lam[:, cc:cc + 1], sb_lam[:, cc:cc + 1],
                                                  SC3[:, cc * 128 + 1:cc * 128 + 2]), sDV)
    v_tables = sDV.n
    if debug_level >= 1:
        b.wait("sync", sDV, v_tables)
        dma("sync", dbg["o_tab"][:, 0:256], tabPr[:, :])
        dma("sync", dbg["o_tab"][:, 256:512], tabPi[:, :])
        dma("sync", dbg["o_tab"][:, 512:768], tabQr[:, :])
        dma("sync", dbg["o_tab"][:, 768:1024], tabQi[:, :])
        dma("sync", dbg["o_tab"][:, 1024:1028], sb_lam[:, :])
        dma("sync", dbg["o_tab"][:, 1028:1032], sb_nt[:, 0:4])

    b.wait("pe", sLD, v_consts)
    b.wait("pe", sDV, v_memset)
    b.op("pe", lambda E: E.matmul(PS[0][:, 0:1], sb_onesrf[:, :], sb_rl[:, :], start=True, stop=True), sPE)
    b.op("pe", lambda E: E.matmul(PS[0][:, 4:5], sb_onesrf[:, :], sb_tok0[0:1, :], start=True, stop=True), sPE)
    v_bc0 = sPE.n
    b.wait("act", sPE, v_bc0)
    b.op("act", lambda E: E.activation(sb_sig[:, :], PS[0][:, 0:1], AF.Sigmoid), sAC)
    v_sig = sAC.n
    b.wait("dve", sPE, v_bc0)
    b.op("dve", lambda E: E.tensor_copy(sb_tok0[:, :], PS[0][:, 4:5]), sDV)
    b.wait("dve", sDV, sDV.n)
    b.wait("dve", sDV, v_iota_f)
    b.op("dve", lambda E: E.tensor_scalar_add(sb_tokid[:, :], sb_tokid[:, :], sb_tok0[:, :]), sDV)
    v_p0 = sDV.n


    psU, psB0, psB1, psB2, psB3, psYa, psYb, psUa = PS
    psUb = psB0

    b.wait("pe", sG, v_bcd)
    b.wait("pe", sDV, v_p0)
    b.wait("pe", sAC, v_sig)
    b.wait("dve", sDV, v_tables)
    b.wait("dve", sDV, v_cim)

    M = {}

    def p1_chunk(k):
        t0 = k * T
        xpar = k % 2
        xsl = sb_x[:, xpar * 4096:(xpar + 1) * 4096]
        xnbuf = sb_xnb if xpar == 0 else sb_xnb1

        b.wait("act", sXL, M[("xld", k)])
        b.wait("pe", sXL, M[("xld", k)])
        if k > 0:
            b.wait("pe", sAC, M[("rbev", k - 1)])
        for kc in range(8):
            par = kc % 2
            xk = xsl[:, kc * T:(kc + 1) * T]
            xsv = WBIG2[:, XSQ + par * 512:XSQ + (par + 1) * 512]
            if kc >= 2:
                b.wait("act", sPE, M[("stmm", k, kc - 2)])
            elif k > 0:
                b.wait("act", sPE, M[("stats", k - 1)])
            b.op("act", lambda E, xk=xk, xsv=xsv: E.activation(xsv, xk, AF.Square), sAC)
            vs = sAC.n
            b.wait("pe", sAC, vs)
            b.op("pe", lambda E, kc=kc, xk=xk: E.matmul(psU[0:1, :], sb_onescb[:, :], xk, start=(kc == 0), stop=(kc == 7)), sPE)
            b.op("pe", lambda E, kc=kc, xsv=xsv: E.matmul(psU[32:33, :], sb_onescb[:, :], xsv, start=(kc == 0), stop=(kc == 7)), sPE)
            M[("stmm", k, kc)] = sPE.n
        M[("stats", k)] = sPE.n

        b.wait("act", sPE, M[("stats", k)])
        b.op("act", lambda E: E.activation(sb_rows[:, 0:512], psU[0:1, :], AF.Copy, scale=INV_D), sAC)
        vm = sAC.n
        b.wait("act", sAC, vm)
        b.op("act", lambda E: E.activation(sb_rows[:, 512:1024], sb_rows[:, 0:512], AF.Square), sAC)
        vq = sAC.n
        b.wait("dve", sAC, vq)
        b.op("dve", lambda E: E.scalar_tensor_tensor(sb_rows[:, 512:1024], psU[32:33, :], INV_D, sb_rows[:, 512:1024], ALU.mult, ALU.subtract), sDV)
        vv = sDV.n
        b.wait("act", sDV, vv)
        b.op("act", lambda E: E.activation(sb_rows[:, 1024:1536], sb_rows[:, 512:1024], AF.Ln, bias=sb_eps[:, :]), sAC)
        b.wait("act", sAC, sAC.n)
        b.op("act", lambda E: E.activation(sb_rows[:, 1024:1536], sb_rows[:, 1024:1536], AF.Exp, scale=-0.5), sAC)
        vr = sAC.n
        b.wait("dve", sAC, vr)
        b.op("dve", lambda E: E.tensor_scalar_mul(sb_rows[:, 512:1024], sb_rows[:, 0:512], -1.0), sDV)
        vnm = sDV.n

        b.wait("pe", sDV, vnm)
        b.wait("pe", sDV, vv)
        b.op("pe", lambda E: E.matmul(psU[:, :], sb_onesrf[:, :], sb_rows[:, 512:1024], start=True, stop=True), sPE)
        vb1 = sPE.n
        b.wait("act", sPE, vb1)
        if k > 0:
            b.wait("act", sDV, M[("pass1", k - 1)])
        b.op("act", lambda E: E.copy(sb_nmb[:, :], psU[:, :]), sAC)
        vnmb = sAC.n
        b.wait("dve", sAC, vnmb)
        for kc in range(8):
            b.op("dve", lambda E, kc=kc, xsl=xsl: E.tensor_add(xsl[:, kc * T:(kc + 1) * T], xsl[:, kc * T:(kc + 1) * T], sb_nmb[:, :]), sDV)
        M[("pass1", k)] = sDV.n
        b.wait("pe", sAC, vnmb)
        b.wait("pe", sAC, vr)
        b.op("pe", lambda E: E.matmul(psU[:, :], sb_onesrf[:, :], sb_rows[:, 1024:1536], start=True, stop=True), sPE)
        vb2 = sPE.n
        b.wait("act", sPE, vb2)
        if k > 0:
            b.wait("act", sDV, M[("xn", k - 1)])
        b.op("act", lambda E: E.copy(sb_rb[:, :], psU[:, :]), sAC)
        M[("rbev", k)] = sAC.n
        b.wait("dve", sAC, M[("rbev", k)])
        if k >= 2:
            b.wait("dve", sPE, M[("ymm", k - 2, 3)])
        for kc in range(8):
            b.op("dve", lambda E, kc=kc, xsl=xsl, xnbuf=xnbuf: E.tensor_mul(xnbuf[:, kc * T:(kc + 1) * T], xsl[:, kc * T:(kc + 1) * T], sb_rb[:, :]), sDV)
        M[("xn", k)] = sDV.n
        M[("xnb", k)] = M[("xn", k)]
        if debug_level >= 1:
            b.wait("sync", sDV, M[("xn", k)])
            dma("sync",
                dbg["o_xn"].rearrange("(kc p) t -> p kc t", p=128)[:, :, t0:t0 + T],
                xnbuf.rearrange("p (kc t) -> p kc t", t=T))
            M[("xnst", k)] = sLD.n

        b.wait("pe", sDV, M[("xnb", k)])
        if k >= 2:
            b.wait("pe", sAC, M[("uev", k - 2, 15)])
        elif k == 1:
            b.wait("pe", sAC, M[("buev", 0)])
        for kc in range(8):
            rsl = xnbuf[:, kc * T:(kc + 1) * T]
            for (dst, base, nh) in ((psB0, 0, 0), (psB1, 0, 1), (psB2, 2048, 0), (psB3, 2048, 1)):
                b.op("pe", lambda E, dst=dst, base=base, nh=nh, kc=kc, rsl=rsl:
                     E.matmul(dst[:, :], WBIG2[:, base + kc * 256 + nh * 128:base + kc * 256 + (nh + 1) * 128],
                              rsl, start=(kc == 0), stop=(kc == 7)), sPE)
        M[("bu", k)] = sPE.n
        b.wait("act", sPE, M[("bu", k)])
        if k > 0:
            b.wait("act", sDV, M[("scan", k - 1)])
        if debug_level >= 1 and k == 1:
            b.wait("act", sLD, M[("dmp_bu", 0)])
        for i, src in enumerate((psB0, psB1, psB2, psB3)):
            b.op("act", lambda E, i=i, src=src: E.copy(sb_bu[:, i * 512:(i + 1) * 512], src[:, :]), sAC)
        M[("buev", k)] = sAC.n

        if debug_level >= 1 and k == 0:
            b.wait("sync", sAC, M[("buev", k)])
            dma("sync", dbg["o_bu"][:, :], sb_bu[:, :])
            M[("dmp_bu", 0)] = sLD.n
        emit_scan(k)
        if debug_level >= 1 and k == 0:
            b.wait("sync", sDV, M[("scan", k)])
            dma("sync", dbg["o_st"][:, :], WBIG2[:, STB:STB + 2048])
            M[("dmp_st", 0)] = sLD.n

        if k > 0:
            emit_y(k - 1)
            emit_w1(k - 1, M[("buev", k)])

        if k + 1 < NCH:
            if k >= 1:
                b.wait("pool", sDV, M[("xn", k - 1)])
            nsl = sb_x[:, ((k + 1) % 2) * 4096:(((k + 1) % 2) + 1) * 4096]
            b.op("pool", lambda E, t0n=(k + 1) * T, nsl=nsl: E.dma_start(
                out=nsl.rearrange("p (kc t) -> p kc t", t=T),
                in_=xT.rearrange("(kc p) t -> p kc t", p=128)[:, :, t0n:t0n + T]), sXL, 16)
            M[("xld", k + 1)] = sXL.n

        if k == NCH - 1:
            emit_y(k)
            emit_w1(k, M[("buev", k)])
        if debug_level >= 1 and k == 1:
            b.wait("sync", sAC, M[("yev", 0, 3)])
            dma("sync", dbg["o_y"][:, :], WBIG2[:, YB:YB + 4096])
            M[("dmp_y", 0)] = sLD.n

    def emit_scan(k):
        stv = WBIG2[:, STB:STB + 2048] if k % 2 == 0 else sb_st1
        b.wait("dve", sAC, M[("buev", k)])
        if k >= 2:
            b.wait("dve", sPE, M[("ymm", k - 2, 3)])
        if debug_level >= 1 and k == 1:
            b.wait("dve", sLD, M[("dmp_st", 0)])
        if k % 4 == 0:
            b.op("dve", lambda E: E.memset(sb_carry[:, :], 0.0), sDV)
            b.wait("dve", sDV, sDV.n)
        for sub in range(4):
            b.op("dve", lambda E: E.tensor_mul(sb_cs[:, 0:2], sb_lam[:, 0:2], sb_carry[:, 0:2]), sDV)
            b.op("dve", lambda E: E.tensor_mul(sb_cst[:, 0:2], sb_lam[:, 2:4], sb_carry[:, 2:4]), sDV)
            b.op("dve", lambda E: E.tensor_mul(sb_cs[:, 2:4], sb_lam[:, 2:4], sb_carry[:, 0:2]), sDV)
            b.op("dve", lambda E: E.tensor_mul(sb_cst[:, 2:4], sb_lam[:, 0:2], sb_carry[:, 2:4]), sDV)
            b.wait("dve", sDV, sDV.n)
            b.op("dve", lambda E: E.tensor_sub(sb_cs[:, 0:2], sb_cs[:, 0:2], sb_cst[:, 0:2]), sDV)
            b.op("dve", lambda E: E.tensor_add(sb_cs[:, 2:4], sb_cs[:, 2:4], sb_cst[:, 2:4]), sDV)
            b.wait("dve", sDV, sDV.n)
            bur = sb_bu[:, 0:1024].rearrange("p (cc t) -> p cc t", cc=2)[:, :, sub * SUB:(sub + 1) * SUB]
            bui = sb_bu[:, 1024:2048].rearrange("p (cc t) -> p cc t", cc=2)[:, :, sub * SUB:(sub + 1) * SUB]
            a_r = scA_r.rearrange("p (cc t) -> p cc t", cc=2)
            a_i = scA_i.rearrange("p (cc t) -> p cc t", cc=2)
            tmpv = sctmp.rearrange("p (cc t) -> p cc t", cc=2)
            prv = tabPr.rearrange("p (cc t) -> p cc t", cc=2)
            piv = tabPi.rearrange("p (cc t) -> p cc t", cc=2)
            b.op("dve", lambda E, a_r=a_r, prv=prv, bur=bur: E.tensor_mul(a_r, prv, bur), sDV)
            b.op("dve", lambda E, tmpv=tmpv, piv=piv, bui=bui: E.tensor_mul(tmpv, piv, bui), sDV)
            b.op("dve", lambda E: E.tensor_sub(scA_r[:, :], scA_r[:, :], sctmp[:, :]), sDV)
            b.op("dve", lambda E, a_i=a_i, prv=prv, bui=bui: E.tensor_mul(a_i, prv, bui), sDV)
            b.op("dve", lambda E, tmpv=tmpv, piv=piv, bur=bur: E.tensor_mul(tmpv, piv, bur), sDV)
            b.op("dve", lambda E: E.tensor_add(scA_i[:, :], scA_i[:, :], sctmp[:, :]), sDV)
            for cc in range(2):
                b.op("dve", lambda E, cc=cc: E.tensor_tensor_scan(scS_r[:, cc * 128:(cc + 1) * 128], sb_ones128[:, :], scA_r[:, cc * 128:(cc + 1) * 128], sb_cs[:, cc:cc + 1], ALU.mult, ALU.add), sDV)
                b.op("dve", lambda E, cc=cc: E.tensor_tensor_scan(scS_i[:, cc * 128:(cc + 1) * 128], sb_ones128[:, :], scA_i[:, cc * 128:(cc + 1) * 128], sb_cs[:, 2 + cc:3 + cc], ALU.mult, ALU.add), sDV)
            b.op("dve", lambda E: E.tensor_mul(scT_r[:, :], tabQr[:, :], scS_r[:, :]), sDV)
            b.op("dve", lambda E: E.tensor_mul(sctmp[:, :], tabQi[:, :], scS_i[:, :]), sDV)
            b.op("dve", lambda E: E.tensor_sub(scT_r[:, :], scT_r[:, :], sctmp[:, :]), sDV)
            b.op("dve", lambda E: E.tensor_mul(scT_i[:, :], tabQr[:, :], scS_i[:, :]), sDV)
            b.op("dve", lambda E: E.tensor_mul(sctmp[:, :], tabQi[:, :], scS_r[:, :]), sDV)
            b.op("dve", lambda E: E.tensor_add(scT_i[:, :], scT_i[:, :], sctmp[:, :]), sDV)
            b.wait("dve", sDV, sDV.n)
            b.op("dve", lambda E: E.tensor_copy(sb_carry[:, 0:2], scT_r.rearrange("p (cc t) -> p cc t", cc=2)[:, :, 127:128]), sDV)
            b.op("dve", lambda E: E.tensor_copy(sb_carry[:, 2:4], scT_i.rearrange("p (cc t) -> p cc t", cc=2)[:, :, 127:128]), sDV)
            str_v = stv[:, 0:1024].rearrange("p (cc t) -> p cc t", cc=2)[:, :, sub * SUB:(sub + 1) * SUB]
            sti_v = stv[:, 1024:2048].rearrange("p (cc t) -> p cc t", cc=2)[:, :, sub * SUB:(sub + 1) * SUB]
            b.op("dve", lambda E, str_v=str_v: E.tensor_copy(str_v, scT_r.rearrange("p (cc t) -> p cc t", cc=2)), sDV)
            b.op("dve", lambda E, sti_v=sti_v: E.tensor_copy(sti_v, scT_i.rearrange("p (cc t) -> p cc t", cc=2)), sDV)
            b.wait("dve", sDV, sDV.n)
            M[("tts", k, sub)] = sDV.n
        M[("scan", k)] = sDV.n

    def emit_y(k):
        stv = WBIG2[:, STB:STB + 2048] if k % 2 == 0 else sb_st1
        ynb = sb_xnb if k % 2 == 0 else sb_xnb1
        b.wait("pe", sDV, M[("scan", k)])
        for mg in range(4):
            if mg > 0:
                b.wait("pe", sAC, M[("yev", k, mg - 1)])
            elif k > 0:
                b.wait("pe", sAC, M[("yev", k - 1, 3)])
            for mi in range(2):
                m = mg * 2 + mi
                dst = psYa if mi == 0 else psYb
                for cc in range(2):
                    b.op("pe", lambda E, dst=dst, cc=cc, m=m: E.matmul(dst[:, :], WBIG2[:, 4096 + cc * 1024 + m * 128:4096 + cc * 1024 + (m + 1) * 128], stv[:, cc * 512:(cc + 1) * 512], start=(cc == 0), stop=False), sPE)
                for cc in range(2):
                    b.op("pe", lambda E, dst=dst, cc=cc, m=m: E.matmul(dst[:, :], WBIG2[:, 6144 + cc * 1024 + m * 128:6144 + cc * 1024 + (m + 1) * 128], stv[:, 1024 + cc * 512:1024 + (cc + 1) * 512], start=False, stop=False), sPE)
                for kc in range(8):
                    b.op("pe", lambda E, dst=dst, kc=kc, m=m: E.matmul(dst[:, :], WBIG2[:, 8192 + kc * 1024 + m * 128:8192 + kc * 1024 + (m + 1) * 128], ynb[:, kc * T:(kc + 1) * T], start=False, stop=(kc == 7)), sPE)
            M[("ymm", k, mg)] = sPE.n
            b.wait("act", sPE, M[("ymm", k, mg)])
            if debug_level >= 1 and k == 1 and mg == 0:
                b.wait("act", sLD, M[("dmp_y", 0)])
            for mi in range(2):
                m = mg * 2 + mi
                src = psYa if mi == 0 else psYb
                b.op("act", lambda E, src=src, m=m: E.copy(WBIG2[:, YB + m * 512:YB + (m + 1) * 512], src[:, :]), sAC)
            M[("yev", k, mg)] = sAC.n

    def emit_w1(k, buev_wait):
        t0 = k * T
        if k == 0:
            b.wait("pe", sG, v_w1)
        b.wait("pe", sAC, buev_wait)
        b.wait("pe", sAC, M[("yev", k, 3)])
        UP = [(psUa, psB1), (psB2, psB3)]
        for hp in range(16):
            pp = hp % 2
            bk0, bk1 = UP[pp]
            if hp >= 2:
                b.wait("pe", sAC, M[("uev", k, hp - 2)])
            elif k > 0:
                b.wait("pe", sAC, M[("uev", k - 1, 15)])
            for kc in range(8):
                b.op("pe", lambda E, bk0=bk0, kc=kc, hp=hp: E.matmul(bk0[:, :], WBIG[:, kc * 4096 + (2 * hp) * 128:kc * 4096 + (2 * hp + 1) * 128], WBIG2[:, YB + kc * 512:YB + (kc + 1) * 512], start=(kc == 0), stop=(kc == 7)), sPE)
            for kc in range(8):
                b.op("pe", lambda E, bk1=bk1, kc=kc, hp=hp: E.matmul(bk1[:, :], WBIG[:, kc * 4096 + (2 * hp + 1) * 128:kc * 4096 + (2 * hp + 2) * 128], WBIG2[:, YB + kc * 512:YB + (kc + 1) * 512], start=(kc == 0), stop=(kc == 7)), sPE)
            M[("u", k, hp)] = sPE.n
            USB = 26624 + pp * 1024
            b.wait("act", sPE, M[("u", k, hp)])
            if hp >= 2:
                b.wait("act", sDV, M[("gel", k, hp - 2)])
            elif k > 0:
                b.wait("act", sDV, M[("gel", k - 1, 14 + pp)])
            b.op("act", lambda E, bk0=bk0, USB=USB: E.copy(WBIG2[:, USB:USB + 512], bk0[:, :]), sAC)
            b.op("act", lambda E, bk1=bk1, USB=USB: E.copy(WBIG2[:, USB + 512:USB + 1024], bk1[:, :]), sAC)
            M[("uev", k, hp)] = sAC.n
            if hp > 0:
                b.wait("act", sDV, M[("t2", k, hp - 1)])
            elif k > 0:
                b.wait("act", sDV, M[("t2", k - 1, 15)])
            b.op("act", lambda E, USB=USB: E.activation(sb_q[:, :], WBIG2[:, USB:USB + 1024], AF.Square, scale=SQ2C2), sAC)
            vq2 = sAC.n
            b.wait("dve", sAC, vq2)
            b.op("dve", lambda E, USB=USB: E.scalar_tensor_tensor(sb_t2[:, :], sb_q[:, :], C1x2, WBIG2[:, USB:USB + 1024], ALU.add, ALU.mult), sDV)
            vt2 = sDV.n
            M[("t2", k, hp)] = vt2
            b.wait("act", sDV, vt2)
            b.op("act", lambda E: E.activation(sb_sg2[:, :], sb_t2[:, :], AF.Sigmoid), sAC)
            vsg = sAC.n
            b.wait("dve", sAC, vsg)
            b.wait("dve", sHS[pp], M[("hst", k, hp - 2)] if hp >= 2 else (M[("hst", k - 1, 14 + pp)] if k > 0 else 0))
            b.op("dve", lambda E, pp=pp, USB=USB: E.tensor_mul(WBIG2[:, HST + pp * 1024:HST + (pp + 1) * 1024], sb_sg2[:, :], WBIG2[:, USB:USB + 1024]), sDV)
            M[("gel", k, hp)] = sDV.n
            b.wait("sync", sDV, M[("gel", k, hp)])
            b.op("sync", lambda E, hp=hp, t0=t0, pp=pp: E.dma_start(
                out=d_h.ap().rearrange("(H p) t -> p H t", p=128)[:, 2 * hp:2 * hp + 2, t0:t0 + T],
                in_=WBIG2[:, HST + pp * 1024:HST + (pp + 1) * 1024].rearrange("p (H t) -> p H t", t=T)), sHS[pp], 16)
            if debug_level >= 1:
                b.op("sync", lambda E, hp=hp, t0=t0, pp=pp: E.dma_start(
                    out=dbg["o_h"].rearrange("(H p) t -> p H t", p=128)[:, 2 * hp:2 * hp + 2, t0:t0 + T],
                    in_=WBIG2[:, HST + pp * 1024:HST + (pp + 1) * 1024].rearrange("p (H t) -> p H t", t=T)), sHS[pp], 16)
            M[("hst", k, hp)] = sHS[pp].n

    b.op("pool", lambda E: E.dma_start(
        out=sb_x[:, 0:4096].rearrange("p (kc t) -> p kc t", t=T),
        in_=xT.rearrange("(kc p) t -> p kc t", p=128)[:, :, 0:T]), sXL, 16)
    M[("xld", 0)] = sXL.n
    for k in range(NCH):
        p1_chunk(k)
        b.op("pool", lambda E, j=k: E.dma_start(out=d_w2bf.ap()[512 * j:512 * (j + 1), :], in_=W2_in[512 * j:512 * (j + 1), :]), sW2C, 16)
    v_w2cast = sW2C.n
    sG_pre = sG.n
    for qeng, qsem, j0, j1 in (("pool", sG, 0, 10), ("act", sW2A, 10, 21), ("sync", sW2B, 21, 32)):
        b.wait(qeng, sAC, M[("uev", NCH - 1, 15)])
        b.wait(qeng, sW2C, v_w2cast)
        for j in range(j0, j1):
            b.op(qeng, lambda E, j=j: E.dma_start(out=WBIG[:, j * 1024:(j + 1) * 1024], in_=d_w2bf.ap()[128 * j:128 * (j + 1), :]), qsem, 16)
    M[("vw2",)] = (sG.n, sW2A.n, sW2B.n)

    b.barrier("p1", extra_waits=[("sync", sHS[0], sHS[0].n), ("sync", sHS[1], sHS[1].n), ("sync", sLD, sLD.n), ("pool", sG, sG_pre)])
    print("sbuf remaining before sc1.close:", nc.sbuf_bytes_remaining)
    sc1.close()
    print("sbuf remaining after sc1.close:", nc.sbuf_bytes_remaining)

    sc2 = Scope(nc)
    sb_x2 = sc2.sb("sb_x2", [128, 4096], F32)
    sb_xn2 = sc2.sb("sb_xn2", [128, 4096], F32)
    sb_ao = sb_xn2
    sb_xnb2 = sc2.sb("sb_xnb2", [128, 4096], BF16)
    sb_cb = sc2.sb("sb_cb", [128, 2048], BF16)
    stg1 = sc2.sb("stg1", [128, 1024], F32)
    stg2 = sc2.sb("stg2", [128, 1024], BF16)
    dbgz0_t = sc2.sb("dbgz0_t", [128, 512], F32)
    rtA = sc2.sb("rtA", [128, 256], F32)
    rtB = sc2.sb("rtB", [128, 256], F32)
    psU2 = sc2.ps("psU2", [128, 512], F32)
    psZ = [sc2.ps(f"psZ{i}", [128, 512], F32) for i in range(4)]
    psLG = sc2.ps("psLG", [128, 512], F32)
    psTRf = sc2.ps("psTRf", [128, 512], F32)
    psTRb = sc2.ps("psTRb", [128, 512], BF16)

    v_w2, v_w2a, v_w2b = M[("vw2",)]
    b.wait("pe", sG, v_w2)
    b.wait("pe", sW2A, v_w2a)
    b.wait("pe", sW2B, v_w2b)
    if debug_level >= 2:
        b.wait("sync", sG, v_w2)
        b.wait("sync", sW2A, v_w2a)
        b.wait("sync", sW2B, v_w2b)
        dma("sync", dbg["o_w2"][:, 0:1024], WBIG[:, 0:1024])
        dma("sync", dbg["o_w2"][:, 1024:2048], WBIG[:, 1024:2048])

    M2 = {}

    def emit_ln2(k, src_t, tag, out_mode, pe_wait=None, src_dv=None):
        if pe_wait is not None:
            b.wait("pe", sDV, pe_wait)
        if src_dv is not None:
            b.wait("act", sDV, src_dv)
        for kc in range(8):
            par = kc % 2
            sk = src_t[:, kc * T:(kc + 1) * T]
            cbv = sb_cb[:, par * 512:(par + 1) * 512]
            sqv = sb_cb[:, 1024 + par * 512:1024 + (par + 1) * 512]
            if kc >= 2:
                b.wait("dve", sPE, M2[("stmm", tag, k, kc - 2)])
                b.wait("act", sPE, M2[("stmm", tag, k, kc - 2)])
            b.op("dve", lambda E, sk=sk, cbv=cbv: E.tensor_copy(cbv, sk), sDV)
            vc = sDV.n
            b.op("act", lambda E, sk=sk, sqv=sqv: E.activation(sqv, sk, AF.Square), sAC)
            vs = sAC.n
            b.wait("pe", sDV, vc)
            b.wait("pe", sAC, vs)
            b.op("pe", lambda E, kc=kc, cbv=cbv: E.matmul(psU2[0:1, :], sb_onescb[:, :], cbv, start=(kc == 0), stop=(kc == 7)), sPE)
            b.op("pe", lambda E, kc=kc, sqv=sqv: E.matmul(psU2[32:33, :], sb_onescb[:, :], sqv, start=(kc == 0), stop=(kc == 7)), sPE)
            M2[("stmm", tag, k, kc)] = sPE.n
        b.wait("act", sPE, sPE.n)
        b.op("act", lambda E: E.activation(sb_rows[:, 0:512], psU2[0:1, :], AF.Copy, scale=INV_D), sAC)
        vm = sAC.n
        b.wait("act", sAC, vm)
        b.op("act", lambda E: E.activation(sb_rows[:, 512:1024], sb_rows[:, 0:512], AF.Square), sAC)
        vq = sAC.n
        b.wait("dve", sAC, vq)
        b.op("dve", lambda E: E.scalar_tensor_tensor(sb_rows[:, 512:1024], psU2[32:33, :], INV_D, sb_rows[:, 512:1024], ALU.mult, ALU.subtract), sDV)
        vv = sDV.n
        b.wait("act", sDV, vv)
        b.op("act", lambda E: E.activation(sb_rows[:, 1024:1536], sb_rows[:, 512:1024], AF.Ln, bias=sb_eps[:, :]), sAC)
        b.wait("act", sAC, sAC.n)
        b.op("act", lambda E: E.activation(sb_rows[:, 1024:1536], sb_rows[:, 1024:1536], AF.Exp, scale=-0.5), sAC)
        vr = sAC.n
        b.wait("dve", sAC, vr)
        b.op("dve", lambda E: E.tensor_scalar_mul(sb_rows[:, 512:1024], sb_rows[:, 0:512], -1.0), sDV)
        vnm = sDV.n
        b.wait("pe", sDV, vnm)
        b.wait("pe", sDV, vv)
        b.op("pe", lambda E: E.matmul(psU2[:, :], sb_onesrf[:, :], sb_rows[:, 512:1024], start=True, stop=True), sPE)
        vb1 = sPE.n
        b.wait("dve", sPE, vb1)
        if out_mode == "bf":
            for kc in range(8):
                b.op("dve", lambda E, kc=kc, src_t=src_t: E.tensor_add(sb_xn2[:, kc * T:(kc + 1) * T], src_t[:, kc * T:(kc + 1) * T], psU2[:, :]), sDV)
        else:
            for kc in range(8):
                b.op("dve", lambda E, kc=kc, src_t=src_t: E.tensor_add((sb_xn2 if out_mode == "xn" else src_t)[:, kc * T:(kc + 1) * T], src_t[:, kc * T:(kc + 1) * T], psU2[:, :]), sDV)
        vsub = sDV.n
        b.wait("pe", sDV, vsub)
        b.wait("pe", sAC, vr)
        b.op("pe", lambda E: E.matmul(psU2[:, :], sb_onesrf[:, :], sb_rows[:, 1024:1536], start=True, stop=True), sPE)
        vb2 = sPE.n
        M2[("bc2", tag, k)] = sPE.n
        b.wait("dve", sPE, vb2)
        for kc in range(8):
            if out_mode == "xn":
                b.op("dve", lambda E, kc=kc: E.tensor_mul(sb_xn2[:, kc * T:(kc + 1) * T], sb_xn2[:, kc * T:(kc + 1) * T], psU2[:, :]), sDV)
            elif out_mode == "inplace":
                b.op("dve", lambda E, kc=kc, src_t=src_t: E.tensor_mul(src_t[:, kc * T:(kc + 1) * T], src_t[:, kc * T:(kc + 1) * T], psU2[:, :]), sDV)
            else:
                b.op("dve", lambda E, kc=kc: E.tensor_mul(sb_xnb2[:, kc * T:(kc + 1) * T], sb_xn2[:, kc * T:(kc + 1) * T], psU2[:, :]), sDV)
        M2[("ln", tag, k)] = sDV.n

    for k in range(NCH):
        par = k % 2
        t0 = k * T
        HOF = par * 16384
        if k >= 2:
            b.wait("act", sPE, M2[("zmm", k - 2)])
        b.op("act", lambda E, t0=t0, HOF=HOF: E.dma_start(
            out=WBIG2[:, HOF:HOF + 16384].rearrange("p (kc t) -> p kc t", t=T),
            in_=d_h.ap().rearrange("(kc p) t -> p kc t", p=128)[:, :, t0:t0 + T]), sXL, 16)
        M2[("hld", k)] = sXL.n
        if k > 0:
            b.wait("act", sPE, M2[("trx1", k - 1)])
        b.op("act", lambda E, t0=t0: E.dma_start(
            out=sb_x2.rearrange("p (kc t) -> p kc t", t=T),
            in_=xT.rearrange("(kc p) t -> p kc t", p=128)[:, :, t0:t0 + T]), sXL, 16)
        M2[("xld", k)] = sXL.n
        b.wait("dve", sXL, M2[("xld", k)])
        b.wait("act", sXL, M2[("xld", k)])
        if k > 0:
            b.wait("dve", sPE, M2[("trx1", k - 1)])
        emit_ln2(k, sb_x2, "x", "xn",
                 pe_wait=(M2[("trcpf", k - 1, 3, 1)] if k > 0 else None))
        if debug_level >= 2 and k == 0:
            b.wait("sync", sDV, M2[("ln", "x", k)])
            dma("sync", dbg["o_xn2"][:, :], sb_xn2[:, :])
            M2[("xndmp",)] = sLD.n
            b.wait("sync", sLD, M2[("hld", k)])
            dma("sync", dbg["o_hrb"][:, :], WBIG2[:, 0:16384])
        b.wait("pe", sXL, M2[("hld", k)])
        for m in range(8):
            bank = psZ[m % 4]
            if m >= 4:
                b.wait("pe", sDV, M2[("ao", k, m - 4)])
            elif k > 0:
                b.wait("pe", sDV, M2[("ao", k - 1, m + 4 - 8 + 8)])
            for kc in range(32):
                b.op("pe", lambda E, bank=bank, kc=kc, m=m, HOF=HOF: E.matmul(bank[:, :], WBIG[:, kc * 1024 + m * 128:kc * 1024 + (m + 1) * 128], WBIG2[:, HOF + kc * 512:HOF + (kc + 1) * 512], start=(kc == 0), stop=(kc == 31)), sPE)
            M2[("zmm_m", k, m)] = sPE.n
            b.wait("dve", sPE, M2[("zmm_m", k, m)])
            b.wait("dve", sDV, M2[("ln", "x", k)])
            if debug_level >= 2 and k == 0 and m == 0:
                b.wait("dve", sLD, M2[("xndmp",)])
                b.op("dve", lambda E: E.tensor_copy(dbgz0_t[:, :], psZ[0][:, :]), sDV)
                b.wait("dve", sDV, sDV.n)
                b.wait("sync", sDV, sDV.n)
                dma("sync", dbg["o_z0"][:, :], dbgz0_t[:, :])
            b.op("dve", lambda E, bank=bank, m=m: E.scalar_tensor_tensor(sb_ao[:, m * 512:(m + 1) * 512], bank[:, :], sb_sig[:, :], sb_xn2[:, m * 512:(m + 1) * 512], ALU.mult, ALU.add), sDV)
            M2[("ao", k, m)] = sDV.n
        M2[("zmm", k)] = sPE.n
        if debug_level >= 2 and k == 0:
            b.wait("sync", sDV, M2[("ao", k, 7)])
            dma("sync", dbg["o_ao"][:, :], sb_ao[:, :])
            M2[("aodmp",)] = sLD.n
        if debug_level >= 2 and k == 0:
            b.wait("dve", sLD, M2[("aodmp",)])
        emit_ln2(k, sb_ao, "ao", "inplace", pe_wait=M2[("ln", "x", k)], src_dv=M2[("ao", k, 7)])
        for kc in range(8):
            b.op("dve", lambda E, kc=kc: E.tensor_add(sb_x2[:, kc * T:(kc + 1) * T], sb_ao[:, kc * T:(kc + 1) * T], sb_x2[:, kc * T:(kc + 1) * T]), sDV)
        M2[("x1", k)] = sDV.n
        emit_ln2(k, sb_x2, "x1", "bf", pe_wait=M2[("ln", "ao", k)], src_dv=M2[("x1", k)])
        b.wait("pe", sDV, M2[("ln", "x1", k)])
        for tg in range(4):
            col = (k * 4 + tg) * 8
            for kc in range(8):
                b.op("pe", lambda E, col=col, kc=kc, tg=tg: E.matmul(psLG[:, col:col + 8], sb_xnb2[:, kc * 512 + tg * 128:kc * 512 + (tg + 1) * 128], sb_wg[:, kc * 8:(kc + 1) * 8], start=(kc == 0), stop=(kc == 7)), sPE)
        M2[("lgmm", k)] = sPE.n
        for tg in range(4):
            if tg > 0:
                b.wait("dve", sST1, M2[("x1st", k, tg - 1)])
                b.wait("dve", sST2, M2[("xn2st", k, tg - 1)])
            elif k > 0:
                b.wait("dve", sST1, M2[("x1st", k - 1, 3)])
                b.wait("dve", sST2, M2[("xn2st", k - 1, 3)])
            for half in range(2):
                trb = psTRf if half == 0 else psU2
                key = ("trcpf", k, tg - 1, half) if tg > 0 else ("trcpf", k - 1, 3, half)
                if key in M2:
                    b.wait("pe", sDV, M2[key])
                for i in range(4):
                    kc = half * 4 + i
                    b.op("pe", lambda E, i=i, kc=kc, tg=tg, trb=trb: E.transpose(trb[:, i * 128:(i + 1) * 128], sb_x2[:, kc * 512 + tg * 128:kc * 512 + (tg + 1) * 128], sb_identf[:, :]), sPE)
                vtr = sPE.n
                b.wait("dve", sPE, vtr)
                b.op("dve", lambda E, half=half, trb=trb: E.tensor_copy(stg1[:, half * 512:(half + 1) * 512], trb[:, :]), sDV)
                M2[("trcpf", k, tg, half)] = sDV.n
            b.wait("sync", sDV, M2[("trcpf", k, tg, 1)])
            if (k, tg) != (0, 0):
                pass
            b.op("sync", lambda E, t0=t0, tg=tg: E.dma_start(out=d_x1t.ap()[t0 + tg * 128:t0 + (tg + 1) * 128, :], in_=stg1[:, :]), sST1, 16)
            M2[("x1st", k, tg)] = sST1.n
            for half in range(2):
                if (k, tg, half) != (0, 0, 0):
                    b.wait("pe", sDV, M2.get(("trcpb", k, tg, half - 1), M2.get(("trcpb", k, tg - 1, 1), M2.get(("trcpb", k - 1, 3, 1), 0))))
                for i in range(4):
                    kc = half * 4 + i
                    b.op("pe", lambda E, i=i, kc=kc, tg=tg: E.transpose(psTRb[:, i * 128:(i + 1) * 128], sb_xnb2[:, kc * 512 + tg * 128:kc * 512 + (tg + 1) * 128], sb_identb[:, :]), sPE)
                vtr = sPE.n
                b.wait("dve", sPE, vtr)
                b.op("dve", lambda E, half=half: E.tensor_copy(stg2[:, half * 512:(half + 1) * 512], psTRb[:, :]), sDV)
                M2[("trcpb", k, tg, half)] = sDV.n
            b.wait("sync", sDV, M2[("trcpb", k, tg, 1)])
            b.op("sync", lambda E, t0=t0, tg=tg: E.dma_start(out=d_xn2bf.ap()[t0 + tg * 128:t0 + (tg + 1) * 128, :], in_=stg2[:, :]), sST2, 16)
            M2[("xn2st", k, tg)] = sST2.n
        M2[("trx1", k)] = sPE.n
        b.wait("dve", sST2, M2[("xn2st", k, 3)])
        if k % 2 == 1:
            cch = k // 2
            b.wait("pool", sST2, M2[("xn2st", k, 3)])
            b.wait("pool", sST1, M2[("x1st", k, 3)])
            b.op("pool", lambda E, cch=cch: E.collective_compute("AllGather", ALU.bypass,
                replica_groups=[list(range(NCORE))],
                ins=[d_xn2bf.ap()[cch * 1024:(cch + 1) * 1024, :]],
                outs=[d_xn2all.ap()[cch * 8192:(cch + 1) * 8192, :]]), sCC, 1)
            M2[("xnag", cch)] = sCC.n

    b.wait("dve", sPE, M2[("lgmm", NCH - 1)])
    sb_lg = psLG[:, 0:256]
    E8 = lambda t: t.rearrange("p (tg e) -> p tg e", e=8)
    B8 = lambda t: t.rearrange("p (tg one) -> p tg one", one=1).broadcast_to([128, 32, 8])

    def dv(fn):
        b.op("dve", fn, sDV)
        b.wait("dve", sDV, sDV.n)

    dv(lambda E: E.tensor_reduce(sb_m1[:, :], E8(sb_lg), axis=AX.X, op=ALU.max))
    dv(lambda E: E.tensor_tensor(E8(rtA), E8(sb_lg), B8(sb_m1), ALU.is_lt))
    dv(lambda E: E.scalar_tensor_tensor(rtA[:, :], rtA[:, :], 1000.0, sb_iotaE[:, :], ALU.mult, ALU.add))
    dv(lambda E: E.tensor_reduce(sb_i1[:, :], E8(rtA), axis=AX.X, op=ALU.min))
    dv(lambda E: E.tensor_tensor(E8(sb_oh1), E8(sb_iotaE), B8(sb_i1), ALU.is_equal))
    dv(lambda E: E.scalar_tensor_tensor(rtB[:, :], sb_oh1[:, :], -1e9, sb_lg[:, :], ALU.mult, ALU.add))
    dv(lambda E: E.tensor_reduce(sb_m2[:, :], E8(rtB), axis=AX.X, op=ALU.max))
    dv(lambda E: E.tensor_tensor(E8(rtA), E8(rtB), B8(sb_m2), ALU.is_lt))
    dv(lambda E: E.scalar_tensor_tensor(rtA[:, :], rtA[:, :], 1000.0, sb_iotaE[:, :], ALU.mult, ALU.add))
    dv(lambda E: E.tensor_reduce(sb_i2[:, :], E8(rtA), axis=AX.X, op=ALU.min))
    dv(lambda E: E.tensor_tensor(E8(sb_oh2), E8(sb_iotaE), B8(sb_i2), ALU.is_equal))
    dv(lambda E: E.tensor_sub(sb_m2[:, :], sb_m2[:, :], sb_m1[:, :]))
    b.wait("act", sDV, sDV.n)
    b.op("act", lambda E: E.activation(sb_g2[:, :], sb_m2[:, :], AF.Sigmoid), sAC)
    b.op("act", lambda E: E.activation(sb_g1[:, :], sb_m2[:, :], AF.Sigmoid, scale=-1.0), sAC)
    v_gates = sAC.n
    dv(lambda E: E.tensor_copy(sb_cb[:, 0:256], sb_oh1[:, :]))
    dv(lambda E: E.tensor_copy(sb_cb[:, 256:512], sb_oh2[:, :]))
    v_ohb = sDV.n
    b.wait("pe", sDV, v_ohb)
    for j in range(2):
        for tg in range(32):
            b.op("pe", lambda E, j=j, tg=tg: E.matmul(psZ[0][:, j * 256 + tg * 8:j * 256 + (tg + 1) * 8], sb_trib[:, :], sb_cb[:, j * 256 + tg * 8:j * 256 + (tg + 1) * 8], start=True, stop=True), sPE)
    for j in range(2):
        for tg in range(32):
            b.op("pe", lambda E, j=j, tg=tg: E.matmul(psZ[1][0:8, j * 32 + tg:j * 32 + tg + 1], sb_cb[:, j * 256 + tg * 8:j * 256 + (tg + 1) * 8], sb_onescb[:, :], start=True, stop=True), sPE)
    v_rkmm = sPE.n
    b.wait("dve", sPE, v_rkmm)
    dv(lambda E: E.tensor_mul(rtA[:, :], psZ[0][:, 0:256], sb_oh1[:, :]))
    dv(lambda E: E.tensor_reduce(sb_pre1[:, :], E8(rtA), axis=AX.X, op=ALU.add))
    dv(lambda E: E.tensor_mul(rtA[:, :], psZ[0][:, 256:512], sb_oh2[:, :]))
    dv(lambda E: E.tensor_reduce(sb_pre2[:, :], E8(rtA), axis=AX.X, op=ALU.add))
    dv(lambda E: E.tensor_copy(sb_ct[0:8, :], psZ[1][0:8, 0:64]))
    for j in range(2):
        dv(lambda E, j=j: E.tensor_tensor_scan(sb_tb[0:8, j * 32:(j + 1) * 32], sb_ones128[0:8, 0:32], sb_ct[0:8, j * 32:(j + 1) * 32], 0.0, ALU.mult, ALU.add))
    b.wait("sync", sDV, sDV.n)
    dma("sync", d_cnt.ap()[0:1, :], sb_tb[0:8, 31:32])
    dma("sync", d_cnt.ap()[1:2, :], sb_tb[0:8, 63:64])
    M2[("cntst",)] = sLD.n
    b.wait("dve", sLD, M2[("cntst",)])
    dv(lambda E: E.tensor_sub(sb_tb[0:8, :], sb_tb[0:8, :], sb_ct[0:8, :]))
    v_p2 = sDV.n

    if debug_level >= 2:
        b.wait("sync", sLD, M2[("xn2st", NCH - 1, 3)])
        dma("sync", dbg["o_x1t"][:, :], d_x1t.ap()[:, :])
        dma("sync", dbg["o_xn2bf"][:, :], d_xn2bf.ap()[:, :])
        b.wait("sync", sDV, v_p2)
        dma("sync", dbg["o_rt"][:, 0:32], sb_i1[:, :])
        dma("sync", dbg["o_rt"][:, 32:64], sb_i2[:, :])
        dma("sync", dbg["o_rt"][:, 64:96], sb_g1[:, :])
        dma("sync", dbg["o_rt"][:, 96:128], sb_g2[:, :])
        dma("sync", dbg["o_rt"][:, 128:160], sb_pre1[:, :])
        dma("sync", dbg["o_rt"][:, 160:192], sb_pre2[:, :])
        dma("sync", dbg["o_rt"][:, 192:256], sb_tb[:, :])
        dma("sync", dbg["o_rt"][:, 256:320], sb_ct[:, :])

    b.barrier("p2", extra_waits=[("sync", sLD, sLD.n), ("sync", sST1, sST1.n), ("sync", sST2, sST2.n), ("pool", sG, sG.n)])
    sc2.close()

    sc3 = Scope(nc)
    sb_h3 = sc3.sb("sb_h3", [128, 16384], BF16)
    sb_xfm = sc3.sb("sb_xfm", [128, 8192], BF16)
    sb_ytm = sc3.sb("sb_ytm", [128, 2048], BF16)
    sb_pmy = sc3.sb("sb_pmy", [128, 128], F32)
    sb_pidi = sc3.sb("sb_pidi", [128, 80], I32)
    sb_pidf = sc3.sb("sb_pidf", [128, 80], F32)
    sb_cnt16 = sc3.sb("sb_cnt16", [16, 8], F32)
    sb_cb2 = sc3.sb("sb_cb2", [128, 2], F32)
    sb_gst = sc3.sb("sb_gst", [128, 2048], BF16)
    sb_fc2bt = sc3.sb("sb_fc2bt", [128, 1024], F32)
    sb_fc2brow = sb_rows[0:1, 0:1024]
    psM = [sc3.ps("psM0", [128, 512], F32), sc3.ps("psM1", [128, 512], F32)]
    psY3 = [sc3.ps("psY0", [128, 512], F32), sc3.ps("psY1", [128, 512], F32)]
    psX3 = sc3.ps("psX3", [128, 512], F32)

    dma("sync", sb_fc2brow, fc2b_in[:, :])
    for kc in range(8):
        dma("pool", WBIG[:, kc * 4096:(kc + 1) * 4096], fc1T_in[128 * kc:128 * (kc + 1), :], sG)
    for hc in range(32):
        dma("pool", WBIG2[:, hc * 1024:(hc + 1) * 1024], fc2_in[128 * hc:128 * (hc + 1), :], sG)
    v_fcw = sG.n
    rg = [list(range(NCORE))]
    b.op("pool", lambda E: E.collective_compute("AllGather", ALU.bypass, replica_groups=rg,
        ins=[d_cnt.ap()[:, :]], outs=[d_cntall.ap()[:, :]]), sCC, 1)
    v_cntag = sCC.n
    v_xnag = M2[("xnag", 3)]

    b.wait("sync", sCC, v_cntag)
    dma("sync", sb_cnt16[:, :], d_cntall.ap()[:, :])
    v_c16 = sLD.n
    b.wait("pe", sLD, v_c16)
    b.op("pe", lambda E: E.matmul(psX3[0:2, 0:8], sb_cmask[:, :], sb_cnt16[:, :], start=True, stop=True), sPE)
    v_cbmm = sPE.n
    b.wait("dve", sPE, v_cbmm)
    b.op("dve", lambda E: E.tensor_copy(sb_cnt16[0:2, :], psX3[0:2, 0:8]), sDV)
    b.wait("dve", sDV, sDV.n)
    b.wait("pe", sDV, sDV.n)
    b.op("pe", lambda E: E.transpose(psX3[0:8, 8:10], sb_cnt16[0:2, 0:8], sb_identf[0:2, 0:2]), sPE)
    v_cbt = sPE.n
    b.wait("dve", sPE, v_cbt)
    b.op("dve", lambda E: E.tensor_copy(sb_cb2[0:8, :], psX3[0:8, 8:10]), sDV)
    b.wait("dve", sDV, sDV.n)
    for j in range(2):
        b.op("dve", lambda E, j=j: E.tensor_scalar_add(sb_tb[0:8, j * 32:(j + 1) * 32], sb_tb[0:8, j * 32:(j + 1) * 32], sb_cb2[0:8, j:j + 1]), sDV)
    b.wait("dve", sDV, sDV.n)
    b.wait("sync", sDV, sDV.n)
    dma("sync", d_fb.ap()[:, :], sb_tb[0:8, :])
    b.wait("sync", sLD, sLD.n)
    b.op("sync", lambda E: E.dma_start(
        out=sb_brow[0:1, 0:512].rearrange("one (j tg e) -> one j tg e", j=2, e=8),
        in_=d_fb.ap().rearrange("e (j tg) -> j tg e", j=2),
        allow_slow_non_contiguous=True), sLD, 16)
    v_brow = sLD.n
    b.wait("pe", sLD, v_brow)
    b.op("pe", lambda E: E.matmul(psX3[:, :], sb_onesrf[:, :], sb_brow[0:1, 0:512], start=True, stop=True), sPE)
    v_bb = sPE.n
    b.wait("dve", sPE, v_bb)

    def dv3(fn):
        b.op("dve", fn, sDV)
        b.wait("dve", sDV, sDV.n)

    E8 = lambda t: t.rearrange("p (tg e) -> p tg e", e=8)
    for j, ohx, prex, gx, ix in ((0, sb_oh1, sb_pre1, sb_g1, sb_i1), (1, sb_oh2, sb_pre2, sb_g2, sb_i2)):
        dv3(lambda E, j=j, ohx=ohx: E.tensor_mul(rtA2[:, :], psX3[:, j * 256:(j + 1) * 256], ohx[:, :]))
        dv3(lambda E: E.tensor_reduce(sb_m1[:, :], E8(rtA2), axis=AX.X, op=ALU.add))
        dv3(lambda E, prex=prex: E.tensor_add(sb_m1[:, :], sb_m1[:, :], prex[:, :]))
        dv3(lambda E: E.tensor_scalar(sb_m2[:, :], sb_m1[:, :], 0.5, 1.0 / 128.0, ALU.add, ALU.mult))
        dv3(lambda E: E.tensor_copy(sb_i32[:, 0:32], sb_m2[:, :]))
        dv3(lambda E: E.tensor_copy(sb_m2[:, :], sb_i32[:, 0:32]))
        dv3(lambda E: E.scalar_tensor_tensor(rtA2[:, 0:32], sb_m2[:, :], -128.0, sb_m1[:, :], ALU.mult, ALU.add))
        dv3(lambda E: E.tensor_scalar(rtA2[:, 32:64], rtA2[:, 0:32], 0.0, None, ALU.is_lt))
        dv3(lambda E: E.tensor_sub(sb_m2[:, :], sb_m2[:, :], rtA2[:, 32:64]))
        dv3(lambda E: E.scalar_tensor_tensor(rtA2[:, 0:32], sb_m2[:, :], -128.0, sb_m1[:, :], ALU.mult, ALU.add))
        dv3(lambda E: E.scalar_tensor_tensor(rtA2[:, 64:96], rtA2[:, 0:32], 80.0, sb_m2[:, :], ALU.mult, ALU.add))
        dv3(lambda E: E.tensor_scalar(rtA2[:, 96:128], sb_m1[:, :], float(CAP), None, ALU.is_lt))
        dv3(lambda E, gx=gx, j=j: E.tensor_mul(sb_gk[:, j * 32:(j + 1) * 32], gx[:, :], rtA2[:, 96:128]))
        dv3(lambda E, ix=ix: E.scalar_tensor_tensor(rtA2[:, 128:160], ix[:, :], float(CAP), rtA2[:, 64:96], ALU.mult, ALU.add))
        dv3(lambda E: E.tensor_scalar(rtA2[:, 96:128], rtA2[:, 96:128], -1.0, 1.0, ALU.mult, ALU.add))
        dv3(lambda E: E.scalar_tensor_tensor(rtA2[:, 128:160], rtA2[:, 96:128], 90000.0, rtA2[:, 128:160], ALU.mult, ALU.add))
        dv3(lambda E, j=j: E.tensor_scalar(sb_pos[:, j * 32:(j + 1) * 32], rtA2[:, 128:160], 81984.0, None, ALU.min))
        dv3(lambda E, j=j: E.tensor_copy(sb_poi[:, j * 32:(j + 1) * 32], sb_pos[:, j * 32:(j + 1) * 32]))
    v_pos = sDV.n

    b.wait("pool", sDV, v_pos)
    for j in range(2):
        for tg in range(32):
            b.op("pool", lambda E, j=j, tg=tg: E.indirect_dma_start(
                out=d_pm.ap()[:, :], out_offset=IndirectOffsetOnAxis(sb_poi[:, j * 32 + tg:j * 32 + tg + 1], 0),
                in_=sb_tokid[:, tg:tg + 1], in_offset=None), sG, 16)
    v_scat = sG.n
    b.wait("dve", sG, v_scat)
    dv3(lambda E: E.tensor_copy(rtA2[:, 0:64], sb_poi[:, :]))
    dv3(lambda E: E.tensor_scalar(rtA2[:, 64:128], rtA2[:, 0:64], 0.5, 1.0 / CAP, ALU.add, ALU.mult))
    dv3(lambda E: E.tensor_copy(sb_i32[:, 0:64], rtA2[:, 64:128]))
    dv3(lambda E: E.tensor_copy(rtA2[:, 64:128], sb_i32[:, 0:64]))
    dv3(lambda E: E.scalar_tensor_tensor(rtA2[:, 128:192], rtA2[:, 64:128], -float(CAP), rtA2[:, 0:64], ALU.mult, ALU.add))
    dv3(lambda E: E.tensor_scalar(rtA2[:, 192:256], rtA2[:, 128:192], 0.0, None, ALU.is_lt))
    dv3(lambda E: E.tensor_sub(rtA2[:, 64:128], rtA2[:, 64:128], rtA2[:, 192:256]))
    dv3(lambda E: E.scalar_tensor_tensor(rtA2[:, 128:192], rtA2[:, 64:128], -float(CAP), rtA2[:, 0:64], ALU.mult, ALU.add))
    dv3(lambda E: E.tensor_scalar(rtA2[:, 0:64], rtA2[:, 128:192], 0.5, 1.0 / 2560.0, ALU.add, ALU.mult))
    dv3(lambda E: E.tensor_copy(sb_i32[:, 0:64], rtA2[:, 0:64]))
    dv3(lambda E: E.tensor_copy(rtA2[:, 0:64], sb_i32[:, 0:64]))
    dv3(lambda E: E.scalar_tensor_tensor(rtA2[:, 192:256], rtA2[:, 0:64], -2560.0, rtA2[:, 128:192], ALU.mult, ALU.add))
    dv3(lambda E: E.tensor_scalar(sb_m1[:, 0:32], rtA2[:, 192:224], 0.0, None, ALU.is_lt))
    dv3(lambda E: E.tensor_scalar(sb_m2[:, 0:32], rtA2[:, 224:256], 0.0, None, ALU.is_lt))
    dv3(lambda E: E.tensor_sub(rtA2[:, 0:32], rtA2[:, 0:32], sb_m1[:, 0:32]))
    dv3(lambda E: E.tensor_sub(rtA2[:, 32:64], rtA2[:, 32:64], sb_m2[:, 0:32]))
    dv3(lambda E: E.scalar_tensor_tensor(rtA2[:, 192:256], rtA2[:, 0:64], -2560.0, rtA2[:, 128:192], ALU.mult, ALU.add))
    dv3(lambda E: E.tensor_scalar_mul(rtA2[:, 0:64], rtA2[:, 0:64], 20480.0))
    dv3(lambda E: E.scalar_tensor_tensor(rtA2[:, 0:64], rtA2[:, 64:128], 2560.0, rtA2[:, 0:64], ALU.mult, ALU.add))
    dv3(lambda E: E.tensor_add(rtA2[:, 0:64], rtA2[:, 0:64], rtA2[:, 192:256]))
    dv3(lambda E: E.tensor_scalar(rtA2[:, 0:64], rtA2[:, 0:64], 81919.0, None, ALU.min))
    dv3(lambda E: E.tensor_copy(sb_poi[:, :], rtA2[:, 0:64]))
    b.wait("pool", sG, v_scat)
    b.op("pool", lambda E: E.collective_compute("ReduceScatter", ALU.max, replica_groups=rg,
        ins=[d_pm.ap()[0:81920, :]], outs=[d_pmy.ap()[:, :]]), sCC, 1)
    v_rs = sCC.n

    b.wait("sync", sCC, v_rs)
    dma("sync", sb_pmy[0:80, :], d_pmy.ap().rearrange("(g p) one -> g (p one)", p=128))
    v_pmyld = sLD.n
    b.wait("pe", sLD, v_pmyld)
    b.wait("pe", sDV, v_pos)
    b.op("pe", lambda E: E.transpose(psX3[:, 0:80], sb_pmy[0:80, :], sb_identf[0:80, 0:80]), sPE)
    v_pmyt = sPE.n
    b.wait("dve", sPE, v_pmyt)
    dv3(lambda E: E.tensor_scalar(sb_pidf[:, :], psX3[:, 0:80], -1.0, 0.0, ALU.add, ALU.max))
    dv3(lambda E: E.tensor_copy(sb_pidi[:, :], sb_pidf[:, :]))
    v_pidi = sDV.n

    b.wait("pe", sDV, v_pidi)
    b.wait("pe", sLD, sLD.n)
    b.op("pe", lambda E: E.matmul(psX3[:, 0:512], sb_onesrf[:, :], sb_rows[0:1, 0:512], start=True, stop=True), sPE)
    b.op("pe", lambda E: E.matmul(psM[0][:, :], sb_onesrf[:, :], sb_rows[0:1, 512:1024], start=True, stop=True), sPE)
    v_fbb = sPE.n
    b.wait("dve", sPE, v_fbb)
    dv3(lambda E: E.tensor_copy(sb_fc2bt[:, 0:512], psX3[:, 0:512]))
    dv3(lambda E: E.tensor_copy(sb_fc2bt[:, 512:1024], psM[0][:, :]))
    v_fc2bt = sDV.n

    b.wait("pool", sDV, v_pidi)
    b.wait("pool", sCC, v_xnag)

    sGS = [b.sem(f"sGS{i}") for i in range(2)]

    def emit_gathers(g0, g1):
        for g in range(g0, g1):
            gpar = g % 2
            gsl = sb_gst[:, gpar * 1024:(gpar + 1) * 1024]
            if g >= 2:
                b.wait("pool", sGS[gpar], sGS[gpar].n)
            b.op("pool", lambda E, g=g, gsl=gsl: E.indirect_dma_start(
                out=gsl, out_offset=None,
                in_=d_xn2all.ap()[:, :], in_offset=IndirectOffsetOnAxis(sb_pidi[:, g:g + 1], 0)), sG, 16)
            M2[("gatp", g)] = sG.n
            b.wait("sync", sG, M2[("gatp", g)])
            b.op("sync", lambda E, g=g, gsl=gsl: E.dma_start(out=d_disp.ap()[g * 128:(g + 1) * 128, :], in_=gsl), sGS[gpar], 16)
            M2[("gat", g)] = (sGS[0].n, sGS[1].n)

    b.wait("pe", sG, v_fcw)
    b.wait("pe", sDV, v_fc2bt)
    NSC = CAP // 512
    emit_gathers(0, 80)
    for sc in range(NSC):
        par = sc % 2
        s0 = sc * 512
        _g4 = M2[("gat", (s0 + 511) // 128)]
        for _i in range(2):
            b.wait("sync", sGS[_i], _g4[_i])
        if sc >= 2:
            b.wait("sync", sPE, M2[("fc1", sc - 2)])
        for kc in range(8):
            b.op("sync", lambda E, kc=kc, s0=s0, par=par: E.dma_start_transpose(
                sb_xfm[:, par * 4096 + kc * 512:par * 4096 + (kc + 1) * 512],
                d_disp.ap()[s0:s0 + 512, kc * 128:(kc + 1) * 128]), sLD, 16)
        M2[("xt", sc)] = sLD.n
        b.wait("pe", sLD, M2[("xt", sc)])
        for hc in range(32):
            mb = psM[hc % 2]
            if hc >= 2:
                b.wait("pe", sAC, M2[("sil", sc, hc - 2)])
            elif sc > 0:
                b.wait("pe", sAC, M2[("sil", sc - 1, 31)])
            for kc in range(8):
                b.op("pe", lambda E, mb=mb, kc=kc, hc=hc, par=par: E.matmul(mb[:, :], WBIG[:, kc * 4096 + hc * 128:kc * 4096 + (hc + 1) * 128], sb_xfm[:, par * 4096 + kc * 512:par * 4096 + (kc + 1) * 512], start=(kc == 0), stop=(kc == 7)), sPE)
            vmm = sPE.n
            b.wait("act", sPE, vmm)
            if sc > 0 and hc == 0:
                b.wait("act", sPE, M2[("fc2", sc - 1)])
            b.op("act", lambda E, mb=mb, hc=hc: E.activation(sb_h3[:, hc * 512:(hc + 1) * 512], mb[:, :], AF.Silu, bias=sb_fc1b[:, hc:hc + 1]), sAC)
            M2[("sil", sc, hc)] = sAC.n
        M2[("fc1", sc)] = sPE.n
        b.wait("pe", sAC, M2[("sil", sc, 31)])
        b.wait("pe", sDV, v_fc2bt)
        for sg in range(4):
            par2 = sg % 2
            for dh in range(2):
                yb = psY3[dh]
                if sc + sg > 0:
                    b.wait("pe", sDV, M2.get(("yev", sc, sg - 1, dh), M2.get(("yev", sc - 1, 3, dh), 0)))
                for hc in range(32):
                    b.op("pe", lambda E, yb=yb, hc=hc, sg=sg, dh=dh: E.matmul(yb[:, :], sb_h3[:, hc * 512 + sg * 128:hc * 512 + (sg + 1) * 128], WBIG2[:, hc * 1024 + dh * 512:hc * 1024 + (dh + 1) * 512], start=(hc == 0), stop=(hc == 31)), sPE)
                vy = sPE.n
                b.wait("dve", sPE, vy)
                if sc + sg > 0:
                    _pv = M2.get(("yst", sc, sg - 2), M2.get(("yst", sc - 1, 2 + sg), 0))
                    b.wait("dve", sYS[sg % 2], _pv)
                b.op("dve", lambda E, yb=yb, dh=dh, par2=par2: E.scalar_tensor_tensor(sb_ytm[:, par2 * 1024 + dh * 512:par2 * 1024 + (dh + 1) * 512], yb[:, :], 1.0, sb_fc2bt[:, dh * 512:(dh + 1) * 512], ALU.mult, ALU.add), sDV)
                M2[("yev", sc, sg, dh)] = sDV.n
            b.wait("sync", sDV, M2[("yev", sc, sg, 1)])
            b.op("sync", lambda E, s0=s0, sg=sg, par2=par2: E.dma_start(out=d_y.ap()[s0 + sg * 128:s0 + (sg + 1) * 128, :], in_=sb_ytm[:, par2 * 1024:(par2 + 1) * 1024]), sYS[par2], 16)
            M2[("yst", sc, sg)] = sYS[par2].n
        M2[("fc2", sc)] = sPE.n
        if sc % 5 == 4:
            ych = sc // 5
            b.wait("pool", sYS[0], sYS[0].n)
            b.wait("pool", sYS[1], sYS[1].n)
            b.wait("pool", sG, sG.n)
            b.wait("pool", sGS[0], sGS[0].n)
            b.wait("pool", sGS[1], sGS[1].n)
            b.op("pool", lambda E, ych=ych: E.collective_compute("AllGather", ALU.bypass,
                replica_groups=[list(range(NCORE))],
                ins=[d_y.ap()[ych * 2560:(ych + 1) * 2560, :]],
                outs=[d_yall.ap()[ych * 20480:(ych + 1) * 20480, :]]), sCC, 1)
            M2[("yag", ych)] = sCC.n


    if debug_level >= 3:
        b.wait("sync", sYS[0], sYS[0].n)
        b.wait("sync", sYS[1], sYS[1].n)
        dma("sync", dbg["o_y3"][:, :], d_y.ap()[:, :])
        dma("sync", dbg["o_disp"][:, :], d_disp.ap()[:, :])
        b.wait("sync", sCC, v_rs)
        dma("sync", dbg["o_pmy"][:, :], d_pmy.ap()[:, :])
        dma("sync", dbg["o_posd"][:, :], sb_pos[:, :])
        dma("sync", dbg["o_gkd"][:, :], sb_gk[:, :])
        dma("sync", dbg["o_cnt"][:, :], d_cntall.ap()[:, :])
        dma("sync", dbg["o_cb2"][:, :], sb_cb2[:, :])

    b.barrier("p3", extra_waits=[("sync", sLD, sLD.n), ("pool", sG, sG.n), ("pool", sCC, sCC.n)])
    sc3.close()
    scW.close()

    sc4 = Scope(nc)
    x1b = sc4.sb("x1b", [128, 8192], F32)
    yb = sc4.sb("yb", [128, 8192], BF16)
    w_scr = sc4.sb("w_scr", [128, 4096], F32)
    wA = sc4.sb("wA", [128, 4096], F32)
    wB = sc4.sb("wB", [128, 4096], F32)
    outb = sc4.sb("outb", [128, 8192], F32)
    gam4 = sc4.sb("gam4", [128, 4096], F32)
    bet4 = sc4.sb("bet4", [128, 4096], F32)
    sb_grow = sc4.sb("sb_grow", [1, 1024], F32)
    sb_brow4 = sc4.sb("sb_brow4", [1, 1024], F32)
    c4 = sc4.sb("c4", [128, 32], F32)
    xnb4 = sc4.sb("xnb4", [128, 4096], BF16)
    sb_eps4 = sc4.sb("sb_eps4", [128, 1], F32)
    psB4 = sc4.ps("psB4", [128, 512], F32)

    dma("sync", sb_grow[:, :], lng_in[:, :])
    dma("sync", sb_brow4[:, :], lnb_in[:, :])
    v_gb = sLD.n
    b.op("dve", lambda E: E.memset(sb_eps4[:, :], EPS), sDV)
    v_ms4 = sDV.n
    b.wait("pe", sLD, v_gb)
    b.wait("pe", sDV, v_ms4)
    for rep in range(4):
        for half in range(2):
            b.op("pe", lambda E, half=half: E.matmul(psB4[:, :], sb_onesrf[:, :], sb_grow[0:1, half * 512:(half + 1) * 512], start=True, stop=True), sPE)
            v = sPE.n
            b.wait("dve", sPE, v)
            b.op("dve", lambda E, rep=rep, half=half: E.tensor_copy(gam4[:, rep * 1024 + half * 512:rep * 1024 + (half + 1) * 512], psB4[:, :]), sDV)
            b.wait("pe", sDV, sDV.n)
            b.op("pe", lambda E, half=half: E.matmul(psB4[:, :], sb_onesrf[:, :], sb_brow4[0:1, half * 512:(half + 1) * 512], start=True, stop=True), sPE)
            v = sPE.n
            b.wait("dve", sPE, v)
            b.op("dve", lambda E, rep=rep, half=half: E.tensor_copy(bet4[:, rep * 1024 + half * 512:rep * 1024 + (half + 1) * 512], psB4[:, :]), sDV)
            b.wait("pe", sDV, sDV.n)
    v_gb4 = sDV.n
    b.wait("pool", sDV, v_gb4)

    NB = 8
    GV = lambda t: t.rearrange("p (g d) -> p g d", g=4)

    def ln_stats(srcv, sqkey=None):
        b.op("dve", lambda E, srcv=srcv: E.tensor_reduce(c4[:, 0:4], GV(srcv), axis=AX.X, op=ALU.add), sDV)
        vr1 = sDV.n
        for g in range(4):
            b.op("act", lambda E, srcv=srcv, g=g: E.activation(w_scr[:, 0:1024], srcv[:, g * 1024:(g + 1) * 1024], AF.Square, accum_out=c4[:, 4 + g:5 + g]), sAC)
        vsq = sAC.n
        if sqkey is not None:
            M2[sqkey] = vsq
        b.wait("dve", sDV, vr1)
        b.op("dve", lambda E: E.tensor_scalar_mul(c4[:, 8:12], c4[:, 0:4], INV_D), sDV)
        b.wait("dve", sDV, sDV.n)
        b.op("dve", lambda E: E.tensor_mul(c4[:, 12:16], c4[:, 8:12], c4[:, 8:12]), sDV)
        b.wait("dve", sAC, vsq)
        b.wait("dve", sDV, sDV.n)
        b.op("dve", lambda E: E.scalar_tensor_tensor(c4[:, 4:8], c4[:, 4:8], INV_D, c4[:, 12:16], ALU.mult, ALU.subtract), sDV)
        vvar = sDV.n
        b.wait("act", sDV, vvar)
        b.op("act", lambda E: E.activation(c4[:, 16:20], c4[:, 4:8], AF.Ln, bias=sb_eps4[:, :]), sAC)
        b.wait("act", sAC, sAC.n)
        b.op("act", lambda E: E.activation(c4[:, 16:20], c4[:, 16:20], AF.Exp, scale=-0.5), sAC)
        vrstd = sAC.n
        b.wait("dve", sAC, vrstd)
        b.op("dve", lambda E: E.scalar_tensor_tensor(c4[:, 12:16], c4[:, 8:12], -1.0, c4[:, 16:20], ALU.mult, ALU.mult), sDV)
        b.wait("dve", sDV, sDV.n)
        return sDV.n

    def ln_apply(srcv, dstv):
        for g in range(4):
            b.op("dve", lambda E, srcv=srcv, dstv=dstv, g=g: E.tensor_scalar(dstv[:, g * 1024:(g + 1) * 1024], srcv[:, g * 1024:(g + 1) * 1024], c4[:, 16 + g:17 + g], c4[:, 12 + g:13 + g], ALU.mult, ALU.add), sDV)
        return sDV.n

    for bi in range(NB):
        par = bi % 2
        x1v = x1b[:, par * 4096:(par + 1) * 4096]
        outv = outb[:, par * 4096:(par + 1) * 4096]
        if bi >= 2:
            b.wait("sync", sPO, M2[("fin", bi - 2)])
            b.wait("sync", sOS[par], M2[("ost", bi - 2)])
        b.op("sync", lambda E, bi=bi, x1v=x1v: E.dma_start(
            out=GV(x1v),
            in_=d_x1t.ap().rearrange("(G p) d -> p G d", p=128)[:, 4 * bi:4 * bi + 4, :]), sLD, 16)
        M2[("x1ld", bi)] = sLD.n
        if bi > 0:
            b.wait("sync", sDV, M2[("g1", bi - 1)])
        b.op("sync", lambda E, bi=bi: E.dma_start(
            out=GV(xnb4),
            in_=d_xn2bf.ap().rearrange("(G p) d -> p G d", p=128)[:, 4 * bi:4 * bi + 4, :]), sLD, 16)
        M2[("xnld", bi)] = sLD.n
        if bi > 0:
            b.wait("pool", sDV, M2[("gmul", bi - 1)])
        for g in range(4):
            tg = 4 * bi + g
            b.op("pool", lambda E, tg=tg, g=g: E.indirect_dma_start(
                out=yb[:, g * 1024:(g + 1) * 1024], out_offset=None, in_=d_yall.ap()[:, :],
                in_offset=IndirectOffsetOnAxis(sb_poi[:, tg:tg + 1], 0)), sG, 16)
            b.op("pool", lambda E, tg=tg, g=g: E.indirect_dma_start(
                out=yb[:, 4096 + g * 1024:4096 + (g + 1) * 1024], out_offset=None, in_=d_yall.ap()[:, :],
                in_offset=IndirectOffsetOnAxis(sb_poi[:, 32 + tg:33 + tg], 0)), sG, 16)
        M2[("ygat", bi)] = sG.n
        b.wait("dve", sG, M2[("ygat", bi)])
        b.wait("dve", sLD, M2[("xnld", bi)])
        if bi >= 1:
            b.wait("dve", sAC, M2[("sqA", bi - 1)])
        for g in range(4):
            tg = 4 * bi + g
            b.op("dve", lambda E, g=g, tg=tg: E.scalar_tensor_tensor(wA[:, g * 1024:(g + 1) * 1024], yb[:, g * 1024:(g + 1) * 1024], sb_gk[:, tg:tg + 1], xnb4[:, g * 1024:(g + 1) * 1024], ALU.mult, ALU.add), sDV)
        M2[("g1", bi)] = sDV.n
        if bi >= 1:
            b.wait("dve", sAC, M2[("sqB", bi - 1)])
            b.wait("dve", sPO, M2[("ob", bi - 1)])
        for g in range(4):
            tg = 4 * bi + g
            b.op("dve", lambda E, g=g, tg=tg: E.tensor_scalar_mul(wB[:, g * 1024:(g + 1) * 1024], yb[:, 4096 + g * 1024:4096 + (g + 1) * 1024], sb_gk[:, 32 + tg:33 + tg]), sDV)
        M2[("gmul", bi)] = sDV.n
        b.op("dve", lambda E: E.tensor_add(wA[:, :], wA[:, :], wB[:, :]), sDV)
        M2[("res1", bi)] = sDV.n
        b.wait("act", sDV, M2[("res1", bi)])
        if bi >= 1:
            b.wait("act", sPO, M2[("fin", bi - 1)])
        ln_stats(wA, ("sqA", bi))
        ln_apply(wA, wB)
        b.op("dve", lambda E: E.tensor_mul(wB[:, :], wB[:, :], gam4[:, :]), sDV)
        v_gm = sDV.n
        b.wait("pool", sDV, v_gm)
        b.op("pool", lambda E: E.tensor_add(wB[:, :], wB[:, :], bet4[:, :]), sPO)
        M2[("ob", bi)] = sPO.n
        b.wait("dve", sPO, M2[("ob", bi)])
        b.wait("act", sPO, M2[("ob", bi)])
        ln_stats(wB, ("sqB", bi))
        b.wait("dve", sLD, M2[("x1ld", bi)])
        ln_apply(wB, w_scr)
        v_a3 = sDV.n
        b.wait("pool", sDV, v_a3)
        if bi >= 2:
            b.wait("pool", sOS[par], M2[("ost", bi - 2)])
        b.op("pool", lambda E, outv=outv, x1v=x1v: E.tensor_add(outv, w_scr[:, :], x1v), sPO)
        M2[("fin", bi)] = sPO.n
        b.wait("sync", sPO, M2[("fin", bi)])
        b.op("sync", lambda E, bi=bi, outv=outv: E.dma_start(
            out=out.rearrange("(G p) d -> p G d", p=128)[:, 4 * bi:4 * bi + 4, :],
            in_=GV(outv)), sOS[par], 16)
        M2[("ost", bi)] = sOS[par].n

    b.wait("sync", sOS[0], sOS[0].n)
    b.wait("sync", sOS[1], sOS[1].n)
    b.wait("pool", sOS[0], sOS[0].n)
    b.wait("pool", sOS[1], sOS[1].n)

    return nc, b, ap, dbg, out


def finalize(nc, b):
    with nc.Block() as blk:
        @blk.tensor
        def _(E):
            for f in b.streams["pe"]:
                f(E)
        @blk.vector
        def _(E):
            for f in b.streams["dve"]:
                f(E)
        @blk.scalar
        def _(E):
            for f in b.streams["act"]:
                f(E)
        @blk.gpsimd
        def _(E):
            for f in b.streams["pool"]:
                f(E)
        @blk.sync
        def _(E):
            for f in b.streams["sync"]:
                f(E)
    return nc


def host_inputs(core, full):
    x = full["L0_attn_x3d"].reshape(NCORE * NLOC, D)
    m = {}
    m["xT"] = np.ascontiguousarray(x[core * NLOC:(core + 1) * NLOC].T)
    m["BreT"] = np.ascontiguousarray(full["L0_attn_B_re"].T)
    m["BimT"] = np.ascontiguousarray(full["L0_attn_B_im"].T)
    m["CreT"] = np.ascontiguousarray(full["L0_attn_C_re"].T)
    m["CimT"] = np.ascontiguousarray(full["L0_attn_C_im"].T)
    m["DT"] = np.ascontiguousarray(full["L0_attn_D_mat"].T)
    m["W1"] = np.ascontiguousarray(full["L0_attn_W1"])
    m["W2"] = np.ascontiguousarray(full["L0_attn_W2"])
    m["wg"] = np.ascontiguousarray(full["L0_ffn_wg"])
    m["fc1T"] = np.ascontiguousarray(full["L0_ffn_fc1_w"][core].T)
    m["fc2"] = np.ascontiguousarray(full["L0_ffn_fc2_w"][core])
    m["fc1b"] = np.ascontiguousarray(full["L0_ffn_fc1_b"][core].reshape(32, 128).T)
    m["fc2b"] = np.ascontiguousarray(full["L0_ffn_fc2_b"][core].reshape(1, D))
    m["lng"] = np.ascontiguousarray(full["L0_ffn_ln_gamma"].reshape(1, D))
    m["lnb"] = np.ascontiguousarray(full["L0_ffn_ln_beta"].reshape(1, D))
    m["nu"] = np.ascontiguousarray(full["L0_attn_nu_log"].reshape(2, 128).T)
    m["th"] = np.ascontiguousarray(full["L0_attn_theta_log"].reshape(2, 128).T)
    m["rl"] = np.asarray(full["L0_attn_res_logit"], np.float32).reshape(1, 1)
    cm = np.zeros((16, 2), np.float32)
    for cp in range(8):
        for jp in range(2):
            rr = 2 * cp + jp
            if jp == 0 and cp < core:
                cm[rr, 0] = 1.0
            if (jp == 1 and cp < core) or jp == 0:
                cm[rr, 1] = 1.0
    m["cmask"] = np.ascontiguousarray(cm)
    m["tok0"] = np.asarray([[core * 1024]], np.float32)
    m["ident_f"] = np.eye(128, dtype=np.float32)
    m["ident_b"] = np.eye(128, dtype=BF)
    m["tri_b"] = np.triu(np.ones((128, 128), dtype=np.float32), 1).astype(BF)
    return m




_BUILD_CACHE = {}


def kernel(**inputs):
    full = {k: np.asarray(v) for k, v in inputs.items()}
    if "prog" not in _BUILD_CACHE:
        nc, b, ap, dbg, out_ap = build_program(debug_level=0)
        nc = finalize(nc, b)
        _BUILD_CACHE["prog"] = nc
    nc = _BUILD_CACHE["prog"]
    in_maps = [host_inputs(c, full) for c in range(NCORE)]
    res = run_bass_kernel_spmd(nc, in_maps, core_ids=list(range(NCORE)))
    out = np.concatenate([res.results[c]["out"] for c in range(NCORE)], 0)
    return out.reshape(16, 2048, D)


# version 16
# speedup vs baseline: 1.0161x; 1.0004x over previous
import sys, time
import numpy as np
import ml_dtypes
import concourse.bass as bass
import concourse.mybir as mybir
from concourse.bass import IndirectOffsetOnAxis
from concourse.bass_utils import run_bass_kernel_spmd

F32 = mybir.dt.float32
BF16 = mybir.dt.bfloat16
I32 = mybir.dt.int32
AF = mybir.ActivationFunctionType
ALU = mybir.AluOpType
AX = mybir.AxisListType
BF = ml_dtypes.bfloat16

NCORE = 8
NLOC = 4096
D = 1024
DS = 256
H = 4096
E = 8
CAP = 10240
T = 512
NCH = NLOC // T
SUB = 128
PMROWS = 82048
C1x2 = 1.5957691216057308
SQ2C2 = 0.2671232181501769
EPS = 1e-5
INV_D = 1.0 / D
TWO_PI = 6.283185307179586
INV_2PI = 1.0 / TWO_PI
PI = 3.141592653589793

ENGS = ("pe", "dve", "act", "pool", "sync")


class Sem:
    __slots__ = ("h", "n")
    def __init__(self, h):
        self.h = h
        self.n = 0


class Builder:
    def __init__(self, nc):
        self.nc = nc
        self.streams = {e: [] for e in ENGS}
        self._guards = []
        self.sems = {}

    def sem(self, name):
        if name not in self.sems:
            g = self.nc.semaphore(name)
            self.sems[name] = Sem(g.__enter__())
            self._guards.append(g)
        return self.sems[name]

    def op(self, eng, fn, inc=None, incv=1):
        if inc is not None:
            inc.n += incv
            self.streams[eng].append(lambda E, fn=fn, s=inc, d=incv: fn(E).then_inc(s.h, d))
        else:
            self.streams[eng].append(lambda E, fn=fn: fn(E))

    def wait(self, eng, s, v=None):
        v = s.n if v is None else v
        self.streams[eng].append(lambda E, s=s, v=v: E.wait_ge(s.h, v))

    def barrier(self, name, extra_waits=()):
        s = self.sem("bar_" + name)
        for eng, sem, val in extra_waits:
            self.wait(eng, sem, val)
        tgt = s.n + len(ENGS)
        for e in ENGS:
            self.op(e, lambda E: E.sem_inc(s.h, 1), None)
        s.n = tgt
        for e in ENGS:
            self.wait(e, s, tgt)


class Scope:
    def __init__(self, nc):
        self.nc = nc
        self.gs = []

    def sb(self, name, shape, dt):
        g = self.nc.sbuf_tensor(name, shape, dt)
        h = g.__enter__()
        self.gs.append(g)
        return h

    def ps(self, name, shape, dt):
        g = self.nc.psum_tensor(name, shape, dt)
        h = g.__enter__()
        self.gs.append(g)
        return h

    def close(self):
        for g in reversed(self.gs):
            g.__exit__(None, None, None)
        self.gs = []


def build_program(debug_level=0):
    nc = bass.Bass("TRN2", target_bir_lowering=False)
    b = Builder(nc)
    ap = {}

    def din(name, shape, dt=F32):
        ap[name] = nc.dram_tensor(name, list(shape), dt, kind="ExternalInput").ap()
        return ap[name]

    xT = din("xT", [D, NLOC])
    BreT = din("BreT", [D, DS]); BimT = din("BimT", [D, DS])
    CreT = din("CreT", [DS, D]); CimT = din("CimT", [DS, D])
    DT_in = din("DT", [D, D])
    W1_in = din("W1", [D, H]); W2_in = din("W2", [H, D])
    wg_in = din("wg", [D, E])
    fc1T_in = din("fc1T", [D, H]); fc2_in = din("fc2", [H, D])
    fc1b_in = din("fc1b", [128, 32])
    fc2b_in = din("fc2b", [1, D])
    lng_in = din("lng", [1, D]); lnb_in = din("lnb", [1, D])
    nu_in = din("nu", [128, 2]); th_in = din("th", [128, 2])
    rl_in = din("rl", [1, 1])
    cmask_in = din("cmask", [16, 2])
    tok0_in = din("tok0", [1, 1])

    out = nc.dram_tensor("out", [NLOC, D], F32, kind="ExternalOutput").ap()

    d_h = nc.dram_tensor("d_h", [H, NLOC], BF16, kind="Internal")
    d_w2bf = nc.dram_tensor("d_w2bf", [4096, 1024], BF16, kind="Internal")
    d_x1t = nc.dram_tensor("d_x1t", [NLOC, D], F32, kind="Internal")
    d_xn2bf = nc.dram_tensor("d_xn2bf", [NLOC, D], BF16, kind="Internal")
    d_xn2all = nc.dram_tensor("d_xn2all", [NCORE * NLOC, D], BF16, kind="Internal")
    d_cnt = nc.dram_tensor("d_cnt", [2, 8], F32, kind="Internal")
    d_cntall = nc.dram_tensor("d_cntall", [16, 8], F32, kind="Internal")
    d_pm = nc.dram_tensor("d_pm", [PMROWS, 1], F32, kind="Internal")
    d_pmy = nc.dram_tensor("d_pmy", [CAP, 1], F32, kind="Internal")
    d_disp = nc.dram_tensor("d_disp", [CAP, D], BF16, kind="Internal")
    d_y = nc.dram_tensor("d_y", [CAP, D], BF16, kind="Internal")
    d_yall = nc.dram_tensor("d_yall", [E * CAP, D], BF16, kind="Internal")
    d_fb = nc.dram_tensor("d_fb", [8, 64], F32, kind="Internal")

    dbg = {}
    if debug_level >= 1:
        dbg["o_xn"] = nc.dram_tensor("o_xn", [D, NLOC], BF16, kind="ExternalOutput").ap()
        dbg["o_h"] = nc.dram_tensor("o_h", [H, NLOC], BF16, kind="ExternalOutput").ap()
        dbg["o_bu"] = nc.dram_tensor("o_bu", [128, 2048], BF16, kind="ExternalOutput").ap()
        dbg["o_st"] = nc.dram_tensor("o_st", [128, 2048], BF16, kind="ExternalOutput").ap()
        dbg["o_y"] = nc.dram_tensor("o_y", [128, 4096], BF16, kind="ExternalOutput").ap()
        dbg["o_tab"] = nc.dram_tensor("o_tab", [128, 1032], F32, kind="ExternalOutput").ap()
    if debug_level >= 2:
        dbg["o_x1t"] = nc.dram_tensor("o_x1t", [NLOC, D], F32, kind="ExternalOutput").ap()
        dbg["o_xn2bf"] = nc.dram_tensor("o_xn2bf", [NLOC, D], BF16, kind="ExternalOutput").ap()
        dbg["o_rt"] = nc.dram_tensor("o_rt", [128, 320], F32, kind="ExternalOutput").ap()
        dbg["o_ao"] = nc.dram_tensor("o_ao", [128, 4096], F32, kind="ExternalOutput").ap()
    if debug_level >= 3:
        dbg["o_y3"] = nc.dram_tensor("o_y3", [CAP, D], BF16, kind="ExternalOutput").ap()
        dbg["o_disp"] = nc.dram_tensor("o_disp", [CAP, D], BF16, kind="ExternalOutput").ap()
        dbg["o_pmy"] = nc.dram_tensor("o_pmy", [CAP, 1], F32, kind="ExternalOutput").ap()
        dbg["o_posd"] = nc.dram_tensor("o_posd", [128, 64], F32, kind="ExternalOutput").ap()
        dbg["o_cnt"] = nc.dram_tensor("o_cnt", [16, 8], F32, kind="ExternalOutput").ap()
        dbg["o_cb2"] = nc.dram_tensor("o_cb2", [128, 2], F32, kind="ExternalOutput").ap()
        dbg["o_gkd"] = nc.dram_tensor("o_gkd", [128, 64], F32, kind="ExternalOutput").ap()
        dbg["o_xn2"] = nc.dram_tensor("o_xn2", [128, 4096], F32, kind="ExternalOutput").ap()
        dbg["o_z0"] = nc.dram_tensor("o_z0", [128, 512], F32, kind="ExternalOutput").ap()
        dbg["o_hrb"] = nc.dram_tensor("o_hrb", [128, 16384], BF16, kind="ExternalOutput").ap()
        dbg["o_w2"] = nc.dram_tensor("o_w2", [128, 2048], BF16, kind="ExternalOutput").ap()

    ident_f = din("ident_f", [128, 128], F32)
    ident_b = din("ident_b", [128, 128], BF16)
    tri_b = din("tri_b", [128, 128], BF16)

    A = nc.alloc_sbuf_tensor
    sb_identf = A("sb_identf", [128, 128], F32)
    sb_identb = A("sb_identb", [128, 128], BF16)
    sb_trib = A("sb_trib", [128, 128], BF16)
    sb_onescb = A("sb_onescb", [128, 1], BF16)
    sb_onesrf = A("sb_onesrf", [1, 128], F32)
    sb_ones128 = A("sb_ones128", [128, 128], F32)
    sb_eps = A("sb_eps", [1, 1], F32)
    sb_iota8 = A("sb_iota8", [128, 8], F32)
    sb_i32 = A("sb_i32", [128, 256], I32)
    sb_wg = A("sb_wg", [128, 64], BF16)
    sb_fc1b = A("sb_fc1b", [128, 32], F32)
    sb_sig = A("sb_sig", [128, 1], F32)
    sb_rl = A("sb_rl", [1, 1], F32)
    sb_tok0 = A("sb_tok0", [128, 1], F32)
    sb_cmask = A("sb_cmask", [16, 2], F32)
    sb_rows = A("sb_rows", [1, 1536], F32)
    sb_oh1 = A("sb_oh1", [128, 256], F32)
    sb_oh2 = A("sb_oh2", [128, 256], F32)
    sb_i1 = A("sb_i1", [128, 32], F32)
    sb_i2 = A("sb_i2", [128, 32], F32)
    sb_g1 = A("sb_g1", [128, 32], F32)
    sb_g2 = A("sb_g2", [128, 32], F32)
    sb_pre1 = A("sb_pre1", [128, 32], F32)
    sb_pre2 = A("sb_pre2", [128, 32], F32)
    sb_ct = A("sb_ct", [128, 64], F32)
    sb_tb = A("sb_tb", [128, 64], F32)
    sb_pos = A("sb_pos", [128, 64], F32)
    sb_gk = A("sb_gk", [128, 64], F32)
    sb_poi = A("sb_poi", [128, 64], I32)
    sb_tokid = A("sb_tokid", [128, 32], F32)
    sb_brow = A("sb_brow", [1, 512], F32)
    sb_iotaE = A("sb_iotaE", [128, 256], F32)
    sb_m1 = A("sb_m1", [128, 32], F32)
    sb_m2 = A("sb_m2", [128, 32], F32)
    rtA2 = A("rtA2", [128, 256], F32)
    sb_zero = A("sb_zero", [128, 129], F32)
    scW = Scope(nc)
    WBIG = scW.sb("WBIG", [128, 32768], BF16)
    WBIG2 = scW.sb("WBIG2", [128, 32768], BF16)
    XB, XSQ, STB, YB, HST = 16384, 17408, 18432, 20480, 24576

    sc1 = Scope(nc)
    PS = [sc1.ps(f"ps{i}", [128, 512], F32) for i in range(8)]
    tabPr = sc1.sb("tabPr", [128, 256], F32); tabPi = sc1.sb("tabPi", [128, 256], F32)
    tabQr = sc1.sb("tabQr", [128, 256], F32); tabQi = sc1.sb("tabQi", [128, 256], F32)
    sb_lam = sc1.sb("sb_lam", [128, 4], F32)
    sb_nt = sc1.sb("sb_nt", [128, 8], F32)
    sb_carry = sc1.sb("sb_carry", [128, 4], F32)
    sb_cs = sc1.sb("sb_cs", [128, 4], F32)
    sb_cst = sc1.sb("sb_cst", [128, 4], F32)
    scA_r = sc1.sb("scA_r", [128, 256], F32); scA_i = sc1.sb("scA_i", [128, 256], F32)
    scS_r = sc1.sb("scS_r", [128, 256], F32); scS_i = sc1.sb("scS_i", [128, 256], F32)
    scT_r = sc1.sb("scT_r", [128, 256], F32); scT_i = sc1.sb("scT_i", [128, 256], F32)
    sctmp = sc1.sb("sctmp", [128, 256], F32)
    SC0, SC1, SC2, SC3, SC4, SC5 = scA_r, scA_i, scS_r, scS_i, scT_r, scT_i
    sb_x = sc1.sb("sb_x", [128, 8192], BF16)
    sb_xnb1 = sc1.sb("sb_xnb1", [128, 4096], BF16)
    sb_st1 = sc1.sb("sb_st1", [128, 2048], BF16)
    sb_nmb = sc1.sb("sb_nmb", [128, 512], BF16)
    sb_rb = sc1.sb("sb_rb", [128, 512], BF16)
    sb_xnb = sc1.sb("sb_xnb", [128, 4096], BF16)
    sb_bu = sc1.sb("sb_bu", [128, 2048], BF16)
    sb_q = sc1.sb("sb_q", [128, 1024], BF16)
    sb_t2 = sc1.sb("sb_t2", [128, 1024], BF16)
    sb_sg2 = sb_q


    sLD = b.sem("sLD"); sG = b.sem("sG")
    sH0 = b.sem("sH0"); sW2A = b.sem("sW2A"); sW2B = b.sem("sW2B"); sW2C = b.sem("sW2C")
    sH0S = [b.sem("sH0S0"), b.sem("sH0S1")]
    sPE = b.sem("sPE"); sDV = b.sem("sDV"); sAC = b.sem("sAC"); sPO = b.sem("sPO")
    sCC = b.sem("sCC"); sST = b.sem("sST"); sXL = b.sem("sXL")
    sST1 = b.sem("sST1"); sST2 = b.sem("sST2"); sHS = [b.sem("sHS0"), b.sem("sHS1")]
    sOS = [b.sem("sOS0"), b.sem("sOS1")]
    sYS = [b.sem("sYS0"), b.sem("sYS1")]

    def dma(eng, outap, inap, sem=None):
        sem = sem if sem is not None else sLD
        b.op(eng, lambda E, o=outap, i=inap: E.dma_start(out=o, in_=i), sem, 16)

    dma("sync", sb_identf[:, :], ident_f[:, :])
    dma("sync", sb_identb[:, :], ident_b[:, :])
    dma("sync", sb_trib[:, :], tri_b[:, :])
    dma("sync", sb_nt[:, 0:2], nu_in[:, :])
    dma("sync", sb_nt[:, 2:4], th_in[:, :])
    dma("sync", sb_fc1b[:, :], fc1b_in[:, :])
    dma("sync", sb_rl[:, :], rl_in[:, :])
    dma("sync", sb_cmask[:, :], cmask_in[:, :])
    dma("sync", sb_tok0[0:1, :], tok0_in[:, :])
    v_consts = sLD.n

    for kc in range(8):
        dma("pool", WBIG2[:, kc * 256:(kc + 1) * 256], BreT[128 * kc:128 * (kc + 1), :], sG)
        dma("pool", WBIG2[:, 2048 + kc * 256:2048 + (kc + 1) * 256], BimT[128 * kc:128 * (kc + 1), :], sG)
    for cc in range(2):
        dma("pool", WBIG2[:, 4096 + cc * 1024:4096 + (cc + 1) * 1024], CreT[128 * cc:128 * (cc + 1), :], sG)
        dma("pool", WBIG2[:, 6144 + cc * 1024:6144 + (cc + 1) * 1024], CimT[128 * cc:128 * (cc + 1), :], sG)
    for kc in range(8):
        dma("pool", WBIG2[:, 8192 + kc * 1024:8192 + (kc + 1) * 1024], DT_in[128 * kc:128 * (kc + 1), :], sG)
    for kc in range(8):
        dma("pool", sb_wg[:, kc * 8:(kc + 1) * 8], wg_in[128 * kc:128 * (kc + 1), :], sG)
    v_bcd = sG.n
    for kc in range(8):
        dma("pool", WBIG[:, kc * 4096:(kc + 1) * 4096], W1_in[128 * kc:128 * (kc + 1), :], sG)
    v_w1 = sG.n

    b.op("dve", lambda E: E.memset(sb_onescb[:, :], 1.0), sDV)
    b.op("dve", lambda E: E.memset(sb_onesrf[:, :], 1.0), sDV)
    b.op("dve", lambda E: E.memset(sb_ones128[:, :], 1.0), sDV)
    b.op("dve", lambda E: E.memset(sb_eps[:, :], EPS), sDV)
    b.op("dve", lambda E: E.memset(sb_zero[:, :], 0.0), sDV)
    b.op("dve", lambda E: E.memset(sb_carry[:, :], 0.0), sDV)
    v_memset = sDV.n

    b.wait("sync", sDV, v_memset)
    _pmv = d_pm.ap().rearrange("(r p) one -> r (p one)", p=128)
    for _i in range(5):
        dma("sync", _pmv[_i * 128:(_i + 1) * 128, :], sb_zero[:, 0:128])
    dma("sync", _pmv[640:641, :], sb_zero[0:1, 0:128])

    b.wait("pool", sG, v_w1)
    b.op("pool", lambda E: E.iota(sb_i32[:, 0:8], pattern=[[1, 8]], base=0, channel_multiplier=0), sPO)
    b.op("pool", lambda E: E.iota(sb_i32[:, 8:40], pattern=[[8192, 4], [128, 8]], base=1, channel_multiplier=1), sPO)
    b.op("pool", lambda E: E.iota(sb_i32[:, 128:256], pattern=[[1, 128]], base=0, channel_multiplier=0), sPO)
    v_iota = sPO.n
    b.wait("dve", sPO, v_iota)
    b.op("dve", lambda E: E.tensor_copy(sb_iota8[:, :], sb_i32[:, 0:8]), sDV)
    b.op("dve", lambda E: E.tensor_copy(sb_tokid[:, :], sb_i32[:, 8:40]), sDV)
    v_iota_f = sDV.n
    b.op("pool", lambda E: E.iota(sb_i32[:, 40:72], pattern=[[0, 4], [1, 8]], base=0, channel_multiplier=0), sPO)
    b.wait("dve", sPO, sPO.n)
    for _q in range(8):
        b.op("dve", lambda E, _q=_q: E.tensor_copy(sb_iotaE[:, _q * 32:(_q + 1) * 32], sb_i32[:, 40:72]), sDV)
    v_iotaE = sDV.n

    b.wait("dve", sG, v_w1)
    b.op("dve", lambda E: E.tensor_scalar_mul(WBIG2[:, 6144:8192], WBIG2[:, 6144:8192], -1.0), sDV)
    v_cim = sDV.n

    b.wait("act", sLD, v_consts)
    b.op("act", lambda E: E.activation(sb_nt[:, 4:6], sb_nt[:, 0:2], AF.Exp), sAC)
    b.op("act", lambda E: E.activation(sb_nt[:, 6:8], sb_nt[:, 2:4], AF.Exp), sAC)
    v1 = sAC.n
    b.op("act", lambda E: E.activation(sb_lam[:, 0:2], sb_nt[:, 4:6], AF.Exp, scale=-2.0), sAC)
    v2 = sAC.n
    b.wait("dve", sAC, v2)
    b.op("dve", lambda E: E.tensor_scalar(sb_lam[:, 0:2], sb_lam[:, 0:2], -1.0, 1.0, ALU.mult, ALU.add), sDV)
    v3 = sDV.n
    b.wait("act", sDV, v3)
    b.op("act", lambda E: E.activation(sb_lam[:, 2:4], sb_lam[:, 0:2], AF.Ln), sAC)
    v4 = sAC.n
    b.wait("act", sAC, v4)
    b.op("act", lambda E: E.activation(sb_nt[:, 0:2], sb_lam[:, 2:4], AF.Exp, scale=0.5), sAC)
    b.op("act", lambda E: E.activation(sb_lam[:, 0:2], sb_nt[:, 4:6], AF.Exp, scale=-1.0), sAC)
    v_gam = sAC.n

    b.wait("dve", sPO, v_iota)
    b.op("dve", lambda E: E.tensor_copy(SC0[:, 0:128], sb_i32[:, 128:256]), sDV)
    b.op("dve", lambda E: E.tensor_copy(SC0[:, 128:256], sb_i32[:, 128:256]), sDV)
    v_tau = sDV.n
    b.wait("dve", sAC, v_gam)
    for cc in range(2):
        b.op("dve", lambda E, cc=cc: E.tensor_scalar_mul(SC1[:, cc * 128:(cc + 1) * 128], SC0[:, cc * 128:(cc + 1) * 128], sb_nt[:, 6 + cc:7 + cc]), sDV)
    b.wait("dve", sDV, sDV.n)
    for cc in range(2):
        b.op("dve", lambda E, cc=cc: E.tensor_scalar_mul(SC0[:, cc * 128:(cc + 1) * 128], SC0[:, cc * 128:(cc + 1) * 128], sb_nt[:, 4 + cc:5 + cc]), sDV)
    v_T = sDV.n
    b.wait("dve", sDV, v_T)
    b.op("dve", lambda E: E.tensor_scalar_mul(SC4[:, :], SC1[:, :], INV_2PI), sDV)
    b.wait("dve", sDV, sDV.n)
    b.op("dve", lambda E: E.tensor_copy(sb_i32[:, 0:256], SC4[:, :]), sDV)
    b.wait("dve", sDV, sDV.n)
    b.op("dve", lambda E: E.tensor_copy(SC4[:, :], sb_i32[:, 0:256]), sDV)
    b.wait("dve", sDV, sDV.n)
    b.op("dve", lambda E: E.scalar_tensor_tensor(SC1[:, :], SC4[:, :], -TWO_PI, SC1[:, :], ALU.mult, ALU.add), sDV)
    b.wait("dve", sDV, sDV.n)

    def wrap_once(dst, src):
        b.op("dve", lambda E: E.tensor_scalar(SC4[:, :], src, PI, None, ALU.is_gt), sDV)
        b.wait("dve", sDV, sDV.n)
        b.op("dve", lambda E: E.scalar_tensor_tensor(dst, SC4[:, :], -TWO_PI, src, ALU.mult, ALU.add), sDV)
        b.wait("dve", sDV, sDV.n)
        b.op("dve", lambda E: E.tensor_scalar(SC4[:, :], dst, -PI, None, ALU.is_lt), sDV)
        b.wait("dve", sDV, sDV.n)
        b.op("dve", lambda E: E.scalar_tensor_tensor(dst, SC4[:, :], TWO_PI, dst, ALU.mult, ALU.add), sDV)
        b.wait("dve", sDV, sDV.n)

    wrap_once(SC1[:, :], SC1[:, :])
    b.wait("act", sDV, sDV.n)
    b.op("act", lambda E: E.activation(SC2[:, :], SC1[:, :], AF.Sin), sAC)
    v_sin = sAC.n
    b.wait("dve", sAC, v_sin)
    b.op("dve", lambda E: E.tensor_scalar(SC1[:, :], SC1[:, :], PI / 2.0, None, ALU.add), sDV)
    b.wait("dve", sDV, sDV.n)
    wrap_once(SC1[:, :], SC1[:, :])
    b.wait("act", sDV, sDV.n)
    b.op("act", lambda E: E.activation(SC3[:, :], SC1[:, :], AF.Sin), sAC)
    b.op("act", lambda E: E.activation(SC4[:, :], SC0[:, :], AF.Exp), sAC)
    b.op("act", lambda E: E.activation(SC5[:, :], SC0[:, :], AF.Exp, scale=-1.0), sAC)
    v_trig = sAC.n
    b.wait("dve", sAC, v_trig)
    for cc in range(2):
        b.op("dve", lambda E, cc=cc: E.tensor_scalar_mul(SC4[:, cc * 128:(cc + 1) * 128], SC4[:, cc * 128:(cc + 1) * 128], sb_nt[:, cc:cc + 1]), sDV)
    b.wait("dve", sDV, sDV.n)
    b.op("dve", lambda E: E.tensor_mul(tabPr[:, :], SC4[:, :], SC3[:, :]), sDV)
    b.op("dve", lambda E: E.scalar_tensor_tensor(tabPi[:, :], SC2[:, :], -1.0, SC4[:, :], ALU.mult, ALU.mult), sDV)
    b.op("dve", lambda E: E.tensor_mul(tabQr[:, :], SC5[:, :], SC3[:, :]), sDV)
    b.op("dve", lambda E: E.tensor_mul(tabQi[:, :], SC5[:, :], SC2[:, :]), sDV)
    b.wait("dve", sDV, sDV.n)
    for cc in range(2):
        b.op("dve", lambda E, cc=cc: E.tensor_mul(sb_lam[:, 2 + cc:3 + cc], sb_lam[:, cc:cc + 1],
                                                  SC2[:, cc * 128 + 1:cc * 128 + 2]), sDV)
    b.wait("dve", sDV, sDV.n)
    for cc in range(2):
        b.op("dve", lambda E, cc=cc: E.tensor_mul(sb_lam[:, cc:cc + 1], sb_lam[:, cc:cc + 1],
                                                  SC3[:, cc * 128 + 1:cc * 128 + 2]), sDV)
    v_tables = sDV.n
    if debug_level >= 1:
        b.wait("sync", sDV, v_tables)
        dma("sync", dbg["o_tab"][:, 0:256], tabPr[:, :])
        dma("sync", dbg["o_tab"][:, 256:512], tabPi[:, :])
        dma("sync", dbg["o_tab"][:, 512:768], tabQr[:, :])
        dma("sync", dbg["o_tab"][:, 768:1024], tabQi[:, :])
        dma("sync", dbg["o_tab"][:, 1024:1028], sb_lam[:, :])
        dma("sync", dbg["o_tab"][:, 1028:1032], sb_nt[:, 0:4])

    b.wait("pe", sLD, v_consts)
    b.wait("pe", sDV, v_memset)
    b.op("pe", lambda E: E.matmul(PS[0][:, 0:1], sb_onesrf[:, :], sb_rl[:, :], start=True, stop=True), sPE)
    b.op("pe", lambda E: E.matmul(PS[0][:, 4:5], sb_onesrf[:, :], sb_tok0[0:1, :], start=True, stop=True), sPE)
    v_bc0 = sPE.n
    b.wait("act", sPE, v_bc0)
    b.op("act", lambda E: E.activation(sb_sig[:, :], PS[0][:, 0:1], AF.Sigmoid), sAC)
    v_sig = sAC.n
    b.wait("dve", sPE, v_bc0)
    b.op("dve", lambda E: E.tensor_copy(sb_tok0[:, :], PS[0][:, 4:5]), sDV)
    b.wait("dve", sDV, sDV.n)
    b.wait("dve", sDV, v_iota_f)
    b.op("dve", lambda E: E.tensor_scalar_add(sb_tokid[:, :], sb_tokid[:, :], sb_tok0[:, :]), sDV)
    v_p0 = sDV.n


    psU, psB0, psB1, psB2, psB3, psYa, psYb, psUa = PS
    psUb = psB0

    b.wait("pe", sG, v_bcd)
    b.wait("pe", sDV, v_p0)
    b.wait("pe", sAC, v_sig)
    b.wait("dve", sDV, v_tables)
    b.wait("dve", sDV, v_cim)

    M = {}

    def p1_chunk(k):
        t0 = k * T
        xpar = k % 2
        xsl = sb_x[:, xpar * 4096:(xpar + 1) * 4096]
        xnbuf = sb_xnb if xpar == 0 else sb_xnb1

        b.wait("act", sXL, M[("xld", k)])
        b.wait("pe", sXL, M[("xld", k)])
        if k > 0:
            b.wait("pe", sAC, M[("rbev", k - 1)])
        for kc in range(8):
            par = kc % 2
            xk = xsl[:, kc * T:(kc + 1) * T]
            xsv = WBIG2[:, XSQ + par * 512:XSQ + (par + 1) * 512]
            if kc >= 2:
                b.wait("act", sPE, M[("stmm", k, kc - 2)])
            elif k > 0:
                b.wait("act", sPE, M[("stats", k - 1)])
            b.op("act", lambda E, xk=xk, xsv=xsv: E.activation(xsv, xk, AF.Square), sAC)
            vs = sAC.n
            b.wait("pe", sAC, vs)
            b.op("pe", lambda E, kc=kc, xk=xk: E.matmul(psU[0:1, :], sb_onescb[:, :], xk, start=(kc == 0), stop=(kc == 7)), sPE)
            b.op("pe", lambda E, kc=kc, xsv=xsv: E.matmul(psU[32:33, :], sb_onescb[:, :], xsv, start=(kc == 0), stop=(kc == 7)), sPE)
            M[("stmm", k, kc)] = sPE.n
        M[("stats", k)] = sPE.n

        b.wait("act", sPE, M[("stats", k)])
        b.op("act", lambda E: E.activation(sb_rows[:, 0:512], psU[0:1, :], AF.Copy, scale=INV_D), sAC)
        vm = sAC.n
        b.wait("act", sAC, vm)
        b.op("act", lambda E: E.activation(sb_rows[:, 512:1024], sb_rows[:, 0:512], AF.Square), sAC)
        vq = sAC.n
        b.wait("dve", sAC, vq)
        b.op("dve", lambda E: E.scalar_tensor_tensor(sb_rows[:, 512:1024], psU[32:33, :], INV_D, sb_rows[:, 512:1024], ALU.mult, ALU.subtract), sDV)
        vv = sDV.n
        b.wait("act", sDV, vv)
        b.op("act", lambda E: E.activation(sb_rows[:, 1024:1536], sb_rows[:, 512:1024], AF.Ln, bias=sb_eps[:, :]), sAC)
        b.wait("act", sAC, sAC.n)
        b.op("act", lambda E: E.activation(sb_rows[:, 1024:1536], sb_rows[:, 1024:1536], AF.Exp, scale=-0.5), sAC)
        vr = sAC.n
        b.wait("dve", sAC, vr)
        b.op("dve", lambda E: E.tensor_scalar_mul(sb_rows[:, 512:1024], sb_rows[:, 0:512], -1.0), sDV)
        vnm = sDV.n

        b.wait("pe", sDV, vnm)
        b.wait("pe", sDV, vv)
        b.op("pe", lambda E: E.matmul(psU[:, :], sb_onesrf[:, :], sb_rows[:, 512:1024], start=True, stop=True), sPE)
        vb1 = sPE.n
        b.wait("act", sPE, vb1)
        if k > 0:
            b.wait("act", sDV, M[("pass1", k - 1)])
        b.op("act", lambda E: E.copy(sb_nmb[:, :], psU[:, :]), sAC)
        vnmb = sAC.n
        b.wait("dve", sAC, vnmb)
        for kc in range(8):
            b.op("dve", lambda E, kc=kc, xsl=xsl: E.tensor_add(xsl[:, kc * T:(kc + 1) * T], xsl[:, kc * T:(kc + 1) * T], sb_nmb[:, :]), sDV)
        M[("pass1", k)] = sDV.n
        b.wait("pe", sAC, vnmb)
        b.wait("pe", sAC, vr)
        b.op("pe", lambda E: E.matmul(psU[:, :], sb_onesrf[:, :], sb_rows[:, 1024:1536], start=True, stop=True), sPE)
        vb2 = sPE.n
        b.wait("act", sPE, vb2)
        if k > 0:
            b.wait("act", sDV, M[("xn", k - 1)])
        b.op("act", lambda E: E.copy(sb_rb[:, :], psU[:, :]), sAC)
        M[("rbev", k)] = sAC.n
        b.wait("dve", sAC, M[("rbev", k)])
        if k >= 2:
            b.wait("dve", sPE, M[("ymm", k - 2, 3)])
        for kc in range(8):
            b.op("dve", lambda E, kc=kc, xsl=xsl, xnbuf=xnbuf: E.tensor_mul(xnbuf[:, kc * T:(kc + 1) * T], xsl[:, kc * T:(kc + 1) * T], sb_rb[:, :]), sDV)
        M[("xn", k)] = sDV.n
        M[("xnb", k)] = M[("xn", k)]
        if debug_level >= 1:
            b.wait("sync", sDV, M[("xn", k)])
            dma("sync",
                dbg["o_xn"].rearrange("(kc p) t -> p kc t", p=128)[:, :, t0:t0 + T],
                xnbuf.rearrange("p (kc t) -> p kc t", t=T))
            M[("xnst", k)] = sLD.n

        b.wait("pe", sDV, M[("xnb", k)])
        if k >= 2:
            b.wait("pe", sAC, M[("uev", k - 2, 15)])
        elif k == 1:
            b.wait("pe", sAC, M[("buev", 0)])
        for kc in range(8):
            rsl = xnbuf[:, kc * T:(kc + 1) * T]
            for (dst, base, nh) in ((psB0, 0, 0), (psB1, 0, 1), (psB2, 2048, 0), (psB3, 2048, 1)):
                b.op("pe", lambda E, dst=dst, base=base, nh=nh, kc=kc, rsl=rsl:
                     E.matmul(dst[:, :], WBIG2[:, base + kc * 256 + nh * 128:base + kc * 256 + (nh + 1) * 128],
                              rsl, start=(kc == 0), stop=(kc == 7)), sPE)
        M[("bu", k)] = sPE.n
        b.wait("act", sPE, M[("bu", k)])
        if k > 0:
            b.wait("act", sDV, M[("scan", k - 1)])
        if debug_level >= 1 and k == 1:
            b.wait("act", sLD, M[("dmp_bu", 0)])
        for i, src in enumerate((psB0, psB1, psB2, psB3)):
            b.op("act", lambda E, i=i, src=src: E.copy(sb_bu[:, i * 512:(i + 1) * 512], src[:, :]), sAC)
        M[("buev", k)] = sAC.n

        if debug_level >= 1 and k == 0:
            b.wait("sync", sAC, M[("buev", k)])
            dma("sync", dbg["o_bu"][:, :], sb_bu[:, :])
            M[("dmp_bu", 0)] = sLD.n
        emit_scan(k)
        if debug_level >= 1 and k == 0:
            b.wait("sync", sDV, M[("scan", k)])
            dma("sync", dbg["o_st"][:, :], WBIG2[:, STB:STB + 2048])
            M[("dmp_st", 0)] = sLD.n

        if k > 0:
            emit_y(k - 1)
            emit_w1(k - 1, M[("buev", k)])

        if 2 <= k + 1 < NCH:
            b.wait("pool", sDV, M[("xn", k - 1)])
            nsl = sb_x[:, ((k + 1) % 2) * 4096:(((k + 1) % 2) + 1) * 4096]
            b.op("pool", lambda E, t0n=(k + 1) * T, nsl=nsl: E.dma_start(
                out=nsl.rearrange("p (kc t) -> p kc t", t=T),
                in_=xT.rearrange("(kc p) t -> p kc t", p=128)[:, :, t0n:t0n + T]), sXL, 16)
            M[("xld", k + 1)] = sXL.n

        if k == NCH - 1:
            emit_y(k)
            emit_w1(k, M[("buev", k)])
        if debug_level >= 1 and k == 1:
            b.wait("sync", sAC, M[("yev", 0, 3)])
            dma("sync", dbg["o_y"][:, :], WBIG2[:, YB:YB + 4096])
            M[("dmp_y", 0)] = sLD.n

    def emit_scan(k):
        stv = WBIG2[:, STB:STB + 2048] if k % 2 == 0 else sb_st1
        b.wait("dve", sAC, M[("buev", k)])
        if k >= 2:
            b.wait("dve", sPE, M[("ymm", k - 2, 3)])
        if debug_level >= 1 and k == 1:
            b.wait("dve", sLD, M[("dmp_st", 0)])
        if k % 4 == 0:
            b.op("dve", lambda E: E.memset(sb_carry[:, :], 0.0), sDV)
            b.wait("dve", sDV, sDV.n)
        for sub in range(4):
            b.op("dve", lambda E: E.tensor_mul(sb_cs[:, 0:2], sb_lam[:, 0:2], sb_carry[:, 0:2]), sDV)
            b.op("dve", lambda E: E.tensor_mul(sb_cst[:, 0:2], sb_lam[:, 2:4], sb_carry[:, 2:4]), sDV)
            b.op("dve", lambda E: E.tensor_mul(sb_cs[:, 2:4], sb_lam[:, 2:4], sb_carry[:, 0:2]), sDV)
            b.op("dve", lambda E: E.tensor_mul(sb_cst[:, 2:4], sb_lam[:, 0:2], sb_carry[:, 2:4]), sDV)
            b.wait("dve", sDV, sDV.n)
            b.op("dve", lambda E: E.tensor_sub(sb_cs[:, 0:2], sb_cs[:, 0:2], sb_cst[:, 0:2]), sDV)
            b.op("dve", lambda E: E.tensor_add(sb_cs[:, 2:4], sb_cs[:, 2:4], sb_cst[:, 2:4]), sDV)
            b.wait("dve", sDV, sDV.n)
            bur = sb_bu[:, 0:1024].rearrange("p (cc t) -> p cc t", cc=2)[:, :, sub * SUB:(sub + 1) * SUB]
            bui = sb_bu[:, 1024:2048].rearrange("p (cc t) -> p cc t", cc=2)[:, :, sub * SUB:(sub + 1) * SUB]
            a_r = scA_r.rearrange("p (cc t) -> p cc t", cc=2)
            a_i = scA_i.rearrange("p (cc t) -> p cc t", cc=2)
            tmpv = sctmp.rearrange("p (cc t) -> p cc t", cc=2)
            prv = tabPr.rearrange("p (cc t) -> p cc t", cc=2)
            piv = tabPi.rearrange("p (cc t) -> p cc t", cc=2)
            b.op("dve", lambda E, a_r=a_r, prv=prv, bur=bur: E.tensor_mul(a_r, prv, bur), sDV)
            b.op("dve", lambda E, tmpv=tmpv, piv=piv, bui=bui: E.tensor_mul(tmpv, piv, bui), sDV)
            b.op("dve", lambda E: E.tensor_sub(scA_r[:, :], scA_r[:, :], sctmp[:, :]), sDV)
            b.op("dve", lambda E, a_i=a_i, prv=prv, bui=bui: E.tensor_mul(a_i, prv, bui), sDV)
            b.op("dve", lambda E, tmpv=tmpv, piv=piv, bur=bur: E.tensor_mul(tmpv, piv, bur), sDV)
            b.op("dve", lambda E: E.tensor_add(scA_i[:, :], scA_i[:, :], sctmp[:, :]), sDV)
            for cc in range(2):
                b.op("dve", lambda E, cc=cc: E.tensor_tensor_scan(scS_r[:, cc * 128:(cc + 1) * 128], sb_ones128[:, :], scA_r[:, cc * 128:(cc + 1) * 128], sb_cs[:, cc:cc + 1], ALU.mult, ALU.add), sDV)
                b.op("dve", lambda E, cc=cc: E.tensor_tensor_scan(scS_i[:, cc * 128:(cc + 1) * 128], sb_ones128[:, :], scA_i[:, cc * 128:(cc + 1) * 128], sb_cs[:, 2 + cc:3 + cc], ALU.mult, ALU.add), sDV)
            b.op("dve", lambda E: E.tensor_mul(scT_r[:, :], tabQr[:, :], scS_r[:, :]), sDV)
            b.op("dve", lambda E: E.tensor_mul(sctmp[:, :], tabQi[:, :], scS_i[:, :]), sDV)
            b.op("dve", lambda E: E.tensor_sub(scT_r[:, :], scT_r[:, :], sctmp[:, :]), sDV)
            b.op("dve", lambda E: E.tensor_mul(scT_i[:, :], tabQr[:, :], scS_i[:, :]), sDV)
            b.op("dve", lambda E: E.tensor_mul(sctmp[:, :], tabQi[:, :], scS_r[:, :]), sDV)
            b.op("dve", lambda E: E.tensor_add(scT_i[:, :], scT_i[:, :], sctmp[:, :]), sDV)
            b.wait("dve", sDV, sDV.n)
            b.op("dve", lambda E: E.tensor_copy(sb_carry[:, 0:2], scT_r.rearrange("p (cc t) -> p cc t", cc=2)[:, :, 127:128]), sDV)
            b.op("dve", lambda E: E.tensor_copy(sb_carry[:, 2:4], scT_i.rearrange("p (cc t) -> p cc t", cc=2)[:, :, 127:128]), sDV)
            str_v = stv[:, 0:1024].rearrange("p (cc t) -> p cc t", cc=2)[:, :, sub * SUB:(sub + 1) * SUB]
            sti_v = stv[:, 1024:2048].rearrange("p (cc t) -> p cc t", cc=2)[:, :, sub * SUB:(sub + 1) * SUB]
            b.op("dve", lambda E, str_v=str_v: E.tensor_copy(str_v, scT_r.rearrange("p (cc t) -> p cc t", cc=2)), sDV)
            b.op("dve", lambda E, sti_v=sti_v: E.tensor_copy(sti_v, scT_i.rearrange("p (cc t) -> p cc t", cc=2)), sDV)
            b.wait("dve", sDV, sDV.n)
            M[("tts", k, sub)] = sDV.n
        M[("scan", k)] = sDV.n

    def emit_y(k):
        stv = WBIG2[:, STB:STB + 2048] if k % 2 == 0 else sb_st1
        ynb = sb_xnb if k % 2 == 0 else sb_xnb1
        b.wait("pe", sDV, M[("scan", k)])
        for mg in range(4):
            if mg > 0:
                b.wait("pe", sAC, M[("yev", k, mg - 1)])
            elif k > 0:
                b.wait("pe", sAC, M[("yev", k - 1, 3)])
            for mi in range(2):
                m = mg * 2 + mi
                dst = psYa if mi == 0 else psYb
                for cc in range(2):
                    b.op("pe", lambda E, dst=dst, cc=cc, m=m: E.matmul(dst[:, :], WBIG2[:, 4096 + cc * 1024 + m * 128:4096 + cc * 1024 + (m + 1) * 128], stv[:, cc * 512:(cc + 1) * 512], start=(cc == 0), stop=False), sPE)
                for cc in range(2):
                    b.op("pe", lambda E, dst=dst, cc=cc, m=m: E.matmul(dst[:, :], WBIG2[:, 6144 + cc * 1024 + m * 128:6144 + cc * 1024 + (m + 1) * 128], stv[:, 1024 + cc * 512:1024 + (cc + 1) * 512], start=False, stop=False), sPE)
                for kc in range(8):
                    b.op("pe", lambda E, dst=dst, kc=kc, m=m: E.matmul(dst[:, :], WBIG2[:, 8192 + kc * 1024 + m * 128:8192 + kc * 1024 + (m + 1) * 128], ynb[:, kc * T:(kc + 1) * T], start=False, stop=(kc == 7)), sPE)
            M[("ymm", k, mg)] = sPE.n
            b.wait("act", sPE, M[("ymm", k, mg)])
            if debug_level >= 1 and k == 1 and mg == 0:
                b.wait("act", sLD, M[("dmp_y", 0)])
            for mi in range(2):
                m = mg * 2 + mi
                src = psYa if mi == 0 else psYb
                b.op("act", lambda E, src=src, m=m: E.copy(WBIG2[:, YB + m * 512:YB + (m + 1) * 512], src[:, :]), sAC)
            M[("yev", k, mg)] = sAC.n

    def emit_w1(k, buev_wait):
        t0 = k * T
        if k == 0:
            b.wait("pe", sG, v_w1)
        b.wait("pe", sAC, buev_wait)
        b.wait("pe", sAC, M[("yev", k, 3)])
        UP = [(psUa, psB1), (psB2, psB3)]
        for hp in range(16):
            pp = hp % 2
            bk0, bk1 = UP[pp]
            if hp >= 2:
                b.wait("pe", sAC, M[("uev", k, hp - 2)])
            elif k > 0:
                b.wait("pe", sAC, M[("uev", k - 1, 15)])
            for kc in range(8):
                b.op("pe", lambda E, bk0=bk0, kc=kc, hp=hp: E.matmul(bk0[:, :], WBIG[:, kc * 4096 + (2 * hp) * 128:kc * 4096 + (2 * hp + 1) * 128], WBIG2[:, YB + kc * 512:YB + (kc + 1) * 512], start=(kc == 0), stop=(kc == 7)), sPE)
            for kc in range(8):
                b.op("pe", lambda E, bk1=bk1, kc=kc, hp=hp: E.matmul(bk1[:, :], WBIG[:, kc * 4096 + (2 * hp + 1) * 128:kc * 4096 + (2 * hp + 2) * 128], WBIG2[:, YB + kc * 512:YB + (kc + 1) * 512], start=(kc == 0), stop=(kc == 7)), sPE)
            M[("u", k, hp)] = sPE.n
            USB = 26624 + pp * 1024
            b.wait("act", sPE, M[("u", k, hp)])
            if hp >= 2:
                b.wait("act", sDV, M[("gel", k, hp - 2)])
            elif k > 0:
                b.wait("act", sDV, M[("gel", k - 1, 14 + pp)])
            b.op("act", lambda E, bk0=bk0, USB=USB: E.copy(WBIG2[:, USB:USB + 512], bk0[:, :]), sAC)
            b.op("act", lambda E, bk1=bk1, USB=USB: E.copy(WBIG2[:, USB + 512:USB + 1024], bk1[:, :]), sAC)
            M[("uev", k, hp)] = sAC.n
            if hp > 0:
                b.wait("act", sDV, M[("t2", k, hp - 1)])
            elif k > 0:
                b.wait("act", sDV, M[("t2", k - 1, 15)])
            b.op("act", lambda E, USB=USB: E.activation(sb_q[:, :], WBIG2[:, USB:USB + 1024], AF.Square, scale=SQ2C2), sAC)
            vq2 = sAC.n
            b.wait("dve", sAC, vq2)
            b.op("dve", lambda E, USB=USB: E.scalar_tensor_tensor(sb_t2[:, :], sb_q[:, :], C1x2, WBIG2[:, USB:USB + 1024], ALU.add, ALU.mult), sDV)
            vt2 = sDV.n
            M[("t2", k, hp)] = vt2
            b.wait("act", sDV, vt2)
            b.op("act", lambda E: E.activation(sb_sg2[:, :], sb_t2[:, :], AF.Sigmoid), sAC)
            vsg = sAC.n
            b.wait("dve", sAC, vsg)
            b.wait("dve", sHS[pp], M[("hst", k, hp - 2)] if hp >= 2 else (M[("hst", k - 1, 14 + pp)] if k > 0 else 0))
            b.op("dve", lambda E, pp=pp, USB=USB: E.tensor_mul(WBIG2[:, HST + pp * 1024:HST + (pp + 1) * 1024], sb_sg2[:, :], WBIG2[:, USB:USB + 1024]), sDV)
            M[("gel", k, hp)] = sDV.n
            b.wait("sync", sDV, M[("gel", k, hp)])
            b.op("sync", lambda E, hp=hp, t0=t0, pp=pp: E.dma_start(
                out=d_h.ap().rearrange("(H p) t -> p H t", p=128)[:, 2 * hp:2 * hp + 2, t0:t0 + T],
                in_=WBIG2[:, HST + pp * 1024:HST + (pp + 1) * 1024].rearrange("p (H t) -> p H t", t=T)), sHS[pp], 16)
            if debug_level >= 1:
                b.op("sync", lambda E, hp=hp, t0=t0, pp=pp: E.dma_start(
                    out=dbg["o_h"].rearrange("(H p) t -> p H t", p=128)[:, 2 * hp:2 * hp + 2, t0:t0 + T],
                    in_=WBIG2[:, HST + pp * 1024:HST + (pp + 1) * 1024].rearrange("p (H t) -> p H t", t=T)), sHS[pp], 16)
            M[("hst", k, hp)] = sHS[pp].n

    for kk in range(2):
        b.op("pool", lambda E, kk=kk: E.dma_start(
            out=sb_x[:, kk * 4096:(kk + 1) * 4096].rearrange("p (kc t) -> p kc t", t=T),
            in_=xT.rearrange("(kc p) t -> p kc t", p=128)[:, :, kk * T:(kk + 1) * T]), sXL, 16)
    M[("xld", 0)] = M[("xld", 1)] = sXL.n
    for k in range(NCH):
        p1_chunk(k)
        b.op("pool", lambda E, j=k: E.dma_start(out=d_w2bf.ap()[512 * j:512 * (j + 1), :], in_=W2_in[512 * j:512 * (j + 1), :]), sW2C, 16)
    v_w2cast = sW2C.n
    sG_pre = sG.n
    for qeng, qsem, j0, j1 in (("pool", sG, 0, 10), ("act", sW2A, 10, 21), ("sync", sW2B, 21, 32)):
        b.wait(qeng, sAC, M[("uev", NCH - 1, 15)])
        b.wait(qeng, sW2C, v_w2cast)
        for j in range(j0, j1):
            b.op(qeng, lambda E, j=j: E.dma_start(out=WBIG[:, j * 1024:(j + 1) * 1024], in_=d_w2bf.ap()[128 * j:128 * (j + 1), :]), qsem, 16)
    M[("vw2",)] = (sG.n, sW2A.n, sW2B.n)

    b.barrier("p1", extra_waits=[("sync", sHS[0], sHS[0].n), ("sync", sHS[1], sHS[1].n), ("sync", sLD, sLD.n), ("pool", sG, sG_pre)])
    print("sbuf remaining before sc1.close:", nc.sbuf_bytes_remaining)
    sc1.close()
    print("sbuf remaining after sc1.close:", nc.sbuf_bytes_remaining)

    sc2 = Scope(nc)
    sb_x2 = sc2.sb("sb_x2", [128, 4096], F32)
    sb_xn2 = sc2.sb("sb_xn2", [128, 4096], F32)
    sb_ao = sb_xn2
    sb_xnb2 = sc2.sb("sb_xnb2", [128, 4096], BF16)
    sb_cb = sc2.sb("sb_cb", [128, 2048], BF16)
    stg1 = sc2.sb("stg1", [128, 1024], F32)
    stg2 = sc2.sb("stg2", [128, 1024], BF16)
    dbgz0_t = sc2.sb("dbgz0_t", [128, 512], F32)
    rtA = sc2.sb("rtA", [128, 256], F32)
    rtB = sc2.sb("rtB", [128, 256], F32)
    psU2 = sc2.ps("psU2", [128, 512], F32)
    psZ = [sc2.ps(f"psZ{i}", [128, 512], F32) for i in range(4)]
    psLG = sc2.ps("psLG", [128, 512], F32)
    psTRf = sc2.ps("psTRf", [128, 512], F32)
    psTRb = sc2.ps("psTRb", [128, 512], BF16)

    v_w2, v_w2a, v_w2b = M[("vw2",)]
    b.wait("pe", sG, v_w2)
    b.wait("pe", sW2A, v_w2a)
    b.wait("pe", sW2B, v_w2b)
    if debug_level >= 2:
        b.wait("sync", sG, v_w2)
        b.wait("sync", sW2A, v_w2a)
        b.wait("sync", sW2B, v_w2b)
        dma("sync", dbg["o_w2"][:, 0:1024], WBIG[:, 0:1024])
        dma("sync", dbg["o_w2"][:, 1024:2048], WBIG[:, 1024:2048])

    M2 = {}

    def emit_ln2(k, src_t, tag, out_mode, pe_wait=None, src_dv=None):
        if pe_wait is not None:
            b.wait("pe", sDV, pe_wait)
        if src_dv is not None:
            b.wait("act", sDV, src_dv)
        for kc in range(8):
            par = kc % 2
            sk = src_t[:, kc * T:(kc + 1) * T]
            cbv = sb_cb[:, par * 512:(par + 1) * 512]
            sqv = sb_cb[:, 1024 + par * 512:1024 + (par + 1) * 512]
            if kc >= 2:
                b.wait("dve", sPE, M2[("stmm", tag, k, kc - 2)])
                b.wait("act", sPE, M2[("stmm", tag, k, kc - 2)])
            b.op("dve", lambda E, sk=sk, cbv=cbv: E.tensor_copy(cbv, sk), sDV)
            vc = sDV.n
            b.op("act", lambda E, sk=sk, sqv=sqv: E.activation(sqv, sk, AF.Square), sAC)
            vs = sAC.n
            b.wait("pe", sDV, vc)
            b.wait("pe", sAC, vs)
            b.op("pe", lambda E, kc=kc, cbv=cbv: E.matmul(psU2[0:1, :], sb_onescb[:, :], cbv, start=(kc == 0), stop=(kc == 7)), sPE)
            b.op("pe", lambda E, kc=kc, sqv=sqv: E.matmul(psU2[32:33, :], sb_onescb[:, :], sqv, start=(kc == 0), stop=(kc == 7)), sPE)
            M2[("stmm", tag, k, kc)] = sPE.n
        b.wait("act", sPE, sPE.n)
        b.op("act", lambda E: E.activation(sb_rows[:, 0:512], psU2[0:1, :], AF.Copy, scale=INV_D), sAC)
        vm = sAC.n
        b.wait("act", sAC, vm)
        b.op("act", lambda E: E.activation(sb_rows[:, 512:1024], sb_rows[:, 0:512], AF.Square), sAC)
        vq = sAC.n
        b.wait("dve", sAC, vq)
        b.op("dve", lambda E: E.scalar_tensor_tensor(sb_rows[:, 512:1024], psU2[32:33, :], INV_D, sb_rows[:, 512:1024], ALU.mult, ALU.subtract), sDV)
        vv = sDV.n
        b.wait("act", sDV, vv)
        b.op("act", lambda E: E.activation(sb_rows[:, 1024:1536], sb_rows[:, 512:1024], AF.Ln, bias=sb_eps[:, :]), sAC)
        b.wait("act", sAC, sAC.n)
        b.op("act", lambda E: E.activation(sb_rows[:, 1024:1536], sb_rows[:, 1024:1536], AF.Exp, scale=-0.5), sAC)
        vr = sAC.n
        b.wait("dve", sAC, vr)
        b.op("dve", lambda E: E.tensor_scalar_mul(sb_rows[:, 512:1024], sb_rows[:, 0:512], -1.0), sDV)
        vnm = sDV.n
        b.wait("pe", sDV, vnm)
        b.wait("pe", sDV, vv)
        b.op("pe", lambda E: E.matmul(psU2[:, :], sb_onesrf[:, :], sb_rows[:, 512:1024], start=True, stop=True), sPE)
        vb1 = sPE.n
        b.wait("dve", sPE, vb1)
        if out_mode == "bf":
            for kc in range(8):
                b.op("dve", lambda E, kc=kc, src_t=src_t: E.tensor_add(sb_xn2[:, kc * T:(kc + 1) * T], src_t[:, kc * T:(kc + 1) * T], psU2[:, :]), sDV)
        else:
            for kc in range(8):
                b.op("dve", lambda E, kc=kc, src_t=src_t: E.tensor_add((sb_xn2 if out_mode == "xn" else src_t)[:, kc * T:(kc + 1) * T], src_t[:, kc * T:(kc + 1) * T], psU2[:, :]), sDV)
        vsub = sDV.n
        b.wait("pe", sDV, vsub)
        b.wait("pe", sAC, vr)
        b.op("pe", lambda E: E.matmul(psU2[:, :], sb_onesrf[:, :], sb_rows[:, 1024:1536], start=True, stop=True), sPE)
        vb2 = sPE.n
        M2[("bc2", tag, k)] = sPE.n
        b.wait("dve", sPE, vb2)
        for kc in range(8):
            if out_mode == "xn":
                b.op("dve", lambda E, kc=kc: E.tensor_mul(sb_xn2[:, kc * T:(kc + 1) * T], sb_xn2[:, kc * T:(kc + 1) * T], psU2[:, :]), sDV)
            elif out_mode == "inplace":
                b.op("dve", lambda E, kc=kc, src_t=src_t: E.tensor_mul(src_t[:, kc * T:(kc + 1) * T], src_t[:, kc * T:(kc + 1) * T], psU2[:, :]), sDV)
            else:
                b.op("dve", lambda E, kc=kc: E.tensor_mul(sb_xnb2[:, kc * T:(kc + 1) * T], sb_xn2[:, kc * T:(kc + 1) * T], psU2[:, :]), sDV)
        M2[("ln", tag, k)] = sDV.n

    for k in range(NCH):
        par = k % 2
        t0 = k * T
        HOF = par * 16384
        if k >= 2:
            b.wait("act", sPE, M2[("zmm", k - 2)])
        b.op("act", lambda E, t0=t0, HOF=HOF: E.dma_start(
            out=WBIG2[:, HOF:HOF + 16384].rearrange("p (kc t) -> p kc t", t=T),
            in_=d_h.ap().rearrange("(kc p) t -> p kc t", p=128)[:, :, t0:t0 + T]), sXL, 16)
        M2[("hld", k)] = sXL.n
        if k > 0:
            b.wait("act", sPE, M2[("trx1", k - 1)])
        b.op("act", lambda E, t0=t0: E.dma_start(
            out=sb_x2.rearrange("p (kc t) -> p kc t", t=T),
            in_=xT.rearrange("(kc p) t -> p kc t", p=128)[:, :, t0:t0 + T]), sXL, 16)
        M2[("xld", k)] = sXL.n
        b.wait("dve", sXL, M2[("xld", k)])
        b.wait("act", sXL, M2[("xld", k)])
        if k > 0:
            b.wait("dve", sPE, M2[("trx1", k - 1)])
        emit_ln2(k, sb_x2, "x", "xn",
                 pe_wait=(M2[("trcpf", k - 1, 3, 1)] if k > 0 else None))
        if debug_level >= 2 and k == 0:
            b.wait("sync", sDV, M2[("ln", "x", k)])
            dma("sync", dbg["o_xn2"][:, :], sb_xn2[:, :])
            M2[("xndmp",)] = sLD.n
            b.wait("sync", sLD, M2[("hld", k)])
            dma("sync", dbg["o_hrb"][:, :], WBIG2[:, 0:16384])
        b.wait("pe", sXL, M2[("hld", k)])
        for m in range(8):
            bank = psZ[m % 4]
            if m >= 4:
                b.wait("pe", sDV, M2[("ao", k, m - 4)])
            elif k > 0:
                b.wait("pe", sDV, M2[("ao", k - 1, m + 4 - 8 + 8)])
            for kc in range(32):
                b.op("pe", lambda E, bank=bank, kc=kc, m=m, HOF=HOF: E.matmul(bank[:, :], WBIG[:, kc * 1024 + m * 128:kc * 1024 + (m + 1) * 128], WBIG2[:, HOF + kc * 512:HOF + (kc + 1) * 512], start=(kc == 0), stop=(kc == 31)), sPE)
            M2[("zmm_m", k, m)] = sPE.n
            b.wait("dve", sPE, M2[("zmm_m", k, m)])
            b.wait("dve", sDV, M2[("ln", "x", k)])
            if debug_level >= 2 and k == 0 and m == 0:
                b.wait("dve", sLD, M2[("xndmp",)])
                b.op("dve", lambda E: E.tensor_copy(dbgz0_t[:, :], psZ[0][:, :]), sDV)
                b.wait("dve", sDV, sDV.n)
                b.wait("sync", sDV, sDV.n)
                dma("sync", dbg["o_z0"][:, :], dbgz0_t[:, :])
            b.op("dve", lambda E, bank=bank, m=m: E.scalar_tensor_tensor(sb_ao[:, m * 512:(m + 1) * 512], bank[:, :], sb_sig[:, :], sb_xn2[:, m * 512:(m + 1) * 512], ALU.mult, ALU.add), sDV)
            M2[("ao", k, m)] = sDV.n
        M2[("zmm", k)] = sPE.n
        if debug_level >= 2 and k == 0:
            b.wait("sync", sDV, M2[("ao", k, 7)])
            dma("sync", dbg["o_ao"][:, :], sb_ao[:, :])
            M2[("aodmp",)] = sLD.n
        if debug_level >= 2 and k == 0:
            b.wait("dve", sLD, M2[("aodmp",)])
        emit_ln2(k, sb_ao, "ao", "inplace", pe_wait=M2[("ln", "x", k)], src_dv=M2[("ao", k, 7)])
        for kc in range(8):
            b.op("dve", lambda E, kc=kc: E.tensor_add(sb_x2[:, kc * T:(kc + 1) * T], sb_ao[:, kc * T:(kc + 1) * T], sb_x2[:, kc * T:(kc + 1) * T]), sDV)
        M2[("x1", k)] = sDV.n
        emit_ln2(k, sb_x2, "x1", "bf", pe_wait=M2[("ln", "ao", k)], src_dv=M2[("x1", k)])
        b.wait("pe", sDV, M2[("ln", "x1", k)])
        for tg in range(4):
            col = (k * 4 + tg) * 8
            for kc in range(8):
                b.op("pe", lambda E, col=col, kc=kc, tg=tg: E.matmul(psLG[:, col:col + 8], sb_xnb2[:, kc * 512 + tg * 128:kc * 512 + (tg + 1) * 128], sb_wg[:, kc * 8:(kc + 1) * 8], start=(kc == 0), stop=(kc == 7)), sPE)
        M2[("lgmm", k)] = sPE.n
        for tg in range(4):
            if tg > 0:
                b.wait("dve", sST1, M2[("x1st", k, tg - 1)])
                b.wait("dve", sST2, M2[("xn2st", k, tg - 1)])
            elif k > 0:
                b.wait("dve", sST1, M2[("x1st", k - 1, 3)])
                b.wait("dve", sST2, M2[("xn2st", k - 1, 3)])
            for half in range(2):
                trb = psTRf if half == 0 else psU2
                key = ("trcpf", k, tg - 1, half) if tg > 0 else ("trcpf", k - 1, 3, half)
                if key in M2:
                    b.wait("pe", sDV, M2[key])
                for i in range(4):
                    kc = half * 4 + i
                    b.op("pe", lambda E, i=i, kc=kc, tg=tg, trb=trb: E.transpose(trb[:, i * 128:(i + 1) * 128], sb_x2[:, kc * 512 + tg * 128:kc * 512 + (tg + 1) * 128], sb_identf[:, :]), sPE)
                vtr = sPE.n
                b.wait("dve", sPE, vtr)
                b.op("dve", lambda E, half=half, trb=trb: E.tensor_copy(stg1[:, half * 512:(half + 1) * 512], trb[:, :]), sDV)
                M2[("trcpf", k, tg, half)] = sDV.n
            b.wait("sync", sDV, M2[("trcpf", k, tg, 1)])
            if (k, tg) != (0, 0):
                pass
            b.op("sync", lambda E, t0=t0, tg=tg: E.dma_start(out=d_x1t.ap()[t0 + tg * 128:t0 + (tg + 1) * 128, :], in_=stg1[:, :]), sST1, 16)
            M2[("x1st", k, tg)] = sST1.n
            for half in range(2):
                if (k, tg, half) != (0, 0, 0):
                    b.wait("pe", sDV, M2.get(("trcpb", k, tg, half - 1), M2.get(("trcpb", k, tg - 1, 1), M2.get(("trcpb", k - 1, 3, 1), 0))))
                for i in range(4):
                    kc = half * 4 + i
                    b.op("pe", lambda E, i=i, kc=kc, tg=tg: E.transpose(psTRb[:, i * 128:(i + 1) * 128], sb_xnb2[:, kc * 512 + tg * 128:kc * 512 + (tg + 1) * 128], sb_identb[:, :]), sPE)
                vtr = sPE.n
                b.wait("dve", sPE, vtr)
                b.op("dve", lambda E, half=half: E.tensor_copy(stg2[:, half * 512:(half + 1) * 512], psTRb[:, :]), sDV)
                M2[("trcpb", k, tg, half)] = sDV.n
            b.wait("sync", sDV, M2[("trcpb", k, tg, 1)])
            b.op("sync", lambda E, t0=t0, tg=tg: E.dma_start(out=d_xn2bf.ap()[t0 + tg * 128:t0 + (tg + 1) * 128, :], in_=stg2[:, :]), sST2, 16)
            M2[("xn2st", k, tg)] = sST2.n
        M2[("trx1", k)] = sPE.n
        b.wait("dve", sST2, M2[("xn2st", k, 3)])
        if k % 2 == 1:
            cch = k // 2
            b.wait("pool", sST2, M2[("xn2st", k, 3)])
            b.wait("pool", sST1, M2[("x1st", k, 3)])
            b.op("pool", lambda E, cch=cch: E.collective_compute("AllGather", ALU.bypass,
                replica_groups=[list(range(NCORE))],
                ins=[d_xn2bf.ap()[cch * 1024:(cch + 1) * 1024, :]],
                outs=[d_xn2all.ap()[cch * 8192:(cch + 1) * 8192, :]]), sCC, 1)
            M2[("xnag", cch)] = sCC.n

    b.wait("dve", sPE, M2[("lgmm", NCH - 1)])
    sb_lg = psLG[:, 0:256]
    E8 = lambda t: t.rearrange("p (tg e) -> p tg e", e=8)
    B8 = lambda t: t.rearrange("p (tg one) -> p tg one", one=1).broadcast_to([128, 32, 8])

    def dv(fn):
        b.op("dve", fn, sDV)
        b.wait("dve", sDV, sDV.n)

    dv(lambda E: E.tensor_reduce(sb_m1[:, :], E8(sb_lg), axis=AX.X, op=ALU.max))
    dv(lambda E: E.tensor_tensor(E8(rtA), E8(sb_lg), B8(sb_m1), ALU.is_lt))
    dv(lambda E: E.scalar_tensor_tensor(rtA[:, :], rtA[:, :], 1000.0, sb_iotaE[:, :], ALU.mult, ALU.add))
    dv(lambda E: E.tensor_reduce(sb_i1[:, :], E8(rtA), axis=AX.X, op=ALU.min))
    dv(lambda E: E.tensor_tensor(E8(sb_oh1), E8(sb_iotaE), B8(sb_i1), ALU.is_equal))
    dv(lambda E: E.scalar_tensor_tensor(rtB[:, :], sb_oh1[:, :], -1e9, sb_lg[:, :], ALU.mult, ALU.add))
    dv(lambda E: E.tensor_reduce(sb_m2[:, :], E8(rtB), axis=AX.X, op=ALU.max))
    dv(lambda E: E.tensor_tensor(E8(rtA), E8(rtB), B8(sb_m2), ALU.is_lt))
    dv(lambda E: E.scalar_tensor_tensor(rtA[:, :], rtA[:, :], 1000.0, sb_iotaE[:, :], ALU.mult, ALU.add))
    dv(lambda E: E.tensor_reduce(sb_i2[:, :], E8(rtA), axis=AX.X, op=ALU.min))
    dv(lambda E: E.tensor_tensor(E8(sb_oh2), E8(sb_iotaE), B8(sb_i2), ALU.is_equal))
    dv(lambda E: E.tensor_sub(sb_m2[:, :], sb_m2[:, :], sb_m1[:, :]))
    b.wait("act", sDV, sDV.n)
    b.op("act", lambda E: E.activation(sb_g2[:, :], sb_m2[:, :], AF.Sigmoid), sAC)
    b.op("act", lambda E: E.activation(sb_g1[:, :], sb_m2[:, :], AF.Sigmoid, scale=-1.0), sAC)
    v_gates = sAC.n
    dv(lambda E: E.tensor_copy(sb_cb[:, 0:256], sb_oh1[:, :]))
    dv(lambda E: E.tensor_copy(sb_cb[:, 256:512], sb_oh2[:, :]))
    v_ohb = sDV.n
    b.wait("pe", sDV, v_ohb)
    for j in range(2):
        for tg in range(32):
            b.op("pe", lambda E, j=j, tg=tg: E.matmul(psZ[0][:, j * 256 + tg * 8:j * 256 + (tg + 1) * 8], sb_trib[:, :], sb_cb[:, j * 256 + tg * 8:j * 256 + (tg + 1) * 8], start=True, stop=True), sPE)
    for j in range(2):
        for tg in range(32):
            b.op("pe", lambda E, j=j, tg=tg: E.matmul(psZ[1][0:8, j * 32 + tg:j * 32 + tg + 1], sb_cb[:, j * 256 + tg * 8:j * 256 + (tg + 1) * 8], sb_onescb[:, :], start=True, stop=True), sPE)
    v_rkmm = sPE.n
    b.wait("dve", sPE, v_rkmm)
    dv(lambda E: E.tensor_mul(rtA[:, :], psZ[0][:, 0:256], sb_oh1[:, :]))
    dv(lambda E: E.tensor_reduce(sb_pre1[:, :], E8(rtA), axis=AX.X, op=ALU.add))
    dv(lambda E: E.tensor_mul(rtA[:, :], psZ[0][:, 256:512], sb_oh2[:, :]))
    dv(lambda E: E.tensor_reduce(sb_pre2[:, :], E8(rtA), axis=AX.X, op=ALU.add))
    dv(lambda E: E.tensor_copy(sb_ct[0:8, :], psZ[1][0:8, 0:64]))
    for j in range(2):
        dv(lambda E, j=j: E.tensor_tensor_scan(sb_tb[0:8, j * 32:(j + 1) * 32], sb_ones128[0:8, 0:32], sb_ct[0:8, j * 32:(j + 1) * 32], 0.0, ALU.mult, ALU.add))
    b.wait("sync", sDV, sDV.n)
    dma("sync", d_cnt.ap()[0:1, :], sb_tb[0:8, 31:32])
    dma("sync", d_cnt.ap()[1:2, :], sb_tb[0:8, 63:64])
    M2[("cntst",)] = sLD.n
    b.wait("dve", sLD, M2[("cntst",)])
    dv(lambda E: E.tensor_sub(sb_tb[0:8, :], sb_tb[0:8, :], sb_ct[0:8, :]))
    v_p2 = sDV.n

    if debug_level >= 2:
        b.wait("sync", sLD, M2[("xn2st", NCH - 1, 3)])
        dma("sync", dbg["o_x1t"][:, :], d_x1t.ap()[:, :])
        dma("sync", dbg["o_xn2bf"][:, :], d_xn2bf.ap()[:, :])
        b.wait("sync", sDV, v_p2)
        dma("sync", dbg["o_rt"][:, 0:32], sb_i1[:, :])
        dma("sync", dbg["o_rt"][:, 32:64], sb_i2[:, :])
        dma("sync", dbg["o_rt"][:, 64:96], sb_g1[:, :])
        dma("sync", dbg["o_rt"][:, 96:128], sb_g2[:, :])
        dma("sync", dbg["o_rt"][:, 128:160], sb_pre1[:, :])
        dma("sync", dbg["o_rt"][:, 160:192], sb_pre2[:, :])
        dma("sync", dbg["o_rt"][:, 192:256], sb_tb[:, :])
        dma("sync", dbg["o_rt"][:, 256:320], sb_ct[:, :])

    b.barrier("p2", extra_waits=[("sync", sLD, sLD.n), ("sync", sST1, sST1.n), ("sync", sST2, sST2.n), ("pool", sG, sG.n)])
    sc2.close()

    sc3 = Scope(nc)
    sb_h3 = sc3.sb("sb_h3", [128, 16384], BF16)
    sb_xfm = sc3.sb("sb_xfm", [128, 8192], BF16)
    sb_ytm = sc3.sb("sb_ytm", [128, 2048], BF16)
    sb_pmy = sc3.sb("sb_pmy", [128, 128], F32)
    sb_pidi = sc3.sb("sb_pidi", [128, 80], I32)
    sb_pidf = sc3.sb("sb_pidf", [128, 80], F32)
    sb_cnt16 = sc3.sb("sb_cnt16", [16, 8], F32)
    sb_cb2 = sc3.sb("sb_cb2", [128, 2], F32)
    sb_gst = sc3.sb("sb_gst", [128, 2048], BF16)
    sb_fc2bt = sc3.sb("sb_fc2bt", [128, 1024], F32)
    sb_fc2brow = sb_rows[0:1, 0:1024]
    psM = [sc3.ps("psM0", [128, 512], F32), sc3.ps("psM1", [128, 512], F32)]
    psY3 = [sc3.ps("psY0", [128, 512], F32), sc3.ps("psY1", [128, 512], F32)]
    psX3 = sc3.ps("psX3", [128, 512], F32)

    dma("sync", sb_fc2brow, fc2b_in[:, :])
    for kc in range(8):
        dma("pool", WBIG[:, kc * 4096:(kc + 1) * 4096], fc1T_in[128 * kc:128 * (kc + 1), :], sG)
    for hc in range(32):
        dma("pool", WBIG2[:, hc * 1024:(hc + 1) * 1024], fc2_in[128 * hc:128 * (hc + 1), :], sG)
    v_fcw = sG.n
    rg = [list(range(NCORE))]
    b.op("pool", lambda E: E.collective_compute("AllGather", ALU.bypass, replica_groups=rg,
        ins=[d_cnt.ap()[:, :]], outs=[d_cntall.ap()[:, :]]), sCC, 1)
    v_cntag = sCC.n
    v_xnag = M2[("xnag", 3)]

    b.wait("sync", sCC, v_cntag)
    dma("sync", sb_cnt16[:, :], d_cntall.ap()[:, :])
    v_c16 = sLD.n
    b.wait("pe", sLD, v_c16)
    b.op("pe", lambda E: E.matmul(psX3[0:2, 0:8], sb_cmask[:, :], sb_cnt16[:, :], start=True, stop=True), sPE)
    v_cbmm = sPE.n
    b.wait("dve", sPE, v_cbmm)
    b.op("dve", lambda E: E.tensor_copy(sb_cnt16[0:2, :], psX3[0:2, 0:8]), sDV)
    b.wait("dve", sDV, sDV.n)
    b.wait("pe", sDV, sDV.n)
    b.op("pe", lambda E: E.transpose(psX3[0:8, 8:10], sb_cnt16[0:2, 0:8], sb_identf[0:2, 0:2]), sPE)
    v_cbt = sPE.n
    b.wait("dve", sPE, v_cbt)
    b.op("dve", lambda E: E.tensor_copy(sb_cb2[0:8, :], psX3[0:8, 8:10]), sDV)
    b.wait("dve", sDV, sDV.n)
    for j in range(2):
        b.op("dve", lambda E, j=j: E.tensor_scalar_add(sb_tb[0:8, j * 32:(j + 1) * 32], sb_tb[0:8, j * 32:(j + 1) * 32], sb_cb2[0:8, j:j + 1]), sDV)
    b.wait("dve", sDV, sDV.n)
    b.wait("sync", sDV, sDV.n)
    dma("sync", d_fb.ap()[:, :], sb_tb[0:8, :])
    b.wait("sync", sLD, sLD.n)
    b.op("sync", lambda E: E.dma_start(
        out=sb_brow[0:1, 0:512].rearrange("one (j tg e) -> one j tg e", j=2, e=8),
        in_=d_fb.ap().rearrange("e (j tg) -> j tg e", j=2),
        allow_slow_non_contiguous=True), sLD, 16)
    v_brow = sLD.n
    b.wait("pe", sLD, v_brow)
    b.op("pe", lambda E: E.matmul(psX3[:, :], sb_onesrf[:, :], sb_brow[0:1, 0:512], start=True, stop=True), sPE)
    v_bb = sPE.n
    b.wait("dve", sPE, v_bb)

    def dv3(fn):
        b.op("dve", fn, sDV)
        b.wait("dve", sDV, sDV.n)

    E8 = lambda t: t.rearrange("p (tg e) -> p tg e", e=8)
    for j, ohx, prex, gx, ix in ((0, sb_oh1, sb_pre1, sb_g1, sb_i1), (1, sb_oh2, sb_pre2, sb_g2, sb_i2)):
        dv3(lambda E, j=j, ohx=ohx: E.tensor_mul(rtA2[:, :], psX3[:, j * 256:(j + 1) * 256], ohx[:, :]))
        dv3(lambda E: E.tensor_reduce(sb_m1[:, :], E8(rtA2), axis=AX.X, op=ALU.add))
        dv3(lambda E, prex=prex: E.tensor_add(sb_m1[:, :], sb_m1[:, :], prex[:, :]))
        dv3(lambda E: E.tensor_scalar(sb_m2[:, :], sb_m1[:, :], 0.5, 1.0 / 128.0, ALU.add, ALU.mult))
        dv3(lambda E: E.tensor_copy(sb_i32[:, 0:32], sb_m2[:, :]))
        dv3(lambda E: E.tensor_copy(sb_m2[:, :], sb_i32[:, 0:32]))
        dv3(lambda E: E.scalar_tensor_tensor(rtA2[:, 0:32], sb_m2[:, :], -128.0, sb_m1[:, :], ALU.mult, ALU.add))
        dv3(lambda E: E.tensor_scalar(rtA2[:, 32:64], rtA2[:, 0:32], 0.0, None, ALU.is_lt))
        dv3(lambda E: E.tensor_sub(sb_m2[:, :], sb_m2[:, :], rtA2[:, 32:64]))
        dv3(lambda E: E.scalar_tensor_tensor(rtA2[:, 0:32], sb_m2[:, :], -128.0, sb_m1[:, :], ALU.mult, ALU.add))
        dv3(lambda E: E.scalar_tensor_tensor(rtA2[:, 64:96], rtA2[:, 0:32], 80.0, sb_m2[:, :], ALU.mult, ALU.add))
        dv3(lambda E: E.tensor_scalar(rtA2[:, 96:128], sb_m1[:, :], float(CAP), None, ALU.is_lt))
        dv3(lambda E, gx=gx, j=j: E.tensor_mul(sb_gk[:, j * 32:(j + 1) * 32], gx[:, :], rtA2[:, 96:128]))
        dv3(lambda E, ix=ix: E.scalar_tensor_tensor(rtA2[:, 128:160], ix[:, :], float(CAP), rtA2[:, 64:96], ALU.mult, ALU.add))
        dv3(lambda E: E.tensor_scalar(rtA2[:, 96:128], rtA2[:, 96:128], -1.0, 1.0, ALU.mult, ALU.add))
        dv3(lambda E: E.scalar_tensor_tensor(rtA2[:, 128:160], rtA2[:, 96:128], 90000.0, rtA2[:, 128:160], ALU.mult, ALU.add))
        dv3(lambda E, j=j: E.tensor_scalar(sb_pos[:, j * 32:(j + 1) * 32], rtA2[:, 128:160], 81984.0, None, ALU.min))
        dv3(lambda E, j=j: E.tensor_copy(sb_poi[:, j * 32:(j + 1) * 32], sb_pos[:, j * 32:(j + 1) * 32]))
    v_pos = sDV.n

    b.wait("pool", sDV, v_pos)
    for j in range(2):
        for tg in range(32):
            b.op("pool", lambda E, j=j, tg=tg: E.indirect_dma_start(
                out=d_pm.ap()[:, :], out_offset=IndirectOffsetOnAxis(sb_poi[:, j * 32 + tg:j * 32 + tg + 1], 0),
                in_=sb_tokid[:, tg:tg + 1], in_offset=None), sG, 16)
    v_scat = sG.n
    b.wait("dve", sG, v_scat)
    dv3(lambda E: E.tensor_copy(rtA2[:, 0:64], sb_poi[:, :]))
    dv3(lambda E: E.tensor_scalar(rtA2[:, 64:128], rtA2[:, 0:64], 0.5, 1.0 / CAP, ALU.add, ALU.mult))
    dv3(lambda E: E.tensor_copy(sb_i32[:, 0:64], rtA2[:, 64:128]))
    dv3(lambda E: E.tensor_copy(rtA2[:, 64:128], sb_i32[:, 0:64]))
    dv3(lambda E: E.scalar_tensor_tensor(rtA2[:, 128:192], rtA2[:, 64:128], -float(CAP), rtA2[:, 0:64], ALU.mult, ALU.add))
    dv3(lambda E: E.tensor_scalar(rtA2[:, 192:256], rtA2[:, 128:192], 0.0, None, ALU.is_lt))
    dv3(lambda E: E.tensor_sub(rtA2[:, 64:128], rtA2[:, 64:128], rtA2[:, 192:256]))
    dv3(lambda E: E.scalar_tensor_tensor(rtA2[:, 128:192], rtA2[:, 64:128], -float(CAP), rtA2[:, 0:64], ALU.mult, ALU.add))
    dv3(lambda E: E.tensor_scalar(rtA2[:, 0:64], rtA2[:, 128:192], 0.5, 1.0 / 2560.0, ALU.add, ALU.mult))
    dv3(lambda E: E.tensor_copy(sb_i32[:, 0:64], rtA2[:, 0:64]))
    dv3(lambda E: E.tensor_copy(rtA2[:, 0:64], sb_i32[:, 0:64]))
    dv3(lambda E: E.scalar_tensor_tensor(rtA2[:, 192:256], rtA2[:, 0:64], -2560.0, rtA2[:, 128:192], ALU.mult, ALU.add))
    dv3(lambda E: E.tensor_scalar(sb_m1[:, 0:32], rtA2[:, 192:224], 0.0, None, ALU.is_lt))
    dv3(lambda E: E.tensor_scalar(sb_m2[:, 0:32], rtA2[:, 224:256], 0.0, None, ALU.is_lt))
    dv3(lambda E: E.tensor_sub(rtA2[:, 0:32], rtA2[:, 0:32], sb_m1[:, 0:32]))
    dv3(lambda E: E.tensor_sub(rtA2[:, 32:64], rtA2[:, 32:64], sb_m2[:, 0:32]))
    dv3(lambda E: E.scalar_tensor_tensor(rtA2[:, 192:256], rtA2[:, 0:64], -2560.0, rtA2[:, 128:192], ALU.mult, ALU.add))
    dv3(lambda E: E.tensor_scalar_mul(rtA2[:, 0:64], rtA2[:, 0:64], 20480.0))
    dv3(lambda E: E.scalar_tensor_tensor(rtA2[:, 0:64], rtA2[:, 64:128], 2560.0, rtA2[:, 0:64], ALU.mult, ALU.add))
    dv3(lambda E: E.tensor_add(rtA2[:, 0:64], rtA2[:, 0:64], rtA2[:, 192:256]))
    dv3(lambda E: E.tensor_scalar(rtA2[:, 0:64], rtA2[:, 0:64], 81919.0, None, ALU.min))
    dv3(lambda E: E.tensor_copy(sb_poi[:, :], rtA2[:, 0:64]))
    b.wait("pool", sG, v_scat)
    b.op("pool", lambda E: E.collective_compute("ReduceScatter", ALU.max, replica_groups=rg,
        ins=[d_pm.ap()[0:81920, :]], outs=[d_pmy.ap()[:, :]]), sCC, 1)
    v_rs = sCC.n

    b.wait("sync", sCC, v_rs)
    dma("sync", sb_pmy[0:80, :], d_pmy.ap().rearrange("(g p) one -> g (p one)", p=128))
    v_pmyld = sLD.n
    b.wait("pe", sLD, v_pmyld)
    b.wait("pe", sDV, v_pos)
    b.op("pe", lambda E: E.transpose(psX3[:, 0:80], sb_pmy[0:80, :], sb_identf[0:80, 0:80]), sPE)
    v_pmyt = sPE.n
    b.wait("dve", sPE, v_pmyt)
    dv3(lambda E: E.tensor_scalar(sb_pidf[:, :], psX3[:, 0:80], -1.0, 0.0, ALU.add, ALU.max))
    dv3(lambda E: E.tensor_copy(sb_pidi[:, :], sb_pidf[:, :]))
    v_pidi = sDV.n

    b.wait("pe", sDV, v_pidi)
    b.wait("pe", sLD, sLD.n)
    b.op("pe", lambda E: E.matmul(psX3[:, 0:512], sb_onesrf[:, :], sb_rows[0:1, 0:512], start=True, stop=True), sPE)
    b.op("pe", lambda E: E.matmul(psM[0][:, :], sb_onesrf[:, :], sb_rows[0:1, 512:1024], start=True, stop=True), sPE)
    v_fbb = sPE.n
    b.wait("dve", sPE, v_fbb)
    dv3(lambda E: E.tensor_copy(sb_fc2bt[:, 0:512], psX3[:, 0:512]))
    dv3(lambda E: E.tensor_copy(sb_fc2bt[:, 512:1024], psM[0][:, :]))
    v_fc2bt = sDV.n

    b.wait("pool", sDV, v_pidi)
    b.wait("pool", sCC, v_xnag)

    sGS = [b.sem(f"sGS{i}") for i in range(2)]

    def emit_gathers(g0, g1):
        for g in range(g0, g1):
            gpar = g % 2
            gsl = sb_gst[:, gpar * 1024:(gpar + 1) * 1024]
            if g >= 2:
                b.wait("pool", sGS[gpar], sGS[gpar].n)
            b.op("pool", lambda E, g=g, gsl=gsl: E.indirect_dma_start(
                out=gsl, out_offset=None,
                in_=d_xn2all.ap()[:, :], in_offset=IndirectOffsetOnAxis(sb_pidi[:, g:g + 1], 0)), sG, 16)
            M2[("gatp", g)] = sG.n
            b.wait("sync", sG, M2[("gatp", g)])
            b.op("sync", lambda E, g=g, gsl=gsl: E.dma_start(out=d_disp.ap()[g * 128:(g + 1) * 128, :], in_=gsl), sGS[gpar], 16)
            M2[("gat", g)] = (sGS[0].n, sGS[1].n)

    b.wait("pe", sG, v_fcw)
    b.wait("pe", sDV, v_fc2bt)
    NSC = CAP // 512
    emit_gathers(0, 80)
    for sc in range(NSC):
        par = sc % 2
        s0 = sc * 512
        _g4 = M2[("gat", (s0 + 511) // 128)]
        for _i in range(2):
            b.wait("sync", sGS[_i], _g4[_i])
        if sc >= 2:
            b.wait("sync", sPE, M2[("fc1", sc - 2)])
        for kc in range(8):
            b.op("sync", lambda E, kc=kc, s0=s0, par=par: E.dma_start_transpose(
                sb_xfm[:, par * 4096 + kc * 512:par * 4096 + (kc + 1) * 512],
                d_disp.ap()[s0:s0 + 512, kc * 128:(kc + 1) * 128]), sLD, 16)
        M2[("xt", sc)] = sLD.n
        b.wait("pe", sLD, M2[("xt", sc)])
        for hc in range(32):
            mb = psM[hc % 2]
            if hc >= 2:
                b.wait("pe", sAC, M2[("sil", sc, hc - 2)])
            elif sc > 0:
                b.wait("pe", sAC, M2[("sil", sc - 1, 31)])
            for kc in range(8):
                b.op("pe", lambda E, mb=mb, kc=kc, hc=hc, par=par: E.matmul(mb[:, :], WBIG[:, kc * 4096 + hc * 128:kc * 4096 + (hc + 1) * 128], sb_xfm[:, par * 4096 + kc * 512:par * 4096 + (kc + 1) * 512], start=(kc == 0), stop=(kc == 7)), sPE)
            vmm = sPE.n
            b.wait("act", sPE, vmm)
            if sc > 0 and hc == 0:
                b.wait("act", sPE, M2[("fc2", sc - 1)])
            b.op("act", lambda E, mb=mb, hc=hc: E.activation(sb_h3[:, hc * 512:(hc + 1) * 512], mb[:, :], AF.Silu, bias=sb_fc1b[:, hc:hc + 1]), sAC)
            M2[("sil", sc, hc)] = sAC.n
        M2[("fc1", sc)] = sPE.n
        b.wait("pe", sAC, M2[("sil", sc, 31)])
        b.wait("pe", sDV, v_fc2bt)
        for sg in range(4):
            par2 = sg % 2
            for dh in range(2):
                yb = psY3[dh]
                if sc + sg > 0:
                    b.wait("pe", sDV, M2.get(("yev", sc, sg - 1, dh), M2.get(("yev", sc - 1, 3, dh), 0)))
                for hc in range(32):
                    b.op("pe", lambda E, yb=yb, hc=hc, sg=sg, dh=dh: E.matmul(yb[:, :], sb_h3[:, hc * 512 + sg * 128:hc * 512 + (sg + 1) * 128], WBIG2[:, hc * 1024 + dh * 512:hc * 1024 + (dh + 1) * 512], start=(hc == 0), stop=(hc == 31)), sPE)
                vy = sPE.n
                b.wait("dve", sPE, vy)
                if sc + sg > 0:
                    _pv = M2.get(("yst", sc, sg - 2), M2.get(("yst", sc - 1, 2 + sg), 0))
                    b.wait("dve", sYS[sg % 2], _pv)
                b.op("dve", lambda E, yb=yb, dh=dh, par2=par2: E.scalar_tensor_tensor(sb_ytm[:, par2 * 1024 + dh * 512:par2 * 1024 + (dh + 1) * 512], yb[:, :], 1.0, sb_fc2bt[:, dh * 512:(dh + 1) * 512], ALU.mult, ALU.add), sDV)
                M2[("yev", sc, sg, dh)] = sDV.n
            b.wait("sync", sDV, M2[("yev", sc, sg, 1)])
            b.op("sync", lambda E, s0=s0, sg=sg, par2=par2: E.dma_start(out=d_y.ap()[s0 + sg * 128:s0 + (sg + 1) * 128, :], in_=sb_ytm[:, par2 * 1024:(par2 + 1) * 1024]), sYS[par2], 16)
            M2[("yst", sc, sg)] = sYS[par2].n
        M2[("fc2", sc)] = sPE.n
        if sc % 5 == 4:
            ych = sc // 5
            b.wait("pool", sYS[0], sYS[0].n)
            b.wait("pool", sYS[1], sYS[1].n)
            b.wait("pool", sG, sG.n)
            b.wait("pool", sGS[0], sGS[0].n)
            b.wait("pool", sGS[1], sGS[1].n)
            b.op("pool", lambda E, ych=ych: E.collective_compute("AllGather", ALU.bypass,
                replica_groups=[list(range(NCORE))],
                ins=[d_y.ap()[ych * 2560:(ych + 1) * 2560, :]],
                outs=[d_yall.ap()[ych * 20480:(ych + 1) * 20480, :]]), sCC, 1)
            M2[("yag", ych)] = sCC.n


    if debug_level >= 3:
        b.wait("sync", sYS[0], sYS[0].n)
        b.wait("sync", sYS[1], sYS[1].n)
        dma("sync", dbg["o_y3"][:, :], d_y.ap()[:, :])
        dma("sync", dbg["o_disp"][:, :], d_disp.ap()[:, :])
        b.wait("sync", sCC, v_rs)
        dma("sync", dbg["o_pmy"][:, :], d_pmy.ap()[:, :])
        dma("sync", dbg["o_posd"][:, :], sb_pos[:, :])
        dma("sync", dbg["o_gkd"][:, :], sb_gk[:, :])
        dma("sync", dbg["o_cnt"][:, :], d_cntall.ap()[:, :])
        dma("sync", dbg["o_cb2"][:, :], sb_cb2[:, :])

    b.barrier("p3", extra_waits=[("sync", sLD, sLD.n), ("pool", sG, sG.n), ("pool", sCC, sCC.n)])
    sc3.close()
    scW.close()

    sc4 = Scope(nc)
    x1b = sc4.sb("x1b", [128, 8192], F32)
    yb = sc4.sb("yb", [128, 8192], BF16)
    w_scr = sc4.sb("w_scr", [128, 4096], F32)
    wA = sc4.sb("wA", [128, 4096], F32)
    wB = sc4.sb("wB", [128, 4096], F32)
    outb = sc4.sb("outb", [128, 8192], F32)
    gam4 = sc4.sb("gam4", [128, 4096], F32)
    bet4 = sc4.sb("bet4", [128, 4096], F32)
    sb_grow = sc4.sb("sb_grow", [1, 1024], F32)
    sb_brow4 = sc4.sb("sb_brow4", [1, 1024], F32)
    c4 = sc4.sb("c4", [128, 32], F32)
    xnb4 = sc4.sb("xnb4", [128, 4096], BF16)
    sqscr = sc4.sb("sqscr", [128, 1024], F32)
    sb_eps4 = sc4.sb("sb_eps4", [128, 1], F32)
    psB4 = sc4.ps("psB4", [128, 512], F32)

    dma("sync", sb_grow[:, :], lng_in[:, :])
    dma("sync", sb_brow4[:, :], lnb_in[:, :])
    v_gb = sLD.n
    b.op("dve", lambda E: E.memset(sb_eps4[:, :], EPS), sDV)
    v_ms4 = sDV.n
    b.wait("pe", sLD, v_gb)
    b.wait("pe", sDV, v_ms4)
    for rep in range(4):
        for half in range(2):
            b.op("pe", lambda E, half=half: E.matmul(psB4[:, :], sb_onesrf[:, :], sb_grow[0:1, half * 512:(half + 1) * 512], start=True, stop=True), sPE)
            v = sPE.n
            b.wait("dve", sPE, v)
            b.op("dve", lambda E, rep=rep, half=half: E.tensor_copy(gam4[:, rep * 1024 + half * 512:rep * 1024 + (half + 1) * 512], psB4[:, :]), sDV)
            b.wait("pe", sDV, sDV.n)
            b.op("pe", lambda E, half=half: E.matmul(psB4[:, :], sb_onesrf[:, :], sb_brow4[0:1, half * 512:(half + 1) * 512], start=True, stop=True), sPE)
            v = sPE.n
            b.wait("dve", sPE, v)
            b.op("dve", lambda E, rep=rep, half=half: E.tensor_copy(bet4[:, rep * 1024 + half * 512:rep * 1024 + (half + 1) * 512], psB4[:, :]), sDV)
            b.wait("pe", sDV, sDV.n)
    v_gb4 = sDV.n
    b.wait("pool", sDV, v_gb4)

    NB = 8
    GV = lambda t: t.rearrange("p (g d) -> p g d", g=4)

    def ln_stats(srcv, sqkey=None):
        b.op("dve", lambda E, srcv=srcv: E.tensor_reduce(c4[:, 0:4], GV(srcv), axis=AX.X, op=ALU.add), sDV)
        vr1 = sDV.n
        for g in range(4):
            b.op("act", lambda E, srcv=srcv, g=g: E.activation(sqscr[:, :], srcv[:, g * 1024:(g + 1) * 1024], AF.Square, accum_out=c4[:, 4 + g:5 + g]), sAC)
        vsq = sAC.n
        if sqkey is not None:
            M2[sqkey] = vsq
        b.wait("dve", sDV, vr1)
        b.op("dve", lambda E: E.tensor_scalar_mul(c4[:, 8:12], c4[:, 0:4], INV_D), sDV)
        b.wait("dve", sDV, sDV.n)
        b.op("dve", lambda E: E.tensor_mul(c4[:, 12:16], c4[:, 8:12], c4[:, 8:12]), sDV)
        b.wait("dve", sAC, vsq)
        b.wait("dve", sDV, sDV.n)
        b.op("dve", lambda E: E.scalar_tensor_tensor(c4[:, 4:8], c4[:, 4:8], INV_D, c4[:, 12:16], ALU.mult, ALU.subtract), sDV)
        vvar = sDV.n
        b.wait("act", sDV, vvar)
        b.op("act", lambda E: E.activation(c4[:, 16:20], c4[:, 4:8], AF.Ln, bias=sb_eps4[:, :]), sAC)
        b.wait("act", sAC, sAC.n)
        b.op("act", lambda E: E.activation(c4[:, 16:20], c4[:, 16:20], AF.Exp, scale=-0.5), sAC)
        vrstd = sAC.n
        b.wait("dve", sAC, vrstd)
        b.op("dve", lambda E: E.scalar_tensor_tensor(c4[:, 12:16], c4[:, 8:12], -1.0, c4[:, 16:20], ALU.mult, ALU.mult), sDV)
        b.wait("dve", sDV, sDV.n)
        return sDV.n

    def ln_apply(srcv, dstv):
        for g in range(4):
            b.op("dve", lambda E, srcv=srcv, dstv=dstv, g=g: E.tensor_scalar(dstv[:, g * 1024:(g + 1) * 1024], srcv[:, g * 1024:(g + 1) * 1024], c4[:, 16 + g:17 + g], c4[:, 12 + g:13 + g], ALU.mult, ALU.add), sDV)
        return sDV.n

    for bi in range(NB):
        par = bi % 2
        x1v = x1b[:, par * 4096:(par + 1) * 4096]
        outv = outb[:, par * 4096:(par + 1) * 4096]
        if bi >= 2:
            b.wait("sync", sPO, M2[("fin", bi - 2)])
            b.wait("sync", sOS[par], M2[("ost", bi - 2)])
        b.op("sync", lambda E, bi=bi, x1v=x1v: E.dma_start(
            out=GV(x1v),
            in_=d_x1t.ap().rearrange("(G p) d -> p G d", p=128)[:, 4 * bi:4 * bi + 4, :]), sLD, 16)
        M2[("x1ld", bi)] = sLD.n
        if bi > 0:
            b.wait("sync", sDV, M2[("g1", bi - 1)])
        b.op("sync", lambda E, bi=bi: E.dma_start(
            out=GV(xnb4),
            in_=d_xn2bf.ap().rearrange("(G p) d -> p G d", p=128)[:, 4 * bi:4 * bi + 4, :]), sLD, 16)
        M2[("xnld", bi)] = sLD.n
        if bi > 0:
            b.wait("pool", sDV, M2[("gmul", bi - 1)])
        for g in range(4):
            tg = 4 * bi + g
            b.op("pool", lambda E, tg=tg, g=g: E.indirect_dma_start(
                out=yb[:, g * 1024:(g + 1) * 1024], out_offset=None, in_=d_yall.ap()[:, :],
                in_offset=IndirectOffsetOnAxis(sb_poi[:, tg:tg + 1], 0)), sG, 16)
            b.op("pool", lambda E, tg=tg, g=g: E.indirect_dma_start(
                out=yb[:, 4096 + g * 1024:4096 + (g + 1) * 1024], out_offset=None, in_=d_yall.ap()[:, :],
                in_offset=IndirectOffsetOnAxis(sb_poi[:, 32 + tg:33 + tg], 0)), sG, 16)
        M2[("ygat", bi)] = sG.n
        b.wait("dve", sG, M2[("ygat", bi)])
        b.wait("dve", sLD, M2[("xnld", bi)])
        if bi >= 1:
            b.wait("dve", sAC, M2[("sqA", bi - 1)])
        for g in range(4):
            tg = 4 * bi + g
            b.op("dve", lambda E, g=g, tg=tg: E.scalar_tensor_tensor(wA[:, g * 1024:(g + 1) * 1024], yb[:, g * 1024:(g + 1) * 1024], sb_gk[:, tg:tg + 1], xnb4[:, g * 1024:(g + 1) * 1024], ALU.mult, ALU.add), sDV)
        M2[("g1", bi)] = sDV.n
        if bi >= 1:
            b.wait("dve", sAC, M2[("sqB", bi - 1)])
            b.wait("dve", sPO, M2[("ob", bi - 1)])
        for g in range(4):
            tg = 4 * bi + g
            b.op("dve", lambda E, g=g, tg=tg: E.tensor_scalar_mul(wB[:, g * 1024:(g + 1) * 1024], yb[:, 4096 + g * 1024:4096 + (g + 1) * 1024], sb_gk[:, 32 + tg:33 + tg]), sDV)
        M2[("gmul", bi)] = sDV.n
        b.op("dve", lambda E: E.tensor_add(wA[:, :], wA[:, :], wB[:, :]), sDV)
        M2[("res1", bi)] = sDV.n
        b.wait("act", sDV, M2[("res1", bi)])
        ln_stats(wA, ("sqA", bi))
        ln_apply(wA, wB)
        b.op("dve", lambda E: E.tensor_mul(wB[:, :], wB[:, :], gam4[:, :]), sDV)
        v_gm = sDV.n
        b.wait("pool", sDV, v_gm)
        b.op("pool", lambda E: E.tensor_add(wB[:, :], wB[:, :], bet4[:, :]), sPO)
        M2[("ob", bi)] = sPO.n
        b.wait("dve", sPO, M2[("ob", bi)])
        b.wait("act", sPO, M2[("ob", bi)])
        ln_stats(wB, ("sqB", bi))
        b.wait("dve", sLD, M2[("x1ld", bi)])
        if bi >= 1:
            b.wait("dve", sPO, M2[("fin", bi - 1)])
        ln_apply(wB, w_scr)
        v_a3 = sDV.n
        b.wait("pool", sDV, v_a3)
        if bi >= 2:
            b.wait("pool", sOS[par], M2[("ost", bi - 2)])
        b.op("pool", lambda E, outv=outv, x1v=x1v: E.tensor_add(outv, w_scr[:, :], x1v), sPO)
        M2[("fin", bi)] = sPO.n
        b.wait("sync", sPO, M2[("fin", bi)])
        b.op("sync", lambda E, bi=bi, outv=outv: E.dma_start(
            out=out.rearrange("(G p) d -> p G d", p=128)[:, 4 * bi:4 * bi + 4, :],
            in_=GV(outv)), sOS[par], 16)
        M2[("ost", bi)] = sOS[par].n

    b.wait("sync", sOS[0], sOS[0].n)
    b.wait("sync", sOS[1], sOS[1].n)
    b.wait("pool", sOS[0], sOS[0].n)
    b.wait("pool", sOS[1], sOS[1].n)

    return nc, b, ap, dbg, out


def finalize(nc, b):
    with nc.Block() as blk:
        @blk.tensor
        def _(E):
            for f in b.streams["pe"]:
                f(E)
        @blk.vector
        def _(E):
            for f in b.streams["dve"]:
                f(E)
        @blk.scalar
        def _(E):
            for f in b.streams["act"]:
                f(E)
        @blk.gpsimd
        def _(E):
            for f in b.streams["pool"]:
                f(E)
        @blk.sync
        def _(E):
            for f in b.streams["sync"]:
                f(E)
    return nc


def host_inputs(core, full):
    x = full["L0_attn_x3d"].reshape(NCORE * NLOC, D)
    m = {}
    m["xT"] = np.ascontiguousarray(x[core * NLOC:(core + 1) * NLOC].T)
    m["BreT"] = np.ascontiguousarray(full["L0_attn_B_re"].T)
    m["BimT"] = np.ascontiguousarray(full["L0_attn_B_im"].T)
    m["CreT"] = np.ascontiguousarray(full["L0_attn_C_re"].T)
    m["CimT"] = np.ascontiguousarray(full["L0_attn_C_im"].T)
    m["DT"] = np.ascontiguousarray(full["L0_attn_D_mat"].T)
    m["W1"] = np.ascontiguousarray(full["L0_attn_W1"])
    m["W2"] = np.ascontiguousarray(full["L0_attn_W2"])
    m["wg"] = np.ascontiguousarray(full["L0_ffn_wg"])
    m["fc1T"] = np.ascontiguousarray(full["L0_ffn_fc1_w"][core].T)
    m["fc2"] = np.ascontiguousarray(full["L0_ffn_fc2_w"][core])
    m["fc1b"] = np.ascontiguousarray(full["L0_ffn_fc1_b"][core].reshape(32, 128).T)
    m["fc2b"] = np.ascontiguousarray(full["L0_ffn_fc2_b"][core].reshape(1, D))
    m["lng"] = np.ascontiguousarray(full["L0_ffn_ln_gamma"].reshape(1, D))
    m["lnb"] = np.ascontiguousarray(full["L0_ffn_ln_beta"].reshape(1, D))
    m["nu"] = np.ascontiguousarray(full["L0_attn_nu_log"].reshape(2, 128).T)
    m["th"] = np.ascontiguousarray(full["L0_attn_theta_log"].reshape(2, 128).T)
    m["rl"] = np.asarray(full["L0_attn_res_logit"], np.float32).reshape(1, 1)
    cm = np.zeros((16, 2), np.float32)
    for cp in range(8):
        for jp in range(2):
            rr = 2 * cp + jp
            if jp == 0 and cp < core:
                cm[rr, 0] = 1.0
            if (jp == 1 and cp < core) or jp == 0:
                cm[rr, 1] = 1.0
    m["cmask"] = np.ascontiguousarray(cm)
    m["tok0"] = np.asarray([[core * 1024]], np.float32)
    m["ident_f"] = np.eye(128, dtype=np.float32)
    m["ident_b"] = np.eye(128, dtype=BF)
    m["tri_b"] = np.triu(np.ones((128, 128), dtype=np.float32), 1).astype(BF)
    return m




_BUILD_CACHE = {}


def kernel(**inputs):
    full = {k: np.asarray(v) for k, v in inputs.items()}
    if "prog" not in _BUILD_CACHE:
        nc, b, ap, dbg, out_ap = build_program(debug_level=0)
        nc = finalize(nc, b)
        _BUILD_CACHE["prog"] = nc
    nc = _BUILD_CACHE["prog"]
    in_maps = [host_inputs(c, full) for c in range(NCORE)]
    res = run_bass_kernel_spmd(nc, in_maps, core_ids=list(range(NCORE)))
    out = np.concatenate([res.results[c]["out"] for c in range(NCORE)], 0)
    return out.reshape(16, 2048, D)


# version 17
# speedup vs baseline: 1.0554x; 1.0386x over previous
import sys, time
import numpy as np
import ml_dtypes
import concourse.bass as bass
import concourse.mybir as mybir
from concourse.bass import IndirectOffsetOnAxis
from concourse.bass_utils import run_bass_kernel_spmd

F32 = mybir.dt.float32
BF16 = mybir.dt.bfloat16
I32 = mybir.dt.int32
AF = mybir.ActivationFunctionType
ALU = mybir.AluOpType
AX = mybir.AxisListType
BF = ml_dtypes.bfloat16

NCORE = 8
NLOC = 4096
D = 1024
DS = 256
H = 4096
E = 8
CAP = 10240
T = 512
NCH = NLOC // T
SUB = 128
PMROWS = 82048
C1x2 = 1.5957691216057308
SQ2C2 = 0.2671232181501769
EPS = 1e-5
INV_D = 1.0 / D
TWO_PI = 6.283185307179586
INV_2PI = 1.0 / TWO_PI
PI = 3.141592653589793

ENGS = ("pe", "dve", "act", "pool", "sync")


class Sem:
    __slots__ = ("h", "n")
    def __init__(self, h):
        self.h = h
        self.n = 0


class Builder:
    def __init__(self, nc):
        self.nc = nc
        self.streams = {e: [] for e in ENGS}
        self._guards = []
        self.sems = {}

    def sem(self, name):
        if name not in self.sems:
            g = self.nc.semaphore(name)
            self.sems[name] = Sem(g.__enter__())
            self._guards.append(g)
        return self.sems[name]

    def op(self, eng, fn, inc=None, incv=1):
        if inc is not None:
            inc.n += incv
            self.streams[eng].append(lambda E, fn=fn, s=inc, d=incv: fn(E).then_inc(s.h, d))
        else:
            self.streams[eng].append(lambda E, fn=fn: fn(E))

    def wait(self, eng, s, v=None):
        v = s.n if v is None else v
        self.streams[eng].append(lambda E, s=s, v=v: E.wait_ge(s.h, v))

    def barrier(self, name, extra_waits=()):
        s = self.sem("bar_" + name)
        for eng, sem, val in extra_waits:
            self.wait(eng, sem, val)
        tgt = s.n + len(ENGS)
        for e in ENGS:
            self.op(e, lambda E: E.sem_inc(s.h, 1), None)
        s.n = tgt
        for e in ENGS:
            self.wait(e, s, tgt)


class Scope:
    def __init__(self, nc):
        self.nc = nc
        self.gs = []

    def sb(self, name, shape, dt):
        g = self.nc.sbuf_tensor(name, shape, dt)
        h = g.__enter__()
        self.gs.append(g)
        return h

    def ps(self, name, shape, dt):
        g = self.nc.psum_tensor(name, shape, dt)
        h = g.__enter__()
        self.gs.append(g)
        return h

    def close(self):
        for g in reversed(self.gs):
            g.__exit__(None, None, None)
        self.gs = []


def build_program(debug_level=0):
    nc = bass.Bass("TRN2", target_bir_lowering=False)
    b = Builder(nc)
    ap = {}

    def din(name, shape, dt=F32):
        ap[name] = nc.dram_tensor(name, list(shape), dt, kind="ExternalInput").ap()
        return ap[name]

    xT = din("xT", [D, NLOC])
    BreT = din("BreT", [D, DS]); BimT = din("BimT", [D, DS])
    CreT = din("CreT", [DS, D]); CimT = din("CimT", [DS, D])
    DT_in = din("DT", [D, D])
    W1_in = din("W1", [D, H]); W2_in = din("W2", [H, D])
    wg_in = din("wg", [D, E])
    fc1T_in = din("fc1T", [D, H]); fc2_in = din("fc2", [H, D])
    fc1b_in = din("fc1b", [128, 32])
    fc2b_in = din("fc2b", [1, D])
    lng_in = din("lng", [1, D]); lnb_in = din("lnb", [1, D])
    nu_in = din("nu", [128, 2]); th_in = din("th", [128, 2])
    rl_in = din("rl", [1, 1])
    cmask_in = din("cmask", [16, 2])
    tok0_in = din("tok0", [1, 1])

    out = nc.dram_tensor("out", [NLOC, D], F32, kind="ExternalOutput").ap()

    d_h = nc.dram_tensor("d_h", [H, NLOC], BF16, kind="Internal")
    d_w2bf = nc.dram_tensor("d_w2bf", [4096, 1024], BF16, kind="Internal")
    d_x1t = nc.dram_tensor("d_x1t", [NLOC, D], F32, kind="Internal")
    d_xn2bf = nc.dram_tensor("d_xn2bf", [NLOC, D], BF16, kind="Internal")
    d_xn2all = nc.dram_tensor("d_xn2all", [NCORE * NLOC, D], BF16, kind="Internal")
    d_cnt = nc.dram_tensor("d_cnt", [2, 8], F32, kind="Internal")
    d_cntall = nc.dram_tensor("d_cntall", [16, 8], F32, kind="Internal")
    d_pm = nc.dram_tensor("d_pm", [PMROWS, 1], F32, kind="Internal")
    d_pmy = nc.dram_tensor("d_pmy", [CAP, 1], F32, kind="Internal")
    d_disp = nc.dram_tensor("d_disp", [CAP, D], BF16, kind="Internal")
    d_y = nc.dram_tensor("d_y", [CAP, D], BF16, kind="Internal")
    d_yall = nc.dram_tensor("d_yall", [E * CAP, D], BF16, kind="Internal")
    d_fb = nc.dram_tensor("d_fb", [8, 64], F32, kind="Internal")

    dbg = {}
    if debug_level >= 1:
        dbg["o_xn"] = nc.dram_tensor("o_xn", [D, NLOC], BF16, kind="ExternalOutput").ap()
        dbg["o_h"] = nc.dram_tensor("o_h", [H, NLOC], BF16, kind="ExternalOutput").ap()
        dbg["o_bu"] = nc.dram_tensor("o_bu", [128, 2048], BF16, kind="ExternalOutput").ap()
        dbg["o_st"] = nc.dram_tensor("o_st", [128, 2048], BF16, kind="ExternalOutput").ap()
        dbg["o_y"] = nc.dram_tensor("o_y", [128, 4096], BF16, kind="ExternalOutput").ap()
        dbg["o_tab"] = nc.dram_tensor("o_tab", [128, 1032], F32, kind="ExternalOutput").ap()
    if debug_level >= 2:
        dbg["o_x1t"] = nc.dram_tensor("o_x1t", [NLOC, D], F32, kind="ExternalOutput").ap()
        dbg["o_xn2bf"] = nc.dram_tensor("o_xn2bf", [NLOC, D], BF16, kind="ExternalOutput").ap()
        dbg["o_rt"] = nc.dram_tensor("o_rt", [128, 320], F32, kind="ExternalOutput").ap()
        dbg["o_ao"] = nc.dram_tensor("o_ao", [128, 4096], F32, kind="ExternalOutput").ap()
    if debug_level >= 3:
        dbg["o_y3"] = nc.dram_tensor("o_y3", [CAP, D], BF16, kind="ExternalOutput").ap()
        dbg["o_disp"] = nc.dram_tensor("o_disp", [CAP, D], BF16, kind="ExternalOutput").ap()
        dbg["o_pmy"] = nc.dram_tensor("o_pmy", [CAP, 1], F32, kind="ExternalOutput").ap()
        dbg["o_posd"] = nc.dram_tensor("o_posd", [128, 64], F32, kind="ExternalOutput").ap()
        dbg["o_cnt"] = nc.dram_tensor("o_cnt", [16, 8], F32, kind="ExternalOutput").ap()
        dbg["o_cb2"] = nc.dram_tensor("o_cb2", [128, 2], F32, kind="ExternalOutput").ap()
        dbg["o_gkd"] = nc.dram_tensor("o_gkd", [128, 64], F32, kind="ExternalOutput").ap()
        dbg["o_xn2"] = nc.dram_tensor("o_xn2", [128, 4096], F32, kind="ExternalOutput").ap()
        dbg["o_z0"] = nc.dram_tensor("o_z0", [128, 512], F32, kind="ExternalOutput").ap()
        dbg["o_hrb"] = nc.dram_tensor("o_hrb", [128, 16384], BF16, kind="ExternalOutput").ap()
        dbg["o_w2"] = nc.dram_tensor("o_w2", [128, 2048], BF16, kind="ExternalOutput").ap()

    ident_f = din("ident_f", [128, 128], F32)
    ident_b = din("ident_b", [128, 128], BF16)
    tri_b = din("tri_b", [128, 128], BF16)

    A = nc.alloc_sbuf_tensor
    sb_identf = A("sb_identf", [128, 128], F32)
    sb_identb = A("sb_identb", [128, 128], BF16)
    sb_trib = A("sb_trib", [128, 128], BF16)
    sb_onescb = A("sb_onescb", [128, 1], BF16)
    sb_onesrf = A("sb_onesrf", [1, 128], F32)
    sb_ones128 = A("sb_ones128", [128, 128], F32)
    sb_eps = A("sb_eps", [1, 1], F32)
    sb_iota8 = A("sb_iota8", [128, 8], F32)
    sb_i32 = A("sb_i32", [128, 256], I32)
    sb_wg = A("sb_wg", [128, 64], BF16)
    sb_fc1b = A("sb_fc1b", [128, 32], F32)
    sb_sig = A("sb_sig", [128, 1], F32)
    sb_rl = A("sb_rl", [1, 1], F32)
    sb_tok0 = A("sb_tok0", [128, 1], F32)
    sb_cmask = A("sb_cmask", [16, 2], F32)
    sb_rows = A("sb_rows", [1, 1536], F32)
    sb_oh1 = A("sb_oh1", [128, 256], F32)
    sb_oh2 = A("sb_oh2", [128, 256], F32)
    sb_i1 = A("sb_i1", [128, 32], F32)
    sb_i2 = A("sb_i2", [128, 32], F32)
    sb_g1 = A("sb_g1", [128, 32], F32)
    sb_g2 = A("sb_g2", [128, 32], F32)
    sb_pre1 = A("sb_pre1", [128, 32], F32)
    sb_pre2 = A("sb_pre2", [128, 32], F32)
    sb_ct = A("sb_ct", [128, 64], F32)
    sb_tb = A("sb_tb", [128, 64], F32)
    sb_pos = A("sb_pos", [128, 64], F32)
    sb_gk = A("sb_gk", [128, 64], F32)
    sb_poi = A("sb_poi", [128, 64], I32)
    sb_tokid = A("sb_tokid", [128, 32], F32)
    sb_brow = A("sb_brow", [1, 512], F32)
    sb_iotaE = A("sb_iotaE", [128, 256], F32)
    sb_m1 = A("sb_m1", [128, 32], F32)
    sb_m2 = A("sb_m2", [128, 32], F32)
    rtA2 = A("rtA2", [128, 256], F32)
    sb_zero = A("sb_zero", [128, 129], F32)
    scW = Scope(nc)
    WBIG = scW.sb("WBIG", [128, 32768], BF16)
    WBIG2 = scW.sb("WBIG2", [128, 32768], BF16)
    XB, XSQ, STB, YB, HST = 16384, 17408, 18432, 20480, 24576

    sc1 = Scope(nc)
    PS = [sc1.ps(f"ps{i}", [128, 512], F32) for i in range(8)]
    tabPr = sc1.sb("tabPr", [128, 256], F32); tabPi = sc1.sb("tabPi", [128, 256], F32)
    tabQr = sc1.sb("tabQr", [128, 256], F32); tabQi = sc1.sb("tabQi", [128, 256], F32)
    sb_lam = sc1.sb("sb_lam", [128, 4], F32)
    sb_nt = sc1.sb("sb_nt", [128, 8], F32)
    sb_carry = sc1.sb("sb_carry", [128, 4], F32)
    sb_cs = sc1.sb("sb_cs", [128, 4], F32)
    sb_cst = sc1.sb("sb_cst", [128, 4], F32)
    scA_r = sc1.sb("scA_r", [128, 256], F32); scA_i = sc1.sb("scA_i", [128, 256], F32)
    scS_r = sc1.sb("scS_r", [128, 256], F32); scS_i = sc1.sb("scS_i", [128, 256], F32)
    scT_r = sc1.sb("scT_r", [128, 256], F32); scT_i = sc1.sb("scT_i", [128, 256], F32)
    sctmp = sc1.sb("sctmp", [128, 256], F32)
    SC0, SC1, SC2, SC3, SC4, SC5 = scA_r, scA_i, scS_r, scS_i, scT_r, scT_i
    sb_x = sc1.sb("sb_x", [128, 8192], BF16)
    sb_xnb1 = sc1.sb("sb_xnb1", [128, 4096], BF16)
    sb_st1 = sc1.sb("sb_st1", [128, 2048], BF16)
    sb_nmb = sc1.sb("sb_nmb", [128, 512], BF16)
    sb_rb = sc1.sb("sb_rb", [128, 512], BF16)
    sb_xnb = sc1.sb("sb_xnb", [128, 4096], BF16)
    sb_bu = sc1.sb("sb_bu", [128, 2048], BF16)
    sb_q = sc1.sb("sb_q", [128, 1024], BF16)
    sb_t2 = sc1.sb("sb_t2", [128, 1024], BF16)
    sb_sg2 = sb_q


    sLD = b.sem("sLD"); sG = b.sem("sG")
    sH0 = b.sem("sH0"); sW2A = b.sem("sW2A"); sW2B = b.sem("sW2B"); sW2C = b.sem("sW2C")
    sH0S = [b.sem("sH0S0"), b.sem("sH0S1")]
    sPE = b.sem("sPE"); sDV = b.sem("sDV"); sAC = b.sem("sAC"); sPO = b.sem("sPO")
    sCC = b.sem("sCC"); sST = b.sem("sST"); sXL = b.sem("sXL")
    sST1 = b.sem("sST1"); sST2 = b.sem("sST2"); sHS = [b.sem("sHS0"), b.sem("sHS1")]
    sOS = [b.sem("sOS0"), b.sem("sOS1")]
    sYS = [b.sem("sYS0"), b.sem("sYS1")]

    def dma(eng, outap, inap, sem=None):
        sem = sem if sem is not None else sLD
        b.op(eng, lambda E, o=outap, i=inap: E.dma_start(out=o, in_=i), sem, 16)

    dma("sync", sb_identf[:, :], ident_f[:, :])
    dma("sync", sb_identb[:, :], ident_b[:, :])
    dma("sync", sb_trib[:, :], tri_b[:, :])
    dma("sync", sb_nt[:, 0:2], nu_in[:, :])
    dma("sync", sb_nt[:, 2:4], th_in[:, :])
    dma("sync", sb_fc1b[:, :], fc1b_in[:, :])
    dma("sync", sb_rl[:, :], rl_in[:, :])
    dma("sync", sb_cmask[:, :], cmask_in[:, :])
    dma("sync", sb_tok0[0:1, :], tok0_in[:, :])
    v_consts = sLD.n

    for kc in range(8):
        dma("pool", WBIG2[:, kc * 256:(kc + 1) * 256], BreT[128 * kc:128 * (kc + 1), :], sG)
        dma("pool", WBIG2[:, 2048 + kc * 256:2048 + (kc + 1) * 256], BimT[128 * kc:128 * (kc + 1), :], sG)
    for cc in range(2):
        dma("pool", WBIG2[:, 4096 + cc * 1024:4096 + (cc + 1) * 1024], CreT[128 * cc:128 * (cc + 1), :], sG)
        dma("pool", WBIG2[:, 6144 + cc * 1024:6144 + (cc + 1) * 1024], CimT[128 * cc:128 * (cc + 1), :], sG)
    for kc in range(8):
        dma("pool", WBIG2[:, 8192 + kc * 1024:8192 + (kc + 1) * 1024], DT_in[128 * kc:128 * (kc + 1), :], sG)
    for kc in range(8):
        dma("pool", sb_wg[:, kc * 8:(kc + 1) * 8], wg_in[128 * kc:128 * (kc + 1), :], sG)
    v_bcd = sG.n
    for kc in range(8):
        dma("pool", WBIG[:, kc * 4096:(kc + 1) * 4096], W1_in[128 * kc:128 * (kc + 1), :], sG)
    v_w1 = sG.n

    b.op("dve", lambda E: E.memset(sb_onescb[:, :], 1.0), sDV)
    b.op("dve", lambda E: E.memset(sb_onesrf[:, :], 1.0), sDV)
    b.op("dve", lambda E: E.memset(sb_ones128[:, :], 1.0), sDV)
    b.op("dve", lambda E: E.memset(sb_eps[:, :], EPS), sDV)
    b.op("dve", lambda E: E.memset(sb_zero[:, :], 0.0), sDV)
    b.op("dve", lambda E: E.memset(sb_carry[:, :], 0.0), sDV)
    v_memset = sDV.n

    b.wait("sync", sDV, v_memset)
    _pmv = d_pm.ap().rearrange("(r p) one -> r (p one)", p=128)
    for _i in range(5):
        dma("sync", _pmv[_i * 128:(_i + 1) * 128, :], sb_zero[:, 0:128])
    dma("sync", _pmv[640:641, :], sb_zero[0:1, 0:128])

    b.wait("pool", sG, v_w1)
    b.op("pool", lambda E: E.iota(sb_i32[:, 0:8], pattern=[[1, 8]], base=0, channel_multiplier=0), sPO)
    b.op("pool", lambda E: E.iota(sb_i32[:, 8:40], pattern=[[8192, 4], [128, 8]], base=1, channel_multiplier=1), sPO)
    b.op("pool", lambda E: E.iota(sb_i32[:, 128:256], pattern=[[1, 128]], base=0, channel_multiplier=0), sPO)
    v_iota = sPO.n
    b.wait("dve", sPO, v_iota)
    b.op("dve", lambda E: E.tensor_copy(sb_iota8[:, :], sb_i32[:, 0:8]), sDV)
    b.op("dve", lambda E: E.tensor_copy(sb_tokid[:, :], sb_i32[:, 8:40]), sDV)
    v_iota_f = sDV.n
    b.op("pool", lambda E: E.iota(sb_i32[:, 40:72], pattern=[[0, 4], [1, 8]], base=0, channel_multiplier=0), sPO)
    b.wait("dve", sPO, sPO.n)
    for _q in range(8):
        b.op("dve", lambda E, _q=_q: E.tensor_copy(sb_iotaE[:, _q * 32:(_q + 1) * 32], sb_i32[:, 40:72]), sDV)
    v_iotaE = sDV.n

    b.wait("dve", sG, v_w1)
    b.op("dve", lambda E: E.tensor_scalar_mul(WBIG2[:, 6144:8192], WBIG2[:, 6144:8192], -1.0), sDV)
    v_cim = sDV.n

    b.wait("act", sLD, v_consts)
    b.op("act", lambda E: E.activation(sb_nt[:, 4:6], sb_nt[:, 0:2], AF.Exp), sAC)
    b.op("act", lambda E: E.activation(sb_nt[:, 6:8], sb_nt[:, 2:4], AF.Exp), sAC)
    v1 = sAC.n
    b.op("act", lambda E: E.activation(sb_lam[:, 0:2], sb_nt[:, 4:6], AF.Exp, scale=-2.0), sAC)
    v2 = sAC.n
    b.wait("dve", sAC, v2)
    b.op("dve", lambda E: E.tensor_scalar(sb_lam[:, 0:2], sb_lam[:, 0:2], -1.0, 1.0, ALU.mult, ALU.add), sDV)
    v3 = sDV.n
    b.wait("act", sDV, v3)
    b.op("act", lambda E: E.activation(sb_lam[:, 2:4], sb_lam[:, 0:2], AF.Ln), sAC)
    v4 = sAC.n
    b.wait("act", sAC, v4)
    b.op("act", lambda E: E.activation(sb_nt[:, 0:2], sb_lam[:, 2:4], AF.Exp, scale=0.5), sAC)
    b.op("act", lambda E: E.activation(sb_lam[:, 0:2], sb_nt[:, 4:6], AF.Exp, scale=-1.0), sAC)
    v_gam = sAC.n

    b.wait("dve", sPO, v_iota)
    b.op("dve", lambda E: E.tensor_copy(SC0[:, 0:128], sb_i32[:, 128:256]), sDV)
    b.op("dve", lambda E: E.tensor_copy(SC0[:, 128:256], sb_i32[:, 128:256]), sDV)
    v_tau = sDV.n
    b.wait("dve", sAC, v_gam)
    for cc in range(2):
        b.op("dve", lambda E, cc=cc: E.tensor_scalar_mul(SC1[:, cc * 128:(cc + 1) * 128], SC0[:, cc * 128:(cc + 1) * 128], sb_nt[:, 6 + cc:7 + cc]), sDV)
    b.wait("dve", sDV, sDV.n)
    for cc in range(2):
        b.op("dve", lambda E, cc=cc: E.tensor_scalar_mul(SC0[:, cc * 128:(cc + 1) * 128], SC0[:, cc * 128:(cc + 1) * 128], sb_nt[:, 4 + cc:5 + cc]), sDV)
    v_T = sDV.n
    b.wait("dve", sDV, v_T)
    b.op("dve", lambda E: E.tensor_scalar_mul(SC4[:, :], SC1[:, :], INV_2PI), sDV)
    b.wait("dve", sDV, sDV.n)
    b.op("dve", lambda E: E.tensor_copy(sb_i32[:, 0:256], SC4[:, :]), sDV)
    b.wait("dve", sDV, sDV.n)
    b.op("dve", lambda E: E.tensor_copy(SC4[:, :], sb_i32[:, 0:256]), sDV)
    b.wait("dve", sDV, sDV.n)
    b.op("dve", lambda E: E.scalar_tensor_tensor(SC1[:, :], SC4[:, :], -TWO_PI, SC1[:, :], ALU.mult, ALU.add), sDV)
    b.wait("dve", sDV, sDV.n)

    def wrap_once(dst, src):
        b.op("dve", lambda E: E.tensor_scalar(SC4[:, :], src, PI, None, ALU.is_gt), sDV)
        b.wait("dve", sDV, sDV.n)
        b.op("dve", lambda E: E.scalar_tensor_tensor(dst, SC4[:, :], -TWO_PI, src, ALU.mult, ALU.add), sDV)
        b.wait("dve", sDV, sDV.n)
        b.op("dve", lambda E: E.tensor_scalar(SC4[:, :], dst, -PI, None, ALU.is_lt), sDV)
        b.wait("dve", sDV, sDV.n)
        b.op("dve", lambda E: E.scalar_tensor_tensor(dst, SC4[:, :], TWO_PI, dst, ALU.mult, ALU.add), sDV)
        b.wait("dve", sDV, sDV.n)

    wrap_once(SC1[:, :], SC1[:, :])
    b.wait("act", sDV, sDV.n)
    b.op("act", lambda E: E.activation(SC2[:, :], SC1[:, :], AF.Sin), sAC)
    v_sin = sAC.n
    b.wait("dve", sAC, v_sin)
    b.op("dve", lambda E: E.tensor_scalar(SC1[:, :], SC1[:, :], PI / 2.0, None, ALU.add), sDV)
    b.wait("dve", sDV, sDV.n)
    wrap_once(SC1[:, :], SC1[:, :])
    b.wait("act", sDV, sDV.n)
    b.op("act", lambda E: E.activation(SC3[:, :], SC1[:, :], AF.Sin), sAC)
    b.op("act", lambda E: E.activation(SC4[:, :], SC0[:, :], AF.Exp), sAC)
    b.op("act", lambda E: E.activation(SC5[:, :], SC0[:, :], AF.Exp, scale=-1.0), sAC)
    v_trig = sAC.n
    b.wait("dve", sAC, v_trig)
    for cc in range(2):
        b.op("dve", lambda E, cc=cc: E.tensor_scalar_mul(SC4[:, cc * 128:(cc + 1) * 128], SC4[:, cc * 128:(cc + 1) * 128], sb_nt[:, cc:cc + 1]), sDV)
    b.wait("dve", sDV, sDV.n)
    b.op("dve", lambda E: E.tensor_mul(tabPr[:, :], SC4[:, :], SC3[:, :]), sDV)
    b.op("dve", lambda E: E.scalar_tensor_tensor(tabPi[:, :], SC2[:, :], -1.0, SC4[:, :], ALU.mult, ALU.mult), sDV)
    b.op("dve", lambda E: E.tensor_mul(tabQr[:, :], SC5[:, :], SC3[:, :]), sDV)
    b.op("dve", lambda E: E.tensor_mul(tabQi[:, :], SC5[:, :], SC2[:, :]), sDV)
    b.wait("dve", sDV, sDV.n)
    for cc in range(2):
        b.op("dve", lambda E, cc=cc: E.tensor_mul(sb_lam[:, 2 + cc:3 + cc], sb_lam[:, cc:cc + 1],
                                                  SC2[:, cc * 128 + 1:cc * 128 + 2]), sDV)
    b.wait("dve", sDV, sDV.n)
    for cc in range(2):
        b.op("dve", lambda E, cc=cc: E.tensor_mul(sb_lam[:, cc:cc + 1], sb_lam[:, cc:cc + 1],
                                                  SC3[:, cc * 128 + 1:cc * 128 + 2]), sDV)
    v_tables = sDV.n
    if debug_level >= 1:
        b.wait("sync", sDV, v_tables)
        dma("sync", dbg["o_tab"][:, 0:256], tabPr[:, :])
        dma("sync", dbg["o_tab"][:, 256:512], tabPi[:, :])
        dma("sync", dbg["o_tab"][:, 512:768], tabQr[:, :])
        dma("sync", dbg["o_tab"][:, 768:1024], tabQi[:, :])
        dma("sync", dbg["o_tab"][:, 1024:1028], sb_lam[:, :])
        dma("sync", dbg["o_tab"][:, 1028:1032], sb_nt[:, 0:4])

    b.wait("pe", sLD, v_consts)
    b.wait("pe", sDV, v_memset)
    b.op("pe", lambda E: E.matmul(PS[0][:, 0:1], sb_onesrf[:, :], sb_rl[:, :], start=True, stop=True), sPE)
    b.op("pe", lambda E: E.matmul(PS[0][:, 4:5], sb_onesrf[:, :], sb_tok0[0:1, :], start=True, stop=True), sPE)
    v_bc0 = sPE.n
    b.wait("act", sPE, v_bc0)
    b.op("act", lambda E: E.activation(sb_sig[:, :], PS[0][:, 0:1], AF.Sigmoid), sAC)
    v_sig = sAC.n
    b.wait("dve", sPE, v_bc0)
    b.op("dve", lambda E: E.tensor_copy(sb_tok0[:, :], PS[0][:, 4:5]), sDV)
    b.wait("dve", sDV, sDV.n)
    b.wait("dve", sDV, v_iota_f)
    b.op("dve", lambda E: E.tensor_scalar_add(sb_tokid[:, :], sb_tokid[:, :], sb_tok0[:, :]), sDV)
    v_p0 = sDV.n


    psU, psB0, psB1, psB2, psB3, psYa, psYb, psUa = PS
    psUb = psB0

    b.wait("pe", sG, v_bcd)
    b.wait("pe", sDV, v_p0)
    b.wait("pe", sAC, v_sig)
    b.wait("dve", sDV, v_tables)
    b.wait("dve", sDV, v_cim)

    M = {}

    def p1_chunk(k):
        t0 = k * T
        xpar = k % 2
        xsl = sb_x[:, xpar * 4096:(xpar + 1) * 4096]
        xnbuf = sb_xnb if xpar == 0 else sb_xnb1

        b.wait("act", sXL, M[("xld", k)])
        b.wait("pe", sXL, M[("xld", k)])
        if k > 0:
            b.wait("pe", sAC, M[("rbev", k - 1)])
        for kc in range(8):
            par = kc % 2
            xk = xsl[:, kc * T:(kc + 1) * T]
            xsv = WBIG2[:, XSQ + par * 512:XSQ + (par + 1) * 512]
            if kc >= 2:
                b.wait("act", sPE, M[("stmm", k, kc - 2)])
            elif k > 0:
                b.wait("act", sPE, M[("stats", k - 1)])
            b.op("act", lambda E, xk=xk, xsv=xsv: E.activation(xsv, xk, AF.Square), sAC)
            vs = sAC.n
            b.wait("pe", sAC, vs)
            b.op("pe", lambda E, kc=kc, xk=xk: E.matmul(psU[0:1, :], sb_onescb[:, :], xk, start=(kc == 0), stop=(kc == 7)), sPE)
            b.op("pe", lambda E, kc=kc, xsv=xsv: E.matmul(psU[32:33, :], sb_onescb[:, :], xsv, start=(kc == 0), stop=(kc == 7)), sPE)
            M[("stmm", k, kc)] = sPE.n
        M[("stats", k)] = sPE.n

        b.wait("act", sPE, M[("stats", k)])
        b.op("act", lambda E: E.activation(sb_rows[:, 0:512], psU[0:1, :], AF.Copy, scale=INV_D), sAC)
        vm = sAC.n
        b.wait("act", sAC, vm)
        b.op("act", lambda E: E.activation(sb_rows[:, 512:1024], sb_rows[:, 0:512], AF.Square), sAC)
        vq = sAC.n
        b.wait("dve", sAC, vq)
        b.op("dve", lambda E: E.scalar_tensor_tensor(sb_rows[:, 512:1024], psU[32:33, :], INV_D, sb_rows[:, 512:1024], ALU.mult, ALU.subtract), sDV)
        vv = sDV.n
        b.wait("act", sDV, vv)
        b.op("act", lambda E: E.activation(sb_rows[:, 1024:1536], sb_rows[:, 512:1024], AF.Ln, bias=sb_eps[:, :]), sAC)
        b.wait("act", sAC, sAC.n)
        b.op("act", lambda E: E.activation(sb_rows[:, 1024:1536], sb_rows[:, 1024:1536], AF.Exp, scale=-0.5), sAC)
        vr = sAC.n
        b.wait("dve", sAC, vr)
        b.op("dve", lambda E: E.tensor_scalar_mul(sb_rows[:, 512:1024], sb_rows[:, 0:512], -1.0), sDV)
        vnm = sDV.n

        b.wait("pe", sDV, vnm)
        b.wait("pe", sDV, vv)
        b.op("pe", lambda E: E.matmul(psU[:, :], sb_onesrf[:, :], sb_rows[:, 512:1024], start=True, stop=True), sPE)
        vb1 = sPE.n
        b.wait("act", sPE, vb1)
        if k > 0:
            b.wait("act", sDV, M[("pass1", k - 1)])
        b.op("act", lambda E: E.copy(sb_nmb[:, :], psU[:, :]), sAC)
        vnmb = sAC.n
        b.wait("dve", sAC, vnmb)
        for kc in range(8):
            b.op("dve", lambda E, kc=kc, xsl=xsl: E.tensor_add(xsl[:, kc * T:(kc + 1) * T], xsl[:, kc * T:(kc + 1) * T], sb_nmb[:, :]), sDV)
        M[("pass1", k)] = sDV.n
        b.wait("pe", sAC, vnmb)
        b.wait("pe", sAC, vr)
        b.op("pe", lambda E: E.matmul(psU[:, :], sb_onesrf[:, :], sb_rows[:, 1024:1536], start=True, stop=True), sPE)
        vb2 = sPE.n
        b.wait("act", sPE, vb2)
        if k > 0:
            b.wait("act", sDV, M[("xn", k - 1)])
        b.op("act", lambda E: E.copy(sb_rb[:, :], psU[:, :]), sAC)
        M[("rbev", k)] = sAC.n
        b.wait("dve", sAC, M[("rbev", k)])
        if k >= 2:
            b.wait("dve", sPE, M[("ymm", k - 2, 3)])
        for kc in range(8):
            b.op("dve", lambda E, kc=kc, xsl=xsl, xnbuf=xnbuf: E.tensor_mul(xnbuf[:, kc * T:(kc + 1) * T], xsl[:, kc * T:(kc + 1) * T], sb_rb[:, :]), sDV)
        M[("xn", k)] = sDV.n
        M[("xnb", k)] = M[("xn", k)]
        if debug_level >= 1:
            b.wait("sync", sDV, M[("xn", k)])
            dma("sync",
                dbg["o_xn"].rearrange("(kc p) t -> p kc t", p=128)[:, :, t0:t0 + T],
                xnbuf.rearrange("p (kc t) -> p kc t", t=T))
            M[("xnst", k)] = sLD.n

        b.wait("pe", sDV, M[("xnb", k)])
        if k >= 2:
            b.wait("pe", sAC, M[("uev", k - 2, 15)])
        elif k == 1:
            b.wait("pe", sAC, M[("buev", 0)])
        for kc in range(8):
            rsl = xnbuf[:, kc * T:(kc + 1) * T]
            for (dst, base, nh) in ((psB0, 0, 0), (psB1, 0, 1), (psB2, 2048, 0), (psB3, 2048, 1)):
                b.op("pe", lambda E, dst=dst, base=base, nh=nh, kc=kc, rsl=rsl:
                     E.matmul(dst[:, :], WBIG2[:, base + kc * 256 + nh * 128:base + kc * 256 + (nh + 1) * 128],
                              rsl, start=(kc == 0), stop=(kc == 7)), sPE)
        M[("bu", k)] = sPE.n
        b.wait("act", sPE, M[("bu", k)])
        if k > 0:
            b.wait("act", sDV, M[("scan", k - 1)])
        if debug_level >= 1 and k == 1:
            b.wait("act", sLD, M[("dmp_bu", 0)])
        for i, src in enumerate((psB0, psB1, psB2, psB3)):
            b.op("act", lambda E, i=i, src=src: E.copy(sb_bu[:, i * 512:(i + 1) * 512], src[:, :]), sAC)
        M[("buev", k)] = sAC.n

        if debug_level >= 1 and k == 0:
            b.wait("sync", sAC, M[("buev", k)])
            dma("sync", dbg["o_bu"][:, :], sb_bu[:, :])
            M[("dmp_bu", 0)] = sLD.n
        emit_scan(k)
        if debug_level >= 1 and k == 0:
            b.wait("sync", sDV, M[("scan", k)])
            dma("sync", dbg["o_st"][:, :], WBIG2[:, STB:STB + 2048])
            M[("dmp_st", 0)] = sLD.n

        if k > 0:
            emit_y(k - 1)
            emit_w1(k - 1, M[("buev", k)])

        if 2 <= k + 1 < NCH:
            b.wait("pool", sDV, M[("xn", k - 1)])
            nsl = sb_x[:, ((k + 1) % 2) * 4096:(((k + 1) % 2) + 1) * 4096]
            b.op("pool", lambda E, t0n=(k + 1) * T, nsl=nsl: E.dma_start(
                out=nsl.rearrange("p (kc t) -> p kc t", t=T),
                in_=xT.rearrange("(kc p) t -> p kc t", p=128)[:, :, t0n:t0n + T]), sXL, 16)
            M[("xld", k + 1)] = sXL.n

        if k == NCH - 1:
            emit_y(k)
            emit_w1(k, M[("buev", k)])
        if debug_level >= 1 and k == 1:
            b.wait("sync", sAC, M[("yev", 0, 3)])
            dma("sync", dbg["o_y"][:, :], WBIG2[:, YB:YB + 4096])
            M[("dmp_y", 0)] = sLD.n

    def emit_scan(k):
        stv = WBIG2[:, STB:STB + 2048] if k % 2 == 0 else sb_st1
        b.wait("dve", sAC, M[("buev", k)])
        if k >= 2:
            b.wait("dve", sPE, M[("ymm", k - 2, 3)])
        if debug_level >= 1 and k == 1:
            b.wait("dve", sLD, M[("dmp_st", 0)])
        if k % 4 == 0:
            b.op("dve", lambda E: E.memset(sb_carry[:, :], 0.0), sDV)
            b.wait("dve", sDV, sDV.n)
        for sub in range(4):
            b.op("dve", lambda E: E.tensor_mul(sb_cs[:, 0:2], sb_lam[:, 0:2], sb_carry[:, 0:2]), sDV)
            b.op("dve", lambda E: E.tensor_mul(sb_cst[:, 0:2], sb_lam[:, 2:4], sb_carry[:, 2:4]), sDV)
            b.op("dve", lambda E: E.tensor_mul(sb_cs[:, 2:4], sb_lam[:, 2:4], sb_carry[:, 0:2]), sDV)
            b.op("dve", lambda E: E.tensor_mul(sb_cst[:, 2:4], sb_lam[:, 0:2], sb_carry[:, 2:4]), sDV)
            b.wait("dve", sDV, sDV.n)
            b.op("dve", lambda E: E.tensor_sub(sb_cs[:, 0:2], sb_cs[:, 0:2], sb_cst[:, 0:2]), sDV)
            b.op("dve", lambda E: E.tensor_add(sb_cs[:, 2:4], sb_cs[:, 2:4], sb_cst[:, 2:4]), sDV)
            b.wait("dve", sDV, sDV.n)
            bur = sb_bu[:, 0:1024].rearrange("p (cc t) -> p cc t", cc=2)[:, :, sub * SUB:(sub + 1) * SUB]
            bui = sb_bu[:, 1024:2048].rearrange("p (cc t) -> p cc t", cc=2)[:, :, sub * SUB:(sub + 1) * SUB]
            a_r = scA_r.rearrange("p (cc t) -> p cc t", cc=2)
            a_i = scA_i.rearrange("p (cc t) -> p cc t", cc=2)
            tmpv = sctmp.rearrange("p (cc t) -> p cc t", cc=2)
            prv = tabPr.rearrange("p (cc t) -> p cc t", cc=2)
            piv = tabPi.rearrange("p (cc t) -> p cc t", cc=2)
            b.op("dve", lambda E, a_r=a_r, prv=prv, bur=bur: E.tensor_mul(a_r, prv, bur), sDV)
            b.op("dve", lambda E, tmpv=tmpv, piv=piv, bui=bui: E.tensor_mul(tmpv, piv, bui), sDV)
            b.op("dve", lambda E: E.tensor_sub(scA_r[:, :], scA_r[:, :], sctmp[:, :]), sDV)
            b.op("dve", lambda E, a_i=a_i, prv=prv, bui=bui: E.tensor_mul(a_i, prv, bui), sDV)
            b.op("dve", lambda E, tmpv=tmpv, piv=piv, bur=bur: E.tensor_mul(tmpv, piv, bur), sDV)
            b.op("dve", lambda E: E.tensor_add(scA_i[:, :], scA_i[:, :], sctmp[:, :]), sDV)
            for cc in range(2):
                b.op("dve", lambda E, cc=cc: E.tensor_tensor_scan(scS_r[:, cc * 128:(cc + 1) * 128], sb_ones128[:, :], scA_r[:, cc * 128:(cc + 1) * 128], sb_cs[:, cc:cc + 1], ALU.mult, ALU.add), sDV)
                b.op("dve", lambda E, cc=cc: E.tensor_tensor_scan(scS_i[:, cc * 128:(cc + 1) * 128], sb_ones128[:, :], scA_i[:, cc * 128:(cc + 1) * 128], sb_cs[:, 2 + cc:3 + cc], ALU.mult, ALU.add), sDV)
            b.op("dve", lambda E: E.tensor_mul(scT_r[:, :], tabQr[:, :], scS_r[:, :]), sDV)
            b.op("dve", lambda E: E.tensor_mul(sctmp[:, :], tabQi[:, :], scS_i[:, :]), sDV)
            b.op("dve", lambda E: E.tensor_sub(scT_r[:, :], scT_r[:, :], sctmp[:, :]), sDV)
            b.op("dve", lambda E: E.tensor_mul(scT_i[:, :], tabQr[:, :], scS_i[:, :]), sDV)
            b.op("dve", lambda E: E.tensor_mul(sctmp[:, :], tabQi[:, :], scS_r[:, :]), sDV)
            b.op("dve", lambda E: E.tensor_add(scT_i[:, :], scT_i[:, :], sctmp[:, :]), sDV)
            b.wait("dve", sDV, sDV.n)
            b.op("dve", lambda E: E.tensor_copy(sb_carry[:, 0:2], scT_r.rearrange("p (cc t) -> p cc t", cc=2)[:, :, 127:128]), sDV)
            b.op("dve", lambda E: E.tensor_copy(sb_carry[:, 2:4], scT_i.rearrange("p (cc t) -> p cc t", cc=2)[:, :, 127:128]), sDV)
            str_v = stv[:, 0:1024].rearrange("p (cc t) -> p cc t", cc=2)[:, :, sub * SUB:(sub + 1) * SUB]
            sti_v = stv[:, 1024:2048].rearrange("p (cc t) -> p cc t", cc=2)[:, :, sub * SUB:(sub + 1) * SUB]
            b.op("dve", lambda E, str_v=str_v: E.tensor_copy(str_v, scT_r.rearrange("p (cc t) -> p cc t", cc=2)), sDV)
            b.op("dve", lambda E, sti_v=sti_v: E.tensor_copy(sti_v, scT_i.rearrange("p (cc t) -> p cc t", cc=2)), sDV)
            b.wait("dve", sDV, sDV.n)
            M[("tts", k, sub)] = sDV.n
        M[("scan", k)] = sDV.n

    def emit_y(k):
        stv = WBIG2[:, STB:STB + 2048] if k % 2 == 0 else sb_st1
        ynb = sb_xnb if k % 2 == 0 else sb_xnb1
        b.wait("pe", sDV, M[("scan", k)])
        for mg in range(4):
            if mg > 0:
                b.wait("pe", sAC, M[("yev", k, mg - 1)])
            elif k > 0:
                b.wait("pe", sAC, M[("yev", k - 1, 3)])
            for mi in range(2):
                m = mg * 2 + mi
                dst = psYa if mi == 0 else psYb
                for cc in range(2):
                    b.op("pe", lambda E, dst=dst, cc=cc, m=m: E.matmul(dst[:, :], WBIG2[:, 4096 + cc * 1024 + m * 128:4096 + cc * 1024 + (m + 1) * 128], stv[:, cc * 512:(cc + 1) * 512], start=(cc == 0), stop=False), sPE)
                for cc in range(2):
                    b.op("pe", lambda E, dst=dst, cc=cc, m=m: E.matmul(dst[:, :], WBIG2[:, 6144 + cc * 1024 + m * 128:6144 + cc * 1024 + (m + 1) * 128], stv[:, 1024 + cc * 512:1024 + (cc + 1) * 512], start=False, stop=False), sPE)
                for kc in range(8):
                    b.op("pe", lambda E, dst=dst, kc=kc, m=m: E.matmul(dst[:, :], WBIG2[:, 8192 + kc * 1024 + m * 128:8192 + kc * 1024 + (m + 1) * 128], ynb[:, kc * T:(kc + 1) * T], start=False, stop=(kc == 7)), sPE)
            M[("ymm", k, mg)] = sPE.n
            b.wait("act", sPE, M[("ymm", k, mg)])
            if debug_level >= 1 and k == 1 and mg == 0:
                b.wait("act", sLD, M[("dmp_y", 0)])
            for mi in range(2):
                m = mg * 2 + mi
                src = psYa if mi == 0 else psYb
                b.op("act", lambda E, src=src, m=m: E.copy(WBIG2[:, YB + m * 512:YB + (m + 1) * 512], src[:, :]), sAC)
            M[("yev", k, mg)] = sAC.n

    def emit_w1(k, buev_wait):
        t0 = k * T
        if k == 0:
            b.wait("pe", sG, v_w1)
        b.wait("pe", sAC, buev_wait)
        b.wait("pe", sAC, M[("yev", k, 3)])
        UP = [(psUa, psB1), (psB2, psB3)]
        for hp in range(16):
            pp = hp % 2
            bk0, bk1 = UP[pp]
            if hp >= 2:
                b.wait("pe", sAC, M[("uev", k, hp - 2)])
            elif k > 0:
                b.wait("pe", sAC, M[("uev", k - 1, 15)])
            for kc in range(8):
                b.op("pe", lambda E, bk0=bk0, kc=kc, hp=hp: E.matmul(bk0[:, :], WBIG[:, kc * 4096 + (2 * hp) * 128:kc * 4096 + (2 * hp + 1) * 128], WBIG2[:, YB + kc * 512:YB + (kc + 1) * 512], start=(kc == 0), stop=(kc == 7)), sPE)
            for kc in range(8):
                b.op("pe", lambda E, bk1=bk1, kc=kc, hp=hp: E.matmul(bk1[:, :], WBIG[:, kc * 4096 + (2 * hp + 1) * 128:kc * 4096 + (2 * hp + 2) * 128], WBIG2[:, YB + kc * 512:YB + (kc + 1) * 512], start=(kc == 0), stop=(kc == 7)), sPE)
            M[("u", k, hp)] = sPE.n
            USB = 26624 + pp * 1024
            b.wait("act", sPE, M[("u", k, hp)])
            if hp >= 2:
                b.wait("act", sDV, M[("gel", k, hp - 2)])
            elif k > 0:
                b.wait("act", sDV, M[("gel", k - 1, 14 + pp)])
            b.op("act", lambda E, bk0=bk0, USB=USB: E.copy(WBIG2[:, USB:USB + 512], bk0[:, :]), sAC)
            b.op("act", lambda E, bk1=bk1, USB=USB: E.copy(WBIG2[:, USB + 512:USB + 1024], bk1[:, :]), sAC)
            M[("uev", k, hp)] = sAC.n
            if hp > 0:
                b.wait("act", sDV, M[("t2", k, hp - 1)])
            elif k > 0:
                b.wait("act", sDV, M[("t2", k - 1, 15)])
            b.op("act", lambda E, USB=USB: E.activation(sb_q[:, :], WBIG2[:, USB:USB + 1024], AF.Square, scale=SQ2C2), sAC)
            vq2 = sAC.n
            b.wait("dve", sAC, vq2)
            b.op("dve", lambda E, USB=USB: E.scalar_tensor_tensor(sb_t2[:, :], sb_q[:, :], C1x2, WBIG2[:, USB:USB + 1024], ALU.add, ALU.mult), sDV)
            vt2 = sDV.n
            M[("t2", k, hp)] = vt2
            b.wait("act", sDV, vt2)
            b.op("act", lambda E: E.activation(sb_sg2[:, :], sb_t2[:, :], AF.Sigmoid), sAC)
            vsg = sAC.n
            b.wait("dve", sAC, vsg)
            b.wait("dve", sHS[pp], M[("hst", k, hp - 2)] if hp >= 2 else (M[("hst", k - 1, 14 + pp)] if k > 0 else 0))
            b.op("dve", lambda E, pp=pp, USB=USB: E.tensor_mul(WBIG2[:, HST + pp * 1024:HST + (pp + 1) * 1024], sb_sg2[:, :], WBIG2[:, USB:USB + 1024]), sDV)
            M[("gel", k, hp)] = sDV.n
            b.wait("sync", sDV, M[("gel", k, hp)])
            b.op("sync", lambda E, hp=hp, t0=t0, pp=pp: E.dma_start(
                out=d_h.ap().rearrange("(H p) t -> p H t", p=128)[:, 2 * hp:2 * hp + 2, t0:t0 + T],
                in_=WBIG2[:, HST + pp * 1024:HST + (pp + 1) * 1024].rearrange("p (H t) -> p H t", t=T)), sHS[pp], 16)
            if debug_level >= 1:
                b.op("sync", lambda E, hp=hp, t0=t0, pp=pp: E.dma_start(
                    out=dbg["o_h"].rearrange("(H p) t -> p H t", p=128)[:, 2 * hp:2 * hp + 2, t0:t0 + T],
                    in_=WBIG2[:, HST + pp * 1024:HST + (pp + 1) * 1024].rearrange("p (H t) -> p H t", t=T)), sHS[pp], 16)
            M[("hst", k, hp)] = sHS[pp].n

    for kk in range(2):
        b.op("pool", lambda E, kk=kk: E.dma_start(
            out=sb_x[:, kk * 4096:(kk + 1) * 4096].rearrange("p (kc t) -> p kc t", t=T),
            in_=xT.rearrange("(kc p) t -> p kc t", p=128)[:, :, kk * T:(kk + 1) * T]), sXL, 16)
    M[("xld", 0)] = M[("xld", 1)] = sXL.n
    for k in range(NCH):
        p1_chunk(k)
        b.op("pool", lambda E, j=k: E.dma_start(out=d_w2bf.ap()[512 * j:512 * (j + 1), :], in_=W2_in[512 * j:512 * (j + 1), :]), sW2C, 16)
    v_w2cast = sW2C.n
    sG_pre = sG.n
    for qeng, qsem, j0, j1 in (("pool", sG, 0, 10), ("act", sW2A, 10, 21), ("sync", sW2B, 21, 32)):
        b.wait(qeng, sAC, M[("uev", NCH - 1, 15)])
        b.wait(qeng, sW2C, v_w2cast)
        for j in range(j0, j1):
            b.op(qeng, lambda E, j=j: E.dma_start(out=WBIG[:, j * 1024:(j + 1) * 1024], in_=d_w2bf.ap()[128 * j:128 * (j + 1), :]), qsem, 16)
    M[("vw2",)] = (sG.n, sW2A.n, sW2B.n)

    b.barrier("p1", extra_waits=[("sync", sHS[0], sHS[0].n), ("sync", sHS[1], sHS[1].n), ("sync", sLD, sLD.n), ("pool", sG, sG_pre)])
    print("sbuf remaining before sc1.close:", nc.sbuf_bytes_remaining)
    sc1.close()
    print("sbuf remaining after sc1.close:", nc.sbuf_bytes_remaining)

    sc2 = Scope(nc)
    sb_x2 = sc2.sb("sb_x2", [128, 4096], F32)
    sb_xn2 = sc2.sb("sb_xn2", [128, 4096], F32)
    sb_ao = sb_xn2
    sb_xnb2 = sc2.sb("sb_xnb2", [128, 4096], BF16)
    sb_cb = sc2.sb("sb_cb", [128, 2048], BF16)
    stg1 = sc2.sb("stg1", [128, 1024], F32)
    stg2 = sc2.sb("stg2", [128, 1024], BF16)
    dbgz0_t = sc2.sb("dbgz0_t", [128, 512], F32)
    rtA = sc2.sb("rtA", [128, 256], F32)
    rtB = sc2.sb("rtB", [128, 256], F32)
    psU2 = sc2.ps("psU2", [128, 512], F32)
    psZ = [sc2.ps(f"psZ{i}", [128, 512], F32) for i in range(4)]
    psLG = sc2.ps("psLG", [128, 512], F32)
    psTRf = sc2.ps("psTRf", [128, 512], F32)
    psTRb = sc2.ps("psTRb", [128, 512], BF16)

    v_w2, v_w2a, v_w2b = M[("vw2",)]
    b.wait("pe", sG, v_w2)
    b.wait("pe", sW2A, v_w2a)
    b.wait("pe", sW2B, v_w2b)
    if debug_level >= 2:
        b.wait("sync", sG, v_w2)
        b.wait("sync", sW2A, v_w2a)
        b.wait("sync", sW2B, v_w2b)
        dma("sync", dbg["o_w2"][:, 0:1024], WBIG[:, 0:1024])
        dma("sync", dbg["o_w2"][:, 1024:2048], WBIG[:, 1024:2048])

    M2 = {}

    def emit_ln2(k, src_t, tag, out_mode, pe_wait=None, src_dv=None):
        if pe_wait is not None:
            b.wait("pe", sDV, pe_wait)
        if src_dv is not None:
            b.wait("act", sDV, src_dv)
        for kc in range(8):
            par = kc % 2
            sk = src_t[:, kc * T:(kc + 1) * T]
            cbv = sb_cb[:, par * 512:(par + 1) * 512]
            sqv = sb_cb[:, 1024 + par * 512:1024 + (par + 1) * 512]
            if kc >= 2:
                b.wait("dve", sPE, M2[("stmm", tag, k, kc - 2)])
                b.wait("act", sPE, M2[("stmm", tag, k, kc - 2)])
            b.op("dve", lambda E, sk=sk, cbv=cbv: E.tensor_copy(cbv, sk), sDV)
            vc = sDV.n
            b.op("act", lambda E, sk=sk, sqv=sqv: E.activation(sqv, sk, AF.Square), sAC)
            vs = sAC.n
            b.wait("pe", sDV, vc)
            b.wait("pe", sAC, vs)
            b.op("pe", lambda E, kc=kc, cbv=cbv: E.matmul(psU2[0:1, :], sb_onescb[:, :], cbv, start=(kc == 0), stop=(kc == 7)), sPE)
            b.op("pe", lambda E, kc=kc, sqv=sqv: E.matmul(psU2[32:33, :], sb_onescb[:, :], sqv, start=(kc == 0), stop=(kc == 7)), sPE)
            M2[("stmm", tag, k, kc)] = sPE.n
        b.wait("act", sPE, sPE.n)
        b.op("act", lambda E: E.activation(sb_rows[:, 0:512], psU2[0:1, :], AF.Copy, scale=INV_D), sAC)
        vm = sAC.n
        b.wait("act", sAC, vm)
        b.op("act", lambda E: E.activation(sb_rows[:, 512:1024], sb_rows[:, 0:512], AF.Square), sAC)
        vq = sAC.n
        b.wait("dve", sAC, vq)
        b.op("dve", lambda E: E.scalar_tensor_tensor(sb_rows[:, 512:1024], psU2[32:33, :], INV_D, sb_rows[:, 512:1024], ALU.mult, ALU.subtract), sDV)
        vv = sDV.n
        b.wait("act", sDV, vv)
        b.op("act", lambda E: E.activation(sb_rows[:, 1024:1536], sb_rows[:, 512:1024], AF.Ln, bias=sb_eps[:, :]), sAC)
        b.wait("act", sAC, sAC.n)
        b.op("act", lambda E: E.activation(sb_rows[:, 1024:1536], sb_rows[:, 1024:1536], AF.Exp, scale=-0.5), sAC)
        vr = sAC.n
        b.wait("dve", sAC, vr)
        b.op("dve", lambda E: E.tensor_scalar_mul(sb_rows[:, 512:1024], sb_rows[:, 0:512], -1.0), sDV)
        vnm = sDV.n
        b.wait("pe", sDV, vnm)
        b.wait("pe", sDV, vv)
        b.op("pe", lambda E: E.matmul(psU2[:, :], sb_onesrf[:, :], sb_rows[:, 512:1024], start=True, stop=True), sPE)
        vb1 = sPE.n
        b.wait("dve", sPE, vb1)
        if out_mode == "bf":
            for kc in range(8):
                b.op("dve", lambda E, kc=kc, src_t=src_t: E.tensor_add(sb_xn2[:, kc * T:(kc + 1) * T], src_t[:, kc * T:(kc + 1) * T], psU2[:, :]), sDV)
        else:
            for kc in range(8):
                b.op("dve", lambda E, kc=kc, src_t=src_t: E.tensor_add((sb_xn2 if out_mode == "xn" else src_t)[:, kc * T:(kc + 1) * T], src_t[:, kc * T:(kc + 1) * T], psU2[:, :]), sDV)
        vsub = sDV.n
        b.wait("pe", sDV, vsub)
        b.wait("pe", sAC, vr)
        b.op("pe", lambda E: E.matmul(psU2[:, :], sb_onesrf[:, :], sb_rows[:, 1024:1536], start=True, stop=True), sPE)
        vb2 = sPE.n
        M2[("bc2", tag, k)] = sPE.n
        b.wait("dve", sPE, vb2)
        for kc in range(8):
            if out_mode == "xn":
                b.op("dve", lambda E, kc=kc: E.tensor_mul(sb_xn2[:, kc * T:(kc + 1) * T], sb_xn2[:, kc * T:(kc + 1) * T], psU2[:, :]), sDV)
            elif out_mode == "inplace":
                b.op("dve", lambda E, kc=kc, src_t=src_t: E.tensor_mul(src_t[:, kc * T:(kc + 1) * T], src_t[:, kc * T:(kc + 1) * T], psU2[:, :]), sDV)
            else:
                b.op("dve", lambda E, kc=kc: E.tensor_mul(sb_xnb2[:, kc * T:(kc + 1) * T], sb_xn2[:, kc * T:(kc + 1) * T], psU2[:, :]), sDV)
        M2[("ln", tag, k)] = sDV.n

    for k in range(NCH):
        par = k % 2
        t0 = k * T
        HOF = par * 16384
        if k >= 2:
            b.wait("act", sPE, M2[("zmm", k - 2)])
        b.op("act", lambda E, t0=t0, HOF=HOF: E.dma_start(
            out=WBIG2[:, HOF:HOF + 16384].rearrange("p (kc t) -> p kc t", t=T),
            in_=d_h.ap().rearrange("(kc p) t -> p kc t", p=128)[:, :, t0:t0 + T]), sXL, 16)
        M2[("hld", k)] = sXL.n
        if k > 0:
            b.wait("act", sPE, M2[("trx1", k - 1)])
        b.op("act", lambda E, t0=t0: E.dma_start(
            out=sb_x2.rearrange("p (kc t) -> p kc t", t=T),
            in_=xT.rearrange("(kc p) t -> p kc t", p=128)[:, :, t0:t0 + T]), sXL, 16)
        M2[("xld", k)] = sXL.n
        b.wait("dve", sXL, M2[("xld", k)])
        b.wait("act", sXL, M2[("xld", k)])
        if k > 0:
            b.wait("dve", sPE, M2[("trx1", k - 1)])
        emit_ln2(k, sb_x2, "x", "xn",
                 pe_wait=(M2[("trcpf", k - 1, 3, 1)] if k > 0 else None))
        if debug_level >= 2 and k == 0:
            b.wait("sync", sDV, M2[("ln", "x", k)])
            dma("sync", dbg["o_xn2"][:, :], sb_xn2[:, :])
            M2[("xndmp",)] = sLD.n
            b.wait("sync", sLD, M2[("hld", k)])
            dma("sync", dbg["o_hrb"][:, :], WBIG2[:, 0:16384])
        b.wait("pe", sXL, M2[("hld", k)])
        for m in range(8):
            bank = psZ[m % 4]
            if m >= 4:
                b.wait("pe", sDV, M2[("ao", k, m - 4)])
            elif k > 0:
                b.wait("pe", sDV, M2[("ao", k - 1, m + 4 - 8 + 8)])
            for kc in range(32):
                b.op("pe", lambda E, bank=bank, kc=kc, m=m, HOF=HOF: E.matmul(bank[:, :], WBIG[:, kc * 1024 + m * 128:kc * 1024 + (m + 1) * 128], WBIG2[:, HOF + kc * 512:HOF + (kc + 1) * 512], start=(kc == 0), stop=(kc == 31)), sPE)
            M2[("zmm_m", k, m)] = sPE.n
            b.wait("dve", sPE, M2[("zmm_m", k, m)])
            b.wait("dve", sDV, M2[("ln", "x", k)])
            if debug_level >= 2 and k == 0 and m == 0:
                b.wait("dve", sLD, M2[("xndmp",)])
                b.op("dve", lambda E: E.tensor_copy(dbgz0_t[:, :], psZ[0][:, :]), sDV)
                b.wait("dve", sDV, sDV.n)
                b.wait("sync", sDV, sDV.n)
                dma("sync", dbg["o_z0"][:, :], dbgz0_t[:, :])
            b.op("dve", lambda E, bank=bank, m=m: E.scalar_tensor_tensor(sb_ao[:, m * 512:(m + 1) * 512], bank[:, :], sb_sig[:, :], sb_xn2[:, m * 512:(m + 1) * 512], ALU.mult, ALU.add), sDV)
            M2[("ao", k, m)] = sDV.n
        M2[("zmm", k)] = sPE.n
        if debug_level >= 2 and k == 0:
            b.wait("sync", sDV, M2[("ao", k, 7)])
            dma("sync", dbg["o_ao"][:, :], sb_ao[:, :])
            M2[("aodmp",)] = sLD.n
        if debug_level >= 2 and k == 0:
            b.wait("dve", sLD, M2[("aodmp",)])
        emit_ln2(k, sb_ao, "ao", "inplace", pe_wait=M2[("ln", "x", k)], src_dv=M2[("ao", k, 7)])
        for kc in range(8):
            b.op("dve", lambda E, kc=kc: E.tensor_add(sb_x2[:, kc * T:(kc + 1) * T], sb_ao[:, kc * T:(kc + 1) * T], sb_x2[:, kc * T:(kc + 1) * T]), sDV)
        M2[("x1", k)] = sDV.n
        emit_ln2(k, sb_x2, "x1", "bf", pe_wait=M2[("ln", "ao", k)], src_dv=M2[("x1", k)])
        b.wait("pe", sDV, M2[("ln", "x1", k)])
        for tg in range(4):
            col = (k * 4 + tg) * 8
            for kc in range(8):
                b.op("pe", lambda E, col=col, kc=kc, tg=tg: E.matmul(psLG[:, col:col + 8], sb_xnb2[:, kc * 512 + tg * 128:kc * 512 + (tg + 1) * 128], sb_wg[:, kc * 8:(kc + 1) * 8], start=(kc == 0), stop=(kc == 7)), sPE)
        M2[("lgmm", k)] = sPE.n
        for tg in range(4):
            if tg > 0:
                b.wait("dve", sST1, M2[("x1st", k, tg - 1)])
                b.wait("dve", sST2, M2[("xn2st", k, tg - 1)])
            elif k > 0:
                b.wait("dve", sST1, M2[("x1st", k - 1, 3)])
                b.wait("dve", sST2, M2[("xn2st", k - 1, 3)])
            for half in range(2):
                trb = psTRf if half == 0 else psU2
                key = ("trcpf", k, tg - 1, half) if tg > 0 else ("trcpf", k - 1, 3, half)
                if key in M2:
                    b.wait("pe", sDV, M2[key])
                for i in range(4):
                    kc = half * 4 + i
                    b.op("pe", lambda E, i=i, kc=kc, tg=tg, trb=trb: E.transpose(trb[:, i * 128:(i + 1) * 128], sb_x2[:, kc * 512 + tg * 128:kc * 512 + (tg + 1) * 128], sb_identf[:, :]), sPE)
                vtr = sPE.n
                b.wait("dve", sPE, vtr)
                b.op("dve", lambda E, half=half, trb=trb: E.tensor_copy(stg1[:, half * 512:(half + 1) * 512], trb[:, :]), sDV)
                M2[("trcpf", k, tg, half)] = sDV.n
            b.wait("sync", sDV, M2[("trcpf", k, tg, 1)])
            if (k, tg) != (0, 0):
                pass
            b.op("sync", lambda E, t0=t0, tg=tg: E.dma_start(out=d_x1t.ap()[t0 + tg * 128:t0 + (tg + 1) * 128, :], in_=stg1[:, :]), sST1, 16)
            M2[("x1st", k, tg)] = sST1.n
            for half in range(2):
                if (k, tg, half) != (0, 0, 0):
                    b.wait("pe", sDV, M2.get(("trcpb", k, tg, half - 1), M2.get(("trcpb", k, tg - 1, 1), M2.get(("trcpb", k - 1, 3, 1), 0))))
                for i in range(4):
                    kc = half * 4 + i
                    b.op("pe", lambda E, i=i, kc=kc, tg=tg: E.transpose(psTRb[:, i * 128:(i + 1) * 128], sb_xnb2[:, kc * 512 + tg * 128:kc * 512 + (tg + 1) * 128], sb_identb[:, :]), sPE)
                vtr = sPE.n
                b.wait("dve", sPE, vtr)
                b.op("dve", lambda E, half=half: E.tensor_copy(stg2[:, half * 512:(half + 1) * 512], psTRb[:, :]), sDV)
                M2[("trcpb", k, tg, half)] = sDV.n
            b.wait("sync", sDV, M2[("trcpb", k, tg, 1)])
            b.op("sync", lambda E, t0=t0, tg=tg: E.dma_start(out=d_xn2bf.ap()[t0 + tg * 128:t0 + (tg + 1) * 128, :], in_=stg2[:, :]), sST2, 16)
            M2[("xn2st", k, tg)] = sST2.n
        M2[("trx1", k)] = sPE.n
        b.wait("dve", sST2, M2[("xn2st", k, 3)])
        if k % 2 == 1:
            cch = k // 2
            b.wait("pool", sST2, M2[("xn2st", k, 3)])
            b.wait("pool", sST1, M2[("x1st", k, 3)])
            b.op("pool", lambda E, cch=cch: E.collective_compute("AllGather", ALU.bypass,
                replica_groups=[list(range(NCORE))],
                ins=[d_xn2bf.ap()[cch * 1024:(cch + 1) * 1024, :]],
                outs=[d_xn2all.ap()[cch * 8192:(cch + 1) * 8192, :]]), sCC, 1)
            M2[("xnag", cch)] = sCC.n

    b.wait("dve", sPE, M2[("lgmm", NCH - 1)])
    sb_lg = psLG[:, 0:256]
    E8 = lambda t: t.rearrange("p (tg e) -> p tg e", e=8)
    B8 = lambda t: t.rearrange("p (tg one) -> p tg one", one=1).broadcast_to([128, 32, 8])

    def dv(fn):
        b.op("dve", fn, sDV)
        b.wait("dve", sDV, sDV.n)

    dv(lambda E: E.tensor_reduce(sb_m1[:, :], E8(sb_lg), axis=AX.X, op=ALU.max))
    dv(lambda E: E.tensor_tensor(E8(rtA), E8(sb_lg), B8(sb_m1), ALU.is_lt))
    dv(lambda E: E.scalar_tensor_tensor(rtA[:, :], rtA[:, :], 1000.0, sb_iotaE[:, :], ALU.mult, ALU.add))
    dv(lambda E: E.tensor_reduce(sb_i1[:, :], E8(rtA), axis=AX.X, op=ALU.min))
    dv(lambda E: E.tensor_tensor(E8(sb_oh1), E8(sb_iotaE), B8(sb_i1), ALU.is_equal))
    dv(lambda E: E.scalar_tensor_tensor(rtB[:, :], sb_oh1[:, :], -1e9, sb_lg[:, :], ALU.mult, ALU.add))
    dv(lambda E: E.tensor_reduce(sb_m2[:, :], E8(rtB), axis=AX.X, op=ALU.max))
    dv(lambda E: E.tensor_tensor(E8(rtA), E8(rtB), B8(sb_m2), ALU.is_lt))
    dv(lambda E: E.scalar_tensor_tensor(rtA[:, :], rtA[:, :], 1000.0, sb_iotaE[:, :], ALU.mult, ALU.add))
    dv(lambda E: E.tensor_reduce(sb_i2[:, :], E8(rtA), axis=AX.X, op=ALU.min))
    dv(lambda E: E.tensor_tensor(E8(sb_oh2), E8(sb_iotaE), B8(sb_i2), ALU.is_equal))
    dv(lambda E: E.tensor_sub(sb_m2[:, :], sb_m2[:, :], sb_m1[:, :]))
    b.wait("act", sDV, sDV.n)
    b.op("act", lambda E: E.activation(sb_g2[:, :], sb_m2[:, :], AF.Sigmoid), sAC)
    b.op("act", lambda E: E.activation(sb_g1[:, :], sb_m2[:, :], AF.Sigmoid, scale=-1.0), sAC)
    v_gates = sAC.n
    dv(lambda E: E.tensor_copy(sb_cb[:, 0:256], sb_oh1[:, :]))
    dv(lambda E: E.tensor_copy(sb_cb[:, 256:512], sb_oh2[:, :]))
    v_ohb = sDV.n
    b.wait("pe", sDV, v_ohb)
    for j in range(2):
        for tg in range(32):
            b.op("pe", lambda E, j=j, tg=tg: E.matmul(psZ[0][:, j * 256 + tg * 8:j * 256 + (tg + 1) * 8], sb_trib[:, :], sb_cb[:, j * 256 + tg * 8:j * 256 + (tg + 1) * 8], start=True, stop=True), sPE)
    for j in range(2):
        for tg in range(32):
            b.op("pe", lambda E, j=j, tg=tg: E.matmul(psZ[1][0:8, j * 32 + tg:j * 32 + tg + 1], sb_cb[:, j * 256 + tg * 8:j * 256 + (tg + 1) * 8], sb_onescb[:, :], start=True, stop=True), sPE)
    v_rkmm = sPE.n
    b.wait("dve", sPE, v_rkmm)
    dv(lambda E: E.tensor_mul(rtA[:, :], psZ[0][:, 0:256], sb_oh1[:, :]))
    dv(lambda E: E.tensor_reduce(sb_pre1[:, :], E8(rtA), axis=AX.X, op=ALU.add))
    dv(lambda E: E.tensor_mul(rtA[:, :], psZ[0][:, 256:512], sb_oh2[:, :]))
    dv(lambda E: E.tensor_reduce(sb_pre2[:, :], E8(rtA), axis=AX.X, op=ALU.add))
    dv(lambda E: E.tensor_copy(sb_ct[0:8, :], psZ[1][0:8, 0:64]))
    for j in range(2):
        dv(lambda E, j=j: E.tensor_tensor_scan(sb_tb[0:8, j * 32:(j + 1) * 32], sb_ones128[0:8, 0:32], sb_ct[0:8, j * 32:(j + 1) * 32], 0.0, ALU.mult, ALU.add))
    b.wait("sync", sDV, sDV.n)
    dma("sync", d_cnt.ap()[0:1, :], sb_tb[0:8, 31:32])
    dma("sync", d_cnt.ap()[1:2, :], sb_tb[0:8, 63:64])
    M2[("cntst",)] = sLD.n
    b.wait("dve", sLD, M2[("cntst",)])
    dv(lambda E: E.tensor_sub(sb_tb[0:8, :], sb_tb[0:8, :], sb_ct[0:8, :]))
    v_p2 = sDV.n

    if debug_level >= 2:
        b.wait("sync", sLD, M2[("xn2st", NCH - 1, 3)])
        dma("sync", dbg["o_x1t"][:, :], d_x1t.ap()[:, :])
        dma("sync", dbg["o_xn2bf"][:, :], d_xn2bf.ap()[:, :])
        b.wait("sync", sDV, v_p2)
        dma("sync", dbg["o_rt"][:, 0:32], sb_i1[:, :])
        dma("sync", dbg["o_rt"][:, 32:64], sb_i2[:, :])
        dma("sync", dbg["o_rt"][:, 64:96], sb_g1[:, :])
        dma("sync", dbg["o_rt"][:, 96:128], sb_g2[:, :])
        dma("sync", dbg["o_rt"][:, 128:160], sb_pre1[:, :])
        dma("sync", dbg["o_rt"][:, 160:192], sb_pre2[:, :])
        dma("sync", dbg["o_rt"][:, 192:256], sb_tb[:, :])
        dma("sync", dbg["o_rt"][:, 256:320], sb_ct[:, :])

    b.barrier("p2", extra_waits=[("sync", sLD, sLD.n), ("sync", sST1, sST1.n), ("sync", sST2, sST2.n), ("pool", sG, sG.n)])
    sc2.close()

    sc3 = Scope(nc)
    sb_h3 = sc3.sb("sb_h3", [128, 16384], BF16)
    sb_xfm = sc3.sb("sb_xfm", [128, 8192], BF16)
    sb_ytm = sc3.sb("sb_ytm", [128, 2048], BF16)
    sb_pmy = sc3.sb("sb_pmy", [128, 128], F32)
    sb_pidi = sc3.sb("sb_pidi", [128, 80], I32)
    sb_pidf = sc3.sb("sb_pidf", [128, 80], F32)
    sb_cnt16 = sc3.sb("sb_cnt16", [16, 8], F32)
    sb_cb2 = sc3.sb("sb_cb2", [128, 2], F32)
    sb_gst = sc3.sb("sb_gst", [128, 2048], BF16)
    sb_fc2bt = sc3.sb("sb_fc2bt", [128, 1024], F32)
    sb_fc2brow = sb_rows[0:1, 0:1024]
    psM = [sc3.ps("psM0", [128, 512], F32), sc3.ps("psM1", [128, 512], F32)]
    psY3 = [sc3.ps("psY0", [128, 512], F32), sc3.ps("psY1", [128, 512], F32)]
    psX3 = sc3.ps("psX3", [128, 512], F32)

    dma("sync", sb_fc2brow, fc2b_in[:, :])
    for kc in range(8):
        dma("pool", WBIG[:, kc * 4096:(kc + 1) * 4096], fc1T_in[128 * kc:128 * (kc + 1), :], sG)
    for hc in range(32):
        dma("pool", WBIG2[:, hc * 1024:(hc + 1) * 1024], fc2_in[128 * hc:128 * (hc + 1), :], sG)
    v_fcw = sG.n
    rg = [list(range(NCORE))]
    b.op("pool", lambda E: E.collective_compute("AllGather", ALU.bypass, replica_groups=rg,
        ins=[d_cnt.ap()[:, :]], outs=[d_cntall.ap()[:, :]]), sCC, 1)
    v_cntag = sCC.n
    v_xnag = M2[("xnag", 3)]

    b.wait("sync", sCC, v_cntag)
    dma("sync", sb_cnt16[:, :], d_cntall.ap()[:, :])
    v_c16 = sLD.n
    b.wait("pe", sLD, v_c16)
    b.op("pe", lambda E: E.matmul(psX3[0:2, 0:8], sb_cmask[:, :], sb_cnt16[:, :], start=True, stop=True), sPE)
    v_cbmm = sPE.n
    b.wait("dve", sPE, v_cbmm)
    b.op("dve", lambda E: E.tensor_copy(sb_cnt16[0:2, :], psX3[0:2, 0:8]), sDV)
    b.wait("dve", sDV, sDV.n)
    b.wait("pe", sDV, sDV.n)
    b.op("pe", lambda E: E.transpose(psX3[0:8, 8:10], sb_cnt16[0:2, 0:8], sb_identf[0:2, 0:2]), sPE)
    v_cbt = sPE.n
    b.wait("dve", sPE, v_cbt)
    b.op("dve", lambda E: E.tensor_copy(sb_cb2[0:8, :], psX3[0:8, 8:10]), sDV)
    b.wait("dve", sDV, sDV.n)
    for j in range(2):
        b.op("dve", lambda E, j=j: E.tensor_scalar_add(sb_tb[0:8, j * 32:(j + 1) * 32], sb_tb[0:8, j * 32:(j + 1) * 32], sb_cb2[0:8, j:j + 1]), sDV)
    b.wait("dve", sDV, sDV.n)
    b.wait("sync", sDV, sDV.n)
    dma("sync", d_fb.ap()[:, :], sb_tb[0:8, :])
    b.wait("sync", sLD, sLD.n)
    b.op("sync", lambda E: E.dma_start(
        out=sb_brow[0:1, 0:512].rearrange("one (j tg e) -> one j tg e", j=2, e=8),
        in_=d_fb.ap().rearrange("e (j tg) -> j tg e", j=2),
        allow_slow_non_contiguous=True), sLD, 16)
    v_brow = sLD.n
    b.wait("pe", sLD, v_brow)
    b.op("pe", lambda E: E.matmul(psX3[:, :], sb_onesrf[:, :], sb_brow[0:1, 0:512], start=True, stop=True), sPE)
    v_bb = sPE.n
    b.wait("dve", sPE, v_bb)

    def dv3(fn):
        b.op("dve", fn, sDV)
        b.wait("dve", sDV, sDV.n)

    E8 = lambda t: t.rearrange("p (tg e) -> p tg e", e=8)
    for j, ohx, prex, gx, ix in ((0, sb_oh1, sb_pre1, sb_g1, sb_i1), (1, sb_oh2, sb_pre2, sb_g2, sb_i2)):
        dv3(lambda E, j=j, ohx=ohx: E.tensor_mul(rtA2[:, :], psX3[:, j * 256:(j + 1) * 256], ohx[:, :]))
        dv3(lambda E: E.tensor_reduce(sb_m1[:, :], E8(rtA2), axis=AX.X, op=ALU.add))
        dv3(lambda E, prex=prex: E.tensor_add(sb_m1[:, :], sb_m1[:, :], prex[:, :]))
        dv3(lambda E: E.tensor_scalar(sb_m2[:, :], sb_m1[:, :], 0.5, 1.0 / 128.0, ALU.add, ALU.mult))
        dv3(lambda E: E.tensor_copy(sb_i32[:, 0:32], sb_m2[:, :]))
        dv3(lambda E: E.tensor_copy(sb_m2[:, :], sb_i32[:, 0:32]))
        dv3(lambda E: E.scalar_tensor_tensor(rtA2[:, 0:32], sb_m2[:, :], -128.0, sb_m1[:, :], ALU.mult, ALU.add))
        dv3(lambda E: E.tensor_scalar(rtA2[:, 32:64], rtA2[:, 0:32], 0.0, None, ALU.is_lt))
        dv3(lambda E: E.tensor_sub(sb_m2[:, :], sb_m2[:, :], rtA2[:, 32:64]))
        dv3(lambda E: E.scalar_tensor_tensor(rtA2[:, 0:32], sb_m2[:, :], -128.0, sb_m1[:, :], ALU.mult, ALU.add))
        dv3(lambda E: E.scalar_tensor_tensor(rtA2[:, 64:96], rtA2[:, 0:32], 80.0, sb_m2[:, :], ALU.mult, ALU.add))
        dv3(lambda E: E.tensor_scalar(rtA2[:, 96:128], sb_m1[:, :], float(CAP), None, ALU.is_lt))
        dv3(lambda E, gx=gx, j=j: E.tensor_mul(sb_gk[:, j * 32:(j + 1) * 32], gx[:, :], rtA2[:, 96:128]))
        dv3(lambda E, ix=ix: E.scalar_tensor_tensor(rtA2[:, 128:160], ix[:, :], float(CAP), rtA2[:, 64:96], ALU.mult, ALU.add))
        dv3(lambda E: E.tensor_scalar(rtA2[:, 96:128], rtA2[:, 96:128], -1.0, 1.0, ALU.mult, ALU.add))
        dv3(lambda E: E.scalar_tensor_tensor(rtA2[:, 128:160], rtA2[:, 96:128], 90000.0, rtA2[:, 128:160], ALU.mult, ALU.add))
        dv3(lambda E, j=j: E.tensor_scalar(sb_pos[:, j * 32:(j + 1) * 32], rtA2[:, 128:160], 81984.0, None, ALU.min))
        dv3(lambda E, j=j: E.tensor_copy(sb_poi[:, j * 32:(j + 1) * 32], sb_pos[:, j * 32:(j + 1) * 32]))
    v_pos = sDV.n

    b.wait("pool", sDV, v_pos)
    for j in range(2):
        for tg in range(32):
            b.op("pool", lambda E, j=j, tg=tg: E.indirect_dma_start(
                out=d_pm.ap()[:, :], out_offset=IndirectOffsetOnAxis(sb_poi[:, j * 32 + tg:j * 32 + tg + 1], 0),
                in_=sb_tokid[:, tg:tg + 1], in_offset=None), sG, 16)
    v_scat = sG.n
    b.wait("dve", sG, v_scat)
    dv3(lambda E: E.tensor_copy(rtA2[:, 0:64], sb_poi[:, :]))
    dv3(lambda E: E.tensor_scalar(rtA2[:, 64:128], rtA2[:, 0:64], 0.5, 1.0 / CAP, ALU.add, ALU.mult))
    dv3(lambda E: E.tensor_copy(sb_i32[:, 0:64], rtA2[:, 64:128]))
    dv3(lambda E: E.tensor_copy(rtA2[:, 64:128], sb_i32[:, 0:64]))
    dv3(lambda E: E.scalar_tensor_tensor(rtA2[:, 128:192], rtA2[:, 64:128], -float(CAP), rtA2[:, 0:64], ALU.mult, ALU.add))
    dv3(lambda E: E.tensor_scalar(rtA2[:, 192:256], rtA2[:, 128:192], 0.0, None, ALU.is_lt))
    dv3(lambda E: E.tensor_sub(rtA2[:, 64:128], rtA2[:, 64:128], rtA2[:, 192:256]))
    dv3(lambda E: E.scalar_tensor_tensor(rtA2[:, 128:192], rtA2[:, 64:128], -float(CAP), rtA2[:, 0:64], ALU.mult, ALU.add))
    dv3(lambda E: E.tensor_scalar(rtA2[:, 0:64], rtA2[:, 128:192], 0.5, 1.0 / 1024.0, ALU.add, ALU.mult))
    dv3(lambda E: E.tensor_copy(sb_i32[:, 0:64], rtA2[:, 0:64]))
    dv3(lambda E: E.tensor_copy(rtA2[:, 0:64], sb_i32[:, 0:64]))
    dv3(lambda E: E.scalar_tensor_tensor(rtA2[:, 192:256], rtA2[:, 0:64], -1024.0, rtA2[:, 128:192], ALU.mult, ALU.add))
    dv3(lambda E: E.tensor_scalar(sb_m1[:, 0:32], rtA2[:, 192:224], 0.0, None, ALU.is_lt))
    dv3(lambda E: E.tensor_scalar(sb_m2[:, 0:32], rtA2[:, 224:256], 0.0, None, ALU.is_lt))
    dv3(lambda E: E.tensor_sub(rtA2[:, 0:32], rtA2[:, 0:32], sb_m1[:, 0:32]))
    dv3(lambda E: E.tensor_sub(rtA2[:, 32:64], rtA2[:, 32:64], sb_m2[:, 0:32]))
    dv3(lambda E: E.scalar_tensor_tensor(rtA2[:, 192:256], rtA2[:, 0:64], -1024.0, rtA2[:, 128:192], ALU.mult, ALU.add))
    dv3(lambda E: E.tensor_scalar_mul(rtA2[:, 0:64], rtA2[:, 0:64], 8192.0))
    dv3(lambda E: E.scalar_tensor_tensor(rtA2[:, 0:64], rtA2[:, 64:128], 1024.0, rtA2[:, 0:64], ALU.mult, ALU.add))
    dv3(lambda E: E.tensor_add(rtA2[:, 0:64], rtA2[:, 0:64], rtA2[:, 192:256]))
    dv3(lambda E: E.tensor_scalar(rtA2[:, 0:64], rtA2[:, 0:64], 81919.0, None, ALU.min))
    dv3(lambda E: E.tensor_copy(sb_poi[:, :], rtA2[:, 0:64]))
    b.wait("pool", sG, v_scat)
    b.op("pool", lambda E: E.collective_compute("ReduceScatter", ALU.max, replica_groups=rg,
        ins=[d_pm.ap()[0:81920, :]], outs=[d_pmy.ap()[:, :]]), sCC, 1)
    v_rs = sCC.n

    b.wait("sync", sCC, v_rs)
    dma("sync", sb_pmy[0:80, :], d_pmy.ap().rearrange("(g p) one -> g (p one)", p=128))
    v_pmyld = sLD.n
    b.wait("pe", sLD, v_pmyld)
    b.wait("pe", sDV, v_pos)
    b.op("pe", lambda E: E.transpose(psX3[:, 0:80], sb_pmy[0:80, :], sb_identf[0:80, 0:80]), sPE)
    v_pmyt = sPE.n
    b.wait("dve", sPE, v_pmyt)
    dv3(lambda E: E.tensor_scalar(sb_pidf[:, :], psX3[:, 0:80], -1.0, 0.0, ALU.add, ALU.max))
    dv3(lambda E: E.tensor_copy(sb_pidi[:, :], sb_pidf[:, :]))
    v_pidi = sDV.n

    b.wait("pe", sDV, v_pidi)
    b.wait("pe", sLD, sLD.n)
    b.op("pe", lambda E: E.matmul(psX3[:, 0:512], sb_onesrf[:, :], sb_rows[0:1, 0:512], start=True, stop=True), sPE)
    b.op("pe", lambda E: E.matmul(psM[0][:, :], sb_onesrf[:, :], sb_rows[0:1, 512:1024], start=True, stop=True), sPE)
    v_fbb = sPE.n
    b.wait("dve", sPE, v_fbb)
    dv3(lambda E: E.tensor_copy(sb_fc2bt[:, 0:512], psX3[:, 0:512]))
    dv3(lambda E: E.tensor_copy(sb_fc2bt[:, 512:1024], psM[0][:, :]))
    v_fc2bt = sDV.n

    b.wait("pool", sDV, v_pidi)
    b.wait("pool", sCC, v_xnag)

    sGS = [b.sem(f"sGS{i}") for i in range(2)]

    def emit_gathers(g0, g1):
        for g in range(g0, g1):
            gpar = g % 2
            gsl = sb_gst[:, gpar * 1024:(gpar + 1) * 1024]
            if g >= 2:
                b.wait("pool", sGS[gpar], sGS[gpar].n)
            b.op("pool", lambda E, g=g, gsl=gsl: E.indirect_dma_start(
                out=gsl, out_offset=None,
                in_=d_xn2all.ap()[:, :], in_offset=IndirectOffsetOnAxis(sb_pidi[:, g:g + 1], 0)), sG, 16)
            M2[("gatp", g)] = sG.n
            b.wait("sync", sG, M2[("gatp", g)])
            b.op("sync", lambda E, g=g, gsl=gsl: E.dma_start(out=d_disp.ap()[g * 128:(g + 1) * 128, :], in_=gsl), sGS[gpar], 16)
            M2[("gat", g)] = (sGS[0].n, sGS[1].n)

    b.wait("pe", sG, v_fcw)
    b.wait("pe", sDV, v_fc2bt)
    NSC = CAP // 512
    emit_gathers(0, 80)
    for sc in range(NSC):
        par = sc % 2
        s0 = sc * 512
        _g4 = M2[("gat", (s0 + 511) // 128)]
        for _i in range(2):
            b.wait("sync", sGS[_i], _g4[_i])
        if sc >= 2:
            b.wait("sync", sPE, M2[("fc1", sc - 2)])
        for kc in range(8):
            b.op("sync", lambda E, kc=kc, s0=s0, par=par: E.dma_start_transpose(
                sb_xfm[:, par * 4096 + kc * 512:par * 4096 + (kc + 1) * 512],
                d_disp.ap()[s0:s0 + 512, kc * 128:(kc + 1) * 128]), sLD, 16)
        M2[("xt", sc)] = sLD.n
        b.wait("pe", sLD, M2[("xt", sc)])
        for hc in range(32):
            mb = psM[hc % 2]
            if hc >= 2:
                b.wait("pe", sAC, M2[("sil", sc, hc - 2)])
            elif sc > 0:
                b.wait("pe", sAC, M2[("sil", sc - 1, 31)])
            for kc in range(8):
                b.op("pe", lambda E, mb=mb, kc=kc, hc=hc, par=par: E.matmul(mb[:, :], WBIG[:, kc * 4096 + hc * 128:kc * 4096 + (hc + 1) * 128], sb_xfm[:, par * 4096 + kc * 512:par * 4096 + (kc + 1) * 512], start=(kc == 0), stop=(kc == 7)), sPE)
            vmm = sPE.n
            b.wait("act", sPE, vmm)
            if sc > 0 and hc == 0:
                b.wait("act", sPE, M2[("fc2", sc - 1)])
            b.op("act", lambda E, mb=mb, hc=hc: E.activation(sb_h3[:, hc * 512:(hc + 1) * 512], mb[:, :], AF.Silu, bias=sb_fc1b[:, hc:hc + 1]), sAC)
            M2[("sil", sc, hc)] = sAC.n
        M2[("fc1", sc)] = sPE.n
        b.wait("pe", sAC, M2[("sil", sc, 31)])
        b.wait("pe", sDV, v_fc2bt)
        for sg in range(4):
            par2 = sg % 2
            for dh in range(2):
                yb = psY3[dh]
                if sc + sg > 0:
                    b.wait("pe", sDV, M2.get(("yev", sc, sg - 1, dh), M2.get(("yev", sc - 1, 3, dh), 0)))
                for hc in range(32):
                    b.op("pe", lambda E, yb=yb, hc=hc, sg=sg, dh=dh: E.matmul(yb[:, :], sb_h3[:, hc * 512 + sg * 128:hc * 512 + (sg + 1) * 128], WBIG2[:, hc * 1024 + dh * 512:hc * 1024 + (dh + 1) * 512], start=(hc == 0), stop=(hc == 31)), sPE)
                vy = sPE.n
                b.wait("dve", sPE, vy)
                if sc + sg > 0:
                    _pv = M2.get(("yst", sc, sg - 2), M2.get(("yst", sc - 1, 2 + sg), 0))
                    b.wait("dve", sYS[sg % 2], _pv)
                b.op("dve", lambda E, yb=yb, dh=dh, par2=par2: E.scalar_tensor_tensor(sb_ytm[:, par2 * 1024 + dh * 512:par2 * 1024 + (dh + 1) * 512], yb[:, :], 1.0, sb_fc2bt[:, dh * 512:(dh + 1) * 512], ALU.mult, ALU.add), sDV)
                M2[("yev", sc, sg, dh)] = sDV.n
            b.wait("sync", sDV, M2[("yev", sc, sg, 1)])
            b.op("sync", lambda E, s0=s0, sg=sg, par2=par2: E.dma_start(out=d_y.ap()[s0 + sg * 128:s0 + (sg + 1) * 128, :], in_=sb_ytm[:, par2 * 1024:(par2 + 1) * 1024]), sYS[par2], 16)
            M2[("yst", sc, sg)] = sYS[par2].n
        M2[("fc2", sc)] = sPE.n
        if sc % 2 == 1:
            ych = sc // 2
            b.wait("pool", sYS[0], sYS[0].n)
            b.wait("pool", sYS[1], sYS[1].n)
            b.wait("pool", sG, sG.n)
            b.wait("pool", sGS[0], sGS[0].n)
            b.wait("pool", sGS[1], sGS[1].n)
            b.op("pool", lambda E, ych=ych: E.collective_compute("AllGather", ALU.bypass,
                replica_groups=[list(range(NCORE))],
                ins=[d_y.ap()[ych * 1024:(ych + 1) * 1024, :]],
                outs=[d_yall.ap()[ych * 8192:(ych + 1) * 8192, :]]), sCC, 1)
            M2[("yag", ych)] = sCC.n


    if debug_level >= 3:
        b.wait("sync", sYS[0], sYS[0].n)
        b.wait("sync", sYS[1], sYS[1].n)
        dma("sync", dbg["o_y3"][:, :], d_y.ap()[:, :])
        dma("sync", dbg["o_disp"][:, :], d_disp.ap()[:, :])
        b.wait("sync", sCC, v_rs)
        dma("sync", dbg["o_pmy"][:, :], d_pmy.ap()[:, :])
        dma("sync", dbg["o_posd"][:, :], sb_pos[:, :])
        dma("sync", dbg["o_gkd"][:, :], sb_gk[:, :])
        dma("sync", dbg["o_cnt"][:, :], d_cntall.ap()[:, :])
        dma("sync", dbg["o_cb2"][:, :], sb_cb2[:, :])

    b.barrier("p3", extra_waits=[("sync", sLD, sLD.n), ("pool", sG, sG.n), ("pool", sCC, sCC.n)])
    sc3.close()
    scW.close()

    sc4 = Scope(nc)
    x1b = sc4.sb("x1b", [128, 8192], F32)
    yb = sc4.sb("yb", [128, 8192], BF16)
    w_scr = sc4.sb("w_scr", [128, 4096], F32)
    wA = sc4.sb("wA", [128, 4096], F32)
    wB = sc4.sb("wB", [128, 4096], F32)
    outb = sc4.sb("outb", [128, 8192], F32)
    gam4 = sc4.sb("gam4", [128, 4096], F32)
    bet4 = sc4.sb("bet4", [128, 4096], F32)
    sb_grow = sc4.sb("sb_grow", [1, 1024], F32)
    sb_brow4 = sc4.sb("sb_brow4", [1, 1024], F32)
    c4 = sc4.sb("c4", [128, 32], F32)
    xnb4 = sc4.sb("xnb4", [128, 4096], BF16)
    sqscr = sc4.sb("sqscr", [128, 1024], F32)
    sb_eps4 = sc4.sb("sb_eps4", [128, 1], F32)
    psB4 = sc4.ps("psB4", [128, 512], F32)

    dma("sync", sb_grow[:, :], lng_in[:, :])
    dma("sync", sb_brow4[:, :], lnb_in[:, :])
    v_gb = sLD.n
    b.op("dve", lambda E: E.memset(sb_eps4[:, :], EPS), sDV)
    v_ms4 = sDV.n
    b.wait("pe", sLD, v_gb)
    b.wait("pe", sDV, v_ms4)
    for rep in range(4):
        for half in range(2):
            b.op("pe", lambda E, half=half: E.matmul(psB4[:, :], sb_onesrf[:, :], sb_grow[0:1, half * 512:(half + 1) * 512], start=True, stop=True), sPE)
            v = sPE.n
            b.wait("dve", sPE, v)
            b.op("dve", lambda E, rep=rep, half=half: E.tensor_copy(gam4[:, rep * 1024 + half * 512:rep * 1024 + (half + 1) * 512], psB4[:, :]), sDV)
            b.wait("pe", sDV, sDV.n)
            b.op("pe", lambda E, half=half: E.matmul(psB4[:, :], sb_onesrf[:, :], sb_brow4[0:1, half * 512:(half + 1) * 512], start=True, stop=True), sPE)
            v = sPE.n
            b.wait("dve", sPE, v)
            b.op("dve", lambda E, rep=rep, half=half: E.tensor_copy(bet4[:, rep * 1024 + half * 512:rep * 1024 + (half + 1) * 512], psB4[:, :]), sDV)
            b.wait("pe", sDV, sDV.n)
    v_gb4 = sDV.n
    b.wait("pool", sDV, v_gb4)

    NB = 8
    GV = lambda t: t.rearrange("p (g d) -> p g d", g=4)

    def ln_stats(srcv, sqkey=None):
        b.op("dve", lambda E, srcv=srcv: E.tensor_reduce(c4[:, 0:4], GV(srcv), axis=AX.X, op=ALU.add), sDV)
        vr1 = sDV.n
        for g in range(4):
            b.op("act", lambda E, srcv=srcv, g=g: E.activation(sqscr[:, :], srcv[:, g * 1024:(g + 1) * 1024], AF.Square, accum_out=c4[:, 4 + g:5 + g]), sAC)
        vsq = sAC.n
        if sqkey is not None:
            M2[sqkey] = vsq
        b.wait("dve", sDV, vr1)
        b.op("dve", lambda E: E.tensor_scalar_mul(c4[:, 8:12], c4[:, 0:4], INV_D), sDV)
        b.wait("dve", sDV, sDV.n)
        b.op("dve", lambda E: E.tensor_mul(c4[:, 12:16], c4[:, 8:12], c4[:, 8:12]), sDV)
        b.wait("dve", sAC, vsq)
        b.wait("dve", sDV, sDV.n)
        b.op("dve", lambda E: E.scalar_tensor_tensor(c4[:, 4:8], c4[:, 4:8], INV_D, c4[:, 12:16], ALU.mult, ALU.subtract), sDV)
        vvar = sDV.n
        b.wait("act", sDV, vvar)
        b.op("act", lambda E: E.activation(c4[:, 16:20], c4[:, 4:8], AF.Ln, bias=sb_eps4[:, :]), sAC)
        b.wait("act", sAC, sAC.n)
        b.op("act", lambda E: E.activation(c4[:, 16:20], c4[:, 16:20], AF.Exp, scale=-0.5), sAC)
        vrstd = sAC.n
        b.wait("dve", sAC, vrstd)
        b.op("dve", lambda E: E.scalar_tensor_tensor(c4[:, 12:16], c4[:, 8:12], -1.0, c4[:, 16:20], ALU.mult, ALU.mult), sDV)
        b.wait("dve", sDV, sDV.n)
        return sDV.n

    def ln_apply(srcv, dstv):
        for g in range(4):
            b.op("dve", lambda E, srcv=srcv, dstv=dstv, g=g: E.tensor_scalar(dstv[:, g * 1024:(g + 1) * 1024], srcv[:, g * 1024:(g + 1) * 1024], c4[:, 16 + g:17 + g], c4[:, 12 + g:13 + g], ALU.mult, ALU.add), sDV)
        return sDV.n

    for bi in range(NB):
        par = bi % 2
        x1v = x1b[:, par * 4096:(par + 1) * 4096]
        outv = outb[:, par * 4096:(par + 1) * 4096]
        if bi >= 2:
            b.wait("sync", sPO, M2[("fin", bi - 2)])
            b.wait("sync", sOS[par], M2[("ost", bi - 2)])
        b.op("sync", lambda E, bi=bi, x1v=x1v: E.dma_start(
            out=GV(x1v),
            in_=d_x1t.ap().rearrange("(G p) d -> p G d", p=128)[:, 4 * bi:4 * bi + 4, :]), sLD, 16)
        M2[("x1ld", bi)] = sLD.n
        if bi > 0:
            b.wait("sync", sDV, M2[("g1", bi - 1)])
        b.op("sync", lambda E, bi=bi: E.dma_start(
            out=GV(xnb4),
            in_=d_xn2bf.ap().rearrange("(G p) d -> p G d", p=128)[:, 4 * bi:4 * bi + 4, :]), sLD, 16)
        M2[("xnld", bi)] = sLD.n
        if bi > 0:
            b.wait("pool", sDV, M2[("gmul", bi - 1)])
        for g in range(4):
            tg = 4 * bi + g
            b.op("pool", lambda E, tg=tg, g=g: E.indirect_dma_start(
                out=yb[:, g * 1024:(g + 1) * 1024], out_offset=None, in_=d_yall.ap()[:, :],
                in_offset=IndirectOffsetOnAxis(sb_poi[:, tg:tg + 1], 0)), sG, 16)
            b.op("pool", lambda E, tg=tg, g=g: E.indirect_dma_start(
                out=yb[:, 4096 + g * 1024:4096 + (g + 1) * 1024], out_offset=None, in_=d_yall.ap()[:, :],
                in_offset=IndirectOffsetOnAxis(sb_poi[:, 32 + tg:33 + tg], 0)), sG, 16)
        M2[("ygat", bi)] = sG.n
        b.wait("dve", sG, M2[("ygat", bi)])
        b.wait("dve", sLD, M2[("xnld", bi)])
        if bi >= 1:
            b.wait("dve", sAC, M2[("sqA", bi - 1)])
        for g in range(4):
            tg = 4 * bi + g
            b.op("dve", lambda E, g=g, tg=tg: E.scalar_tensor_tensor(wA[:, g * 1024:(g + 1) * 1024], yb[:, g * 1024:(g + 1) * 1024], sb_gk[:, tg:tg + 1], xnb4[:, g * 1024:(g + 1) * 1024], ALU.mult, ALU.add), sDV)
        M2[("g1", bi)] = sDV.n
        if bi >= 1:
            b.wait("dve", sAC, M2[("sqB", bi - 1)])
            b.wait("dve", sPO, M2[("ob", bi - 1)])
        for g in range(4):
            tg = 4 * bi + g
            b.op("dve", lambda E, g=g, tg=tg: E.tensor_scalar_mul(wB[:, g * 1024:(g + 1) * 1024], yb[:, 4096 + g * 1024:4096 + (g + 1) * 1024], sb_gk[:, 32 + tg:33 + tg]), sDV)
        M2[("gmul", bi)] = sDV.n
        b.op("dve", lambda E: E.tensor_add(wA[:, :], wA[:, :], wB[:, :]), sDV)
        M2[("res1", bi)] = sDV.n
        b.wait("act", sDV, M2[("res1", bi)])
        ln_stats(wA, ("sqA", bi))
        ln_apply(wA, wB)
        b.op("dve", lambda E: E.tensor_mul(wB[:, :], wB[:, :], gam4[:, :]), sDV)
        v_gm = sDV.n
        b.wait("pool", sDV, v_gm)
        b.op("pool", lambda E: E.tensor_add(wB[:, :], wB[:, :], bet4[:, :]), sPO)
        M2[("ob", bi)] = sPO.n
        b.wait("dve", sPO, M2[("ob", bi)])
        b.wait("act", sPO, M2[("ob", bi)])
        ln_stats(wB, ("sqB", bi))
        b.wait("dve", sLD, M2[("x1ld", bi)])
        if bi >= 1:
            b.wait("dve", sPO, M2[("fin", bi - 1)])
        ln_apply(wB, w_scr)
        v_a3 = sDV.n
        b.wait("pool", sDV, v_a3)
        if bi >= 2:
            b.wait("pool", sOS[par], M2[("ost", bi - 2)])
        b.op("pool", lambda E, outv=outv, x1v=x1v: E.tensor_add(outv, w_scr[:, :], x1v), sPO)
        M2[("fin", bi)] = sPO.n
        b.wait("sync", sPO, M2[("fin", bi)])
        b.op("sync", lambda E, bi=bi, outv=outv: E.dma_start(
            out=out.rearrange("(G p) d -> p G d", p=128)[:, 4 * bi:4 * bi + 4, :],
            in_=GV(outv)), sOS[par], 16)
        M2[("ost", bi)] = sOS[par].n

    b.wait("sync", sOS[0], sOS[0].n)
    b.wait("sync", sOS[1], sOS[1].n)
    b.wait("pool", sOS[0], sOS[0].n)
    b.wait("pool", sOS[1], sOS[1].n)

    return nc, b, ap, dbg, out


def finalize(nc, b):
    with nc.Block() as blk:
        @blk.tensor
        def _(E):
            for f in b.streams["pe"]:
                f(E)
        @blk.vector
        def _(E):
            for f in b.streams["dve"]:
                f(E)
        @blk.scalar
        def _(E):
            for f in b.streams["act"]:
                f(E)
        @blk.gpsimd
        def _(E):
            for f in b.streams["pool"]:
                f(E)
        @blk.sync
        def _(E):
            for f in b.streams["sync"]:
                f(E)
    return nc


def host_inputs(core, full):
    x = full["L0_attn_x3d"].reshape(NCORE * NLOC, D)
    m = {}
    m["xT"] = np.ascontiguousarray(x[core * NLOC:(core + 1) * NLOC].T)
    m["BreT"] = np.ascontiguousarray(full["L0_attn_B_re"].T)
    m["BimT"] = np.ascontiguousarray(full["L0_attn_B_im"].T)
    m["CreT"] = np.ascontiguousarray(full["L0_attn_C_re"].T)
    m["CimT"] = np.ascontiguousarray(full["L0_attn_C_im"].T)
    m["DT"] = np.ascontiguousarray(full["L0_attn_D_mat"].T)
    m["W1"] = np.ascontiguousarray(full["L0_attn_W1"])
    m["W2"] = np.ascontiguousarray(full["L0_attn_W2"])
    m["wg"] = np.ascontiguousarray(full["L0_ffn_wg"])
    m["fc1T"] = np.ascontiguousarray(full["L0_ffn_fc1_w"][core].T)
    m["fc2"] = np.ascontiguousarray(full["L0_ffn_fc2_w"][core])
    m["fc1b"] = np.ascontiguousarray(full["L0_ffn_fc1_b"][core].reshape(32, 128).T)
    m["fc2b"] = np.ascontiguousarray(full["L0_ffn_fc2_b"][core].reshape(1, D))
    m["lng"] = np.ascontiguousarray(full["L0_ffn_ln_gamma"].reshape(1, D))
    m["lnb"] = np.ascontiguousarray(full["L0_ffn_ln_beta"].reshape(1, D))
    m["nu"] = np.ascontiguousarray(full["L0_attn_nu_log"].reshape(2, 128).T)
    m["th"] = np.ascontiguousarray(full["L0_attn_theta_log"].reshape(2, 128).T)
    m["rl"] = np.asarray(full["L0_attn_res_logit"], np.float32).reshape(1, 1)
    cm = np.zeros((16, 2), np.float32)
    for cp in range(8):
        for jp in range(2):
            rr = 2 * cp + jp
            if jp == 0 and cp < core:
                cm[rr, 0] = 1.0
            if (jp == 1 and cp < core) or jp == 0:
                cm[rr, 1] = 1.0
    m["cmask"] = np.ascontiguousarray(cm)
    m["tok0"] = np.asarray([[core * 1024]], np.float32)
    m["ident_f"] = np.eye(128, dtype=np.float32)
    m["ident_b"] = np.eye(128, dtype=BF)
    m["tri_b"] = np.triu(np.ones((128, 128), dtype=np.float32), 1).astype(BF)
    return m




_BUILD_CACHE = {}


def kernel(**inputs):
    full = {k: np.asarray(v) for k, v in inputs.items()}
    if "prog" not in _BUILD_CACHE:
        nc, b, ap, dbg, out_ap = build_program(debug_level=0)
        nc = finalize(nc, b)
        _BUILD_CACHE["prog"] = nc
    nc = _BUILD_CACHE["prog"]
    in_maps = [host_inputs(c, full) for c in range(NCORE)]
    res = run_bass_kernel_spmd(nc, in_maps, core_ids=list(range(NCORE)))
    out = np.concatenate([res.results[c]["out"] for c in range(NCORE)], 0)
    return out.reshape(16, 2048, D)
